# version 15
# baseline (speedup 1.0000x reference)
import numpy as np
import ml_dtypes
from contextlib import ExitStack

import concourse.bass as bass
import concourse.mybir as mybir
from concourse.bass_utils import run_bass_kernel_spmd

F32 = mybir.dt.float32
BF16 = mybir.dt.bfloat16
AF = mybir.ActivationFunctionType
ALU = mybir.AluOpType

T = 4096
D = 1024
NC = 8
W = T + 1
QB = 512
NQB = T // QB
EPS = 1e-5

LEVEL = 4
P2_STAGE = 4
DEBUG_ZT = False
DBG_QB = 0


class Ctr:
    _registry = {}

    def __init__(self, en, nc, name):
        self.sem = en(nc.semaphore(name))
        self.val = 0

    @classmethod
    def get(cls, en, nc, name):
        if name not in cls._registry:
            cls._registry[name] = Ctr(en, nc, name)
        return cls._registry[name]

    @classmethod
    def reset_registry(cls):
        cls._registry = {}


def _ln_stats(nc, v, s, t, ones_bf, rb_slices, sq_slices, ps_s, ps_q, s_mu, s_var, s_rs,
              c_rdy, c_sq, c_st, c_vs, c_rs):
    n = len(rb_slices)
    for c in range(n):
        t.matmul(ps_s, ones_bf, rb_slices[c], start=(c == 0), stop=(c == n - 1))
    for c in range(n):
        inst = t.matmul(ps_q, ones_bf, sq_slices[c], start=(c == 0), stop=(c == n - 1))
    inst.then_inc(c_st.sem, 2)
    c_st.val += 2




def _p1_outer_attn(nc, en_sem, xf0, xb0, xf1, xb1, qt_hbm, wq, wk, wv, wo, bout_d,
                   mask_diag_o, ident_f32, ident_bf, ones_bf, ones64f, sel65_f,
                   dbg_zt=None):
    NH = 16
    ctx_p1 = ExitStack()
    ctx_a = ExitStack()
    enp = ctx_p1.enter_context
    ena = ctx_a.enter_context

    wo_bf = enp(nc.sbuf_tensor("p1_wo", [128, NC * D], BF16))
    bout_cols = enp(nc.sbuf_tensor("p1_bout", [128, NC], F32))
    kt_bf = enp(nc.sbuf_tensor("p1_kt", [128, 4 * T], BF16))
    v_bf = enp(nc.sbuf_tensor("p1_v", [128, 32 * 520], BF16))
    wq_bf = ena(nc.sbuf_tensor("p1_wq", [128, NC * D], BF16))
    wk_bf = ena(nc.sbuf_tensor("p1_wk", [128, NC * 512], BF16))
    wv_bf = ena(nc.sbuf_tensor("p1_wv", [128, NC * 512], BF16))
    xq = ena(nc.sbuf_tensor("p1_xq", [128, 2 * NC * 512], BF16))
    qt_stage = ena(nc.sbuf_tensor("p1_qst", [128, 4 * 512], BF16))

    ps_a = [ena(nc.psum_tensor(f"p1_psa{i}", [128, 512], F32)) for i in range(2)]
    ps_b = [ena(nc.psum_tensor(f"p1_psb{i}", [128, 512], F32)) for i in range(2)]

    wl = Ctr(en_sem, nc, "p1_wl")
    ld2 = [Ctr(en_sem, nc, f"p1_ld{i}") for i in range(2)]
    tq = Ctr(en_sem, nc, "p1_tq")
    tv = Ctr(en_sem, nc, "p1_tv")
    se = Ctr(en_sem, nc, "p1_se")
    vv = Ctr(en_sem, nc, "p1_vv")
    sq4 = [Ctr(en_sem, nc, f"p1_sq{i}") for i in range(4)]

    with nc.Block() as blk:
        @blk.gpsimd
        def _(g):
            for c in range(NC):
                g.dma_start(wq_bf[:, c * D:(c + 1) * D],
                            wq[c * 128:(c + 1) * 128, :]).then_inc(wl.sem, 16)
                g.dma_start(wk_bf[:, c * 512:(c + 1) * 512],
                            wk[c * 128:(c + 1) * 128, :]).then_inc(wl.sem, 16)
                g.dma_start(wv_bf[:, c * 512:(c + 1) * 512],
                            wv[c * 128:(c + 1) * 128, :]).then_inc(wl.sem, 16)
                g.dma_start(wo_bf[:, c * D:(c + 1) * D],
                            wo[c * 128:(c + 1) * 128, :]).then_inc(wl.sem, 16)
            g.dma_start(bout_cols[:, :], bout_d).then_inc(wl.sem, 16)
            wl.val += (4 * NC + 1) * 16
            g.memset(bass.AP(v_bf, 64, [[32 * 520, 128], [520, 32], [65, 8], [1, 1]]), 1.0)

        @blk.sync
        def _(sy):
            for nb in range(NQB):
                if nb >= 2:
                    sy.wait_ge(tq.sem, (nb - 1) * 12)
                    sy.wait_ge(tv.sem, (nb - 1) * 4)
                sy.dma_start(xq[:, (nb % 2) * NC * 512:(nb % 2) * NC * 512 + NC * 512]
                             .rearrange("p (c j) -> p c j", c=NC),
                             xb0[:, 1 + nb * 512:1 + (nb + 1) * 512]
                             .rearrange("(c p) j -> p c j", p=128)).then_inc(ld2[nb % 2].sem, 16)
                ld2[nb % 2].val += 16
                for mc in range(NC):
                    qidx = nb * NC + mc
                    sy.wait_ge(se.sem, nb * 12 + mc + 1)
                    sy.dma_start(qt_hbm[mc * 128:(mc + 1) * 128, nb * 512:(nb + 1) * 512],
                                 qt_stage[:, (qidx % 4) * 512:(qidx % 4) * 512 + 512]
                                 ).then_inc(sq4[qidx % 4].sem, 16)
                    sq4[qidx % 4].val += 16
            for i in range(4):
                sy.wait_ge(sq4[i].sem, sq4[i].val)

        @blk.tensor
        def _(t):
            t.wait_ge(wl.sem, wl.val)
            for nb in range(NQB):
                t.wait_ge(ld2[nb % 2].sem, (nb // 2 + 1) * 16)
                xs = (nb % 2) * NC * 512
                for mc in range(8):
                    gq = nb * 12 + mc
                    if gq >= 2:
                        t.wait_ge(se.sem, gq - 1)
                    for kc in range(NC):
                        inst = t.matmul(ps_a[gq % 2][:, :],
                                        wq_bf[:, kc * D + mc * 128:kc * D + mc * 128 + 128],
                                        xq[:, xs + kc * 512:xs + (kc + 1) * 512],
                                        start=(kc == 0), stop=(kc == NC - 1))
                    inst.then_inc(tq.sem, 1)
                    tq.val += 1
                for mc in range(4):
                    gq = nb * 12 + 8 + mc
                    if gq >= 2:
                        t.wait_ge(se.sem, gq - 1)
                    for kc in range(NC):
                        inst = t.matmul(ps_a[gq % 2][:, :],
                                        wk_bf[:, kc * 512 + mc * 128:kc * 512 + mc * 128 + 128],
                                        xq[:, xs + kc * 512:xs + (kc + 1) * 512],
                                        start=(kc == 0), stop=(kc == NC - 1))
                    inst.then_inc(tq.sem, 1)
                    tq.val += 1
                for tt in range(4):
                    gv = nb * 4 + tt
                    if gv >= 2:
                        t.wait_ge(vv.sem, gv - 1)
                    for kc in range(NC):
                        inst = t.matmul(ps_b[gv % 2][:, :],
                                        xq[:, xs + kc * 512 + tt * 128:xs + kc * 512 + tt * 128 + 128],
                                        wv_bf[:, kc * 512:(kc + 1) * 512],
                                        start=(kc == 0), stop=(kc == NC - 1))
                    inst.then_inc(tv.sem, 1)
                    tv.val += 1

        @blk.scalar
        def _(s):
            for nb in range(NQB):
                for i in range(12):
                    gq = nb * 12 + i
                    s.wait_ge(tq.sem, gq + 1)
                    if i < 8:
                        qidx = nb * NC + i
                        if qidx >= 4:
                            s.wait_ge(sq4[qidx % 4].sem, (qidx // 4) * 16)
                        s.copy(qt_stage[:, (qidx % 4) * 512:(qidx % 4) * 512 + 512],
                               ps_a[gq % 2][:, :]).then_inc(se.sem, 1)
                    else:
                        mc = i - 8
                        s.copy(kt_bf[:, mc * T + nb * 512:mc * T + (nb + 1) * 512],
                               ps_a[gq % 2][:, :]).then_inc(se.sem, 1)
                    se.val += 1

        @blk.vector
        def _(v):
            for nb in range(NQB):
                for tt in range(4):
                    gv = nb * 4 + tt
                    v.wait_ge(tv.sem, gv + 1)
                    base = (nb * 4 + tt) * 520
                    v.tensor_copy(
                        bass.AP(v_bf, base, [[32 * 520, 128], [65, 8], [1, 64]]),
                        ps_b[gv % 2][:, :].rearrange("p (g e) -> p g e", g=8)
                        ).then_inc(vv.sem, 1)
                    vv.val += 1

    ctx_a.close()
    en = enp
    qtb = en(nc.sbuf_tensor("p1_qtb", [128, 2 * NC * 512], BF16))
    x0f = en(nc.sbuf_tensor("p1_x0f", [128, NC * 512], F32))
    pt = [en(nc.sbuf_tensor(f"p1_pt{i}", [128, 512], BF16)) for i in range(8)]
    dn_s = en(nc.sbuf_tensor("p1_dn", [128, 512], BF16))
    dn_l = en(nc.sbuf_tensor("p1_dnl", [128, 512], F32))
    bc_s = en(nc.sbuf_tensor("p1_bc", [128, 512], F32))
    zt_bf = en(nc.sbuf_tensor("p1_zt", [128, 2 * NC * 512], BF16))
    mk = en(nc.sbuf_tensor("p1_mk", [128, 128], BF16))
    rb_bf = en(nc.sbuf_tensor("p1_rbb", [128, NC * 512], BF16))
    sq_bf = en(nc.sbuf_tensor("p1_sqb", [128, NC * 512], BF16))
    s_mu = en(nc.sbuf_tensor("p1_mu", [128, 512], F32))
    s_var = en(nc.sbuf_tensor("p1_var", [128, 512], F32))
    s_rs = en(nc.sbuf_tensor("p1_rs", [128, 512], F32))
    s_tmp = en(nc.sbuf_tensor("p1_tmp", [128, 512], F32))
    x1_st = en(nc.sbuf_tensor("p1_x1st", [128, 2 * NC * 512], F32))

    ps_s = [en(nc.psum_tensor(f"p1_pss{i}", [128, 512], F32)) for i in range(3)]
    ps_z = [en(nc.psum_tensor(f"p1_psz{i}", [128, 512], F32)) for i in range(2)]
    ps_bc = [en(nc.psum_tensor(f"p1_psbc{i}", [128, 512], F32)) for i in range(1)]
    ps_t6 = [en(nc.psum_tensor(f"p1_pst6{i}", [128, 512], F32)) for i in range(2)]

    lqq = Ctr.get(en_sem, nc, "p1b_lqq")
    lqq2 = Ctr.get(en_sem, nc, "p1b_lqq2")
    gpx = Ctr.get(en_sem, nc, "p1b_gpx")
    lqx = Ctr.get(en_sem, nc, "p1b_lqx")
    tS = Ctr.get(en_sem, nc, "g2")
    vM = Ctr.get(en_sem, nc, "g3")
    sE = Ctr.get(en_sem, nc, "g4")
    tP = Ctr.get(en_sem, nc, "g5")
    vR = Ctr.get(en_sem, nc, "g6")
    tB = Ctr.get(en_sem, nc, "g7")
    vZ = Ctr.get(en_sem, nc, "g8")
    tO = Ctr.get(en_sem, nc, "g9")
    vRb = Ctr.get(en_sem, nc, "g10")
    sRb = Ctr.get(en_sem, nc, "g11")
    vSq = Ctr.get(en_sem, nc, "g12")
    tT = Ctr.get(en_sem, nc, "g13")
    vVs = Ctr.get(en_sem, nc, "g14")
    sRs = Ctr.get(en_sem, nc, "g15")
    vXn = Ctr.get(en_sem, nc, "g16")
    sXb = Ctr.get(en_sem, nc, "g17")
    stx = Ctr.get(en_sem, nc, "p1b_stx")
    vch = Ctr.get(en_sem, nc, "g19")
    sch = Ctr.get(en_sem, nc, "g20")
    vBC = Ctr.get(en_sem, nc, "g21")

    with nc.Block() as blk:
        @blk.sync
        def _(sy):
            sy.dma_start(mk[:, :], mask_diag_o).then_inc(lqq.sem, 16)
            sy.wait_ge(lqq.sem, 16)

        @blk.vector
        def _(v):
            v.memset(dn_s[:, :], 0.0)
    lqq.val += 16

    nkts = [4 * qb + 4 for qb in range(NQB)]
    cumS = [0]
    for qb in range(NQB):
        cumS.append(cumS[-1] + 16 * nkts[qb])

    def gidx(qb, h, kt):
        return cumS[qb] + h * nkts[qb] + kt

    PAIRS = [(0, 2), (1, 3), (4, 6), (5, 7), (8, 10), (9, 11), (12, 14), (13, 15)]
    SIDX, PIDX, MSIDX, ORD_S = {}, {}, {}, {}
    for qb in range(NQB):
        nkt = nkts[qb]
        order_S, order_PV = [], []
        for hA, hB in PAIRS:
            for kt in range(nkt):
                order_S += [(hA, kt), (hB, kt)]
                if kt >= 1:
                    order_PV += [(hA, kt - 1), (hB, kt - 1)]
            order_PV += [(hA, nkt - 1), (hB, nkt - 1)]
        sidx = {u: cumS[qb] + i + 1 for i, u in enumerate(order_S)}
        pidx = {u: cumS[qb] + i + 1 for i, u in enumerate(order_PV)}
        em_order = []
        for hA, hB in PAIRS:
            for h_ in (hA, hB):
                for kt_ in range(4 * qb + 1, nkt):
                    em_order.append((h_, kt_))
        mpos = {u: i + 48 * qb + 1 for i, u in enumerate(em_order)}
        msidx = {}
        for u in order_S:
            h_, kt_ = u
            if kt_ > 4 * qb:
                msidx[u] = mpos[(h_, kt_)]
            else:
                msidx[u] = 0
        SIDX[qb], PIDX[qb], MSIDX[qb], ORD_S[qb] = sidx, pidx, msidx, order_S
    ZBANK = {}
    for _hA, _hB in PAIRS:
        ZBANK[_hA], ZBANK[_hB] = 0, 1

    NH = 16
    b_lqq, b_lqx, b_stx, b_gpx = lqq.val, lqx.val, stx.val, gpx.val
    b_tS, b_sE, b_tP, b_vM = tS.val, sE.val, tP.val, vM.val
    b_vR, b_tB, b_vZ, b_tO = vR.val, tB.val, vZ.val, tO.val
    b_vRb, b_sRb, b_vSq, b_tT = vRb.val, sRb.val, vSq.val, tT.val
    b_vVs, b_sRs, b_vXn, b_vBC = vVs.val, sRs.val, vXn.val, vBC.val

    with nc.Block() as blk:
        @blk.sync
        def _(sy):
            sy.dma_start(qtb[:, 0:NC * 512].rearrange("p (c j) -> p c j", c=NC),
                         qt_hbm[:, 0:512].rearrange("(c p) j -> p c j", p=128)
                         ).then_inc(lqq.sem, 16)
            lqq.val += 16
            for qb in range(NQB + 2):
                if qb + 1 < NQB:
                    sy.wait_ge(tS.sem, b_tS + cumS[qb])
                    sl_sem = lqq if (qb + 1) % 2 == 0 else lqq2
                    sy.dma_start(
                        qtb[:, ((qb + 1) % 2) * NC * 512:((qb + 1) % 2 + 1) * NC * 512]
                        .rearrange("p (c j) -> p c j", c=NC),
                        qt_hbm[:, (qb + 1) * 512:(qb + 2) * 512]
                        .rearrange("(c p) j -> p c j", p=128)).then_inc(sl_sem.sem, 16)
                    sl_sem.val += 16
                if qb < NQB:
                    sy.wait_ge(vRb.sem, b_vRb + 8 * qb)
                    sy.dma_start(x0f[:, :].rearrange("p (c j) -> p c j", c=NC),
                                 xf0[:, 1 + qb * 512:1 + (qb + 1) * 512]
                                 .rearrange("(c p) j -> p c j", p=128)).then_inc(lqx.sem, 16)
                    lqx.val += 16
                if qb >= 2:
                    q0 = qb - 2
                    for mc in range(NC):
                        sy.wait_ge(vXn.sem, b_vXn + q0 * NC + mc + 1)
                        sy.dma_start(xf1[mc * 128:(mc + 1) * 128,
                                         1 + q0 * 512:1 + (q0 + 1) * 512],
                                     x1_st[:, (q0 % 2) * NC * 512 + mc * 512:
                                           (q0 % 2) * NC * 512 + (mc + 1) * 512]
                                     ).then_inc(stx.sem, 16)
                    stx.val += NC * 16
            sy.wait_ge(stx.sem, stx.val)
            sy.wait_ge(gpx.sem, b_gpx + NQB * NC * 16)

        @blk.gpsimd
        def _(g):
            for qb in range(NQB):
                for mc in range(NC):
                    g.wait_ge(vXn.sem, b_vXn + qb * NC + mc + 1)
                    g.dma_start(xb1[mc * 128:(mc + 1) * 128, 1 + qb * 512:1 + (qb + 1) * 512],
                                x1_st[:, (qb % 2) * NC * 512 + mc * 512:
                                      (qb % 2) * NC * 512 + (mc + 1) * 512]
                                ).then_inc(gpx.sem, 16)
                gpx.val += NC * 16

        @blk.tensor
        def _(t):
            def attn_tensor(qb):
                nkt = nkts[qb]
                qoff = (qb % 2) * NC * 512
                sidx, pidx, msidx = SIDX[qb], PIDX[qb], MSIDX[qb]
                order_S = ORD_S[qb]
                if qb % 2 == 0:
                    t.wait_ge(lqq.sem, b_lqq + 16 * (qb // 2 + 1))
                else:
                    t.wait_ge(lqq2.sem, 16 * ((qb - 1) // 2 + 1))

                def emit_S(h, kt):
                    g = h // 2
                    qch = (h // 4) * 2 + h % 2
                    qhf = (h // 2) % 2
                    GS = sidx[(h, kt)] - 1
                    off = max(0, (kt - 4 * qb) * 128)
                    diag = kt >= 4 * qb
                    inst = t.matmul(ps_s[GS % 3][:, off:512],
                                    kt_bf[(g % 2) * 64:(g % 2) * 64 + 64,
                                          (g // 2) * T + kt * 128:(g // 2) * T + kt * 128 + 128],
                                    qtb[qhf * 64:qhf * 64 + 64,
                                        qoff + qch * 512 + off:qoff + qch * 512 + 512],
                                    start=True, stop=not diag)
                    if diag:
                        inst = t.matmul(ps_s[GS % 3][:, off:off + 128], ident_bf[:, :],
                                        mk[:, :], start=False, stop=True)
                    inst.then_inc(tS.sem, 1)
                    tS.val += 1

                def emit_PV(h, kt):
                    g = h // 2
                    GS = sidx[(h, kt)] - 1
                    t.matmul(ps_z[ZBANK[h]][0:65, 0:512],
                             v_bf[:, kt * 520 + g * 65:kt * 520 + g * 65 + 65],
                             pt[GS % 8][:, 0:512],
                             start=(kt == 0), stop=(kt == nkt - 1)).then_inc(tP.sem, 1)
                    tP.val += 1

                def emit_pair_bcast(p):
                    t.wait_ge(vR.sem, b_vR + 8 * qb + p + 1)
                    if 8 * qb + p >= 1:
                        t.wait_ge(vBC.sem, b_vBC + 8 * qb + p)
                    t.matmul(ps_bc[0][0:128, 0:512], sel65_f[0:65, 0:128],
                             dn_s[0:65, 0:512], start=True, stop=True).then_inc(tB.sem, 1)
                    tB.val += 1

                for p, (hA, hB) in enumerate(PAIRS):
                    for kt in range(nkt):
                        GSb = sidx[(hB, kt)] - 1
                        if GSb >= 3:
                            t.wait_ge(sE.sem, b_sE + GSb - 2)
                        emit_S(hA, kt)
                        emit_S(hB, kt)
                        if p >= 1 and kt == 0:
                            emit_pair_bcast(p - 1)
                        if kt >= 1:
                            t.wait_ge(sE.sem, b_sE + sidx[(hB, kt - 1)])
                            nmz = max(msidx[(hA, kt - 1)], msidx[(hB, kt - 1)])
                            if nmz > 0:
                                t.wait_ge(vM.sem, b_vM + nmz)
                            if kt == 1 and 16 * qb + 2 * p > 0:
                                t.wait_ge(vZ.sem, b_vZ + 16 * qb + 2 * p)
                            emit_PV(hA, kt - 1)
                            emit_PV(hB, kt - 1)
                    t.wait_ge(sE.sem, b_sE + sidx[(hB, nkt - 1)])
                    nmz = max(msidx[(hA, nkt - 1)], msidx[(hB, nkt - 1)])
                    if nmz > 0:
                        t.wait_ge(vM.sem, b_vM + nmz)
                    emit_PV(hA, nkt - 1)
                    emit_PV(hB, nkt - 1)
                emit_pair_bcast(7)

            def TA(qb):
                t.wait_ge(vZ.sem, b_vZ + 16 * (qb + 1))
                if qb >= 2:
                    t.wait_ge(vVs.sem, b_vVs + qb - 1)
                for mc in range(NC):
                    if mc >= 2:
                        t.wait_ge(vRb.sem, b_vRb + 8 * qb + mc - 1)
                    for kc in range(NC):
                        inst = t.matmul(ps_t6[mc % 2][:, :],
                                        wo_bf[:, kc * D + mc * 128:kc * D + mc * 128 + 128],
                                        zt_bf[:, (qb % 2) * NC * 512 + kc * 512:
                                              (qb % 2) * NC * 512 + (kc + 1) * 512],
                                        start=(kc == 0), stop=(kc == NC - 1))
                    inst.then_inc(tO.sem, 1)
                    tO.val += 1

            def TB(qb):
                t.wait_ge(sRb.sem, b_sRb + 8 * (qb + 1))
                t.wait_ge(vRb.sem, b_vRb + 8 * min(qb + 2, NQB))
                if qb >= 1:
                    t.wait_ge(vVs.sem, b_vVs + qb)
                for c in range(NC):
                    t.matmul(ps_t6[0][:, :], ones_bf[:, :],
                             rb_bf[:, c * 512:(c + 1) * 512],
                             start=(c == 0), stop=(c == NC - 1))
                t.wait_ge(vSq.sem, b_vSq + 8 * (qb + 1))
                for c in range(NC):
                    inst = t.matmul(ps_t6[1][:, :], ones_bf[:, :],
                                    sq_bf[:, c * 512:(c + 1) * 512],
                                    start=(c == 0), stop=(c == NC - 1))
                inst.then_inc(tT.sem, 2)
                tT.val += 2

            for qb in range(NQB + 2):
                if qb < NQB:
                    attn_tensor(qb)
                if qb >= 1 and qb - 1 < NQB:
                    TA(qb - 1)
                if qb >= 2:
                    TB(qb - 2)

        @blk.vector
        def _(v):
            def attn_vector(qb):
                nkt = nkts[qb]
                sidx, pidx, msidx = SIDX[qb], PIDX[qb], MSIDX[qb]
                order_S = ORD_S[qb]
                if qb >= 2:
                    v.wait_ge(tO.sem, b_tO + 8 * (qb - 1))

                def msets_h(h_):
                    for kt in range(4 * qb + 1, nkt):
                        GS = sidx[(h_, kt)] - 1
                        off = (kt - 4 * qb) * 128
                        if GS >= 8:
                            u6 = order_S[GS - 8 - cumS[qb]] if GS - 8 >= cumS[qb] else None
                            if u6 is not None:
                                v.wait_ge(tP.sem, b_tP + pidx[u6])
                        v.memset(pt[GS % 8][:, 0:off], 0.0).then_inc(vM.sem, 1)
                        vM.val += 1

                msets_h(PAIRS[0][0])
                msets_h(PAIRS[0][1])
                for p, (hA, hB) in enumerate(PAIRS):
                    v.wait_ge(tB.sem, b_tB + 8 * qb + p + 1)
                    v.wait_ge(vZ.sem, vZ.val)
                    v.tensor_copy(bc_s[0:128, 0:512],
                                  ps_bc[0][0:128, 0:512]).then_inc(vBC.sem, 1)
                    vBC.val += 1
                    v.wait_ge(vBC.sem, vBC.val)
                    for pos, h in enumerate((hA, hB)):
                        qch = (h // 4) * 2 + h % 2
                        qhf = (h // 2) % 2
                        v.tensor_tensor(zt_bf[qhf * 64:qhf * 64 + 64,
                                              (qb % 2) * NC * 512 + qch * 512:
                                              (qb % 2) * NC * 512 + qch * 512 + 512],
                                        ps_z[pos][0:64, 0:512],
                                        bc_s[pos * 64:pos * 64 + 64, 0:512],
                                        ALU.mult).then_inc(vZ.sem, 1)
                        vZ.val += 1
                    if p + 1 < 8:
                        msets_h(PAIRS[p + 1][0])
                        msets_h(PAIRS[p + 1][1])

            def VA(qb):
                xs = (qb % 2) * NC * 512
                v.wait_ge(lqx.sem, b_lqx + 16 * (qb + 1))
                if qb >= 2:
                    v.wait_ge(stx.sem, b_stx + (qb - 1) * NC * 16)
                    v.wait_ge(gpx.sem, b_gpx + (qb - 1) * NC * 16)
                for mc in range(NC):
                    v.wait_ge(tO.sem, b_tO + 8 * qb + mc + 1)
                    v.scalar_tensor_tensor(x1_st[:, xs + mc * 512:xs + (mc + 1) * 512],
                                           ps_t6[mc % 2][:, :], bout_cols[:, mc:mc + 1],
                                           x0f[:, mc * 512:(mc + 1) * 512],
                                           ALU.add, ALU.add).then_inc(vRb.sem, 1)
                    vRb.val += 1
                if qb >= 1:
                    v.wait_ge(tT.sem, b_tT + 2 * qb)
                for mc in range(NC):
                    v.wait_ge(vRb.sem, b_vRb + 8 * qb + mc + 1)
                    v.tensor_mul(sq_bf[:, mc * 512:(mc + 1) * 512],
                                 x1_st[:, xs + mc * 512:xs + (mc + 1) * 512],
                                 x1_st[:, xs + mc * 512:xs + (mc + 1) * 512]).then_inc(vSq.sem, 1)
                    vSq.val += 1

            def VB(qb):
                v.wait_ge(tT.sem, b_tT + 2 * (qb + 1))
                v.tensor_single_scalar(s_mu[:, :], ps_t6[0][:, :], 1.0 / D,
                                       ALU.mult).then_inc(vch.sem, 1)
                vch.val += 1
                v.wait_ge(vch.sem, vch.val)
                v.tensor_tensor(s_var[:, :], s_mu[:, :], s_mu[:, :], ALU.mult)
                v.tensor_scalar(s_tmp[:, :], ps_t6[1][:, :], 1.0 / D, EPS, ALU.mult,
                                ALU.add).then_inc(vch.sem, 1)
                vch.val += 1
                v.wait_ge(vch.sem, vch.val)
                v.tensor_tensor(s_var[:, :], s_tmp[:, :], s_var[:, :],
                                ALU.subtract).then_inc(vVs.sem, 1)
                vVs.val += 1

            def VC(qb):
                xs = (qb % 2) * NC * 512
                v.wait_ge(sRs.sem, b_sRs + 2 * (qb + 1))
                for mc in range(NC):
                    sl = slice(xs + mc * 512, xs + (mc + 1) * 512)
                    v.tensor_tensor(x1_st[:, sl], x1_st[:, sl], s_mu[:, :],
                                    ALU.subtract).then_inc(vch.sem, 1)
                    vch.val += 1
                    v.wait_ge(vch.sem, vch.val)
                    v.tensor_tensor(x1_st[:, sl], x1_st[:, sl], s_rs[:, :],
                                    ALU.mult).then_inc(vXn.sem, 1)
                    vXn.val += 1

            for qb in range(NQB + 2):
                if qb < NQB:
                    attn_vector(qb)
                if qb >= 1 and qb - 1 < NQB:
                    VA(qb - 1)
                if qb >= 2:
                    VB(qb - 2)
                    VC(qb - 2)

        @blk.scalar
        def _(s):
            def den_pair(qb, p, hA, hB):
                pidx = PIDX[qb]
                if 8 * qb + p >= 1:
                    s.wait_ge(tB.sem, b_tB + 8 * qb + p)
                s.wait_ge(tP.sem, b_tP + pidx[(hB, nkts[qb] - 1)])
                s.activation(dn_l[0:1, 0:512], ps_z[0][64:65, 0:512],
                             AF.Ln).then_inc(sch.sem, 1)
                sch.val += 1
                s.activation(dn_l[64:65, 0:512], ps_z[1][64:65, 0:512],
                             AF.Ln).then_inc(sch.sem, 1)
                sch.val += 1
                s.wait_ge(sch.sem, sch.val)
                s.activation(dn_s[0:1, 0:512], dn_l[0:1, 0:512],
                             AF.Exp, scale=-1.0)
                s.activation(dn_s[64:65, 0:512], dn_l[64:65, 0:512],
                             AF.Exp, scale=-1.0).then_inc(vR.sem, 1)
                vR.val += 1

            def attn_scalar(qb):
                nkt = nkts[qb]
                sidx, pidx = SIDX[qb], PIDX[qb]
                order_S = ORD_S[qb]
                for p, (hA, hB) in enumerate(PAIRS):
                    for kt in range(nkt):
                        GSb = sidx[(hB, kt)] - 1
                        if GSb >= 8 and GSb - 8 >= cumS[qb]:
                            u6 = order_S[GSb - 8 - cumS[qb]]
                            s.wait_ge(tP.sem, b_tP + pidx[u6])
                        s.wait_ge(tS.sem, b_tS + GSb + 1)
                        for h in (hA, hB):
                            GS = sidx[(h, kt)] - 1
                            off = max(0, (kt - 4 * qb) * 128)
                            s.activation(pt[GS % 8][:, off:512], ps_s[GS % 3][:, off:512],
                                         AF.Exp, scale=0.125).then_inc(sE.sem, 1)
                            sE.val += 1
                    den_pair(qb, p, hA, hB)

            def SA(qb):
                xs = (qb % 2) * NC * 512
                if qb >= 1:
                    s.wait_ge(tT.sem, b_tT + 2 * qb)
                for mc in range(NC):
                    s.wait_ge(vRb.sem, b_vRb + 8 * qb + mc + 1)
                    s.copy(rb_bf[:, mc * 512:(mc + 1) * 512],
                           x1_st[:, xs + mc * 512:xs + (mc + 1) * 512]).then_inc(sRb.sem, 1)
                    sRb.val += 1

            def SB(qb):
                s.wait_ge(vVs.sem, b_vVs + qb + 1)
                s.activation(s_rs[:, :], s_var[:, :], AF.Ln).then_inc(sRs.sem, 1)
                sRs.val += 1
                s.wait_ge(sRs.sem, sRs.val)
                s.activation(s_rs[:, :], s_rs[:, :], AF.Exp, scale=-0.5).then_inc(sRs.sem, 1)
                sRs.val += 1

            for qb in range(NQB + 2):
                if qb < NQB:
                    attn_scalar(qb)
                if qb >= 1 and qb - 1 < NQB:
                    SA(qb - 1)
                if qb >= 2:
                    SB(qb - 2)
    ctx_p1.close()


def _p2_inner_attn(nc, en_sem, lid, xf_in, xb_in, xf_out, xb_out,
                   wqkv_d, wo_d, scol_d,
                   tab_cs, tab_sn, tab_ci, tab_si, r2_d, bdsel_d, selbc_d,
                   mask_di, mask_ai, sel65_f, ident_bf):
    NHI = 8
    ctx = ExitStack()
    en = ctx.enter_context
    L = str(lid)

    wqkv_bf = en(nc.sbuf_tensor("p2_wqkv" + L, [128, NC * 1536], BF16))
    wo_bf = en(nc.sbuf_tensor("p2_wo" + L, [128, 4 * D], BF16))
    r2_f = en(nc.sbuf_tensor("p2_r2" + L, [128, 128], F32))
    scol = en(nc.sbuf_tensor("p2_sc" + L, [128, 4], F32))
    bdsel = en(nc.sbuf_tensor("p2_bd" + L, [128, 4 * 8], BF16))
    selbc = en(nc.sbuf_tensor("p2_sb" + L, [128, 4 * 128], BF16))
    mk_d = en(nc.sbuf_tensor("p2_mkd" + L, [128, 128], BF16))
    mk_a = en(nc.sbuf_tensor("p2_mka" + L, [128, 128], BF16))

    kr = en(nc.sbuf_tensor("p2_kr" + L, [128, 2 * 4 * 512], BF16))
    vr = en(nc.sbuf_tensor("p2_vr" + L, [128, 2 * 4 * 520], BF16))
    qfin = en(nc.sbuf_tensor("p2_qf" + L, [128, 4 * 512], BF16))
    xq_in = en(nc.sbuf_tensor("p2_xq" + L, [128, NC * 512], BF16))
    xf_cur = en(nc.sbuf_tensor("p2_xf" + L, [128, NC * 512], F32))
    tabs = en(nc.sbuf_tensor("p2_tab" + L, [128, 4 * 512], F32))
    ctabs = en(nc.sbuf_tensor("p2_ctab" + L, [128, 4 * 512], F32))
    qt_f = en(nc.sbuf_tensor("p2_qtf" + L, [128, NC * 512], F32))
    t1b = en(nc.sbuf_tensor("p2_t1" + L, [128, NC * 512], F32))
    t3b = en(nc.sbuf_tensor("p2_t3" + L, [128, 2 * 512], F32))
    sqt = en(nc.sbuf_tensor("p2_sqt" + L, [128, 8 * 512], BF16))
    ssrow = en(nc.sbuf_tensor("p2_ss" + L, [128, 2 * 512], F32))
    invr = en(nc.sbuf_tensor("p2_inv" + L, [128, 2 * 512], BF16))
    pt = [en(nc.sbuf_tensor(f"p2_pt{i}" + L, [128, 512], BF16)) for i in range(8)]
    dn_s = en(nc.sbuf_tensor("p2_dn" + L, [128, 512], BF16))
    dn_l = en(nc.sbuf_tensor("p2_dnl" + L, [128, 512], F32))
    bc_s = en(nc.sbuf_tensor("p2_bc" + L, [128, 512], F32))
    zin = en(nc.sbuf_tensor("p2_zin" + L, [128, 4 * 512], BF16))
    x2_st = en(nc.sbuf_tensor("p2_x2" + L, [128, NC * 512], F32))

    pb = [en(nc.psum_tensor(f"p2_pb{i}" + L, [128, 512], F32)) for i in range(8)]

    wl = Ctr.get(en_sem, nc, "p2_wl")
    lxq = Ctr.get(en_sem, nc, "p2_lxq")
    lxf = Ctr.get(en_sem, nc, "p2_lxf")
    ltb = Ctr.get(en_sem, nc, "p2_ltb")
    tQ = Ctr.get(en_sem, nc, "g4")
    vQ = Ctr.get(en_sem, nc, "g5")
    tV = Ctr.get(en_sem, nc, "g6")
    vV = Ctr.get(en_sem, nc, "g7")
    sSq = Ctr.get(en_sem, nc, "g8")
    tSS = Ctr.get(en_sem, nc, "g9")
    vSS = Ctr.get(en_sem, nc, "g10")
    sIv = Ctr.get(en_sem, nc, "g11")
    tBc = Ctr.get(en_sem, nc, "g12")
    tRo = Ctr.get(en_sem, nc, "g13")
    vT3 = Ctr.get(en_sem, nc, "g14")
    vQf = Ctr.get(en_sem, nc, "g15")
    tS = Ctr.get(en_sem, nc, "g16")
    vM = Ctr.get(en_sem, nc, "g17")
    sE = Ctr.get(en_sem, nc, "g18")
    tP = Ctr.get(en_sem, nc, "g19")
    vR = Ctr.get(en_sem, nc, "g20")
    tB = Ctr.get(en_sem, nc, "g21")
    vZ = Ctr.get(en_sem, nc, "g22")
    tO = Ctr.get(en_sem, nc, "g23")
    vO = Ctr.get(en_sem, nc, "g24")
    stg = Ctr.get(en_sem, nc, "p2_stg")
    vch = Ctr.get(en_sem, nc, "g27")
    sch = Ctr.get(en_sem, nc, "g28")
    vBC = Ctr.get(en_sem, nc, "g29")

    with nc.Block() as blk:
        @blk.gpsimd
        def _(g):
            for c in range(NC):
                g.dma_start(wqkv_bf[:, c * 1536:(c + 1) * 1536],
                            wqkv_d[c * 128:(c + 1) * 128, :]).then_inc(wl.sem, 16)
            for c in range(4):
                g.dma_start(wo_bf[:, c * D:(c + 1) * D],
                            wo_d[c * 128:(c + 1) * 128, :]).then_inc(wl.sem, 16)
            g.dma_start(r2_f[:, :], r2_d).then_inc(wl.sem, 16)
            g.dma_start(scol[:, :], scol_d).then_inc(wl.sem, 16)
            g.dma_start(bdsel[:, :], bdsel_d).then_inc(wl.sem, 16)
            g.dma_start(selbc[:, :], selbc_d).then_inc(wl.sem, 16)
            g.dma_start(mk_d[:, :], mask_di).then_inc(wl.sem, 16)
            g.dma_start(mk_a[:, :], mask_ai).then_inc(wl.sem, 16)
            wl.val += (NC + 4 + 6) * 16
            g.memset(bass.AP(vr, 64, [[2 * 4 * 520, 128], [520, 8], [65, 8], [1, 1]]), 1.0)
            g.wait_ge(wl.sem, wl.val)

        @blk.vector
        def _(v):
            v.memset(dn_s[:, :], 0.0)
            v.memset(invr[:, :], 0.0)

    for tb in range(NQB):
        kts = list(range(max(0, 4 * tb - 4), 4 * tb + 4))
        nkt = len(kts)

        def span(kt):
            if kt < 4 * tb:
                j = kt - 4 * (tb - 1)
                return 0, 128 * (j + 1), ("anti", 128 * j)
            j = kt - 4 * tb
            return 128 * j, 512, ("diag", 128 * j)

        order_S, order_PV = [], []
        for hp_ in range(4):
            h0_, h1_ = 2 * hp_, 2 * hp_ + 1
            for ki_ in range(nkt):
                order_S += [(h0_, ki_), (h1_, ki_)]
                if ki_ >= 1:
                    order_PV += [(h0_, ki_ - 1), (h1_, ki_ - 1)]
            order_PV += [(h0_, nkt - 1), (h1_, nkt - 1)]
        sidx = {u: i + 1 for i, u in enumerate(order_S)}
        pidx = {u: i + 1 for i, u in enumerate(order_PV)}
        em_order = []
        for hp_ in range(4):
            for ki_ in range(nkt):
                lo_, hi_, _mm = span(kts[ki_])
                if lo_ > 0 or hi_ < 512:
                    em_order.append((2 * hp_, ki_))
                    em_order.append((2 * hp_ + 1, ki_))
        mpos = {u: i + 1 for i, u in enumerate(em_order)}
        msidx = {}
        for u in order_S:
            msidx[u] = mpos.get(u, 0)

        lxq0, lxf0, ltb0 = lxq.val, lxf.val, ltb.val
        tQ0, vQ0, tV0, vV0, sSq0 = tQ.val, vQ.val, tV.val, vV.val, sSq.val
        tSS0, vSS0, sIv0, tBc0, tRo0, vT30, vQf0 = (tSS.val, vSS.val, sIv.val, tBc.val,
                                                    tRo.val, vT3.val, vQf.val)
        tS0, vM0, sE0, tP0 = tS.val, vM.val, sE.val, tP.val
        vR0, tB0, vZ0, tO0, vO0, stg0 = (vR.val, tB.val, vZ.val, tO.val, vO.val,
                                         stg.val)
        vBC0 = vBC.val
        slot = tb % 2

        with nc.Block() as blk:
            @blk.sync
            def _(sy):
                sy.dma_start(xq_in[:, 0:4 * 512].rearrange("p (c j) -> p c j", c=4),
                             xb_in[0:512, 1 + tb * 512:1 + (tb + 1) * 512]
                             .rearrange("(c p) j -> p c j", p=128)).then_inc(lxq.sem, 16)
                sy.dma_start(xq_in[:, 4 * 512:NC * 512].rearrange("p (c j) -> p c j", c=4),
                             xb_in[512:1024, 0 + tb * 512:0 + (tb + 1) * 512]
                             .rearrange("(c p) j -> p c j", p=128)).then_inc(lxq.sem, 16)
                for i, tab in enumerate([tab_cs, tab_sn, tab_ci, tab_si]):
                    sy.dma_start(tabs[:, i * 512:(i + 1) * 512],
                                 tab[:, tb * 512:(tb + 1) * 512]).then_inc(ltb.sem, 16)
                sy.dma_start(xf_cur[:, :].rearrange("p (c j) -> p c j", c=NC),
                             xf_in[:, 1 + tb * 512:1 + (tb + 1) * 512]
                             .rearrange("(c p) j -> p c j", p=128)).then_inc(lxf.sem, 16)
                lxq.val += 32
                lxf.val += 16
                ltb.val += 64

            @blk.gpsimd
            def _(g):
                for mc in range(NC):
                    g.wait_ge(vO.sem, vO0 + mc + 1)
                    g.dma_start(xf_out[mc * 128:(mc + 1) * 128,
                                       1 + tb * 512:1 + (tb + 1) * 512],
                                x2_st[:, mc * 512:(mc + 1) * 512]).then_inc(stg.sem, 16)
                    g.dma_start(xb_out[mc * 128:(mc + 1) * 128,
                                       1 + tb * 512:1 + (tb + 1) * 512],
                                x2_st[:, mc * 512:(mc + 1) * 512]).then_inc(stg.sem, 16)
                stg.val += NC * 32
                if tb == NQB - 1:
                    g.wait_ge(stg.sem, stg.val)

            @blk.tensor
            def _(t):
                t.wait_ge(lxq.sem, lxq0 + 32)
                for mc in range(NC):
                    if mc >= 2:
                        t.wait_ge(vQ.sem, vQ0 + mc - 1)
                    for kc in range(NC):
                        inst = t.matmul(pb[mc % 2][:, :],
                                        wqkv_bf[:, kc * 1536 + mc * 128:kc * 1536 + mc * 128 + 128],
                                        xq_in[:, kc * 512:(kc + 1) * 512],
                                        start=(kc == 0), stop=(kc == NC - 1))
                    inst.then_inc(tQ.sem, 1)
                    tQ.val += 1
                for tt in range(4):
                    if tt >= 2:
                        t.wait_ge(vV.sem, vV0 + tt - 1)
                    for kc in range(NC):
                        inst = t.matmul(pb[2 + tt % 2][:, :],
                                        xq_in[:, kc * 512 + tt * 128:kc * 512 + tt * 128 + 128],
                                        wqkv_bf[:, kc * 1536 + 1024:kc * 1536 + 1536],
                                        start=(kc == 0), stop=(kc == NC - 1))
                    inst.then_inc(tV.sem, 1)
                    tV.val += 1
                for mc in range(NC if P2_STAGE >= 2 else 0):
                    t.wait_ge(vQ.sem, vQ0 + mc + 1)
                    if mc >= 2:
                        t.wait_ge(vT3.sem, vT30 + mc - 1)
                    t.matmul(pb[4 + mc % 2][:, :], r2_f[:, :],
                             qt_f[:, mc * 512:(mc + 1) * 512],
                             start=True, stop=True).then_inc(tRo.sem, 1)
                    tRo.val += 1
                for side in range(2 if P2_STAGE >= 2 else 0):
                    for c in range(4):
                        t.wait_ge(sSq.sem, sSq0 + side * 4 + c + 1)
                        inst = t.matmul(pb[6 + side][0:8, :], bdsel[:, c * 8:(c + 1) * 8],
                                        sqt[:, (side * 4 + c) * 512:(side * 4 + c + 1) * 512],
                                        start=(c == 0), stop=(c == 3))
                    inst.then_inc(tSS.sem, 1)
                    tSS.val += 1
                if P2_STAGE >= 2:
                    t.wait_ge(vSS.sem, vSS0 + 2)
                for idx in range(NC if P2_STAGE >= 2 else 0):
                    side, c = idx // 4, idx % 4
                    t.wait_ge(sIv.sem, sIv0 + side + 1)
                    if idx >= 2:
                        t.wait_ge(vQf.sem, vQf0 + idx - 1)
                    t.matmul(pb[6 + idx % 2][0:128, :],
                             selbc[0:65, c * 128:(c + 1) * 128],
                             invr[0:65, side * 512:side * 512 + 512],
                             start=True, stop=True).then_inc(tBc.sem, 1)
                    tBc.val += 1
                if P2_STAGE >= 3:
                    t.wait_ge(vQf.sem, vQf0 + NC)

                def emit_S(h, ki):
                    hc, hf = h // 2, (h % 2) * 64
                    kt = kts[ki]
                    GS = sidx[(h, ki)] - 1
                    lo, hi, (mtyp, moff) = span(kt)
                    kslot = (kt // 4) % 2
                    ktt = kt % 4
                    t.matmul(pb[GS % 3][:, lo:hi],
                             kr[hf:hf + 64, kslot * 2048 + hc * 512 + ktt * 128:
                                kslot * 2048 + hc * 512 + ktt * 128 + 128],
                             qfin[hf:hf + 64, hc * 512 + lo:hc * 512 + hi],
                             start=True, stop=False)
                    mten = mk_a if mtyp == "anti" else mk_d
                    t.matmul(pb[GS % 3][:, moff:moff + 128], ident_bf[:, :], mten[:, :],
                             start=False, stop=True).then_inc(tS.sem, 1)
                    tS.val += 1

                def emit_PV(h, ki):
                    kt = kts[ki]
                    GS = sidx[(h, ki)] - 1
                    t.matmul(pb[3 + h % 3][0:65, 0:512],
                             vr[:, ((kt // 4) % 2) * 2080 + (kt % 4) * 520 + h * 65:
                                ((kt // 4) % 2) * 2080 + (kt % 4) * 520 + h * 65 + 65],
                             pt[GS % 8][:, 0:512],
                             start=(ki == 0), stop=(ki == nkt - 1)).then_inc(tP.sem, 1)
                    tP.val += 1

                def emit_pair_bcast(p):
                    t.wait_ge(vR.sem, vR0 + p + 1)
                    if p >= 2:
                        t.wait_ge(vBC.sem, vBC0 + p - 1)
                    t.matmul(pb[6 + p % 2][0:128, 0:512], sel65_f[0:65, 0:128],
                             dn_s[0:65, 0:512], start=True, stop=True).then_inc(tB.sem, 1)
                    tB.val += 1

                for hp in range(NHI // 2 if P2_STAGE >= 3 else 0):
                    h0, h1 = 2 * hp, 2 * hp + 1
                    for ki in range(nkt):
                        GSb = sidx[(h1, ki)] - 1
                        if GSb >= 3:
                            t.wait_ge(sE.sem, sE0 + GSb - 2)
                        emit_S(h0, ki)
                        emit_S(h1, ki)
                        if hp >= 1 and ki == 0:
                            emit_pair_bcast(hp - 1)
                        if ki >= 1:
                            t.wait_ge(sE.sem, sE0 + sidx[(h1, ki - 1)])
                            nmz = max(msidx[(h0, ki - 1)], msidx[(h1, ki - 1)])
                            if nmz > 0:
                                t.wait_ge(vM.sem, vM0 + nmz)
                            if ki == 1 and h1 >= 3:
                                t.wait_ge(vZ.sem, vZ0 + h1 - 2)
                            emit_PV(h0, ki - 1)
                            emit_PV(h1, ki - 1)
                    t.wait_ge(sE.sem, sE0 + sidx[(h1, nkt - 1)])
                    nmz = max(msidx[(h0, nkt - 1)], msidx[(h1, nkt - 1)])
                    if nmz > 0:
                        t.wait_ge(vM.sem, vM0 + nmz)
                    emit_PV(h0, nkt - 1)
                    emit_PV(h1, nkt - 1)
                if P2_STAGE >= 3:
                    emit_pair_bcast(NHI // 2 - 1)
                t.wait_ge(vZ.sem, vZ0 + NHI)
                for mc in range(NC if P2_STAGE >= 4 else 0):
                    if mc >= 2:
                        t.wait_ge(vO.sem, vO0 + mc - 1)
                    for kc in range(4):
                        inst = t.matmul(pb[mc % 2][:, :],
                                        wo_bf[:, kc * D + mc * 128:kc * D + mc * 128 + 128],
                                        zin[:, kc * 512:(kc + 1) * 512],
                                        start=(kc == 0), stop=(kc == 3))
                    inst.then_inc(tO.sem, 1)
                    tO.val += 1

            @blk.vector
            def _(v):
                v.wait_ge(ltb.sem, ltb0 + 64)
                for i in range(4):
                    v.tensor_single_scalar(ctabs[:, i * 512:(i + 1) * 512],
                                           tabs[:, i * 512:(i + 1) * 512],
                                           scol[:, i:i + 1], ALU.mult).then_inc(vch.sem, 1)
                    vch.val += 1
                v.wait_ge(vch.sem, vch.val)
                for mc in range(NC):
                    v.wait_ge(tQ.sem, tQ0 + mc + 1)
                    v.tensor_copy(qt_f[:, mc * 512:(mc + 1) * 512],
                                  pb[mc % 2][:, :]).then_inc(vQ.sem, 1)
                    vQ.val += 1
                    v.wait_ge(vQ.sem, vQ.val)
                    cs_off = (0 if mc < 4 else 2) * 512
                    v.tensor_tensor(t1b[:, mc * 512:(mc + 1) * 512],
                                    qt_f[:, mc * 512:(mc + 1) * 512],
                                    ctabs[:, cs_off:cs_off + 512], ALU.mult)
                for tt in range(4):
                    v.wait_ge(tV.sem, tV0 + tt + 1)
                    v.tensor_copy(
                        bass.AP(vr, slot * 2080 + tt * 520,
                                [[2 * 4 * 520, 128], [65, 8], [1, 64]]),
                        pb[2 + tt % 2][:, :].rearrange("p (g e) -> p g e", g=8)
                        ).then_inc(vV.sem, 1)
                    vV.val += 1
                for mc in range(NC if P2_STAGE >= 2 else 0):
                    sn_off = (1 if mc < 4 else 3) * 512
                    v.wait_ge(tRo.sem, tRo0 + mc + 1)
                    v.tensor_tensor(t3b[:, (mc % 2) * 512:(mc % 2) * 512 + 512],
                                    pb[4 + mc % 2][:, :],
                                    ctabs[:, sn_off:sn_off + 512], ALU.mult).then_inc(vT3.sem, 1)
                    vT3.val += 1
                    v.wait_ge(vT3.sem, vT3.val)
                    v.tensor_tensor(t1b[:, mc * 512:(mc + 1) * 512],
                                    t1b[:, mc * 512:(mc + 1) * 512],
                                    t3b[:, (mc % 2) * 512:(mc % 2) * 512 + 512],
                                    ALU.add).then_inc(vch.sem, 1)
                    vch.val += 1
                for side in range(2 if P2_STAGE >= 2 else 0):
                    v.wait_ge(tSS.sem, tSS0 + side + 1)
                    v.tensor_single_scalar(ssrow[0:8, side * 512:side * 512 + 512],
                                           pb[6 + side][0:8, :], 1e-24,
                                           ALU.max).then_inc(vSS.sem, 1)
                    vSS.val += 1
                if P2_STAGE >= 2:
                    v.wait_ge(vch.sem, vch.val)
                for mc in range(NC if P2_STAGE >= 2 else 0):
                    v.wait_ge(tBc.sem, tBc0 + mc + 1)
                    dest = qfin if mc < 4 else kr
                    doff = (mc % 4) * 512 if mc < 4 else slot * 2048 + (mc - 4) * 512
                    v.tensor_tensor(dest[:, doff:doff + 512],
                                    t1b[:, mc * 512:(mc + 1) * 512],
                                    pb[6 + mc % 2][:, :], ALU.mult).then_inc(vQf.sem, 1)
                    vQf.val += 1
                def msets_pair(hp_):
                    for ki_ in range(nkt):
                        kt_ = kts[ki_]
                        lo_, hi_, _mm = span(kt_)
                        if lo_ > 0 or hi_ < 512:
                            for h_ in (2 * hp_, 2 * hp_ + 1):
                                GS = sidx[(h_, ki_)] - 1
                                if GS >= 8:
                                    u6 = order_S[GS - 8]
                                    v.wait_ge(tP.sem, tP0 + pidx[u6])
                                if lo_ > 0 and hi_ < 512:
                                    v.memset(pt[GS % 8][:, 0:lo_], 0.0)
                                    v.memset(pt[GS % 8][:, hi_:512], 0.0).then_inc(vM.sem, 1)
                                elif lo_ > 0:
                                    v.memset(pt[GS % 8][:, 0:lo_], 0.0).then_inc(vM.sem, 1)
                                else:
                                    v.memset(pt[GS % 8][:, hi_:512], 0.0).then_inc(vM.sem, 1)
                                vM.val += 1

                if P2_STAGE >= 3:
                    msets_pair(0)
                for hp in range(NHI // 2 if P2_STAGE >= 3 else 0):
                    v.wait_ge(tB.sem, tB0 + hp + 1)
                    v.wait_ge(vZ.sem, vZ.val)
                    v.tensor_copy(bc_s[0:128, 0:512],
                                  pb[6 + hp % 2][0:128, 0:512]).then_inc(vBC.sem, 1)
                    vBC.val += 1
                    v.wait_ge(vBC.sem, vBC.val)
                    for pos, h in enumerate((2 * hp, 2 * hp + 1)):
                        hc, hf = h // 2, (h % 2) * 64
                        v.tensor_tensor(zin[hf:hf + 64, hc * 512:hc * 512 + 512],
                                        pb[3 + h % 3][0:64, 0:512],
                                        bc_s[pos * 64:pos * 64 + 64, 0:512],
                                        ALU.mult).then_inc(vZ.sem, 1)
                        vZ.val += 1
                    if 2 * hp + 2 < NHI:
                        msets_pair(hp + 1)
                v.wait_ge(lxf.sem, lxf0 + 16)
                v.wait_ge(stg.sem, stg0)
                for mc in range(NC):
                    if P2_STAGE >= 4:
                        v.wait_ge(tO.sem, tO0 + mc + 1)
                        v.tensor_tensor(x2_st[:, mc * 512:(mc + 1) * 512], pb[mc % 2][:, :],
                                        xf_cur[:, mc * 512:(mc + 1) * 512],
                                        ALU.add).then_inc(vO.sem, 1)
                    else:
                        v.wait_ge(vQ.sem, vQ0 + NC)
                        v.tensor_copy(x2_st[:, mc * 512:(mc + 1) * 512],
                                      xf_cur[:, mc * 512:(mc + 1) * 512]).then_inc(vO.sem, 1)
                    vO.val += 1

            @blk.scalar
            def _(s):
                for mc in range(NC if P2_STAGE >= 2 else 0):
                    s.wait_ge(vQ.sem, vQ0 + mc + 1)
                    s.activation(sqt[:, mc * 512:(mc + 1) * 512],
                                 qt_f[:, mc * 512:(mc + 1) * 512],
                                 AF.Square).then_inc(sSq.sem, 1)
                    sSq.val += 1
                for side in range(2 if P2_STAGE >= 2 else 0):
                    s.wait_ge(vSS.sem, vSS0 + side + 1)
                    s.activation(ssrow[0:8, side * 512:side * 512 + 512],
                                 ssrow[0:8, side * 512:side * 512 + 512],
                                 AF.Ln).then_inc(sch.sem, 1)
                    sch.val += 1
                    s.wait_ge(sch.sem, sch.val)
                    s.activation(invr[0:8, side * 512:side * 512 + 512],
                                 ssrow[0:8, side * 512:side * 512 + 512],
                                 AF.Exp, scale=-0.5).then_inc(sIv.sem, 1)
                    sIv.val += 1
                def den_pair_i(hp):
                    h0, h1 = 2 * hp, 2 * hp + 1
                    if hp >= 1:
                        s.wait_ge(tB.sem, tB0 + hp)
                    s.wait_ge(tP.sem, tP0 + pidx[(h1, nkt - 1)])
                    s.activation(dn_l[0:1, 0:512], pb[3 + h0 % 3][64:65, 0:512],
                                 AF.Ln).then_inc(sch.sem, 1)
                    sch.val += 1
                    s.activation(dn_l[64:65, 0:512], pb[3 + h1 % 3][64:65, 0:512],
                                 AF.Ln).then_inc(sch.sem, 1)
                    sch.val += 1
                    s.wait_ge(sch.sem, sch.val)
                    s.activation(dn_s[0:1, 0:512], dn_l[0:1, 0:512],
                                 AF.Exp, scale=-1.0)
                    s.activation(dn_s[64:65, 0:512], dn_l[64:65, 0:512],
                                 AF.Exp, scale=-1.0).then_inc(vR.sem, 1)
                    vR.val += 1

                for hp in range(NHI // 2 if P2_STAGE >= 3 else 0):
                    h0, h1 = 2 * hp, 2 * hp + 1
                    for ki in range(nkt):
                        GSb = sidx[(h1, ki)] - 1
                        kt = kts[ki]
                        lo, hi, _m = span(kt)
                        if GSb >= 8:
                            u6 = order_S[GSb - 8]
                            s.wait_ge(tP.sem, tP0 + pidx[u6])
                        s.wait_ge(tS.sem, tS0 + GSb + 1)
                        for h in (h0, h1):
                            GS = sidx[(h, ki)] - 1
                            s.activation(pt[GS % 8][:, lo:hi], pb[GS % 3][:, lo:hi],
                                         AF.Exp, scale=8.0).then_inc(sE.sem, 1)
                            sE.val += 1
                    den_pair_i(hp)
    ctx.close()


def _p3_inner_ffn(nc, en_sem, lid, xf_in, xb_in, xf_out, xb_out,
                  w1_d, w2_d, b1a_d, b1g_d, b2_d, lng_d, lnb_d, ones_bf):
    ctx = ExitStack()
    en = ctx.enter_context
    L = str(lid)
    NJ = 22
    JW = [128] * 21 + [42]

    w2_bf = en(nc.sbuf_tensor("p3_w2" + L, [128, NJ * D], BF16))
    h2q = en(nc.sbuf_tensor("p3_h2" + L, [128, NJ * 1024], BF16))
    w1buf = en(nc.sbuf_tensor("p3_w1" + L, [128, 4 * NC * 128], BF16))
    hn_q = en(nc.sbuf_tensor("p3_hn" + L, [128, NC * 1024], BF16))
    hf = en(nc.sbuf_tensor("p3_hf" + L, [128, NC * 512], F32))
    hb = en(nc.sbuf_tensor("p3_hb" + L, [128, NC * 512], BF16))
    sqb = en(nc.sbuf_tensor("p3_sq" + L, [128, NC * 512], BF16))
    s_mu = en(nc.sbuf_tensor("p3_mu" + L, [128, 512], F32))
    s_var = en(nc.sbuf_tensor("p3_var" + L, [128, 512], F32))
    s_rs = en(nc.sbuf_tensor("p3_rs" + L, [128, 512], F32))
    s_tmp = en(nc.sbuf_tensor("p3_tmp" + L, [128, 512], F32))
    glu = en(nc.sbuf_tensor("p3_glu" + L, [128, 2 * 512], BF16))
    bcols = en(nc.sbuf_tensor("p3_bc" + L, [128, 2 * NJ + NC * 3], F32))
    xres = en(nc.sbuf_tensor("p3_xr" + L, [128, NC * 512], F32))
    x3_st = en(nc.sbuf_tensor("p3_x3" + L, [128, NC * 512], F32))

    pb = [en(nc.psum_tensor(f"p3_pb{i}" + L, [128, 512], F32)) for i in range(6)]

    wl = Ctr.get(en_sem, nc, "p3_wl")
    lhf = Ctr.get(en_sem, nc, "p3_lhf")
    lhb = Ctr.get(en_sem, nc, "p3_lhb")
    lw1 = [Ctr.get(en_sem, nc, f"p3_lw1{i}") for i in range(4)]
    lxr = Ctr.get(en_sem, nc, "p3_lxr")
    vSq = Ctr.get(en_sem, nc, "g5")
    tLn = Ctr.get(en_sem, nc, "g6")
    vMu = Ctr.get(en_sem, nc, "g7")
    sRs = Ctr.get(en_sem, nc, "g8")
    vHn = Ctr.get(en_sem, nc, "g9")
    tY = Ctr.get(en_sem, nc, "g10")
    sGl = Ctr.get(en_sem, nc, "g11")
    vH2 = Ctr.get(en_sem, nc, "g12")
    tW2 = Ctr.get(en_sem, nc, "g13")
    vX3 = Ctr.get(en_sem, nc, "g14")
    st3 = Ctr.get(en_sem, nc, "p3_st3")
    stb = Ctr.get(en_sem, nc, "p3_stb")
    vch = Ctr.get(en_sem, nc, "g17")

    with nc.Block() as blk:
        @blk.gpsimd
        def _(g):
            g.dma_start(w2_bf[:, 0:21 * D].rearrange("p (c j) -> p c j", c=21),
                        w2_d[0:21 * 128, :].rearrange("(c p) j -> p c j", p=128)
                        ).then_inc(wl.sem, 16)
            g.dma_start(w2_bf[0:42, 21 * D:22 * D], w2_d[21 * 128:2730, :]).then_inc(wl.sem, 16)
            g.dma_start(bcols[:, 0:NJ], b1a_d).then_inc(wl.sem, 16)
            g.dma_start(bcols[:, NJ:2 * NJ], b1g_d).then_inc(wl.sem, 16)
            g.dma_start(bcols[:, 2 * NJ:2 * NJ + NC], b2_d).then_inc(wl.sem, 16)
            g.dma_start(bcols[:, 2 * NJ + NC:2 * NJ + 2 * NC], lng_d).then_inc(wl.sem, 16)
            g.dma_start(bcols[:, 2 * NJ + 2 * NC:2 * NJ + 3 * NC], lnb_d).then_inc(wl.sem, 16)
            wl.val += 7 * 16
            g.wait_ge(wl.sem, wl.val)

    for qt in range(4):
        lhf0, lhb0, lxr0 = lhf.val, lhb.val, lxr.val
        vSq0, tLn0, vMu0, sRs0, vHn0 = vSq.val, tLn.val, vMu.val, sRs.val, vHn.val
        tY0, sGl0, vH20, tW20 = tY.val, sGl.val, vH2.val, tW2.val
        vX30, st30, stb0 = vX3.val, st3.val, stb.val
        lw10 = [c.val for c in lw1]

        with nc.Block() as blk:
            @blk.gpsimd
            def _(g):
                for seq in range(2 * NJ):
                    j, which = seq // 2, seq % 2
                    half = 1 - which
                    jw = JW[j]
                    j0 = half * 2730 + j * 128
                    if seq >= 4:
                        g.wait_ge(tY.sem, tY0 + (seq - 3) * 2)
                    g.dma_start(
                        w1buf[:, (seq % 4) * NC * 128:(seq % 4) * NC * 128 + NC * jw]
                        .rearrange("p (c j) -> p c j", c=NC),
                        w1_d[:, j0:j0 + jw].rearrange("(c p) j -> p c j", p=128)
                        ).then_inc(lw1[seq % 4].sem, 16)
                    lw1[seq % 4].val += 16
                for sub in range(2):
                    t0 = qt * 1024 + sub * 512
                    for mc in range(NC):
                        g.wait_ge(vX3.sem, vX30 + sub * NC + mc + 1)
                        g.dma_start(xb_out[mc * 128:(mc + 1) * 128, 1 + t0:1 + t0 + 512],
                                    x3_st[:, mc * 512:(mc + 1) * 512]).then_inc(stb.sem, 16)
                    stb.val += NC * 16
                if qt == 3:
                    g.wait_ge(stb.sem, stb.val)

            @blk.sync
            def _(sy):
                for sub in range(2):
                    t0 = qt * 1024 + sub * 512
                    sy.dma_start(hf[:, 0:4 * 512].rearrange("p (c j) -> p c j", c=4),
                                 xf_in[0:512, 1 + t0:1 + t0 + 512]
                                 .rearrange("(c p) j -> p c j", p=128)).then_inc(lhf.sem, 16)
                    sy.dma_start(hf[:, 4 * 512:NC * 512].rearrange("p (c j) -> p c j", c=4),
                                 xf_in[512:1024, 0 + t0:0 + t0 + 512]
                                 .rearrange("(c p) j -> p c j", p=128)).then_inc(lhf.sem, 16)
                    sy.dma_start(hb[:, 0:4 * 512].rearrange("p (c j) -> p c j", c=4),
                                 xb_in[0:512, 1 + t0:1 + t0 + 512]
                                 .rearrange("(c p) j -> p c j", p=128)).then_inc(lhb.sem, 16)
                    sy.dma_start(hb[:, 4 * 512:NC * 512].rearrange("p (c j) -> p c j", c=4),
                                 xb_in[512:1024, 0 + t0:0 + t0 + 512]
                                 .rearrange("(c p) j -> p c j", p=128)).then_inc(lhb.sem, 16)
                    lhf.val += 32
                    lhb.val += 32
                    sy.wait_ge(vHn.sem, vHn0 + (sub + 1) * NC)
                for sub in range(2):
                    t0 = qt * 1024 + sub * 512
                    if sub == 1:
                        sy.wait_ge(vX3.sem, vX30 + NC)
                    sy.dma_start(xres[:, :].rearrange("p (c j) -> p c j", c=NC),
                                 xf_in[:, 1 + t0:1 + t0 + 512]
                                 .rearrange("(c p) j -> p c j", p=128)).then_inc(lxr.sem, 16)
                    lxr.val += 16
                    for mc in range(NC):
                        sy.wait_ge(vX3.sem, vX30 + sub * NC + mc + 1)
                        sy.dma_start(xf_out[mc * 128:(mc + 1) * 128, 1 + t0:1 + t0 + 512],
                                     x3_st[:, mc * 512:(mc + 1) * 512]).then_inc(st3.sem, 16)
                    st3.val += NC * 16
                    sy.wait_ge(st3.sem, st30 + (sub + 1) * NC * 16)

            @blk.vector
            def _(v):
                v.wait_ge(stb.sem, stb0)
                for sub in range(2):
                    v.wait_ge(tLn.sem, tLn0 + (sub + 1) * 2)
                    v.tensor_single_scalar(s_mu[:, :], pb[4][:, :], 1.0 / D,
                                           ALU.mult).then_inc(vch.sem, 1)
                    vch.val += 1
                    v.wait_ge(vch.sem, vch.val)
                    v.tensor_tensor(s_var[:, :], s_mu[:, :], s_mu[:, :], ALU.mult)
                    v.tensor_scalar(s_tmp[:, :], pb[5][:, :], 1.0 / D, EPS, ALU.mult,
                                    ALU.add).then_inc(vch.sem, 1)
                    vch.val += 1
                    v.wait_ge(vch.sem, vch.val)
                    v.tensor_tensor(s_var[:, :], s_tmp[:, :], s_var[:, :],
                                    ALU.subtract).then_inc(vMu.sem, 1)
                    vMu.val += 1
                    v.wait_ge(sRs.sem, sRs0 + 2 * (sub + 1))
                    v.wait_ge(lhf.sem, lhf0 + (sub + 1) * 32)
                    for c in range(NC):
                        sl = slice(c * 512, (c + 1) * 512)
                        dst = slice(c * 1024 + sub * 512, c * 1024 + sub * 512 + 512)
                        v.tensor_tensor(x3_st[:, sl], hf[:, sl], s_mu[:, :],
                                        ALU.subtract).then_inc(vch.sem, 1)
                        vch.val += 1
                        v.wait_ge(vch.sem, vch.val)
                        v.tensor_tensor(x3_st[:, sl], x3_st[:, sl], s_rs[:, :],
                                        ALU.mult).then_inc(vch.sem, 1)
                        vch.val += 1
                        v.wait_ge(vch.sem, vch.val)
                        v.tensor_scalar(hn_q[:, dst], x3_st[:, sl],
                                        bcols[:, 2 * NJ + NC + c:2 * NJ + NC + c + 1],
                                        bcols[:, 2 * NJ + 2 * NC + c:2 * NJ + 2 * NC + c + 1],
                                        ALU.mult, ALU.add).then_inc(vHn.sem, 1)
                        vHn.val += 1
                for j in range(NJ):
                    for nb in range(2):
                        gi = j * 2 + nb
                        v.wait_ge(sGl.sem, sGl0 + gi + 1)
                        v.wait_ge(tY.sem, tY0 + j * 4 + 3 + nb)
                        v.scalar_tensor_tensor(
                            h2q[:, j * 1024 + nb * 512:j * 1024 + nb * 512 + 512],
                            pb[2 + gi % 2][:, :], bcols[:, j:j + 1],
                            glu[:, (gi % 2) * 512:(gi % 2) * 512 + 512],
                            ALU.add, ALU.mult).then_inc(vH2.sem, 1)
                        vH2.val += 1
                for sub in range(2):
                    if sub == 1:
                        v.wait_ge(stb.sem, stb0 + NC * 16)
                    v.wait_ge(lxr.sem, lxr0 + (sub + 1) * 16)
                    for mc in range(NC):
                        v.wait_ge(tW2.sem, tW20 + sub * NC + mc + 1)
                        v.scalar_tensor_tensor(x3_st[:, mc * 512:(mc + 1) * 512],
                                               pb[4 + mc % 2][:, :],
                                               bcols[:, 2 * NJ + mc:2 * NJ + mc + 1],
                                               xres[:, mc * 512:(mc + 1) * 512],
                                               ALU.add, ALU.add).then_inc(vX3.sem, 1)
                        vX3.val += 1

            @blk.scalar
            def _(s):
                for sub in range(2):
                    s.wait_ge(lhb.sem, lhb0 + (sub + 1) * 32)
                    for c in range(NC):
                        s.activation(sqb[:, c * 512:(c + 1) * 512],
                                     hb[:, c * 512:(c + 1) * 512],
                                     AF.Square).then_inc(vSq.sem, 1)
                        vSq.val += 1
                    s.wait_ge(vMu.sem, vMu0 + sub + 1)
                    s.activation(s_rs[:, :], s_var[:, :], AF.Ln).then_inc(sRs.sem, 1)
                    sRs.val += 1
                    s.wait_ge(sRs.sem, sRs.val)
                    s.activation(s_rs[:, :], s_rs[:, :], AF.Exp, scale=-0.5).then_inc(sRs.sem, 1)
                    sRs.val += 1
                for j in range(NJ):
                    for nb in range(2):
                        gi = j * 2 + nb
                        s.wait_ge(tY.sem, tY0 + j * 4 + nb + 1)
                        if gi >= 2:
                            s.wait_ge(vH2.sem, vH20 + gi - 1)
                        s.activation(glu[:, (gi % 2) * 512:(gi % 2) * 512 + 512],
                                     pb[gi % 2][:, :], AF.Gelu,
                                     bias=bcols[:, NJ + j:NJ + j + 1]).then_inc(sGl.sem, 1)
                        sGl.val += 1

            @blk.tensor
            def _(t):
                for sub in range(2):
                    t.wait_ge(lhb.sem, lhb0 + (sub + 1) * 32)
                    for c in range(NC):
                        t.matmul(pb[4][:, :], ones_bf[:, :], hb[:, c * 512:(c + 1) * 512],
                                 start=(c == 0), stop=(c == NC - 1))
                    t.wait_ge(vSq.sem, vSq0 + (sub + 1) * NC)
                    for c in range(NC):
                        inst = t.matmul(pb[5][:, :], ones_bf[:, :], sqb[:, c * 512:(c + 1) * 512],
                                        start=(c == 0), stop=(c == NC - 1))
                    inst.then_inc(tLn.sem, 2)
                    tLn.val += 2
                    t.wait_ge(vMu.sem, vMu0 + sub + 1)
                t.wait_ge(vHn.sem, vHn0 + 2 * NC)
                for j in range(NJ):
                    jw = JW[j]
                    for which, bank in ((0, 0), (1, 2)):
                        seq = j * 2 + which
                        t.wait_ge(lw1[seq % 4].sem, lw10[seq % 4] + (seq // 4 + 1) * 16)
                        for nb in range(2):
                            gi = j * 2 + nb
                            if gi >= 2:
                                if bank == 0:
                                    t.wait_ge(sGl.sem, sGl0 + gi - 1)
                                else:
                                    t.wait_ge(vH2.sem, vH20 + gi - 1)
                            for kc in range(NC):
                                inst = t.matmul(
                                    pb[bank + gi % 2][0:jw, :] if jw < 128 else pb[bank + gi % 2][:, :],
                                    w1buf[:, (seq % 4) * NC * 128 + kc * jw:
                                          (seq % 4) * NC * 128 + kc * jw + jw],
                                    hn_q[:, kc * 1024 + nb * 512:kc * 1024 + nb * 512 + 512],
                                    start=(kc == 0), stop=(kc == NC - 1))
                            inst.then_inc(tY.sem, 1)
                            tY.val += 1
                t.wait_ge(vH2.sem, vH20 + NJ * 2)
                for sub in range(2):
                    for mc in range(NC):
                        if sub * NC + mc >= 2:
                            t.wait_ge(vX3.sem, vX30 + sub * NC + mc - 1)
                        for j in range(NJ):
                            jw = JW[j]
                            inst = t.matmul(pb[4 + mc % 2][:, :],
                                            w2_bf[0:jw, j * D + mc * 128:j * D + mc * 128 + 128],
                                            h2q[0:jw, j * 1024 + sub * 512:j * 1024 + sub * 512 + 512],
                                            start=(j == 0), stop=(j == NJ - 1))
                        inst.then_inc(tW2.sem, 1)
                        tW2.val += 1
    ctx.close()


def build_program():
    Ctr.reset_registry()
    nc = bass.Bass("TRN2", target_bir_lowering=False)

    resid = nc.dram_tensor("resid", [T, D], F32, kind="ExternalInput").ap()
    ident = nc.dram_tensor("ident", [128, 128], F32, kind="ExternalInput").ap()
    out = nc.dram_tensor("out", [T, D], F32, kind="ExternalOutput").ap()

    xf0 = nc.dram_tensor("xf0", [D, W], F32, kind="Internal").ap()
    xb0 = nc.dram_tensor("xb0", [D, W], BF16, kind="Internal").ap()
    xf1 = nc.dram_tensor("xf1", [D, W], F32, kind="Internal").ap()
    xb1 = nc.dram_tensor("xb1", [D, W], BF16, kind="Internal").ap()
    qt_hbm = nc.dram_tensor("qt_hbm", [D, T], BF16, kind="Internal").ap()

    dbg_zt = nc.dram_tensor("dbg_zt", [128, NC * 512], BF16, kind="ExternalOutput").ap() \
        if DEBUG_ZT else None
    wq = nc.dram_tensor("wq", [D, D], F32, kind="ExternalInput").ap()
    wk = nc.dram_tensor("wk", [D, 512], F32, kind="ExternalInput").ap()
    wv = nc.dram_tensor("wv", [D, 512], F32, kind="ExternalInput").ap()
    wo = nc.dram_tensor("wo", [D, D], F32, kind="ExternalInput").ap()
    bout_d = nc.dram_tensor("bout", [128, NC], F32, kind="ExternalInput").ap()
    mask_diag_o = nc.dram_tensor("mask_diag_o", [128, 128], BF16, kind="ExternalInput").ap()
    wqkv_l = [nc.dram_tensor(f"wqkv{l}", [D, 1536], F32, kind="ExternalInput").ap()
              for l in range(2)]
    wo_il = [nc.dram_tensor(f"wo_i{l}", [512, D], F32, kind="ExternalInput").ap()
             for l in range(2)]
    scol_l = [nc.dram_tensor(f"scol{l}", [128, 4], F32, kind="ExternalInput").ap()
              for l in range(2)]
    tab_cs = nc.dram_tensor("tab_cs", [128, T], F32, kind="ExternalInput").ap()
    tab_sn = nc.dram_tensor("tab_sn", [128, T], F32, kind="ExternalInput").ap()
    tab_ci = nc.dram_tensor("tab_ci", [128, T], F32, kind="ExternalInput").ap()
    tab_si = nc.dram_tensor("tab_si", [128, T], F32, kind="ExternalInput").ap()
    r2_d = nc.dram_tensor("r2", [128, 128], F32, kind="ExternalInput").ap()
    bdsel_d = nc.dram_tensor("bdsel", [128, 32], BF16, kind="ExternalInput").ap()
    selbc_d = nc.dram_tensor("selbc", [128, 512], F32, kind="ExternalInput").ap()
    mask_di = nc.dram_tensor("mask_di", [128, 128], BF16, kind="ExternalInput").ap()
    mask_ai = nc.dram_tensor("mask_ai", [128, 128], BF16, kind="ExternalInput").ap()
    xf2 = nc.dram_tensor("xf2", [D, W], F32, kind="Internal").ap()
    xb2 = nc.dram_tensor("xb2", [D, W], BF16, kind="Internal").ap()
    xf3 = nc.dram_tensor("xf3", [D, W], F32, kind="Internal").ap()
    xb3 = nc.dram_tensor("xb3", [D, W], BF16, kind="Internal").ap()
    w1_l = [nc.dram_tensor(f"w1_{l}", [D, 5460], F32, kind="ExternalInput").ap()
            for l in range(2)]
    w2_l = [nc.dram_tensor(f"w2_{l}", [2730, D], F32, kind="ExternalInput").ap()
            for l in range(2)]
    b1a_l = [nc.dram_tensor(f"b1a_{l}", [128, 22], F32, kind="ExternalInput").ap()
             for l in range(2)]
    b1g_l = [nc.dram_tensor(f"b1g_{l}", [128, 22], F32, kind="ExternalInput").ap()
             for l in range(2)]
    b2_l = [nc.dram_tensor(f"b2_{l}", [128, NC], F32, kind="ExternalInput").ap()
            for l in range(2)]
    lng_l = [nc.dram_tensor(f"lng_{l}", [128, NC], F32, kind="ExternalInput").ap()
             for l in range(2)]
    lnb_l = [nc.dram_tensor(f"lnb_{l}", [128, NC], F32, kind="ExternalInput").ap()
             for l in range(2)]

    with ExitStack() as ctx:
        en = ctx.enter_context
        ident_f32 = en(nc.sbuf_tensor("ident_f32", [128, 128], F32))
        ident_bf = en(nc.sbuf_tensor("ident_bf", [128, 128], BF16))
        ones_bf = en(nc.sbuf_tensor("ones_bf", [128, 128], BF16))
        ones64f = en(nc.sbuf_tensor("ones64f", [128, 64], F32))
        sel65_f = en(nc.sbuf_tensor("sel65_f", [128, 128], BF16))
        boot = Ctr(en, nc, "boot")
        boot2 = Ctr(en, nc, "boot2")
        with nc.Block() as blk:
            @blk.gpsimd
            def _(g):
                g.dma_start(ident_f32[:, :], ident).then_inc(boot.sem, 16)
                g.dma_start(ident_bf[:, :], ident).then_inc(boot.sem, 16)
                g.memset(ones_bf[:, :], 1.0)
                g.memset(ones64f[:, :], 1.0)
                g.wait_ge(boot.sem, 32)

            @blk.vector
            def _(v):
                v.memset(sel65_f[:, :], 0.0).then_inc(boot2.sem, 1)
                v.wait_ge(boot2.sem, 1)
                v.memset(sel65_f[0:1, 0:64], 1.0)
                v.memset(sel65_f[64:65, 64:128], 1.0)

        _p0_transpose_in(nc, en, resid, xf0, xb0, ident_f32,
                         [xf1, xf2, xf3], [xb1, xb2, xb3])
        if LEVEL >= 1:
            _p1_outer_attn(nc, en, xf0, xb0, xf1, xb1, qt_hbm, wq, wk, wv, wo, bout_d,
                           mask_diag_o, ident_f32, ident_bf, ones_bf, ones64f,
                           sel65_f, dbg_zt)
            if LEVEL >= 2:
                _p2_inner_attn(nc, en, 0, xf1, xb1, xf2, xb2,
                               wqkv_l[0], wo_il[0], scol_l[0],
                               tab_cs, tab_sn, tab_ci, tab_si, r2_d, bdsel_d, selbc_d,
                               mask_di, mask_ai, sel65_f, ident_bf)
                if LEVEL >= 3:
                    _p3_inner_ffn(nc, en, 0, xf2, xb2, xf3, xb3,
                                  w1_l[0], w2_l[0], b1a_l[0], b1g_l[0], b2_l[0],
                                  lng_l[0], lnb_l[0], ones_bf)
                    if LEVEL >= 4:
                        _p2_inner_attn(nc, en, 1, xf3, xb3, xf2, xb2,
                                       wqkv_l[1], wo_il[1], scol_l[1],
                                       tab_cs, tab_sn, tab_ci, tab_si, r2_d, bdsel_d,
                                       selbc_d, mask_di, mask_ai, sel65_f, ident_bf)
                        _p3_inner_ffn(nc, en, 1, xf2, xb2, xf3, xb3,
                                      w1_l[1], w2_l[1], b1a_l[1], b1g_l[1], b2_l[1],
                                      lng_l[1], lnb_l[1], ones_bf)
                        _p6_final(nc, en, xf1, xf3, out, ident_f32, ones_bf)
                    else:
                        _p6_final(nc, en, xf3, xf3, out, ident_f32, ones_bf)
                else:
                    _p6_final(nc, en, xf2, xf2, out, ident_f32, ones_bf)
            else:
                _p6_final(nc, en, xf1, xf1, out, ident_f32, ones_bf)
        else:
            _p6_final(nc, en, xf0, xf0, out, ident_f32, ones_bf)

    return nc


def _p0_transpose_in(nc, en_sem, resid, xf0, xb0, ident_f32, xf_more=(), xb_more=()):
    ctx = ExitStack()
    en = ctx.enter_context
    t_in = en(nc.sbuf_tensor("p0_in", [128, 4 * D], F32))
    t_st = en(nc.sbuf_tensor("p0_stage", [128, 4 * D], F32))
    t_z = en(nc.sbuf_tensor("p0_z", [128, NC], F32))
    t_zb = en(nc.sbuf_tensor("p0_zb", [128, NC], BF16))
    ps = [en(nc.psum_tensor(f"p0_ps{i}", [128, 512], F32)) for i in range(4)]

    ld4 = [Ctr.get(en_sem, nc, f"p0_ld{i}") for i in range(4)]
    tp = Ctr.get(en_sem, nc, "p0_tp")
    evv = Ctr.get(en_sem, nc, "p0_evv")
    stf4 = [Ctr.get(en_sem, nc, f"p0_stf{i}") for i in range(4)]
    stb4 = [Ctr.get(en_sem, nc, f"p0_stb{i}") for i in range(4)]
    stz = Ctr.get(en_sem, nc, "p0_stz")

    NT = T // 128

    mz = Ctr.get(en_sem, nc, "p0_mz")
    with nc.Block() as blk:
        @blk.gpsimd
        def _(g):
            g.memset(t_z[:, :], 0.0)
            g.memset(t_zb[:, :], 0.0).then_inc(mz.sem, 1)
            for tt in range(NT):
                if tt >= 4:
                    g.wait_ge(tp.sem, (tt - 3) * NC)
                g.dma_start(t_in[:, (tt % 4) * D:(tt % 4) * D + D],
                            resid[tt * 128:(tt + 1) * 128, :]).then_inc(ld4[tt % 4].sem, 16)
                ld4[tt % 4].val += 16
                if tt >= 1:
                    g.wait_ge(evv.sem, tt * NC)
                    g.dma_start(xb0[:, 1 + (tt - 1) * 128:1 + tt * 128]
                                .rearrange("(c p) j -> p c j", p=128),
                                t_st[:, ((tt - 1) % 4) * D:((tt - 1) % 4) * D + D]
                                .rearrange("p (c j) -> p c j", c=NC)
                                ).then_inc(stb4[(tt - 1) % 4].sem, 16)
                    stb4[(tt - 1) % 4].val += 16
            g.wait_ge(evv.sem, NT * NC)
            g.dma_start(xb0[:, 1 + (NT - 1) * 128:1 + NT * 128]
                        .rearrange("(c p) j -> p c j", p=128),
                        t_st[:, ((NT - 1) % 4) * D:((NT - 1) % 4) * D + D]
                        .rearrange("p (c j) -> p c j", c=NC)).then_inc(stb4[(NT - 1) % 4].sem, 16)
            stb4[(NT - 1) % 4].val += 16
            for i in range(4):
                g.wait_ge(stb4[i].sem, stb4[i].val)

        @blk.tensor
        def _(t):
            for tt in range(NT):
                t.wait_ge(ld4[tt % 4].sem, (tt // 4 + 1) * 16)
                for c in range(NC):
                    gi = tt * NC + c
                    if gi >= 4:
                        t.wait_ge(evv.sem, gi - 3)
                    t.transpose(ps[gi % 4][:, 0:128],
                                t_in[:, (tt % 4) * D + c * 128:(tt % 4) * D + c * 128 + 128],
                                ident_f32[:, :]).then_inc(tp.sem, 1)
            tp.val += NT * NC

        @blk.vector
        def _(v):
            for tt in range(NT):
                if tt >= 4:
                    v.wait_ge(stf4[tt % 4].sem, ((tt - 4) // 4 + 1) * 16)
                    v.wait_ge(stb4[tt % 4].sem, ((tt - 4) // 4 + 1) * 16)
                for c in range(NC):
                    gi = tt * NC + c
                    v.wait_ge(tp.sem, gi + 1)
                    v.tensor_copy(t_st[:, (tt % 4) * D + c * 128:(tt % 4) * D + c * 128 + 128],
                                  ps[gi % 4][:, 0:128]).then_inc(evv.sem, 1)
            evv.val += NT * NC

        @blk.sync
        def _(sy):
            sy.wait_ge(mz.sem, 1)
            for xf_t in [xf0] + list(xf_more):
                for c in range(NC):
                    sy.dma_start(xf_t[c * 128:(c + 1) * 128, 0:1], t_z[:, c:c + 1],
                                 allow_slow_non_contiguous=True).then_inc(stz.sem, 16)
                    stz.val += 16
            for xb_t in [xb0] + list(xb_more):
                for c in range(NC):
                    sy.dma_start(xb_t[c * 128:(c + 1) * 128, 0:1], t_zb[:, c:c + 1],
                                 allow_slow_non_contiguous=True).then_inc(stz.sem, 16)
                    stz.val += 16
            for tt in range(NT):
                sy.wait_ge(evv.sem, (tt + 1) * NC)
                sy.dma_start(
                    xf0[:, 1 + tt * 128:1 + (tt + 1) * 128].rearrange("(c p) j -> p c j", p=128),
                    t_st[:, (tt % 4) * D:(tt % 4) * D + D].rearrange("p (c j) -> p c j", c=NC)
                    ).then_inc(stf4[tt % 4].sem, 16)
                stf4[tt % 4].val += 16
            sy.wait_ge(stz.sem, stz.val)
            for i in range(4):
                sy.wait_ge(stf4[i].sem, stf4[i].val)
    ctx.close()


def _p6_final(nc, en_sem, xfa, xfb, out, ident_f32, ones_bf):
    ctx = ExitStack()
    en = ctx.enter_context
    s_a = en(nc.sbuf_tensor("p6_a", [128, 2 * NC * QB], F32))
    s_b = en(nc.sbuf_tensor("p6_b", [128, 2 * NC * QB], F32))
    s_r = en(nc.sbuf_tensor("p6_r", [128, 2 * NC * QB], F32))
    s_rb = en(nc.sbuf_tensor("p6_rb", [128, 2 * NC * QB], BF16))
    s_sq = en(nc.sbuf_tensor("p6_sq", [128, 2 * NC * QB], BF16))
    s_mu = en(nc.sbuf_tensor("p6_mu", [128, 2 * QB], F32))
    s_var = en(nc.sbuf_tensor("p6_var", [128, 2 * QB], F32))
    s_rs = en(nc.sbuf_tensor("p6_rs", [128, 2 * QB], F32))
    s_ot = en(nc.sbuf_tensor("p6_ot", [128, 4 * D], F32))
    ps_s = [en(nc.psum_tensor(f"p6_pss{i}", [128, 512], F32)) for i in range(2)]
    ps_q = [en(nc.psum_tensor(f"p6_psq{i}", [128, 512], F32)) for i in range(2)]
    ps_t = [en(nc.psum_tensor(f"p6_pst{i}", [128, 512], F32)) for i in range(4)]

    ld2 = [Ctr.get(en_sem, nc, f"p6_ld{i}") for i in range(2)]
    vrr = Ctr.get(en_sem, nc, "g1")
    ssq = Ctr.get(en_sem, nc, "g2")
    tst = Ctr.get(en_sem, nc, "g3")
    vvs = Ctr.get(en_sem, nc, "g4")
    srs = Ctr.get(en_sem, nc, "g5")
    vxn = Ctr.get(en_sem, nc, "g6")
    ttp = Ctr.get(en_sem, nc, "g7")
    vev = Ctr.get(en_sem, nc, "g8")
    st = Ctr.get(en_sem, nc, "p6_st")
    vch = Ctr.get(en_sem, nc, "g10")
    sch = Ctr.get(en_sem, nc, "g11")
    b_ld2 = [c.val for c in ld2]
    b_vrr, b_ssq, b_tst = vrr.val, ssq.val, tst.val
    b_vvs, b_srs, b_vxn, b_ttp, b_vev, b_st = (vvs.val, srs.val, vxn.val, ttp.val,
                                               vev.val, st.val)

    with nc.Block() as blk:
        @blk.sync
        def _(sy):
            for qb in range(NQB + 1):
                if qb < NQB:
                    if qb >= 2:
                        sy.wait_ge(vrr.sem, b_vrr + (qb - 1) * NC)
                    xs = (qb % 2) * NC * QB
                    sy.dma_start(s_a[:, xs:xs + NC * QB].rearrange("p (c j) -> p c j", c=NC),
                                 xfa[:, 1 + qb * QB:1 + (qb + 1) * QB].rearrange(
                                     "(c p) j -> p c j", p=128)).then_inc(ld2[qb % 2].sem, 16)
                    sy.dma_start(s_b[:, xs:xs + NC * QB].rearrange("p (c j) -> p c j", c=NC),
                                 xfb[:, 1 + qb * QB:1 + (qb + 1) * QB].rearrange(
                                     "(c p) j -> p c j", p=128)).then_inc(ld2[qb % 2].sem, 16)
                    ld2[qb % 2].val += 32
                if qb >= 1:
                    q0 = qb - 1
                    for half in range(2):
                        sy.wait_ge(vev.sem, b_vev + q0 * 4 * NC + (half + 1) * 2 * NC)
                        sy.dma_start(out[q0 * QB + half * 256:q0 * QB + half * 256 + 256, :]
                                     .rearrange("(i p) j -> p i j", p=128),
                                     s_ot[:, half * 2 * D:(half * 2 + 2) * D]
                                     .rearrange("p (i j) -> p i j", i=2)
                                     ).then_inc(st.sem, 16)
                        st.val += 16
            sy.wait_ge(st.sem, st.val)

        @blk.vector
        def _(v):
            for qb in range(NQB):
                xs = (qb % 2) * NC * QB
                ms = (qb % 2) * QB
                v.wait_ge(ld2[qb % 2].sem, b_ld2[qb % 2] + 32 * (qb // 2 + 1))
                if qb >= 2:
                    v.wait_ge(ttp.sem, b_ttp + (qb - 1) * 4 * NC)
                    v.wait_ge(tst.sem, b_tst + 2 * (qb - 1))
                    v.wait_ge(srs.sem, b_srs + qb - 1)
                for c in range(NC):
                    sl = slice(xs + c * QB, xs + (c + 1) * QB)
                    v.tensor_tensor(s_r[:, sl], s_a[:, sl], s_b[:, sl],
                                    ALU.add).then_inc(vch.sem, 1)
                    vch.val += 1
                    v.wait_ge(vch.sem, vch.val)
                    v.tensor_copy(s_rb[:, sl], s_r[:, sl]).then_inc(vrr.sem, 1)
                    vrr.val += 1
                v.wait_ge(tst.sem, b_tst + 2 * (qb + 1))
                v.tensor_single_scalar(s_mu[:, ms:ms + QB], ps_s[qb % 2][:, :], 1.0 / D,
                                       ALU.mult).then_inc(vch.sem, 1)
                vch.val += 1
                v.wait_ge(vch.sem, vch.val)
                v.tensor_tensor(s_var[:, ms:ms + QB], s_mu[:, ms:ms + QB],
                                s_mu[:, ms:ms + QB], ALU.mult)
                v.tensor_scalar(s_rs[:, ms:ms + QB], ps_q[qb % 2][:, :], 1.0 / D, EPS,
                                ALU.mult, ALU.add).then_inc(vch.sem, 1)
                vch.val += 1
                v.wait_ge(vch.sem, vch.val)
                v.tensor_tensor(s_var[:, ms:ms + QB], s_rs[:, ms:ms + QB],
                                s_var[:, ms:ms + QB], ALU.subtract).then_inc(vvs.sem, 1)
                vvs.val += 1
                v.wait_ge(srs.sem, b_srs + qb + 1)
                for c in range(NC):
                    sl = slice(xs + c * QB, xs + (c + 1) * QB)
                    v.tensor_tensor(s_r[:, sl], s_r[:, sl], s_mu[:, ms:ms + QB],
                                    ALU.subtract).then_inc(vch.sem, 1)
                    vch.val += 1
                    v.wait_ge(vch.sem, vch.val)
                    v.tensor_tensor(s_r[:, sl], s_r[:, sl], s_rs[:, ms:ms + QB],
                                    ALU.mult).then_inc(vxn.sem, 1)
                    vxn.val += 1
                for ttile in range(4):
                    if qb >= 1 and ttile == 0:
                        v.wait_ge(st.sem, b_st + qb * 32)
                    for c in range(NC):
                        g = qb * 4 * NC + ttile * NC + c
                        v.wait_ge(ttp.sem, b_ttp + g + 1)
                        v.tensor_copy(s_ot[:, ttile * D + c * 128:ttile * D + c * 128 + 128],
                                      ps_t[g % 4][:, 0:128]).then_inc(vev.sem, 1)
                        vev.val += 1

        @blk.scalar
        def _(s):
            for qb in range(NQB):
                xs = (qb % 2) * NC * QB
                ms = (qb % 2) * QB
                if qb >= 2:
                    s.wait_ge(tst.sem, b_tst + 2 * (qb - 1))
                for c in range(NC):
                    sl = slice(xs + c * QB, xs + (c + 1) * QB)
                    s.wait_ge(vrr.sem, b_vrr + qb * NC + c + 1)
                    s.activation(s_sq[:, sl], s_rb[:, sl], AF.Square).then_inc(ssq.sem, 1)
                    ssq.val += 1
                s.wait_ge(vvs.sem, b_vvs + qb + 1)
                s.activation(s_rs[:, ms:ms + QB], s_var[:, ms:ms + QB],
                             AF.Ln).then_inc(sch.sem, 1)
                sch.val += 1
                s.wait_ge(sch.sem, sch.val)
                s.activation(s_rs[:, ms:ms + QB], s_rs[:, ms:ms + QB],
                             AF.Exp, scale=-0.5).then_inc(srs.sem, 1)
                srs.val += 1

        @blk.tensor
        def _(t):
            for qb in range(NQB):
                xs = (qb % 2) * NC * QB
                t.wait_ge(vrr.sem, b_vrr + (qb + 1) * NC)
                if qb >= 2:
                    t.wait_ge(vvs.sem, b_vvs + qb - 1)
                for c in range(NC):
                    t.matmul(ps_s[qb % 2][:, :], ones_bf[:, :],
                             s_rb[:, xs + c * QB:xs + (c + 1) * QB],
                             start=(c == 0), stop=(c == NC - 1))
                t.wait_ge(ssq.sem, b_ssq + (qb + 1) * NC)
                for c in range(NC):
                    inst = t.matmul(ps_q[qb % 2][:, :], ones_bf[:, :],
                                    s_sq[:, xs + c * QB:xs + (c + 1) * QB],
                                    start=(c == 0), stop=(c == NC - 1))
                inst.then_inc(tst.sem, 2)
                tst.val += 2
                for ttile in range(4):
                    for c in range(NC):
                        g = qb * 4 * NC + ttile * NC + c
                        if g >= 4:
                            t.wait_ge(vev.sem, b_vev + g - 3)
                        if ttile == 0 and c == 0:
                            t.wait_ge(vxn.sem, b_vxn + (qb + 1) * NC)
                        t.transpose(ps_t[g % 4][:, 0:128],
                                    s_r[:, xs + c * QB + ttile * 128:
                                        xs + c * QB + ttile * 128 + 128],
                                    ident_f32[:, :]).then_inc(ttp.sem, 1)
                        ttp.val += 1
    ctx.close()


_cache = {}


def kernel(**inputs):
    resid_full = np.ascontiguousarray(np.asarray(inputs["L0_attn_resid"], np.float32))
    B = resid_full.shape[0]
    ident = np.eye(128, dtype=np.float32)

    hperm = np.zeros(D, np.int64)
    for h in range(16):
        qch, qhf = (h // 4) * 2 + h % 2, (h // 2) % 2
        hperm[qch * 128 + qhf * 64: qch * 128 + qhf * 64 + 64] = np.arange(h * 64, h * 64 + 64)
    wq = np.ascontiguousarray(np.asarray(inputs["L0_attn_W_Q"], np.float32)
                              .reshape(D, D)[:, hperm])
    wk = np.ascontiguousarray(np.asarray(inputs["L0_attn_W_K"], np.float32).reshape(D, 512))
    wv = np.ascontiguousarray(np.asarray(inputs["L0_attn_W_V"], np.float32).reshape(D, 512))
    wo = np.ascontiguousarray(
        np.asarray(inputs["L0_attn_W_out"], np.float32).transpose(1, 0, 2).reshape(D, D)[hperm, :])
    bout = np.ascontiguousarray(
        np.asarray(inputs["L0_attn_b_out"], np.float32).reshape(NC, 128).T)
    ii = np.arange(128)
    mask_do = np.where(ii[:, None] <= ii[None, :], 0.0, -1e6).astype(ml_dtypes.bfloat16)
    mask_di = np.where(ii[:, None] <= ii[None, :], 0.0, -1e36).astype(ml_dtypes.bfloat16)
    mask_ai = np.where(ii[:, None] >= ii[None, :], 0.0, -1e36).astype(ml_dtypes.bfloat16)

    half = np.arange(0, 64, 2, dtype=np.float64)
    inv_freq = 1.0 / 10000.0 ** (half / 64.0)
    tt_ = np.arange(T, dtype=np.float64)
    freqs = np.concatenate([tt_[:, None] * inv_freq, tt_[:, None] * inv_freq], axis=1)
    base = np.concatenate([(half + 25.6) / 89.6] * 2)
    power = (tt_ - T // 2) / 256.0
    scale = base[None, :] ** power[:, None]
    cosf, sinf = np.cos(freqs), np.sin(freqs)
    t2 = lambda a: np.ascontiguousarray(np.tile(a.T, (2, 1)).astype(np.float32))
    tab_cs = t2(cosf * scale)
    tab_sn = t2(sinf * scale)
    tab_ci = t2(cosf / scale)
    tab_si = t2(sinf / scale)

    def scol_for(l):
        qs = np.asarray(inputs["L0_ffn_q_scale"], np.float32)[l]
        ks = np.asarray(inputs["L0_ffn_k_scale"], np.float32)[l]
        rot = lambda a: np.concatenate([a[32:], a[:32]])
        return np.ascontiguousarray(np.stack(
            [np.tile(qs, 2), np.tile(rot(qs), 2), np.tile(ks, 2), np.tile(rot(ks), 2)],
            axis=1))

    r2 = np.zeros((128, 128), np.float32)
    for b0 in (0, 64):
        for i in range(32):
            r2[b0 + 32 + i, b0 + i] = -1.0
            r2[b0 + i, b0 + 32 + i] = 1.0
    bdsel = np.zeros((128, 32), np.float32)
    for c in range(4):
        for p in range(128):
            bdsel[p, c * 8 + 2 * c + p // 64] = 1.0
    bdsel = bdsel.astype(ml_dtypes.bfloat16)
    selbc = np.zeros((128, 512), np.float32)
    for c in range(4):
        for p in range(128):
            selbc[2 * c + p // 64, c * 128 + p] = 1.0

    def colpad(v):
        out = np.zeros((128, 22), np.float32)
        for j in range(22):
            w = min(128, 2730 - j * 128)
            out[:w, j] = v[j * 128:j * 128 + w]
        return out

    nc = build_program()
    in_maps = []
    for b in range(B):
        in_maps.append(dict(
            resid=np.ascontiguousarray(resid_full[b]),
            ident=ident,
            wq=wq, wk=wk, wv=wv, wo=wo, bout=bout, mask_diag_o=mask_do,
            wqkv0=np.ascontiguousarray(np.asarray(inputs["L0_ffn_wqkv"], np.float32)[0]),
            wqkv1=np.ascontiguousarray(np.asarray(inputs["L0_ffn_wqkv"], np.float32)[1]),
            wo_i0=np.ascontiguousarray(np.asarray(inputs["L0_ffn_wo"], np.float32)[0]),
            wo_i1=np.ascontiguousarray(np.asarray(inputs["L0_ffn_wo"], np.float32)[1]),
            scol0=scol_for(0), scol1=scol_for(1),
            tab_cs=tab_cs, tab_sn=tab_sn, tab_ci=tab_ci, tab_si=tab_si,
            r2=r2, bdsel=bdsel, selbc=selbc, mask_di=mask_di, mask_ai=mask_ai,
            **{f"w1_{l}": np.ascontiguousarray(np.asarray(inputs["L0_ffn_w1"], np.float32)[l])
               for l in range(2)},
            **{f"w2_{l}": np.ascontiguousarray(np.asarray(inputs["L0_ffn_w2"], np.float32)[l])
               for l in range(2)},
            **{f"b1a_{l}": colpad(np.asarray(inputs["L0_ffn_b1"], np.float32)[l][:2730])
               for l in range(2)},
            **{f"b1g_{l}": colpad(np.asarray(inputs["L0_ffn_b1"], np.float32)[l][2730:])
               for l in range(2)},
            **{f"b2_{l}": np.ascontiguousarray(
                np.asarray(inputs["L0_ffn_b2"], np.float32)[l].reshape(NC, 128).T)
               for l in range(2)},
            **{f"lng_{l}": np.ascontiguousarray(
                np.asarray(inputs["L0_ffn_ln_g"], np.float32)[l].reshape(NC, 128).T)
               for l in range(2)},
            **{f"lnb_{l}": np.ascontiguousarray(
                np.asarray(inputs["L0_ffn_ln_b"], np.float32)[l].reshape(NC, 128).T)
               for l in range(2)},
        ))
    res = run_bass_kernel_spmd(nc, in_maps, core_ids=list(range(B)))
    if DEBUG_ZT:
        kernel._dbg_zt = np.stack([np.asarray(r["dbg_zt"], np.float32) for r in res.results])
    return np.stack([np.asarray(r["out"], np.float32) for r in res.results], axis=0)


# version 16
# speedup vs baseline: 1.0568x; 1.0568x over previous
import numpy as np
import ml_dtypes
from contextlib import ExitStack

import concourse.bass as bass
import concourse.mybir as mybir
from concourse.bass_utils import run_bass_kernel_spmd

F32 = mybir.dt.float32
BF16 = mybir.dt.bfloat16
AF = mybir.ActivationFunctionType
ALU = mybir.AluOpType

T = 4096
D = 1024
NC = 8
W = T + 1
QB = 512
NQB = T // QB
EPS = 1e-5

LEVEL = 4
P2_STAGE = 4
DEBUG_ZT = False
DBG_QB = 0


class Ctr:
    _registry = {}

    def __init__(self, en, nc, name):
        self.sem = en(nc.semaphore(name))
        self.val = 0

    @classmethod
    def get(cls, en, nc, name):
        if name not in cls._registry:
            cls._registry[name] = Ctr(en, nc, name)
        return cls._registry[name]

    @classmethod
    def reset_registry(cls):
        cls._registry = {}


def _ln_stats(nc, v, s, t, ones_bf, rb_slices, sq_slices, ps_s, ps_q, s_mu, s_var, s_rs,
              c_rdy, c_sq, c_st, c_vs, c_rs):
    n = len(rb_slices)
    for c in range(n):
        t.matmul(ps_s, ones_bf, rb_slices[c], start=(c == 0), stop=(c == n - 1))
    for c in range(n):
        inst = t.matmul(ps_q, ones_bf, sq_slices[c], start=(c == 0), stop=(c == n - 1))
    inst.then_inc(c_st.sem, 2)
    c_st.val += 2




def _p1_outer_attn(nc, en_sem, xf0, xb0, xf1, xb1, qt_hbm, wq, wk, wv, wo, bout_d,
                   mask_diag_o, ident_f32, ident_bf, ones_bf, ones64f, sel65_f,
                   dbg_zt=None):
    NH = 16
    ctx_p1 = ExitStack()
    ctx_a = ExitStack()
    enp = ctx_p1.enter_context
    ena = ctx_a.enter_context

    wo_bf = enp(nc.sbuf_tensor("p1_wo", [128, NC * D], BF16))
    bout_cols = enp(nc.sbuf_tensor("p1_bout", [128, NC], F32))
    kt_bf = enp(nc.sbuf_tensor("p1_kt", [128, 4 * T], BF16))
    v_bf = enp(nc.sbuf_tensor("p1_v", [128, 32 * 520], BF16))
    wq_bf = ena(nc.sbuf_tensor("p1_wq", [128, NC * D], BF16))
    wk_bf = ena(nc.sbuf_tensor("p1_wk", [128, NC * 512], BF16))
    wv_bf = ena(nc.sbuf_tensor("p1_wv", [128, NC * 512], BF16))
    xq = ena(nc.sbuf_tensor("p1_xq", [128, 2 * NC * 512], BF16))
    qt_stage = ena(nc.sbuf_tensor("p1_qst", [128, 4 * 512], BF16))

    ps_a = [ena(nc.psum_tensor(f"p1_psa{i}", [128, 512], F32)) for i in range(2)]
    ps_b = [ena(nc.psum_tensor(f"p1_psb{i}", [128, 512], F32)) for i in range(2)]

    wl = Ctr(en_sem, nc, "p1_wl")
    ld2 = [Ctr(en_sem, nc, f"p1_ld{i}") for i in range(2)]
    tq = Ctr(en_sem, nc, "p1_tq")
    tv = Ctr(en_sem, nc, "p1_tv")
    se = Ctr(en_sem, nc, "p1_se")
    vv = Ctr(en_sem, nc, "p1_vv")
    sq4 = [Ctr(en_sem, nc, f"p1_sq{i}") for i in range(4)]

    with nc.Block() as blk:
        @blk.gpsimd
        def _(g):
            for c in range(NC):
                g.dma_start(wq_bf[:, c * D:(c + 1) * D],
                            wq[c * 128:(c + 1) * 128, :]).then_inc(wl.sem, 16)
                g.dma_start(wk_bf[:, c * 512:(c + 1) * 512],
                            wk[c * 128:(c + 1) * 128, :]).then_inc(wl.sem, 16)
                g.dma_start(wv_bf[:, c * 512:(c + 1) * 512],
                            wv[c * 128:(c + 1) * 128, :]).then_inc(wl.sem, 16)
                g.dma_start(wo_bf[:, c * D:(c + 1) * D],
                            wo[c * 128:(c + 1) * 128, :]).then_inc(wl.sem, 16)
            g.dma_start(bout_cols[:, :], bout_d).then_inc(wl.sem, 16)
            wl.val += (4 * NC + 1) * 16
            g.memset(bass.AP(v_bf, 64, [[32 * 520, 128], [520, 32], [65, 8], [1, 1]]), 1.0)

        @blk.sync
        def _(sy):
            for nb in range(NQB):
                if nb >= 2:
                    sy.wait_ge(tq.sem, (nb - 1) * 12)
                    sy.wait_ge(tv.sem, (nb - 1) * 4)
                sy.dma_start(xq[:, (nb % 2) * NC * 512:(nb % 2) * NC * 512 + NC * 512]
                             .rearrange("p (c j) -> p c j", c=NC),
                             xb0[:, 1 + nb * 512:1 + (nb + 1) * 512]
                             .rearrange("(c p) j -> p c j", p=128)).then_inc(ld2[nb % 2].sem, 16)
                ld2[nb % 2].val += 16
                for mc in range(NC):
                    qidx = nb * NC + mc
                    sy.wait_ge(se.sem, nb * 12 + mc + 1)
                    sy.dma_start(qt_hbm[mc * 128:(mc + 1) * 128, nb * 512:(nb + 1) * 512],
                                 qt_stage[:, (qidx % 4) * 512:(qidx % 4) * 512 + 512]
                                 ).then_inc(sq4[qidx % 4].sem, 16)
                    sq4[qidx % 4].val += 16
            for i in range(4):
                sy.wait_ge(sq4[i].sem, sq4[i].val)

        @blk.tensor
        def _(t):
            t.wait_ge(wl.sem, wl.val)
            for nb in range(NQB):
                t.wait_ge(ld2[nb % 2].sem, (nb // 2 + 1) * 16)
                xs = (nb % 2) * NC * 512
                for mc in range(8):
                    gq = nb * 12 + mc
                    if gq >= 2:
                        t.wait_ge(se.sem, gq - 1)
                    for kc in range(NC):
                        inst = t.matmul(ps_a[gq % 2][:, :],
                                        wq_bf[:, kc * D + mc * 128:kc * D + mc * 128 + 128],
                                        xq[:, xs + kc * 512:xs + (kc + 1) * 512],
                                        start=(kc == 0), stop=(kc == NC - 1))
                    inst.then_inc(tq.sem, 1)
                    tq.val += 1
                for mc in range(4):
                    gq = nb * 12 + 8 + mc
                    if gq >= 2:
                        t.wait_ge(se.sem, gq - 1)
                    for kc in range(NC):
                        inst = t.matmul(ps_a[gq % 2][:, :],
                                        wk_bf[:, kc * 512 + mc * 128:kc * 512 + mc * 128 + 128],
                                        xq[:, xs + kc * 512:xs + (kc + 1) * 512],
                                        start=(kc == 0), stop=(kc == NC - 1))
                    inst.then_inc(tq.sem, 1)
                    tq.val += 1
                for tt in range(4):
                    gv = nb * 4 + tt
                    if gv >= 2:
                        t.wait_ge(vv.sem, gv - 1)
                    for kc in range(NC):
                        inst = t.matmul(ps_b[gv % 2][:, :],
                                        xq[:, xs + kc * 512 + tt * 128:xs + kc * 512 + tt * 128 + 128],
                                        wv_bf[:, kc * 512:(kc + 1) * 512],
                                        start=(kc == 0), stop=(kc == NC - 1))
                    inst.then_inc(tv.sem, 1)
                    tv.val += 1

        @blk.scalar
        def _(s):
            for nb in range(NQB):
                for i in range(12):
                    gq = nb * 12 + i
                    s.wait_ge(tq.sem, gq + 1)
                    if i < 8:
                        qidx = nb * NC + i
                        if qidx >= 4:
                            s.wait_ge(sq4[qidx % 4].sem, (qidx // 4) * 16)
                        s.copy(qt_stage[:, (qidx % 4) * 512:(qidx % 4) * 512 + 512],
                               ps_a[gq % 2][:, :]).then_inc(se.sem, 1)
                    else:
                        mc = i - 8
                        s.copy(kt_bf[:, mc * T + nb * 512:mc * T + (nb + 1) * 512],
                               ps_a[gq % 2][:, :]).then_inc(se.sem, 1)
                    se.val += 1

        @blk.vector
        def _(v):
            for nb in range(NQB):
                for tt in range(4):
                    gv = nb * 4 + tt
                    v.wait_ge(tv.sem, gv + 1)
                    base = (nb * 4 + tt) * 520
                    v.tensor_copy(
                        bass.AP(v_bf, base, [[32 * 520, 128], [65, 8], [1, 64]]),
                        ps_b[gv % 2][:, :].rearrange("p (g e) -> p g e", g=8)
                        ).then_inc(vv.sem, 1)
                    vv.val += 1

    ctx_a.close()
    en = enp
    qtb = en(nc.sbuf_tensor("p1_qtb", [128, 2 * NC * 512], BF16))
    x0f = en(nc.sbuf_tensor("p1_x0f", [128, NC * 512], F32))
    pt = [en(nc.sbuf_tensor(f"p1_pt{i}", [128, 512], BF16)) for i in range(8)]
    dn_s = en(nc.sbuf_tensor("p1_dn", [128, 512], BF16))
    dn_l = en(nc.sbuf_tensor("p1_dnl", [128, 512], F32))
    bc_s = en(nc.sbuf_tensor("p1_bc", [128, 512], F32))
    zt_bf = en(nc.sbuf_tensor("p1_zt", [128, 2 * NC * 512], BF16))
    mk = en(nc.sbuf_tensor("p1_mk", [128, 128], BF16))
    rb_bf = en(nc.sbuf_tensor("p1_rbb", [128, NC * 512], BF16))
    sq_bf = en(nc.sbuf_tensor("p1_sqb", [128, NC * 512], BF16))
    s_mu = en(nc.sbuf_tensor("p1_mu", [128, 512], F32))
    s_var = en(nc.sbuf_tensor("p1_var", [128, 512], F32))
    s_rs = en(nc.sbuf_tensor("p1_rs", [128, 512], F32))
    s_tmp = en(nc.sbuf_tensor("p1_tmp", [128, 512], F32))
    x1_st = en(nc.sbuf_tensor("p1_x1st", [128, 2 * NC * 512], F32))

    ps_s = [en(nc.psum_tensor(f"p1_pss{i}", [128, 512], F32)) for i in range(3)]
    ps_z = [en(nc.psum_tensor(f"p1_psz{i}", [128, 512], F32)) for i in range(2)]
    ps_bc = [en(nc.psum_tensor(f"p1_psbc{i}", [128, 512], F32)) for i in range(1)]
    ps_t6 = [en(nc.psum_tensor(f"p1_pst6{i}", [128, 512], F32)) for i in range(2)]

    lqq = Ctr.get(en_sem, nc, "p1b_lqq")
    lqq2 = Ctr.get(en_sem, nc, "p1b_lqq2")
    gpx = Ctr.get(en_sem, nc, "p1b_gpx")
    lqx = Ctr.get(en_sem, nc, "p1b_lqx")
    tS = Ctr.get(en_sem, nc, "g2")
    vM = Ctr.get(en_sem, nc, "g3")
    sE = Ctr.get(en_sem, nc, "g4")
    tP = Ctr.get(en_sem, nc, "g5")
    vR = Ctr.get(en_sem, nc, "g6")
    tB = Ctr.get(en_sem, nc, "g7")
    vZ = Ctr.get(en_sem, nc, "g8")
    tO = Ctr.get(en_sem, nc, "g9")
    vRb = Ctr.get(en_sem, nc, "g10")
    sRb = Ctr.get(en_sem, nc, "g11")
    vSq = Ctr.get(en_sem, nc, "g12")
    tT = Ctr.get(en_sem, nc, "g13")
    vVs = Ctr.get(en_sem, nc, "g14")
    sRs = Ctr.get(en_sem, nc, "g15")
    vXn = Ctr.get(en_sem, nc, "g16")
    sXb = Ctr.get(en_sem, nc, "g17")
    stx = Ctr.get(en_sem, nc, "p1b_stx")
    vch = Ctr.get(en_sem, nc, "g19")
    sch = Ctr.get(en_sem, nc, "g20")
    vBC = Ctr.get(en_sem, nc, "g21")

    with nc.Block() as blk:
        @blk.sync
        def _(sy):
            sy.dma_start(mk[:, :], mask_diag_o).then_inc(lqq.sem, 16)
            sy.wait_ge(lqq.sem, 16)

        @blk.vector
        def _(v):
            v.memset(dn_s[:, :], 0.0)
    lqq.val += 16

    nkts = [4 * qb + 4 for qb in range(NQB)]
    cumS = [0]
    for qb in range(NQB):
        cumS.append(cumS[-1] + 16 * nkts[qb])

    def gidx(qb, h, kt):
        return cumS[qb] + h * nkts[qb] + kt

    PAIRS = [(0, 2), (1, 3), (4, 6), (5, 7), (8, 10), (9, 11), (12, 14), (13, 15)]
    SIDX, PIDX, MSIDX, ORD_S = {}, {}, {}, {}
    for qb in range(NQB):
        nkt = nkts[qb]
        order_S, order_PV = [], []
        for hA, hB in PAIRS:
            for kt in range(nkt):
                order_S += [(hA, kt), (hB, kt)]
                if kt >= 1:
                    order_PV += [(hA, kt - 1), (hB, kt - 1)]
            order_PV += [(hA, nkt - 1), (hB, nkt - 1)]
        sidx = {u: cumS[qb] + i + 1 for i, u in enumerate(order_S)}
        pidx = {u: cumS[qb] + i + 1 for i, u in enumerate(order_PV)}
        em_order = []
        for hA, hB in PAIRS:
            for h_ in (hA, hB):
                for kt_ in range(4 * qb + 1, nkt):
                    em_order.append((h_, kt_))
        mpos = {u: i + 48 * qb + 1 for i, u in enumerate(em_order)}
        msidx = {}
        for u in order_S:
            h_, kt_ = u
            if kt_ > 4 * qb:
                msidx[u] = mpos[(h_, kt_)]
            else:
                msidx[u] = 0
        SIDX[qb], PIDX[qb], MSIDX[qb], ORD_S[qb] = sidx, pidx, msidx, order_S
    ZBANK = {}
    for _hA, _hB in PAIRS:
        ZBANK[_hA], ZBANK[_hB] = 0, 1

    NH = 16
    b_lqq, b_lqx, b_stx, b_gpx = lqq.val, lqx.val, stx.val, gpx.val
    b_tS, b_sE, b_tP, b_vM = tS.val, sE.val, tP.val, vM.val
    b_vR, b_tB, b_vZ, b_tO = vR.val, tB.val, vZ.val, tO.val
    b_vRb, b_sRb, b_vSq, b_tT = vRb.val, sRb.val, vSq.val, tT.val
    b_vVs, b_sRs, b_vXn, b_vBC = vVs.val, sRs.val, vXn.val, vBC.val

    with nc.Block() as blk:
        @blk.sync
        def _(sy):
            sy.dma_start(qtb[:, 0:NC * 512].rearrange("p (c j) -> p c j", c=NC),
                         qt_hbm[:, 0:512].rearrange("(c p) j -> p c j", p=128)
                         ).then_inc(lqq.sem, 16)
            lqq.val += 16
            for qb in range(NQB + 2):
                if qb + 1 < NQB:
                    sy.wait_ge(tS.sem, b_tS + cumS[qb])
                    sl_sem = lqq if (qb + 1) % 2 == 0 else lqq2
                    sy.dma_start(
                        qtb[:, ((qb + 1) % 2) * NC * 512:((qb + 1) % 2 + 1) * NC * 512]
                        .rearrange("p (c j) -> p c j", c=NC),
                        qt_hbm[:, (qb + 1) * 512:(qb + 2) * 512]
                        .rearrange("(c p) j -> p c j", p=128)).then_inc(sl_sem.sem, 16)
                    sl_sem.val += 16
                if qb < NQB:
                    sy.wait_ge(vRb.sem, b_vRb + 8 * qb)
                    sy.dma_start(x0f[:, :].rearrange("p (c j) -> p c j", c=NC),
                                 xf0[:, 1 + qb * 512:1 + (qb + 1) * 512]
                                 .rearrange("(c p) j -> p c j", p=128)).then_inc(lqx.sem, 16)
                    lqx.val += 16
                if qb >= 2:
                    q0 = qb - 2
                    for mc in range(NC):
                        sy.wait_ge(vXn.sem, b_vXn + q0 * NC + mc + 1)
                        sy.dma_start(xf1[mc * 128:(mc + 1) * 128,
                                         1 + q0 * 512:1 + (q0 + 1) * 512],
                                     x1_st[:, (q0 % 2) * NC * 512 + mc * 512:
                                           (q0 % 2) * NC * 512 + (mc + 1) * 512]
                                     ).then_inc(stx.sem, 16)
                    stx.val += NC * 16
            sy.wait_ge(stx.sem, stx.val)
            sy.wait_ge(gpx.sem, b_gpx + NQB * NC * 16)

        @blk.gpsimd
        def _(g):
            for qb in range(NQB):
                for mc in range(NC):
                    g.wait_ge(vXn.sem, b_vXn + qb * NC + mc + 1)
                    g.dma_start(xb1[mc * 128:(mc + 1) * 128, 1 + qb * 512:1 + (qb + 1) * 512],
                                x1_st[:, (qb % 2) * NC * 512 + mc * 512:
                                      (qb % 2) * NC * 512 + (mc + 1) * 512]
                                ).then_inc(gpx.sem, 16)
                gpx.val += NC * 16

        @blk.tensor
        def _(t):
            def attn_tensor(qb):
                nkt = nkts[qb]
                qoff = (qb % 2) * NC * 512
                sidx, pidx, msidx = SIDX[qb], PIDX[qb], MSIDX[qb]
                order_S = ORD_S[qb]
                if qb % 2 == 0:
                    t.wait_ge(lqq.sem, b_lqq + 16 * (qb // 2 + 1))
                else:
                    t.wait_ge(lqq2.sem, 16 * ((qb - 1) // 2 + 1))

                def emit_S(h, kt):
                    g = h // 2
                    qch = (h // 4) * 2 + h % 2
                    qhf = (h // 2) % 2
                    GS = sidx[(h, kt)] - 1
                    off = max(0, (kt - 4 * qb) * 128)
                    diag = kt >= 4 * qb
                    inst = t.matmul(ps_s[GS % 3][:, off:512],
                                    kt_bf[(g % 2) * 64:(g % 2) * 64 + 64,
                                          (g // 2) * T + kt * 128:(g // 2) * T + kt * 128 + 128],
                                    qtb[qhf * 64:qhf * 64 + 64,
                                        qoff + qch * 512 + off:qoff + qch * 512 + 512],
                                    start=True, stop=not diag)
                    if diag:
                        inst = t.matmul(ps_s[GS % 3][:, off:off + 128], ident_bf[:, :],
                                        mk[:, :], start=False, stop=True)
                    inst.then_inc(tS.sem, 1)
                    tS.val += 1

                def emit_PV(h, kt):
                    g = h // 2
                    GS = sidx[(h, kt)] - 1
                    t.matmul(ps_z[ZBANK[h]][0:65, 0:512],
                             v_bf[:, kt * 520 + g * 65:kt * 520 + g * 65 + 65],
                             pt[GS % 8][:, 0:512],
                             start=(kt == 0), stop=(kt == nkt - 1)).then_inc(tP.sem, 1)
                    tP.val += 1

                def emit_pair_bcast(p):
                    t.wait_ge(vR.sem, b_vR + 8 * qb + p + 1)
                    if 8 * qb + p >= 1:
                        t.wait_ge(vBC.sem, b_vBC + 8 * qb + p)
                    t.matmul(ps_bc[0][0:128, 0:512], sel65_f[0:65, 0:128],
                             dn_s[0:65, 0:512], start=True, stop=True).then_inc(tB.sem, 1)
                    tB.val += 1

                for p, (hA, hB) in enumerate(PAIRS):
                    for kt in range(nkt):
                        GSb = sidx[(hB, kt)] - 1
                        if GSb >= 3:
                            t.wait_ge(sE.sem, b_sE + GSb - 2)
                        emit_S(hA, kt)
                        emit_S(hB, kt)
                        if p >= 1 and kt == 0:
                            emit_pair_bcast(p - 1)
                        if kt >= 1:
                            t.wait_ge(sE.sem, b_sE + sidx[(hB, kt - 1)])
                            nmz = max(msidx[(hA, kt - 1)], msidx[(hB, kt - 1)])
                            if nmz > 0:
                                t.wait_ge(vM.sem, b_vM + nmz)
                            if kt == 1 and 16 * qb + 2 * p > 0:
                                t.wait_ge(vZ.sem, b_vZ + 16 * qb + 2 * p)
                            emit_PV(hA, kt - 1)
                            emit_PV(hB, kt - 1)
                    t.wait_ge(sE.sem, b_sE + sidx[(hB, nkt - 1)])
                    nmz = max(msidx[(hA, nkt - 1)], msidx[(hB, nkt - 1)])
                    if nmz > 0:
                        t.wait_ge(vM.sem, b_vM + nmz)
                    emit_PV(hA, nkt - 1)
                    emit_PV(hB, nkt - 1)
                emit_pair_bcast(7)

            def TA(qb):
                t.wait_ge(vZ.sem, b_vZ + 16 * (qb + 1))
                if qb >= 2:
                    t.wait_ge(vVs.sem, b_vVs + qb - 1)
                for mc in range(NC):
                    if mc >= 2:
                        t.wait_ge(vRb.sem, b_vRb + 8 * qb + mc - 1)
                    for kc in range(NC):
                        inst = t.matmul(ps_t6[mc % 2][:, :],
                                        wo_bf[:, kc * D + mc * 128:kc * D + mc * 128 + 128],
                                        zt_bf[:, (qb % 2) * NC * 512 + kc * 512:
                                              (qb % 2) * NC * 512 + (kc + 1) * 512],
                                        start=(kc == 0), stop=(kc == NC - 1))
                    inst.then_inc(tO.sem, 1)
                    tO.val += 1

            def TB(qb):
                t.wait_ge(sRb.sem, b_sRb + 8 * (qb + 1))
                t.wait_ge(vRb.sem, b_vRb + 8 * min(qb + 2, NQB))
                if qb >= 1:
                    t.wait_ge(vVs.sem, b_vVs + qb)
                for c in range(NC):
                    t.matmul(ps_t6[0][:, :], ones_bf[:, :],
                             rb_bf[:, c * 512:(c + 1) * 512],
                             start=(c == 0), stop=(c == NC - 1))
                t.wait_ge(vSq.sem, b_vSq + 8 * (qb + 1))
                for c in range(NC):
                    inst = t.matmul(ps_t6[1][:, :], ones_bf[:, :],
                                    sq_bf[:, c * 512:(c + 1) * 512],
                                    start=(c == 0), stop=(c == NC - 1))
                inst.then_inc(tT.sem, 2)
                tT.val += 2

            for qb in range(NQB + 2):
                if qb < NQB:
                    attn_tensor(qb)
                if qb >= 1 and qb - 1 < NQB:
                    TA(qb - 1)
                if qb >= 2:
                    TB(qb - 2)

        @blk.vector
        def _(v):
            def attn_vector(qb):
                nkt = nkts[qb]
                sidx, pidx, msidx = SIDX[qb], PIDX[qb], MSIDX[qb]
                order_S = ORD_S[qb]
                if qb >= 2:
                    v.wait_ge(tO.sem, b_tO + 8 * (qb - 1))

                def msets_h(h_):
                    for kt in range(4 * qb + 1, nkt):
                        GS = sidx[(h_, kt)] - 1
                        off = (kt - 4 * qb) * 128
                        if GS >= 8:
                            u6 = order_S[GS - 8 - cumS[qb]] if GS - 8 >= cumS[qb] else None
                            if u6 is not None:
                                v.wait_ge(tP.sem, b_tP + pidx[u6])
                        v.memset(pt[GS % 8][:, 0:off], 0.0).then_inc(vM.sem, 1)
                        vM.val += 1

                msets_h(PAIRS[0][0])
                msets_h(PAIRS[0][1])
                for p, (hA, hB) in enumerate(PAIRS):
                    v.wait_ge(tB.sem, b_tB + 8 * qb + p + 1)
                    v.wait_ge(vZ.sem, vZ.val)
                    v.tensor_copy(bc_s[0:128, 0:512],
                                  ps_bc[0][0:128, 0:512]).then_inc(vBC.sem, 1)
                    vBC.val += 1
                    v.wait_ge(vBC.sem, vBC.val)
                    for pos, h in enumerate((hA, hB)):
                        qch = (h // 4) * 2 + h % 2
                        qhf = (h // 2) % 2
                        v.tensor_tensor(zt_bf[qhf * 64:qhf * 64 + 64,
                                              (qb % 2) * NC * 512 + qch * 512:
                                              (qb % 2) * NC * 512 + qch * 512 + 512],
                                        ps_z[pos][0:64, 0:512],
                                        bc_s[pos * 64:pos * 64 + 64, 0:512],
                                        ALU.mult).then_inc(vZ.sem, 1)
                        vZ.val += 1
                    if p + 1 < 8:
                        msets_h(PAIRS[p + 1][0])
                        msets_h(PAIRS[p + 1][1])

            def VA(qb):
                xs = (qb % 2) * NC * 512
                v.wait_ge(lqx.sem, b_lqx + 16 * (qb + 1))
                if qb >= 2:
                    v.wait_ge(stx.sem, b_stx + (qb - 1) * NC * 16)
                    v.wait_ge(gpx.sem, b_gpx + (qb - 1) * NC * 16)
                for mc in range(NC):
                    v.wait_ge(tO.sem, b_tO + 8 * qb + mc + 1)
                    v.scalar_tensor_tensor(x1_st[:, xs + mc * 512:xs + (mc + 1) * 512],
                                           ps_t6[mc % 2][:, :], bout_cols[:, mc:mc + 1],
                                           x0f[:, mc * 512:(mc + 1) * 512],
                                           ALU.add, ALU.add).then_inc(vRb.sem, 1)
                    vRb.val += 1
                if qb >= 1:
                    v.wait_ge(tT.sem, b_tT + 2 * qb)
                for mc in range(NC):
                    v.wait_ge(vRb.sem, b_vRb + 8 * qb + mc + 1)
                    v.tensor_mul(sq_bf[:, mc * 512:(mc + 1) * 512],
                                 x1_st[:, xs + mc * 512:xs + (mc + 1) * 512],
                                 x1_st[:, xs + mc * 512:xs + (mc + 1) * 512]).then_inc(vSq.sem, 1)
                    vSq.val += 1

            def VB(qb):
                v.wait_ge(tT.sem, b_tT + 2 * (qb + 1))
                v.tensor_single_scalar(s_mu[:, :], ps_t6[0][:, :], 1.0 / D,
                                       ALU.mult).then_inc(vch.sem, 1)
                vch.val += 1
                v.wait_ge(vch.sem, vch.val)
                v.tensor_tensor(s_var[:, :], s_mu[:, :], s_mu[:, :], ALU.mult)
                v.tensor_scalar(s_tmp[:, :], ps_t6[1][:, :], 1.0 / D, EPS, ALU.mult,
                                ALU.add).then_inc(vch.sem, 1)
                vch.val += 1
                v.wait_ge(vch.sem, vch.val)
                v.tensor_tensor(s_var[:, :], s_tmp[:, :], s_var[:, :],
                                ALU.subtract).then_inc(vVs.sem, 1)
                vVs.val += 1

            def VC(qb):
                xs = (qb % 2) * NC * 512
                v.wait_ge(sRs.sem, b_sRs + 2 * (qb + 1))
                for mc in range(NC):
                    sl = slice(xs + mc * 512, xs + (mc + 1) * 512)
                    v.tensor_tensor(x1_st[:, sl], x1_st[:, sl], s_mu[:, :],
                                    ALU.subtract).then_inc(vch.sem, 1)
                    vch.val += 1
                    v.wait_ge(vch.sem, vch.val)
                    v.tensor_tensor(x1_st[:, sl], x1_st[:, sl], s_rs[:, :],
                                    ALU.mult).then_inc(vXn.sem, 1)
                    vXn.val += 1

            for qb in range(NQB + 2):
                if qb < NQB:
                    attn_vector(qb)
                if qb >= 1 and qb - 1 < NQB:
                    VA(qb - 1)
                if qb >= 2:
                    VB(qb - 2)
                    VC(qb - 2)

        @blk.scalar
        def _(s):
            def den_pair(qb, p, hA, hB):
                pidx = PIDX[qb]
                if 8 * qb + p >= 1:
                    s.wait_ge(tB.sem, b_tB + 8 * qb + p)
                s.wait_ge(tP.sem, b_tP + pidx[(hB, nkts[qb] - 1)])
                s.activation(dn_l[0:1, 0:512], ps_z[0][64:65, 0:512],
                             AF.Ln).then_inc(sch.sem, 1)
                sch.val += 1
                s.activation(dn_l[64:65, 0:512], ps_z[1][64:65, 0:512],
                             AF.Ln).then_inc(sch.sem, 1)
                sch.val += 1
                s.wait_ge(sch.sem, sch.val)
                s.activation(dn_s[0:1, 0:512], dn_l[0:1, 0:512],
                             AF.Exp, scale=-1.0)
                s.activation(dn_s[64:65, 0:512], dn_l[64:65, 0:512],
                             AF.Exp, scale=-1.0).then_inc(vR.sem, 1)
                vR.val += 1

            def attn_scalar(qb):
                nkt = nkts[qb]
                sidx, pidx = SIDX[qb], PIDX[qb]
                order_S = ORD_S[qb]
                for p, (hA, hB) in enumerate(PAIRS):
                    for kt in range(nkt):
                        GSb = sidx[(hB, kt)] - 1
                        if GSb >= 8 and GSb - 8 >= cumS[qb]:
                            u6 = order_S[GSb - 8 - cumS[qb]]
                            s.wait_ge(tP.sem, b_tP + pidx[u6])
                        s.wait_ge(tS.sem, b_tS + GSb + 1)
                        for h in (hA, hB):
                            GS = sidx[(h, kt)] - 1
                            off = max(0, (kt - 4 * qb) * 128)
                            s.activation(pt[GS % 8][:, off:512], ps_s[GS % 3][:, off:512],
                                         AF.Exp, scale=0.125).then_inc(sE.sem, 1)
                            sE.val += 1
                    den_pair(qb, p, hA, hB)

            def SA(qb):
                xs = (qb % 2) * NC * 512
                if qb >= 1:
                    s.wait_ge(tT.sem, b_tT + 2 * qb)
                for mc in range(NC):
                    s.wait_ge(vRb.sem, b_vRb + 8 * qb + mc + 1)
                    s.copy(rb_bf[:, mc * 512:(mc + 1) * 512],
                           x1_st[:, xs + mc * 512:xs + (mc + 1) * 512]).then_inc(sRb.sem, 1)
                    sRb.val += 1

            def SB(qb):
                s.wait_ge(vVs.sem, b_vVs + qb + 1)
                s.activation(s_rs[:, :], s_var[:, :], AF.Ln).then_inc(sRs.sem, 1)
                sRs.val += 1
                s.wait_ge(sRs.sem, sRs.val)
                s.activation(s_rs[:, :], s_rs[:, :], AF.Exp, scale=-0.5).then_inc(sRs.sem, 1)
                sRs.val += 1

            for qb in range(NQB + 2):
                if qb < NQB:
                    attn_scalar(qb)
                if qb >= 1 and qb - 1 < NQB:
                    SA(qb - 1)
                if qb >= 2:
                    SB(qb - 2)
    ctx_p1.close()


def _p2_inner_attn(nc, en_sem, lid, xf_in, xb_in, xf_out, xb_out,
                   wqkv_d, wo_d, scol_d,
                   tab_cs, tab_sn, tab_ci, tab_si, r2_d, bdsel_d, selbc_d,
                   mask_di, mask_ai, sel65_f, ident_bf):
    NHI = 8
    ctx = ExitStack()
    en = ctx.enter_context
    L = str(lid)

    wqkv_bf = en(nc.sbuf_tensor("p2_wqkv" + L, [128, NC * 1536], BF16))
    wo_bf = en(nc.sbuf_tensor("p2_wo" + L, [128, 4 * D], BF16))
    r2_f = en(nc.sbuf_tensor("p2_r2" + L, [128, 128], F32))
    scol = en(nc.sbuf_tensor("p2_sc" + L, [128, 4], F32))
    bdsel = en(nc.sbuf_tensor("p2_bd" + L, [128, 4 * 8], BF16))
    selbc = en(nc.sbuf_tensor("p2_sb" + L, [128, 4 * 128], BF16))
    mk_d = en(nc.sbuf_tensor("p2_mkd" + L, [128, 128], BF16))
    mk_a = en(nc.sbuf_tensor("p2_mka" + L, [128, 128], BF16))

    kr = en(nc.sbuf_tensor("p2_kr" + L, [128, 2 * 4 * 512], BF16))
    vr = en(nc.sbuf_tensor("p2_vr" + L, [128, 2 * 4 * 520], BF16))
    qfin = en(nc.sbuf_tensor("p2_qf" + L, [128, 4 * 512], BF16))
    xq_in = en(nc.sbuf_tensor("p2_xq" + L, [128, NC * 512], BF16))
    xf_cur = en(nc.sbuf_tensor("p2_xf" + L, [128, NC * 512], F32))
    tabs = en(nc.sbuf_tensor("p2_tab" + L, [128, 4 * 512], F32))
    ctabs = en(nc.sbuf_tensor("p2_ctab" + L, [128, 4 * 512], F32))
    qt_f = en(nc.sbuf_tensor("p2_qtf" + L, [128, NC * 512], F32))
    t1b = en(nc.sbuf_tensor("p2_t1" + L, [128, NC * 512], F32))
    t3b = en(nc.sbuf_tensor("p2_t3" + L, [128, 2 * 512], F32))
    sqt = en(nc.sbuf_tensor("p2_sqt" + L, [128, 8 * 512], BF16))
    ssrow = en(nc.sbuf_tensor("p2_ss" + L, [128, 2 * 512], F32))
    invr = en(nc.sbuf_tensor("p2_inv" + L, [128, 2 * 512], BF16))
    pt = [en(nc.sbuf_tensor(f"p2_pt{i}" + L, [128, 512], BF16)) for i in range(8)]
    dn_s = en(nc.sbuf_tensor("p2_dn" + L, [128, 512], BF16))
    dn_l = en(nc.sbuf_tensor("p2_dnl" + L, [128, 512], F32))
    bc_s = en(nc.sbuf_tensor("p2_bc" + L, [128, 512], F32))
    zin = en(nc.sbuf_tensor("p2_zin" + L, [128, 4 * 512], BF16))
    x2_st = en(nc.sbuf_tensor("p2_x2" + L, [128, NC * 512], F32))

    pb = [en(nc.psum_tensor(f"p2_pb{i}" + L, [128, 512], F32)) for i in range(8)]

    wl = Ctr.get(en_sem, nc, "p2_wl")
    lxq = Ctr.get(en_sem, nc, "p2_lxq")
    lxf = Ctr.get(en_sem, nc, "p2_lxf")
    ltb = Ctr.get(en_sem, nc, "p2_ltb")
    tQ = Ctr.get(en_sem, nc, "g4")
    vQ = Ctr.get(en_sem, nc, "g5")
    tV = Ctr.get(en_sem, nc, "g6")
    vV = Ctr.get(en_sem, nc, "g7")
    sSq = Ctr.get(en_sem, nc, "g8")
    tSS = Ctr.get(en_sem, nc, "g9")
    vSS = Ctr.get(en_sem, nc, "g10")
    sIv = Ctr.get(en_sem, nc, "g11")
    tBc = Ctr.get(en_sem, nc, "g12")
    tRo = Ctr.get(en_sem, nc, "g13")
    vT3 = Ctr.get(en_sem, nc, "g14")
    vQf = Ctr.get(en_sem, nc, "g15")
    tS = Ctr.get(en_sem, nc, "g16")
    vM = Ctr.get(en_sem, nc, "g17")
    sE = Ctr.get(en_sem, nc, "g18")
    tP = Ctr.get(en_sem, nc, "g19")
    vR = Ctr.get(en_sem, nc, "g20")
    tB = Ctr.get(en_sem, nc, "g21")
    vZ = Ctr.get(en_sem, nc, "g22")
    tO = Ctr.get(en_sem, nc, "g23")
    vO = Ctr.get(en_sem, nc, "g24")
    stg = Ctr.get(en_sem, nc, "p2_stg")
    vch = Ctr.get(en_sem, nc, "g27")
    sch = Ctr.get(en_sem, nc, "g28")
    vBC = Ctr.get(en_sem, nc, "g29")

    with nc.Block() as blk:
        @blk.gpsimd
        def _(g):
            for c in range(NC):
                g.dma_start(wqkv_bf[:, c * 1536:(c + 1) * 1536],
                            wqkv_d[c * 128:(c + 1) * 128, :]).then_inc(wl.sem, 16)
            for c in range(4):
                g.dma_start(wo_bf[:, c * D:(c + 1) * D],
                            wo_d[c * 128:(c + 1) * 128, :]).then_inc(wl.sem, 16)
            g.dma_start(r2_f[:, :], r2_d).then_inc(wl.sem, 16)
            g.dma_start(scol[:, :], scol_d).then_inc(wl.sem, 16)
            g.dma_start(bdsel[:, :], bdsel_d).then_inc(wl.sem, 16)
            g.dma_start(selbc[:, :], selbc_d).then_inc(wl.sem, 16)
            g.dma_start(mk_d[:, :], mask_di).then_inc(wl.sem, 16)
            g.dma_start(mk_a[:, :], mask_ai).then_inc(wl.sem, 16)
            wl.val += (NC + 4 + 6) * 16
            g.memset(bass.AP(vr, 64, [[2 * 4 * 520, 128], [520, 8], [65, 8], [1, 1]]), 1.0)
            g.wait_ge(wl.sem, wl.val)

        @blk.vector
        def _(v):
            v.memset(dn_s[:, :], 0.0)
            v.memset(invr[:, :], 0.0)

    for tb in range(NQB):
        kts = list(range(max(0, 4 * tb - 4), 4 * tb + 4))
        nkt = len(kts)

        def span(kt):
            if kt < 4 * tb:
                j = kt - 4 * (tb - 1)
                return 0, 128 * (j + 1), ("anti", 128 * j)
            j = kt - 4 * tb
            return 128 * j, 512, ("diag", 128 * j)

        order_S, order_PV = [], []
        for hp_ in range(4):
            h0_, h1_ = 2 * hp_, 2 * hp_ + 1
            for ki_ in range(nkt):
                order_S += [(h0_, ki_), (h1_, ki_)]
                if ki_ >= 1:
                    order_PV += [(h0_, ki_ - 1), (h1_, ki_ - 1)]
            order_PV += [(h0_, nkt - 1), (h1_, nkt - 1)]
        sidx = {u: i + 1 for i, u in enumerate(order_S)}
        pidx = {u: i + 1 for i, u in enumerate(order_PV)}
        em_order = []
        for hp_ in range(4):
            for ki_ in range(nkt):
                lo_, hi_, _mm = span(kts[ki_])
                if lo_ > 0 or hi_ < 512:
                    em_order.append((2 * hp_, ki_))
                    em_order.append((2 * hp_ + 1, ki_))
        mpos = {u: i + 1 for i, u in enumerate(em_order)}
        msidx = {}
        for u in order_S:
            msidx[u] = mpos.get(u, 0)

        lxq0, lxf0, ltb0 = lxq.val, lxf.val, ltb.val
        tQ0, vQ0, tV0, vV0, sSq0 = tQ.val, vQ.val, tV.val, vV.val, sSq.val
        tSS0, vSS0, sIv0, tBc0, tRo0, vT30, vQf0 = (tSS.val, vSS.val, sIv.val, tBc.val,
                                                    tRo.val, vT3.val, vQf.val)
        tS0, vM0, sE0, tP0 = tS.val, vM.val, sE.val, tP.val
        vR0, tB0, vZ0, tO0, vO0, stg0 = (vR.val, tB.val, vZ.val, tO.val, vO.val,
                                         stg.val)
        vBC0 = vBC.val
        slot = tb % 2

        with nc.Block() as blk:
            @blk.sync
            def _(sy):
                sy.dma_start(xq_in[:, 0:4 * 512].rearrange("p (c j) -> p c j", c=4),
                             xb_in[0:512, 1 + tb * 512:1 + (tb + 1) * 512]
                             .rearrange("(c p) j -> p c j", p=128)).then_inc(lxq.sem, 16)
                sy.dma_start(xq_in[:, 4 * 512:NC * 512].rearrange("p (c j) -> p c j", c=4),
                             xb_in[512:1024, 0 + tb * 512:0 + (tb + 1) * 512]
                             .rearrange("(c p) j -> p c j", p=128)).then_inc(lxq.sem, 16)
                for i, tab in enumerate([tab_cs, tab_sn, tab_ci, tab_si]):
                    sy.dma_start(tabs[:, i * 512:(i + 1) * 512],
                                 tab[:, tb * 512:(tb + 1) * 512]).then_inc(ltb.sem, 16)
                sy.dma_start(xf_cur[:, :].rearrange("p (c j) -> p c j", c=NC),
                             xf_in[:, 1 + tb * 512:1 + (tb + 1) * 512]
                             .rearrange("(c p) j -> p c j", p=128)).then_inc(lxf.sem, 16)
                lxq.val += 32
                lxf.val += 16
                ltb.val += 64

            @blk.gpsimd
            def _(g):
                for mc in range(NC):
                    g.wait_ge(vO.sem, vO0 + mc + 1)
                    g.dma_start(xf_out[mc * 128:(mc + 1) * 128,
                                       1 + tb * 512:1 + (tb + 1) * 512],
                                x2_st[:, mc * 512:(mc + 1) * 512]).then_inc(stg.sem, 16)
                    g.dma_start(xb_out[mc * 128:(mc + 1) * 128,
                                       1 + tb * 512:1 + (tb + 1) * 512],
                                x2_st[:, mc * 512:(mc + 1) * 512]).then_inc(stg.sem, 16)
                stg.val += NC * 32
                if tb == NQB - 1:
                    g.wait_ge(stg.sem, stg.val)

            @blk.tensor
            def _(t):
                t.wait_ge(lxq.sem, lxq0 + 32)
                for mc in range(NC):
                    if mc >= 2:
                        t.wait_ge(vQ.sem, vQ0 + mc - 1)
                    for kc in range(NC):
                        inst = t.matmul(pb[mc % 2][:, :],
                                        wqkv_bf[:, kc * 1536 + mc * 128:kc * 1536 + mc * 128 + 128],
                                        xq_in[:, kc * 512:(kc + 1) * 512],
                                        start=(kc == 0), stop=(kc == NC - 1))
                    inst.then_inc(tQ.sem, 1)
                    tQ.val += 1
                for tt in range(4):
                    if tt >= 2:
                        t.wait_ge(vV.sem, vV0 + tt - 1)
                    for kc in range(NC):
                        inst = t.matmul(pb[2 + tt % 2][:, :],
                                        xq_in[:, kc * 512 + tt * 128:kc * 512 + tt * 128 + 128],
                                        wqkv_bf[:, kc * 1536 + 1024:kc * 1536 + 1536],
                                        start=(kc == 0), stop=(kc == NC - 1))
                    inst.then_inc(tV.sem, 1)
                    tV.val += 1
                for mc in range(NC if P2_STAGE >= 2 else 0):
                    t.wait_ge(vQ.sem, vQ0 + mc + 1)
                    if mc >= 2:
                        t.wait_ge(vT3.sem, vT30 + mc - 1)
                    t.matmul(pb[4 + mc % 2][:, :], r2_f[:, :],
                             qt_f[:, mc * 512:(mc + 1) * 512],
                             start=True, stop=True).then_inc(tRo.sem, 1)
                    tRo.val += 1
                for side in range(2 if P2_STAGE >= 2 else 0):
                    for c in range(4):
                        t.wait_ge(sSq.sem, sSq0 + side * 4 + c + 1)
                        inst = t.matmul(pb[6 + side][0:8, :], bdsel[:, c * 8:(c + 1) * 8],
                                        sqt[:, (side * 4 + c) * 512:(side * 4 + c + 1) * 512],
                                        start=(c == 0), stop=(c == 3))
                    inst.then_inc(tSS.sem, 1)
                    tSS.val += 1
                if P2_STAGE >= 2:
                    t.wait_ge(vSS.sem, vSS0 + 2)
                for idx in range(NC if P2_STAGE >= 2 else 0):
                    side, c = idx // 4, idx % 4
                    t.wait_ge(sIv.sem, sIv0 + side + 1)
                    if idx >= 2:
                        t.wait_ge(vQf.sem, vQf0 + idx - 1)
                    t.matmul(pb[6 + idx % 2][0:128, :],
                             selbc[0:65, c * 128:(c + 1) * 128],
                             invr[0:65, side * 512:side * 512 + 512],
                             start=True, stop=True).then_inc(tBc.sem, 1)
                    tBc.val += 1
                if P2_STAGE >= 3:
                    t.wait_ge(vQf.sem, vQf0 + NC)

                def emit_S(h, ki):
                    hc, hf = h // 2, (h % 2) * 64
                    kt = kts[ki]
                    GS = sidx[(h, ki)] - 1
                    lo, hi, (mtyp, moff) = span(kt)
                    kslot = (kt // 4) % 2
                    ktt = kt % 4
                    t.matmul(pb[GS % 3][:, lo:hi],
                             kr[hf:hf + 64, kslot * 2048 + hc * 512 + ktt * 128:
                                kslot * 2048 + hc * 512 + ktt * 128 + 128],
                             qfin[hf:hf + 64, hc * 512 + lo:hc * 512 + hi],
                             start=True, stop=False)
                    mten = mk_a if mtyp == "anti" else mk_d
                    t.matmul(pb[GS % 3][:, moff:moff + 128], ident_bf[:, :], mten[:, :],
                             start=False, stop=True).then_inc(tS.sem, 1)
                    tS.val += 1

                def emit_PV(h, ki):
                    kt = kts[ki]
                    GS = sidx[(h, ki)] - 1
                    t.matmul(pb[3 + h % 3][0:65, 0:512],
                             vr[:, ((kt // 4) % 2) * 2080 + (kt % 4) * 520 + h * 65:
                                ((kt // 4) % 2) * 2080 + (kt % 4) * 520 + h * 65 + 65],
                             pt[GS % 8][:, 0:512],
                             start=(ki == 0), stop=(ki == nkt - 1)).then_inc(tP.sem, 1)
                    tP.val += 1

                def emit_pair_bcast(p):
                    t.wait_ge(vR.sem, vR0 + p + 1)
                    if p >= 2:
                        t.wait_ge(vBC.sem, vBC0 + p - 1)
                    t.matmul(pb[6 + p % 2][0:128, 0:512], sel65_f[0:65, 0:128],
                             dn_s[0:65, 0:512], start=True, stop=True).then_inc(tB.sem, 1)
                    tB.val += 1

                for hp in range(NHI // 2 if P2_STAGE >= 3 else 0):
                    h0, h1 = 2 * hp, 2 * hp + 1
                    for ki in range(nkt):
                        GSb = sidx[(h1, ki)] - 1
                        if GSb >= 3:
                            t.wait_ge(sE.sem, sE0 + GSb - 2)
                        emit_S(h0, ki)
                        emit_S(h1, ki)
                        if hp >= 1 and ki == 0:
                            emit_pair_bcast(hp - 1)
                        if ki >= 1:
                            t.wait_ge(sE.sem, sE0 + sidx[(h1, ki - 1)])
                            nmz = max(msidx[(h0, ki - 1)], msidx[(h1, ki - 1)])
                            if nmz > 0:
                                t.wait_ge(vM.sem, vM0 + nmz)
                            if ki == 1 and h1 >= 3:
                                t.wait_ge(vZ.sem, vZ0 + h1 - 2)
                            emit_PV(h0, ki - 1)
                            emit_PV(h1, ki - 1)
                    t.wait_ge(sE.sem, sE0 + sidx[(h1, nkt - 1)])
                    nmz = max(msidx[(h0, nkt - 1)], msidx[(h1, nkt - 1)])
                    if nmz > 0:
                        t.wait_ge(vM.sem, vM0 + nmz)
                    emit_PV(h0, nkt - 1)
                    emit_PV(h1, nkt - 1)
                if P2_STAGE >= 3:
                    emit_pair_bcast(NHI // 2 - 1)
                t.wait_ge(vZ.sem, vZ0 + NHI)
                for mc in range(NC if P2_STAGE >= 4 else 0):
                    if mc >= 2:
                        t.wait_ge(vO.sem, vO0 + mc - 1)
                    for kc in range(4):
                        inst = t.matmul(pb[mc % 2][:, :],
                                        wo_bf[:, kc * D + mc * 128:kc * D + mc * 128 + 128],
                                        zin[:, kc * 512:(kc + 1) * 512],
                                        start=(kc == 0), stop=(kc == 3))
                    inst.then_inc(tO.sem, 1)
                    tO.val += 1

            @blk.vector
            def _(v):
                v.wait_ge(ltb.sem, ltb0 + 64)
                for i in range(4):
                    v.tensor_single_scalar(ctabs[:, i * 512:(i + 1) * 512],
                                           tabs[:, i * 512:(i + 1) * 512],
                                           scol[:, i:i + 1], ALU.mult).then_inc(vch.sem, 1)
                    vch.val += 1
                v.wait_ge(vch.sem, vch.val)
                for mc in range(NC):
                    v.wait_ge(tQ.sem, tQ0 + mc + 1)
                    v.tensor_copy(qt_f[:, mc * 512:(mc + 1) * 512],
                                  pb[mc % 2][:, :]).then_inc(vQ.sem, 1)
                    vQ.val += 1
                    v.wait_ge(vQ.sem, vQ.val)
                    cs_off = (0 if mc < 4 else 2) * 512
                    v.tensor_tensor(t1b[:, mc * 512:(mc + 1) * 512],
                                    qt_f[:, mc * 512:(mc + 1) * 512],
                                    ctabs[:, cs_off:cs_off + 512], ALU.mult)
                for tt in range(4):
                    v.wait_ge(tV.sem, tV0 + tt + 1)
                    v.tensor_copy(
                        bass.AP(vr, slot * 2080 + tt * 520,
                                [[2 * 4 * 520, 128], [65, 8], [1, 64]]),
                        pb[2 + tt % 2][:, :].rearrange("p (g e) -> p g e", g=8)
                        ).then_inc(vV.sem, 1)
                    vV.val += 1
                for mc in range(NC if P2_STAGE >= 2 else 0):
                    sn_off = (1 if mc < 4 else 3) * 512
                    v.wait_ge(tRo.sem, tRo0 + mc + 1)
                    v.tensor_tensor(t3b[:, (mc % 2) * 512:(mc % 2) * 512 + 512],
                                    pb[4 + mc % 2][:, :],
                                    ctabs[:, sn_off:sn_off + 512], ALU.mult).then_inc(vT3.sem, 1)
                    vT3.val += 1
                    v.wait_ge(vT3.sem, vT3.val)
                    v.tensor_tensor(t1b[:, mc * 512:(mc + 1) * 512],
                                    t1b[:, mc * 512:(mc + 1) * 512],
                                    t3b[:, (mc % 2) * 512:(mc % 2) * 512 + 512],
                                    ALU.add).then_inc(vch.sem, 1)
                    vch.val += 1
                for side in range(2 if P2_STAGE >= 2 else 0):
                    v.wait_ge(tSS.sem, tSS0 + side + 1)
                    v.tensor_single_scalar(ssrow[0:8, side * 512:side * 512 + 512],
                                           pb[6 + side][0:8, :], 1e-24,
                                           ALU.max).then_inc(vSS.sem, 1)
                    vSS.val += 1
                if P2_STAGE >= 2:
                    v.wait_ge(vch.sem, vch.val)
                for mc in range(NC if P2_STAGE >= 2 else 0):
                    v.wait_ge(tBc.sem, tBc0 + mc + 1)
                    dest = qfin if mc < 4 else kr
                    doff = (mc % 4) * 512 if mc < 4 else slot * 2048 + (mc - 4) * 512
                    v.tensor_tensor(dest[:, doff:doff + 512],
                                    t1b[:, mc * 512:(mc + 1) * 512],
                                    pb[6 + mc % 2][:, :], ALU.mult).then_inc(vQf.sem, 1)
                    vQf.val += 1
                def msets_pair(hp_):
                    for ki_ in range(nkt):
                        kt_ = kts[ki_]
                        lo_, hi_, _mm = span(kt_)
                        if lo_ > 0 or hi_ < 512:
                            for h_ in (2 * hp_, 2 * hp_ + 1):
                                GS = sidx[(h_, ki_)] - 1
                                if GS >= 8:
                                    u6 = order_S[GS - 8]
                                    v.wait_ge(tP.sem, tP0 + pidx[u6])
                                if lo_ > 0 and hi_ < 512:
                                    v.memset(pt[GS % 8][:, 0:lo_], 0.0)
                                    v.memset(pt[GS % 8][:, hi_:512], 0.0).then_inc(vM.sem, 1)
                                elif lo_ > 0:
                                    v.memset(pt[GS % 8][:, 0:lo_], 0.0).then_inc(vM.sem, 1)
                                else:
                                    v.memset(pt[GS % 8][:, hi_:512], 0.0).then_inc(vM.sem, 1)
                                vM.val += 1

                if P2_STAGE >= 3:
                    msets_pair(0)
                for hp in range(NHI // 2 if P2_STAGE >= 3 else 0):
                    v.wait_ge(tB.sem, tB0 + hp + 1)
                    v.wait_ge(vZ.sem, vZ.val)
                    v.tensor_copy(bc_s[0:128, 0:512],
                                  pb[6 + hp % 2][0:128, 0:512]).then_inc(vBC.sem, 1)
                    vBC.val += 1
                    v.wait_ge(vBC.sem, vBC.val)
                    for pos, h in enumerate((2 * hp, 2 * hp + 1)):
                        hc, hf = h // 2, (h % 2) * 64
                        v.tensor_tensor(zin[hf:hf + 64, hc * 512:hc * 512 + 512],
                                        pb[3 + h % 3][0:64, 0:512],
                                        bc_s[pos * 64:pos * 64 + 64, 0:512],
                                        ALU.mult).then_inc(vZ.sem, 1)
                        vZ.val += 1
                    if 2 * hp + 2 < NHI:
                        msets_pair(hp + 1)
                v.wait_ge(lxf.sem, lxf0 + 16)
                v.wait_ge(stg.sem, stg0)
                for mc in range(NC):
                    if P2_STAGE >= 4:
                        v.wait_ge(tO.sem, tO0 + mc + 1)
                        v.tensor_tensor(x2_st[:, mc * 512:(mc + 1) * 512], pb[mc % 2][:, :],
                                        xf_cur[:, mc * 512:(mc + 1) * 512],
                                        ALU.add).then_inc(vO.sem, 1)
                    else:
                        v.wait_ge(vQ.sem, vQ0 + NC)
                        v.tensor_copy(x2_st[:, mc * 512:(mc + 1) * 512],
                                      xf_cur[:, mc * 512:(mc + 1) * 512]).then_inc(vO.sem, 1)
                    vO.val += 1

            @blk.scalar
            def _(s):
                for mc in range(NC if P2_STAGE >= 2 else 0):
                    s.wait_ge(vQ.sem, vQ0 + mc + 1)
                    s.activation(sqt[:, mc * 512:(mc + 1) * 512],
                                 qt_f[:, mc * 512:(mc + 1) * 512],
                                 AF.Square).then_inc(sSq.sem, 1)
                    sSq.val += 1
                for side in range(2 if P2_STAGE >= 2 else 0):
                    s.wait_ge(vSS.sem, vSS0 + side + 1)
                    s.activation(ssrow[0:8, side * 512:side * 512 + 512],
                                 ssrow[0:8, side * 512:side * 512 + 512],
                                 AF.Ln).then_inc(sch.sem, 1)
                    sch.val += 1
                    s.wait_ge(sch.sem, sch.val)
                    s.activation(invr[0:8, side * 512:side * 512 + 512],
                                 ssrow[0:8, side * 512:side * 512 + 512],
                                 AF.Exp, scale=-0.5).then_inc(sIv.sem, 1)
                    sIv.val += 1
                def den_pair_i(hp):
                    h0, h1 = 2 * hp, 2 * hp + 1
                    if hp >= 1:
                        s.wait_ge(tB.sem, tB0 + hp)
                    s.wait_ge(tP.sem, tP0 + pidx[(h1, nkt - 1)])
                    s.activation(dn_l[0:1, 0:512], pb[3 + h0 % 3][64:65, 0:512],
                                 AF.Ln).then_inc(sch.sem, 1)
                    sch.val += 1
                    s.activation(dn_l[64:65, 0:512], pb[3 + h1 % 3][64:65, 0:512],
                                 AF.Ln).then_inc(sch.sem, 1)
                    sch.val += 1
                    s.wait_ge(sch.sem, sch.val)
                    s.activation(dn_s[0:1, 0:512], dn_l[0:1, 0:512],
                                 AF.Exp, scale=-1.0)
                    s.activation(dn_s[64:65, 0:512], dn_l[64:65, 0:512],
                                 AF.Exp, scale=-1.0).then_inc(vR.sem, 1)
                    vR.val += 1

                for hp in range(NHI // 2 if P2_STAGE >= 3 else 0):
                    h0, h1 = 2 * hp, 2 * hp + 1
                    for ki in range(nkt):
                        GSb = sidx[(h1, ki)] - 1
                        kt = kts[ki]
                        lo, hi, _m = span(kt)
                        if GSb >= 8:
                            u6 = order_S[GSb - 8]
                            s.wait_ge(tP.sem, tP0 + pidx[u6])
                        s.wait_ge(tS.sem, tS0 + GSb + 1)
                        for h in (h0, h1):
                            GS = sidx[(h, ki)] - 1
                            s.activation(pt[GS % 8][:, lo:hi], pb[GS % 3][:, lo:hi],
                                         AF.Exp, scale=8.0).then_inc(sE.sem, 1)
                            sE.val += 1
                    den_pair_i(hp)
    ctx.close()


def _p3_inner_ffn(nc, en_sem, lid, xf_in, xb_in, xf_out, xb_out,
                  w1_d, w2_d, b1a_d, b1g_d, b2_d, lng_d, lnb_d, ones_bf):
    ctx = ExitStack()
    en = ctx.enter_context
    L = str(lid)
    NJ = 22
    JW = [128] * 21 + [42]

    w2_bf = en(nc.sbuf_tensor("p3_w2" + L, [128, NJ * D], BF16))
    h2q = en(nc.sbuf_tensor("p3_h2" + L, [128, NJ * 1024], BF16))
    w1buf = en(nc.sbuf_tensor("p3_w1" + L, [128, 4 * NC * 128], BF16))
    hn_q = en(nc.sbuf_tensor("p3_hn" + L, [128, NC * 1024], BF16))
    hf = en(nc.sbuf_tensor("p3_hf" + L, [128, NC * 512], F32))
    hb = en(nc.sbuf_tensor("p3_hb" + L, [128, NC * 512], BF16))
    sqb = en(nc.sbuf_tensor("p3_sq" + L, [128, NC * 512], BF16))
    s_mu = en(nc.sbuf_tensor("p3_mu" + L, [128, 512], F32))
    s_var = en(nc.sbuf_tensor("p3_var" + L, [128, 512], F32))
    s_rs = en(nc.sbuf_tensor("p3_rs" + L, [128, 512], F32))
    s_tmp = en(nc.sbuf_tensor("p3_tmp" + L, [128, 512], F32))
    glu = en(nc.sbuf_tensor("p3_glu" + L, [128, 2 * 512], BF16))
    bcols = en(nc.sbuf_tensor("p3_bc" + L, [128, 2 * NJ + NC * 3], F32))
    xres = en(nc.sbuf_tensor("p3_xr" + L, [128, NC * 512], F32))
    x3_st = en(nc.sbuf_tensor("p3_x3" + L, [128, NC * 512], F32))

    pb = [en(nc.psum_tensor(f"p3_pb{i}" + L, [128, 512], F32)) for i in range(6)]

    wl = Ctr.get(en_sem, nc, "p3_wl")
    lhf = Ctr.get(en_sem, nc, "p3_lhf")
    lhb = Ctr.get(en_sem, nc, "p3_lhb")
    lw1 = [Ctr.get(en_sem, nc, f"p3_lw1{i}") for i in range(4)]
    lxr = Ctr.get(en_sem, nc, "p3_lxr")
    vSq = Ctr.get(en_sem, nc, "g5")
    tLn = Ctr.get(en_sem, nc, "g6")
    vMu = Ctr.get(en_sem, nc, "g7")
    sRs = Ctr.get(en_sem, nc, "g8")
    vHn = Ctr.get(en_sem, nc, "g9")
    tY = Ctr.get(en_sem, nc, "g10")
    sGl = Ctr.get(en_sem, nc, "g11")
    vH2 = Ctr.get(en_sem, nc, "g12")
    tW2 = Ctr.get(en_sem, nc, "g13")
    vX3 = Ctr.get(en_sem, nc, "g14")
    st3 = Ctr.get(en_sem, nc, "p3_st3")
    stb = Ctr.get(en_sem, nc, "p3_stb")
    vch = Ctr.get(en_sem, nc, "g17")

    with nc.Block() as blk:
        @blk.gpsimd
        def _(g):
            g.dma_start(w2_bf[:, 0:21 * D].rearrange("p (c j) -> p c j", c=21),
                        w2_d[0:21 * 128, :].rearrange("(c p) j -> p c j", p=128)
                        ).then_inc(wl.sem, 16)
            g.dma_start(w2_bf[0:42, 21 * D:22 * D], w2_d[21 * 128:2730, :]).then_inc(wl.sem, 16)
            g.dma_start(bcols[:, 0:NJ], b1a_d).then_inc(wl.sem, 16)
            g.dma_start(bcols[:, NJ:2 * NJ], b1g_d).then_inc(wl.sem, 16)
            g.dma_start(bcols[:, 2 * NJ:2 * NJ + NC], b2_d).then_inc(wl.sem, 16)
            g.dma_start(bcols[:, 2 * NJ + NC:2 * NJ + 2 * NC], lng_d).then_inc(wl.sem, 16)
            g.dma_start(bcols[:, 2 * NJ + 2 * NC:2 * NJ + 3 * NC], lnb_d).then_inc(wl.sem, 16)
            wl.val += 7 * 16
            g.wait_ge(wl.sem, wl.val)

    for qt in range(4):
        lhf0, lhb0, lxr0 = lhf.val, lhb.val, lxr.val
        vSq0, tLn0, vMu0, sRs0, vHn0 = vSq.val, tLn.val, vMu.val, sRs.val, vHn.val
        tY0, sGl0, vH20, tW20 = tY.val, sGl.val, vH2.val, tW2.val
        vX30, st30, stb0 = vX3.val, st3.val, stb.val
        lw10 = [c.val for c in lw1]

        with nc.Block() as blk:
            @blk.gpsimd
            def _(g):
                for seq in range(2 * NJ):
                    j, which = seq // 2, seq % 2
                    half = 1 - which
                    jw = JW[j]
                    j0 = half * 2730 + j * 128
                    if seq >= 4:
                        g.wait_ge(tY.sem, tY0 + (seq - 3) * 2)
                    g.dma_start(
                        w1buf[:, (seq % 4) * NC * 128:(seq % 4) * NC * 128 + NC * jw]
                        .rearrange("p (c j) -> p c j", c=NC),
                        w1_d[:, j0:j0 + jw].rearrange("(c p) j -> p c j", p=128)
                        ).then_inc(lw1[seq % 4].sem, 16)
                    lw1[seq % 4].val += 16
                for sub in range(2):
                    t0 = qt * 1024 + sub * 512
                    for mc in range(NC):
                        g.wait_ge(vX3.sem, vX30 + sub * NC + mc + 1)
                        g.dma_start(xb_out[mc * 128:(mc + 1) * 128, 1 + t0:1 + t0 + 512],
                                    x3_st[:, mc * 512:(mc + 1) * 512]).then_inc(stb.sem, 16)
                    stb.val += NC * 16
                if qt == 3:
                    g.wait_ge(stb.sem, stb.val)

            @blk.sync
            def _(sy):
                for sub in range(2):
                    t0 = qt * 1024 + sub * 512
                    sy.dma_start(hf[:, 0:4 * 512].rearrange("p (c j) -> p c j", c=4),
                                 xf_in[0:512, 1 + t0:1 + t0 + 512]
                                 .rearrange("(c p) j -> p c j", p=128)).then_inc(lhf.sem, 16)
                    sy.dma_start(hf[:, 4 * 512:NC * 512].rearrange("p (c j) -> p c j", c=4),
                                 xf_in[512:1024, 0 + t0:0 + t0 + 512]
                                 .rearrange("(c p) j -> p c j", p=128)).then_inc(lhf.sem, 16)
                    sy.dma_start(hb[:, 0:4 * 512].rearrange("p (c j) -> p c j", c=4),
                                 xb_in[0:512, 1 + t0:1 + t0 + 512]
                                 .rearrange("(c p) j -> p c j", p=128)).then_inc(lhb.sem, 16)
                    sy.dma_start(hb[:, 4 * 512:NC * 512].rearrange("p (c j) -> p c j", c=4),
                                 xb_in[512:1024, 0 + t0:0 + t0 + 512]
                                 .rearrange("(c p) j -> p c j", p=128)).then_inc(lhb.sem, 16)
                    lhf.val += 32
                    lhb.val += 32
                    sy.wait_ge(vHn.sem, vHn0 + (sub + 1) * NC)
                for sub in range(2):
                    t0 = qt * 1024 + sub * 512
                    if sub == 1:
                        sy.wait_ge(vX3.sem, vX30 + NC)
                    sy.dma_start(xres[:, :].rearrange("p (c j) -> p c j", c=NC),
                                 xf_in[:, 1 + t0:1 + t0 + 512]
                                 .rearrange("(c p) j -> p c j", p=128)).then_inc(lxr.sem, 16)
                    lxr.val += 16
                    for mc in range(NC):
                        sy.wait_ge(vX3.sem, vX30 + sub * NC + mc + 1)
                        sy.dma_start(xf_out[mc * 128:(mc + 1) * 128, 1 + t0:1 + t0 + 512],
                                     x3_st[:, mc * 512:(mc + 1) * 512]).then_inc(st3.sem, 16)
                    st3.val += NC * 16
                if qt == 3:
                    sy.wait_ge(st3.sem, st3.val)

            @blk.vector
            def _(v):
                v.wait_ge(stb.sem, stb0)
                v.wait_ge(st3.sem, st30)
                for sub in range(2):
                    v.wait_ge(tLn.sem, tLn0 + (sub + 1) * 2)
                    v.tensor_single_scalar(s_mu[:, :], pb[4][:, :], 1.0 / D,
                                           ALU.mult).then_inc(vch.sem, 1)
                    vch.val += 1
                    v.wait_ge(vch.sem, vch.val)
                    v.tensor_tensor(s_var[:, :], s_mu[:, :], s_mu[:, :], ALU.mult)
                    v.tensor_scalar(s_tmp[:, :], pb[5][:, :], 1.0 / D, EPS, ALU.mult,
                                    ALU.add).then_inc(vch.sem, 1)
                    vch.val += 1
                    v.wait_ge(vch.sem, vch.val)
                    v.tensor_tensor(s_var[:, :], s_tmp[:, :], s_var[:, :],
                                    ALU.subtract).then_inc(vMu.sem, 1)
                    vMu.val += 1
                    v.wait_ge(sRs.sem, sRs0 + 2 * (sub + 1))
                    v.wait_ge(lhf.sem, lhf0 + (sub + 1) * 32)
                    for c in range(NC):
                        sl = slice(c * 512, (c + 1) * 512)
                        dst = slice(c * 1024 + sub * 512, c * 1024 + sub * 512 + 512)
                        v.tensor_tensor(x3_st[:, sl], hf[:, sl], s_mu[:, :],
                                        ALU.subtract).then_inc(vch.sem, 1)
                        vch.val += 1
                        v.wait_ge(vch.sem, vch.val)
                        v.tensor_tensor(x3_st[:, sl], x3_st[:, sl], s_rs[:, :],
                                        ALU.mult).then_inc(vch.sem, 1)
                        vch.val += 1
                        v.wait_ge(vch.sem, vch.val)
                        v.tensor_scalar(hn_q[:, dst], x3_st[:, sl],
                                        bcols[:, 2 * NJ + NC + c:2 * NJ + NC + c + 1],
                                        bcols[:, 2 * NJ + 2 * NC + c:2 * NJ + 2 * NC + c + 1],
                                        ALU.mult, ALU.add).then_inc(vHn.sem, 1)
                        vHn.val += 1
                for j in range(NJ):
                    for nb in range(2):
                        gi = j * 2 + nb
                        v.wait_ge(sGl.sem, sGl0 + gi + 1)
                        v.wait_ge(tY.sem, tY0 + j * 4 + 3 + nb)
                        v.scalar_tensor_tensor(
                            h2q[:, j * 1024 + nb * 512:j * 1024 + nb * 512 + 512],
                            pb[2 + gi % 2][:, :], bcols[:, j:j + 1],
                            glu[:, (gi % 2) * 512:(gi % 2) * 512 + 512],
                            ALU.add, ALU.mult).then_inc(vH2.sem, 1)
                        vH2.val += 1
                for sub in range(2):
                    if sub == 1:
                        v.wait_ge(stb.sem, stb0 + NC * 16)
                        v.wait_ge(st3.sem, st30 + NC * 16)
                    v.wait_ge(lxr.sem, lxr0 + (sub + 1) * 16)
                    for mc in range(NC):
                        v.wait_ge(tW2.sem, tW20 + sub * NC + mc + 1)
                        v.scalar_tensor_tensor(x3_st[:, mc * 512:(mc + 1) * 512],
                                               pb[4 + mc % 2][:, :],
                                               bcols[:, 2 * NJ + mc:2 * NJ + mc + 1],
                                               xres[:, mc * 512:(mc + 1) * 512],
                                               ALU.add, ALU.add).then_inc(vX3.sem, 1)
                        vX3.val += 1

            @blk.scalar
            def _(s):
                for sub in range(2):
                    s.wait_ge(lhb.sem, lhb0 + (sub + 1) * 32)
                    for c in range(NC):
                        s.activation(sqb[:, c * 512:(c + 1) * 512],
                                     hb[:, c * 512:(c + 1) * 512],
                                     AF.Square).then_inc(vSq.sem, 1)
                        vSq.val += 1
                    s.wait_ge(vMu.sem, vMu0 + sub + 1)
                    s.activation(s_rs[:, :], s_var[:, :], AF.Ln).then_inc(sRs.sem, 1)
                    sRs.val += 1
                    s.wait_ge(sRs.sem, sRs.val)
                    s.activation(s_rs[:, :], s_rs[:, :], AF.Exp, scale=-0.5).then_inc(sRs.sem, 1)
                    sRs.val += 1
                for j in range(NJ):
                    for nb in range(2):
                        gi = j * 2 + nb
                        s.wait_ge(tY.sem, tY0 + j * 4 + nb + 1)
                        if gi >= 2:
                            s.wait_ge(vH2.sem, vH20 + gi - 1)
                        s.activation(glu[:, (gi % 2) * 512:(gi % 2) * 512 + 512],
                                     pb[gi % 2][:, :], AF.Gelu,
                                     bias=bcols[:, NJ + j:NJ + j + 1]).then_inc(sGl.sem, 1)
                        sGl.val += 1

            @blk.tensor
            def _(t):
                for sub in range(2):
                    t.wait_ge(lhb.sem, lhb0 + (sub + 1) * 32)
                    for c in range(NC):
                        t.matmul(pb[4][:, :], ones_bf[:, :], hb[:, c * 512:(c + 1) * 512],
                                 start=(c == 0), stop=(c == NC - 1))
                    t.wait_ge(vSq.sem, vSq0 + (sub + 1) * NC)
                    for c in range(NC):
                        inst = t.matmul(pb[5][:, :], ones_bf[:, :], sqb[:, c * 512:(c + 1) * 512],
                                        start=(c == 0), stop=(c == NC - 1))
                    inst.then_inc(tLn.sem, 2)
                    tLn.val += 2
                    t.wait_ge(vMu.sem, vMu0 + sub + 1)
                for j in range(NJ):
                    jw = JW[j]
                    for which, bank in ((0, 0), (1, 2)):
                        seq = j * 2 + which
                        t.wait_ge(lw1[seq % 4].sem, lw10[seq % 4] + (seq // 4 + 1) * 16)
                        for nb in range(2):
                            gi = j * 2 + nb
                            t.wait_ge(vHn.sem, vHn0 + (nb + 1) * NC)
                            if gi >= 2:
                                if bank == 0:
                                    t.wait_ge(sGl.sem, sGl0 + gi - 1)
                                else:
                                    t.wait_ge(vH2.sem, vH20 + gi - 1)
                            for kc in range(NC):
                                inst = t.matmul(
                                    pb[bank + gi % 2][0:jw, :] if jw < 128 else pb[bank + gi % 2][:, :],
                                    w1buf[:, (seq % 4) * NC * 128 + kc * jw:
                                          (seq % 4) * NC * 128 + kc * jw + jw],
                                    hn_q[:, kc * 1024 + nb * 512:kc * 1024 + nb * 512 + 512],
                                    start=(kc == 0), stop=(kc == NC - 1))
                            inst.then_inc(tY.sem, 1)
                            tY.val += 1
                t.wait_ge(vH2.sem, vH20 + NJ * 2)
                for sub in range(2):
                    for mc in range(NC):
                        if sub * NC + mc >= 2:
                            t.wait_ge(vX3.sem, vX30 + sub * NC + mc - 1)
                        for j in range(NJ):
                            jw = JW[j]
                            inst = t.matmul(pb[4 + mc % 2][:, :],
                                            w2_bf[0:jw, j * D + mc * 128:j * D + mc * 128 + 128],
                                            h2q[0:jw, j * 1024 + sub * 512:j * 1024 + sub * 512 + 512],
                                            start=(j == 0), stop=(j == NJ - 1))
                        inst.then_inc(tW2.sem, 1)
                        tW2.val += 1
    ctx.close()


def build_program():
    Ctr.reset_registry()
    nc = bass.Bass("TRN2", target_bir_lowering=False)

    resid = nc.dram_tensor("resid", [T, D], F32, kind="ExternalInput").ap()
    ident = nc.dram_tensor("ident", [128, 128], F32, kind="ExternalInput").ap()
    out = nc.dram_tensor("out", [T, D], F32, kind="ExternalOutput").ap()

    xf0 = nc.dram_tensor("xf0", [D, W], F32, kind="Internal").ap()
    xb0 = nc.dram_tensor("xb0", [D, W], BF16, kind="Internal").ap()
    xf1 = nc.dram_tensor("xf1", [D, W], F32, kind="Internal").ap()
    xb1 = nc.dram_tensor("xb1", [D, W], BF16, kind="Internal").ap()
    qt_hbm = nc.dram_tensor("qt_hbm", [D, T], BF16, kind="Internal").ap()

    dbg_zt = nc.dram_tensor("dbg_zt", [128, NC * 512], BF16, kind="ExternalOutput").ap() \
        if DEBUG_ZT else None
    wq = nc.dram_tensor("wq", [D, D], F32, kind="ExternalInput").ap()
    wk = nc.dram_tensor("wk", [D, 512], F32, kind="ExternalInput").ap()
    wv = nc.dram_tensor("wv", [D, 512], F32, kind="ExternalInput").ap()
    wo = nc.dram_tensor("wo", [D, D], F32, kind="ExternalInput").ap()
    bout_d = nc.dram_tensor("bout", [128, NC], F32, kind="ExternalInput").ap()
    mask_diag_o = nc.dram_tensor("mask_diag_o", [128, 128], BF16, kind="ExternalInput").ap()
    wqkv_l = [nc.dram_tensor(f"wqkv{l}", [D, 1536], F32, kind="ExternalInput").ap()
              for l in range(2)]
    wo_il = [nc.dram_tensor(f"wo_i{l}", [512, D], F32, kind="ExternalInput").ap()
             for l in range(2)]
    scol_l = [nc.dram_tensor(f"scol{l}", [128, 4], F32, kind="ExternalInput").ap()
              for l in range(2)]
    tab_cs = nc.dram_tensor("tab_cs", [128, T], F32, kind="ExternalInput").ap()
    tab_sn = nc.dram_tensor("tab_sn", [128, T], F32, kind="ExternalInput").ap()
    tab_ci = nc.dram_tensor("tab_ci", [128, T], F32, kind="ExternalInput").ap()
    tab_si = nc.dram_tensor("tab_si", [128, T], F32, kind="ExternalInput").ap()
    r2_d = nc.dram_tensor("r2", [128, 128], F32, kind="ExternalInput").ap()
    bdsel_d = nc.dram_tensor("bdsel", [128, 32], BF16, kind="ExternalInput").ap()
    selbc_d = nc.dram_tensor("selbc", [128, 512], F32, kind="ExternalInput").ap()
    mask_di = nc.dram_tensor("mask_di", [128, 128], BF16, kind="ExternalInput").ap()
    mask_ai = nc.dram_tensor("mask_ai", [128, 128], BF16, kind="ExternalInput").ap()
    xf2 = nc.dram_tensor("xf2", [D, W], F32, kind="Internal").ap()
    xb2 = nc.dram_tensor("xb2", [D, W], BF16, kind="Internal").ap()
    xf3 = nc.dram_tensor("xf3", [D, W], F32, kind="Internal").ap()
    xb3 = nc.dram_tensor("xb3", [D, W], BF16, kind="Internal").ap()
    w1_l = [nc.dram_tensor(f"w1_{l}", [D, 5460], F32, kind="ExternalInput").ap()
            for l in range(2)]
    w2_l = [nc.dram_tensor(f"w2_{l}", [2730, D], F32, kind="ExternalInput").ap()
            for l in range(2)]
    b1a_l = [nc.dram_tensor(f"b1a_{l}", [128, 22], F32, kind="ExternalInput").ap()
             for l in range(2)]
    b1g_l = [nc.dram_tensor(f"b1g_{l}", [128, 22], F32, kind="ExternalInput").ap()
             for l in range(2)]
    b2_l = [nc.dram_tensor(f"b2_{l}", [128, NC], F32, kind="ExternalInput").ap()
            for l in range(2)]
    lng_l = [nc.dram_tensor(f"lng_{l}", [128, NC], F32, kind="ExternalInput").ap()
             for l in range(2)]
    lnb_l = [nc.dram_tensor(f"lnb_{l}", [128, NC], F32, kind="ExternalInput").ap()
             for l in range(2)]

    with ExitStack() as ctx:
        en = ctx.enter_context
        ident_f32 = en(nc.sbuf_tensor("ident_f32", [128, 128], F32))
        ident_bf = en(nc.sbuf_tensor("ident_bf", [128, 128], BF16))
        ones_bf = en(nc.sbuf_tensor("ones_bf", [128, 128], BF16))
        ones64f = en(nc.sbuf_tensor("ones64f", [128, 64], F32))
        sel65_f = en(nc.sbuf_tensor("sel65_f", [128, 128], BF16))
        boot = Ctr(en, nc, "boot")
        boot2 = Ctr(en, nc, "boot2")
        with nc.Block() as blk:
            @blk.gpsimd
            def _(g):
                g.dma_start(ident_f32[:, :], ident).then_inc(boot.sem, 16)
                g.dma_start(ident_bf[:, :], ident).then_inc(boot.sem, 16)
                g.memset(ones_bf[:, :], 1.0)
                g.memset(ones64f[:, :], 1.0)
                g.wait_ge(boot.sem, 32)

            @blk.vector
            def _(v):
                v.memset(sel65_f[:, :], 0.0).then_inc(boot2.sem, 1)
                v.wait_ge(boot2.sem, 1)
                v.memset(sel65_f[0:1, 0:64], 1.0)
                v.memset(sel65_f[64:65, 64:128], 1.0)

        _p0_transpose_in(nc, en, resid, xf0, xb0, ident_f32,
                         [xf1, xf2, xf3], [xb1, xb2, xb3])
        if LEVEL >= 1:
            _p1_outer_attn(nc, en, xf0, xb0, xf1, xb1, qt_hbm, wq, wk, wv, wo, bout_d,
                           mask_diag_o, ident_f32, ident_bf, ones_bf, ones64f,
                           sel65_f, dbg_zt)
            if LEVEL >= 2:
                _p2_inner_attn(nc, en, 0, xf1, xb1, xf2, xb2,
                               wqkv_l[0], wo_il[0], scol_l[0],
                               tab_cs, tab_sn, tab_ci, tab_si, r2_d, bdsel_d, selbc_d,
                               mask_di, mask_ai, sel65_f, ident_bf)
                if LEVEL >= 3:
                    _p3_inner_ffn(nc, en, 0, xf2, xb2, xf3, xb3,
                                  w1_l[0], w2_l[0], b1a_l[0], b1g_l[0], b2_l[0],
                                  lng_l[0], lnb_l[0], ones_bf)
                    if LEVEL >= 4:
                        _p2_inner_attn(nc, en, 1, xf3, xb3, xf2, xb2,
                                       wqkv_l[1], wo_il[1], scol_l[1],
                                       tab_cs, tab_sn, tab_ci, tab_si, r2_d, bdsel_d,
                                       selbc_d, mask_di, mask_ai, sel65_f, ident_bf)
                        _p3_inner_ffn(nc, en, 1, xf2, xb2, xf3, xb3,
                                      w1_l[1], w2_l[1], b1a_l[1], b1g_l[1], b2_l[1],
                                      lng_l[1], lnb_l[1], ones_bf)
                        _p6_final(nc, en, xf1, xf3, out, ident_f32, ones_bf)
                    else:
                        _p6_final(nc, en, xf3, xf3, out, ident_f32, ones_bf)
                else:
                    _p6_final(nc, en, xf2, xf2, out, ident_f32, ones_bf)
            else:
                _p6_final(nc, en, xf1, xf1, out, ident_f32, ones_bf)
        else:
            _p6_final(nc, en, xf0, xf0, out, ident_f32, ones_bf)

    return nc


def _p0_transpose_in(nc, en_sem, resid, xf0, xb0, ident_f32, xf_more=(), xb_more=()):
    ctx = ExitStack()
    en = ctx.enter_context
    t_in = en(nc.sbuf_tensor("p0_in", [128, 4 * D], F32))
    t_st = en(nc.sbuf_tensor("p0_stage", [128, 4 * D], F32))
    t_z = en(nc.sbuf_tensor("p0_z", [128, NC], F32))
    t_zb = en(nc.sbuf_tensor("p0_zb", [128, NC], BF16))
    ps = [en(nc.psum_tensor(f"p0_ps{i}", [128, 512], F32)) for i in range(4)]

    ld4 = [Ctr.get(en_sem, nc, f"p0_ld{i}") for i in range(4)]
    tp = Ctr.get(en_sem, nc, "p0_tp")
    evv = Ctr.get(en_sem, nc, "p0_evv")
    stf4 = [Ctr.get(en_sem, nc, f"p0_stf{i}") for i in range(4)]
    stb4 = [Ctr.get(en_sem, nc, f"p0_stb{i}") for i in range(4)]
    stz = Ctr.get(en_sem, nc, "p0_stz")

    NT = T // 128

    mz = Ctr.get(en_sem, nc, "p0_mz")
    with nc.Block() as blk:
        @blk.gpsimd
        def _(g):
            g.memset(t_z[:, :], 0.0)
            g.memset(t_zb[:, :], 0.0).then_inc(mz.sem, 1)
            for tt in range(NT):
                if tt >= 4:
                    g.wait_ge(tp.sem, (tt - 3) * NC)
                g.dma_start(t_in[:, (tt % 4) * D:(tt % 4) * D + D],
                            resid[tt * 128:(tt + 1) * 128, :]).then_inc(ld4[tt % 4].sem, 16)
                ld4[tt % 4].val += 16
                if tt >= 1:
                    g.wait_ge(evv.sem, tt * NC)
                    g.dma_start(xb0[:, 1 + (tt - 1) * 128:1 + tt * 128]
                                .rearrange("(c p) j -> p c j", p=128),
                                t_st[:, ((tt - 1) % 4) * D:((tt - 1) % 4) * D + D]
                                .rearrange("p (c j) -> p c j", c=NC)
                                ).then_inc(stb4[(tt - 1) % 4].sem, 16)
                    stb4[(tt - 1) % 4].val += 16
            g.wait_ge(evv.sem, NT * NC)
            g.dma_start(xb0[:, 1 + (NT - 1) * 128:1 + NT * 128]
                        .rearrange("(c p) j -> p c j", p=128),
                        t_st[:, ((NT - 1) % 4) * D:((NT - 1) % 4) * D + D]
                        .rearrange("p (c j) -> p c j", c=NC)).then_inc(stb4[(NT - 1) % 4].sem, 16)
            stb4[(NT - 1) % 4].val += 16
            for i in range(4):
                g.wait_ge(stb4[i].sem, stb4[i].val)

        @blk.tensor
        def _(t):
            for tt in range(NT):
                t.wait_ge(ld4[tt % 4].sem, (tt // 4 + 1) * 16)
                for c in range(NC):
                    gi = tt * NC + c
                    if gi >= 4:
                        t.wait_ge(evv.sem, gi - 3)
                    t.transpose(ps[gi % 4][:, 0:128],
                                t_in[:, (tt % 4) * D + c * 128:(tt % 4) * D + c * 128 + 128],
                                ident_f32[:, :]).then_inc(tp.sem, 1)
            tp.val += NT * NC

        @blk.vector
        def _(v):
            for tt in range(NT):
                if tt >= 4:
                    v.wait_ge(stf4[tt % 4].sem, ((tt - 4) // 4 + 1) * 16)
                    v.wait_ge(stb4[tt % 4].sem, ((tt - 4) // 4 + 1) * 16)
                for c in range(NC):
                    gi = tt * NC + c
                    v.wait_ge(tp.sem, gi + 1)
                    v.tensor_copy(t_st[:, (tt % 4) * D + c * 128:(tt % 4) * D + c * 128 + 128],
                                  ps[gi % 4][:, 0:128]).then_inc(evv.sem, 1)
            evv.val += NT * NC

        @blk.sync
        def _(sy):
            sy.wait_ge(mz.sem, 1)
            for xf_t in [xf0] + list(xf_more):
                for c in range(NC):
                    sy.dma_start(xf_t[c * 128:(c + 1) * 128, 0:1], t_z[:, c:c + 1],
                                 allow_slow_non_contiguous=True).then_inc(stz.sem, 16)
                    stz.val += 16
            for xb_t in [xb0] + list(xb_more):
                for c in range(NC):
                    sy.dma_start(xb_t[c * 128:(c + 1) * 128, 0:1], t_zb[:, c:c + 1],
                                 allow_slow_non_contiguous=True).then_inc(stz.sem, 16)
                    stz.val += 16
            for tt in range(NT):
                sy.wait_ge(evv.sem, (tt + 1) * NC)
                sy.dma_start(
                    xf0[:, 1 + tt * 128:1 + (tt + 1) * 128].rearrange("(c p) j -> p c j", p=128),
                    t_st[:, (tt % 4) * D:(tt % 4) * D + D].rearrange("p (c j) -> p c j", c=NC)
                    ).then_inc(stf4[tt % 4].sem, 16)
                stf4[tt % 4].val += 16
            sy.wait_ge(stz.sem, stz.val)
            for i in range(4):
                sy.wait_ge(stf4[i].sem, stf4[i].val)
    ctx.close()


def _p6_final(nc, en_sem, xfa, xfb, out, ident_f32, ones_bf):
    ctx = ExitStack()
    en = ctx.enter_context
    s_a = en(nc.sbuf_tensor("p6_a", [128, 2 * NC * QB], F32))
    s_b = en(nc.sbuf_tensor("p6_b", [128, 2 * NC * QB], F32))
    s_r = en(nc.sbuf_tensor("p6_r", [128, 2 * NC * QB], F32))
    s_rb = en(nc.sbuf_tensor("p6_rb", [128, 2 * NC * QB], BF16))
    s_sq = en(nc.sbuf_tensor("p6_sq", [128, 2 * NC * QB], BF16))
    s_mu = en(nc.sbuf_tensor("p6_mu", [128, 2 * QB], F32))
    s_var = en(nc.sbuf_tensor("p6_var", [128, 2 * QB], F32))
    s_rs = en(nc.sbuf_tensor("p6_rs", [128, 2 * QB], F32))
    s_ot = en(nc.sbuf_tensor("p6_ot", [128, 4 * D], F32))
    ps_s = [en(nc.psum_tensor(f"p6_pss{i}", [128, 512], F32)) for i in range(2)]
    ps_q = [en(nc.psum_tensor(f"p6_psq{i}", [128, 512], F32)) for i in range(2)]
    ps_t = [en(nc.psum_tensor(f"p6_pst{i}", [128, 512], F32)) for i in range(4)]

    ld2 = [Ctr.get(en_sem, nc, f"p6_ld{i}") for i in range(2)]
    vrr = Ctr.get(en_sem, nc, "g1")
    ssq = Ctr.get(en_sem, nc, "g2")
    tst = Ctr.get(en_sem, nc, "g3")
    vvs = Ctr.get(en_sem, nc, "g4")
    srs = Ctr.get(en_sem, nc, "g5")
    vxn = Ctr.get(en_sem, nc, "g6")
    ttp = Ctr.get(en_sem, nc, "g7")
    vev = Ctr.get(en_sem, nc, "g8")
    st = Ctr.get(en_sem, nc, "p6_st")
    vch = Ctr.get(en_sem, nc, "g10")
    sch = Ctr.get(en_sem, nc, "g11")
    b_ld2 = [c.val for c in ld2]
    b_vrr, b_ssq, b_tst = vrr.val, ssq.val, tst.val
    b_vvs, b_srs, b_vxn, b_ttp, b_vev, b_st = (vvs.val, srs.val, vxn.val, ttp.val,
                                               vev.val, st.val)

    with nc.Block() as blk:
        @blk.sync
        def _(sy):
            for qb in range(NQB + 1):
                if qb < NQB:
                    if qb >= 2:
                        sy.wait_ge(vrr.sem, b_vrr + (qb - 1) * NC)
                    xs = (qb % 2) * NC * QB
                    sy.dma_start(s_a[:, xs:xs + NC * QB].rearrange("p (c j) -> p c j", c=NC),
                                 xfa[:, 1 + qb * QB:1 + (qb + 1) * QB].rearrange(
                                     "(c p) j -> p c j", p=128)).then_inc(ld2[qb % 2].sem, 16)
                    sy.dma_start(s_b[:, xs:xs + NC * QB].rearrange("p (c j) -> p c j", c=NC),
                                 xfb[:, 1 + qb * QB:1 + (qb + 1) * QB].rearrange(
                                     "(c p) j -> p c j", p=128)).then_inc(ld2[qb % 2].sem, 16)
                    ld2[qb % 2].val += 32
                if qb >= 1:
                    q0 = qb - 1
                    for half in range(2):
                        sy.wait_ge(vev.sem, b_vev + q0 * 4 * NC + (half + 1) * 2 * NC)
                        sy.dma_start(out[q0 * QB + half * 256:q0 * QB + half * 256 + 256, :]
                                     .rearrange("(i p) j -> p i j", p=128),
                                     s_ot[:, half * 2 * D:(half * 2 + 2) * D]
                                     .rearrange("p (i j) -> p i j", i=2)
                                     ).then_inc(st.sem, 16)
                        st.val += 16
            sy.wait_ge(st.sem, st.val)

        @blk.vector
        def _(v):
            for qb in range(NQB):
                xs = (qb % 2) * NC * QB
                ms = (qb % 2) * QB
                v.wait_ge(ld2[qb % 2].sem, b_ld2[qb % 2] + 32 * (qb // 2 + 1))
                if qb >= 2:
                    v.wait_ge(ttp.sem, b_ttp + (qb - 1) * 4 * NC)
                    v.wait_ge(tst.sem, b_tst + 2 * (qb - 1))
                    v.wait_ge(srs.sem, b_srs + qb - 1)
                for c in range(NC):
                    sl = slice(xs + c * QB, xs + (c + 1) * QB)
                    v.tensor_tensor(s_r[:, sl], s_a[:, sl], s_b[:, sl],
                                    ALU.add).then_inc(vch.sem, 1)
                    vch.val += 1
                    v.wait_ge(vch.sem, vch.val)
                    v.tensor_copy(s_rb[:, sl], s_r[:, sl]).then_inc(vrr.sem, 1)
                    vrr.val += 1
                v.wait_ge(tst.sem, b_tst + 2 * (qb + 1))
                v.tensor_single_scalar(s_mu[:, ms:ms + QB], ps_s[qb % 2][:, :], 1.0 / D,
                                       ALU.mult).then_inc(vch.sem, 1)
                vch.val += 1
                v.wait_ge(vch.sem, vch.val)
                v.tensor_tensor(s_var[:, ms:ms + QB], s_mu[:, ms:ms + QB],
                                s_mu[:, ms:ms + QB], ALU.mult)
                v.tensor_scalar(s_rs[:, ms:ms + QB], ps_q[qb % 2][:, :], 1.0 / D, EPS,
                                ALU.mult, ALU.add).then_inc(vch.sem, 1)
                vch.val += 1
                v.wait_ge(vch.sem, vch.val)
                v.tensor_tensor(s_var[:, ms:ms + QB], s_rs[:, ms:ms + QB],
                                s_var[:, ms:ms + QB], ALU.subtract).then_inc(vvs.sem, 1)
                vvs.val += 1
                v.wait_ge(srs.sem, b_srs + qb + 1)
                for c in range(NC):
                    sl = slice(xs + c * QB, xs + (c + 1) * QB)
                    v.tensor_tensor(s_r[:, sl], s_r[:, sl], s_mu[:, ms:ms + QB],
                                    ALU.subtract).then_inc(vch.sem, 1)
                    vch.val += 1
                    v.wait_ge(vch.sem, vch.val)
                    v.tensor_tensor(s_r[:, sl], s_r[:, sl], s_rs[:, ms:ms + QB],
                                    ALU.mult).then_inc(vxn.sem, 1)
                    vxn.val += 1
                for ttile in range(4):
                    if qb >= 1 and ttile == 0:
                        v.wait_ge(st.sem, b_st + qb * 32)
                    for c in range(NC):
                        g = qb * 4 * NC + ttile * NC + c
                        v.wait_ge(ttp.sem, b_ttp + g + 1)
                        v.tensor_copy(s_ot[:, ttile * D + c * 128:ttile * D + c * 128 + 128],
                                      ps_t[g % 4][:, 0:128]).then_inc(vev.sem, 1)
                        vev.val += 1

        @blk.scalar
        def _(s):
            for qb in range(NQB):
                xs = (qb % 2) * NC * QB
                ms = (qb % 2) * QB
                if qb >= 2:
                    s.wait_ge(tst.sem, b_tst + 2 * (qb - 1))
                for c in range(NC):
                    sl = slice(xs + c * QB, xs + (c + 1) * QB)
                    s.wait_ge(vrr.sem, b_vrr + qb * NC + c + 1)
                    s.activation(s_sq[:, sl], s_rb[:, sl], AF.Square).then_inc(ssq.sem, 1)
                    ssq.val += 1
                s.wait_ge(vvs.sem, b_vvs + qb + 1)
                s.activation(s_rs[:, ms:ms + QB], s_var[:, ms:ms + QB],
                             AF.Ln).then_inc(sch.sem, 1)
                sch.val += 1
                s.wait_ge(sch.sem, sch.val)
                s.activation(s_rs[:, ms:ms + QB], s_rs[:, ms:ms + QB],
                             AF.Exp, scale=-0.5).then_inc(srs.sem, 1)
                srs.val += 1

        @blk.tensor
        def _(t):
            for qb in range(NQB):
                xs = (qb % 2) * NC * QB
                t.wait_ge(vrr.sem, b_vrr + (qb + 1) * NC)
                if qb >= 2:
                    t.wait_ge(vvs.sem, b_vvs + qb - 1)
                for c in range(NC):
                    t.matmul(ps_s[qb % 2][:, :], ones_bf[:, :],
                             s_rb[:, xs + c * QB:xs + (c + 1) * QB],
                             start=(c == 0), stop=(c == NC - 1))
                t.wait_ge(ssq.sem, b_ssq + (qb + 1) * NC)
                for c in range(NC):
                    inst = t.matmul(ps_q[qb % 2][:, :], ones_bf[:, :],
                                    s_sq[:, xs + c * QB:xs + (c + 1) * QB],
                                    start=(c == 0), stop=(c == NC - 1))
                inst.then_inc(tst.sem, 2)
                tst.val += 2
                for ttile in range(4):
                    for c in range(NC):
                        g = qb * 4 * NC + ttile * NC + c
                        if g >= 4:
                            t.wait_ge(vev.sem, b_vev + g - 3)
                        if ttile == 0 and c == 0:
                            t.wait_ge(vxn.sem, b_vxn + (qb + 1) * NC)
                        t.transpose(ps_t[g % 4][:, 0:128],
                                    s_r[:, xs + c * QB + ttile * 128:
                                        xs + c * QB + ttile * 128 + 128],
                                    ident_f32[:, :]).then_inc(ttp.sem, 1)
                        ttp.val += 1
    ctx.close()


_cache = {}


def kernel(**inputs):
    resid_full = np.ascontiguousarray(np.asarray(inputs["L0_attn_resid"], np.float32))
    B = resid_full.shape[0]
    ident = np.eye(128, dtype=np.float32)

    hperm = np.zeros(D, np.int64)
    for h in range(16):
        qch, qhf = (h // 4) * 2 + h % 2, (h // 2) % 2
        hperm[qch * 128 + qhf * 64: qch * 128 + qhf * 64 + 64] = np.arange(h * 64, h * 64 + 64)
    wq = np.ascontiguousarray(np.asarray(inputs["L0_attn_W_Q"], np.float32)
                              .reshape(D, D)[:, hperm])
    wk = np.ascontiguousarray(np.asarray(inputs["L0_attn_W_K"], np.float32).reshape(D, 512))
    wv = np.ascontiguousarray(np.asarray(inputs["L0_attn_W_V"], np.float32).reshape(D, 512))
    wo = np.ascontiguousarray(
        np.asarray(inputs["L0_attn_W_out"], np.float32).transpose(1, 0, 2).reshape(D, D)[hperm, :])
    bout = np.ascontiguousarray(
        np.asarray(inputs["L0_attn_b_out"], np.float32).reshape(NC, 128).T)
    ii = np.arange(128)
    mask_do = np.where(ii[:, None] <= ii[None, :], 0.0, -1e6).astype(ml_dtypes.bfloat16)
    mask_di = np.where(ii[:, None] <= ii[None, :], 0.0, -1e36).astype(ml_dtypes.bfloat16)
    mask_ai = np.where(ii[:, None] >= ii[None, :], 0.0, -1e36).astype(ml_dtypes.bfloat16)

    half = np.arange(0, 64, 2, dtype=np.float64)
    inv_freq = 1.0 / 10000.0 ** (half / 64.0)
    tt_ = np.arange(T, dtype=np.float64)
    freqs = np.concatenate([tt_[:, None] * inv_freq, tt_[:, None] * inv_freq], axis=1)
    base = np.concatenate([(half + 25.6) / 89.6] * 2)
    power = (tt_ - T // 2) / 256.0
    scale = base[None, :] ** power[:, None]
    cosf, sinf = np.cos(freqs), np.sin(freqs)
    t2 = lambda a: np.ascontiguousarray(np.tile(a.T, (2, 1)).astype(np.float32))
    tab_cs = t2(cosf * scale)
    tab_sn = t2(sinf * scale)
    tab_ci = t2(cosf / scale)
    tab_si = t2(sinf / scale)

    def scol_for(l):
        qs = np.asarray(inputs["L0_ffn_q_scale"], np.float32)[l]
        ks = np.asarray(inputs["L0_ffn_k_scale"], np.float32)[l]
        rot = lambda a: np.concatenate([a[32:], a[:32]])
        return np.ascontiguousarray(np.stack(
            [np.tile(qs, 2), np.tile(rot(qs), 2), np.tile(ks, 2), np.tile(rot(ks), 2)],
            axis=1))

    r2 = np.zeros((128, 128), np.float32)
    for b0 in (0, 64):
        for i in range(32):
            r2[b0 + 32 + i, b0 + i] = -1.0
            r2[b0 + i, b0 + 32 + i] = 1.0
    bdsel = np.zeros((128, 32), np.float32)
    for c in range(4):
        for p in range(128):
            bdsel[p, c * 8 + 2 * c + p // 64] = 1.0
    bdsel = bdsel.astype(ml_dtypes.bfloat16)
    selbc = np.zeros((128, 512), np.float32)
    for c in range(4):
        for p in range(128):
            selbc[2 * c + p // 64, c * 128 + p] = 1.0

    def colpad(v):
        out = np.zeros((128, 22), np.float32)
        for j in range(22):
            w = min(128, 2730 - j * 128)
            out[:w, j] = v[j * 128:j * 128 + w]
        return out

    nc = build_program()
    in_maps = []
    for b in range(B):
        in_maps.append(dict(
            resid=np.ascontiguousarray(resid_full[b]),
            ident=ident,
            wq=wq, wk=wk, wv=wv, wo=wo, bout=bout, mask_diag_o=mask_do,
            wqkv0=np.ascontiguousarray(np.asarray(inputs["L0_ffn_wqkv"], np.float32)[0]),
            wqkv1=np.ascontiguousarray(np.asarray(inputs["L0_ffn_wqkv"], np.float32)[1]),
            wo_i0=np.ascontiguousarray(np.asarray(inputs["L0_ffn_wo"], np.float32)[0]),
            wo_i1=np.ascontiguousarray(np.asarray(inputs["L0_ffn_wo"], np.float32)[1]),
            scol0=scol_for(0), scol1=scol_for(1),
            tab_cs=tab_cs, tab_sn=tab_sn, tab_ci=tab_ci, tab_si=tab_si,
            r2=r2, bdsel=bdsel, selbc=selbc, mask_di=mask_di, mask_ai=mask_ai,
            **{f"w1_{l}": np.ascontiguousarray(np.asarray(inputs["L0_ffn_w1"], np.float32)[l])
               for l in range(2)},
            **{f"w2_{l}": np.ascontiguousarray(np.asarray(inputs["L0_ffn_w2"], np.float32)[l])
               for l in range(2)},
            **{f"b1a_{l}": colpad(np.asarray(inputs["L0_ffn_b1"], np.float32)[l][:2730])
               for l in range(2)},
            **{f"b1g_{l}": colpad(np.asarray(inputs["L0_ffn_b1"], np.float32)[l][2730:])
               for l in range(2)},
            **{f"b2_{l}": np.ascontiguousarray(
                np.asarray(inputs["L0_ffn_b2"], np.float32)[l].reshape(NC, 128).T)
               for l in range(2)},
            **{f"lng_{l}": np.ascontiguousarray(
                np.asarray(inputs["L0_ffn_ln_g"], np.float32)[l].reshape(NC, 128).T)
               for l in range(2)},
            **{f"lnb_{l}": np.ascontiguousarray(
                np.asarray(inputs["L0_ffn_ln_b"], np.float32)[l].reshape(NC, 128).T)
               for l in range(2)},
        ))
    res = run_bass_kernel_spmd(nc, in_maps, core_ids=list(range(B)))
    if DEBUG_ZT:
        kernel._dbg_zt = np.stack([np.asarray(r["dbg_zt"], np.float32) for r in res.results])
    return np.stack([np.asarray(r["out"], np.float32) for r in res.results], axis=0)


# version 17
# speedup vs baseline: 1.0907x; 1.0321x over previous
import numpy as np
import ml_dtypes
from contextlib import ExitStack

import concourse.bass as bass
import concourse.mybir as mybir
from concourse.bass_utils import run_bass_kernel_spmd

F32 = mybir.dt.float32
BF16 = mybir.dt.bfloat16
AF = mybir.ActivationFunctionType
ALU = mybir.AluOpType

T = 4096
D = 1024
NC = 8
W = T + 1
QB = 512
NQB = T // QB
EPS = 1e-5

LEVEL = 4
P2_STAGE = 4
DEBUG_ZT = False
DBG_QB = 0


class Ctr:
    _registry = {}

    def __init__(self, en, nc, name):
        self.sem = en(nc.semaphore(name))
        self.val = 0

    @classmethod
    def get(cls, en, nc, name):
        if name not in cls._registry:
            cls._registry[name] = Ctr(en, nc, name)
        return cls._registry[name]

    @classmethod
    def reset_registry(cls):
        cls._registry = {}


def _ln_stats(nc, v, s, t, ones_bf, rb_slices, sq_slices, ps_s, ps_q, s_mu, s_var, s_rs,
              c_rdy, c_sq, c_st, c_vs, c_rs):
    n = len(rb_slices)
    for c in range(n):
        t.matmul(ps_s, ones_bf, rb_slices[c], start=(c == 0), stop=(c == n - 1))
    for c in range(n):
        inst = t.matmul(ps_q, ones_bf, sq_slices[c], start=(c == 0), stop=(c == n - 1))
    inst.then_inc(c_st.sem, 2)
    c_st.val += 2




def _p1_outer_attn(nc, en_sem, xf0, xb0, xf1, xb1, qt_hbm, wq, wk, wv, wo, bout_d,
                   mask_diag_o, ident_f32, ident_bf, ones_bf, ones64f, sel65_f,
                   dbg_zt=None):
    NH = 16
    ctx_p1 = ExitStack()
    ctx_a = ExitStack()
    enp = ctx_p1.enter_context
    ena = ctx_a.enter_context

    wo_bf = enp(nc.sbuf_tensor("p1_wo", [128, NC * D], BF16))
    bout_cols = enp(nc.sbuf_tensor("p1_bout", [128, NC], F32))
    kt_bf = enp(nc.sbuf_tensor("p1_kt", [128, 4 * T], BF16))
    v_bf = enp(nc.sbuf_tensor("p1_v", [128, 32 * 520], BF16))
    wq_bf = ena(nc.sbuf_tensor("p1_wq", [128, NC * D], BF16))
    wk_bf = ena(nc.sbuf_tensor("p1_wk", [128, NC * 512], BF16))
    wv_bf = ena(nc.sbuf_tensor("p1_wv", [128, NC * 512], BF16))
    xq = ena(nc.sbuf_tensor("p1_xq", [128, 2 * NC * 512], BF16))
    qt_stage = ena(nc.sbuf_tensor("p1_qst", [128, 4 * 512], BF16))

    ps_a = [ena(nc.psum_tensor(f"p1_psa{i}", [128, 512], F32)) for i in range(4)]
    ps_b = [ena(nc.psum_tensor(f"p1_psb{i}", [128, 512], F32)) for i in range(4)]

    wl = Ctr(en_sem, nc, "p1_wl")
    ld2 = [Ctr(en_sem, nc, f"p1_ld{i}") for i in range(2)]
    tq = Ctr(en_sem, nc, "p1_tq")
    tv = Ctr(en_sem, nc, "p1_tv")
    se = Ctr(en_sem, nc, "p1_se")
    vv = Ctr(en_sem, nc, "p1_vv")
    sq4 = [Ctr(en_sem, nc, f"p1_sq{i}") for i in range(4)]

    with nc.Block() as blk:
        @blk.gpsimd
        def _(g):
            for c in range(NC):
                g.dma_start(wq_bf[:, c * D:(c + 1) * D],
                            wq[c * 128:(c + 1) * 128, :]).then_inc(wl.sem, 16)
                g.dma_start(wk_bf[:, c * 512:(c + 1) * 512],
                            wk[c * 128:(c + 1) * 128, :]).then_inc(wl.sem, 16)
                g.dma_start(wv_bf[:, c * 512:(c + 1) * 512],
                            wv[c * 128:(c + 1) * 128, :]).then_inc(wl.sem, 16)
                g.dma_start(wo_bf[:, c * D:(c + 1) * D],
                            wo[c * 128:(c + 1) * 128, :]).then_inc(wl.sem, 16)
            g.dma_start(bout_cols[:, :], bout_d).then_inc(wl.sem, 16)
            wl.val += (4 * NC + 1) * 16
            g.memset(bass.AP(v_bf, 64, [[32 * 520, 128], [520, 32], [65, 8], [1, 1]]), 1.0)

        @blk.sync
        def _(sy):
            for nb in range(NQB):
                if nb >= 2:
                    sy.wait_ge(tq.sem, (nb - 1) * 12)
                    sy.wait_ge(tv.sem, (nb - 1) * 4)
                sy.dma_start(xq[:, (nb % 2) * NC * 512:(nb % 2) * NC * 512 + NC * 512]
                             .rearrange("p (c j) -> p c j", c=NC),
                             xb0[:, 1 + nb * 512:1 + (nb + 1) * 512]
                             .rearrange("(c p) j -> p c j", p=128)).then_inc(ld2[nb % 2].sem, 16)
                ld2[nb % 2].val += 16
                for mc in range(NC):
                    qidx = nb * NC + mc
                    sy.wait_ge(se.sem, nb * 12 + mc + 1)
                    sy.dma_start(qt_hbm[mc * 128:(mc + 1) * 128, nb * 512:(nb + 1) * 512],
                                 qt_stage[:, (qidx % 4) * 512:(qidx % 4) * 512 + 512]
                                 ).then_inc(sq4[qidx % 4].sem, 16)
                    sq4[qidx % 4].val += 16
            for i in range(4):
                sy.wait_ge(sq4[i].sem, sq4[i].val)

        @blk.tensor
        def _(t):
            t.wait_ge(wl.sem, wl.val)
            for nb in range(NQB):
                t.wait_ge(ld2[nb % 2].sem, (nb // 2 + 1) * 16)
                xs = (nb % 2) * NC * 512
                for mc in range(8):
                    gq = nb * 12 + mc
                    if gq >= 4:
                        t.wait_ge(se.sem, gq - 3)
                    for kc in range(NC):
                        inst = t.matmul(ps_a[gq % 4][:, :],
                                        wq_bf[:, kc * D + mc * 128:kc * D + mc * 128 + 128],
                                        xq[:, xs + kc * 512:xs + (kc + 1) * 512],
                                        start=(kc == 0), stop=(kc == NC - 1))
                    inst.then_inc(tq.sem, 1)
                    tq.val += 1
                for mc in range(4):
                    gq = nb * 12 + 8 + mc
                    if gq >= 4:
                        t.wait_ge(se.sem, gq - 3)
                    for kc in range(NC):
                        inst = t.matmul(ps_a[gq % 4][:, :],
                                        wk_bf[:, kc * 512 + mc * 128:kc * 512 + mc * 128 + 128],
                                        xq[:, xs + kc * 512:xs + (kc + 1) * 512],
                                        start=(kc == 0), stop=(kc == NC - 1))
                    inst.then_inc(tq.sem, 1)
                    tq.val += 1
                for tt in range(4):
                    gv = nb * 4 + tt
                    if gv >= 4:
                        t.wait_ge(vv.sem, gv - 3)
                    for kc in range(NC):
                        inst = t.matmul(ps_b[gv % 4][:, :],
                                        xq[:, xs + kc * 512 + tt * 128:xs + kc * 512 + tt * 128 + 128],
                                        wv_bf[:, kc * 512:(kc + 1) * 512],
                                        start=(kc == 0), stop=(kc == NC - 1))
                    inst.then_inc(tv.sem, 1)
                    tv.val += 1

        @blk.scalar
        def _(s):
            for nb in range(NQB):
                for i in range(12):
                    gq = nb * 12 + i
                    s.wait_ge(tq.sem, gq + 1)
                    if i < 8:
                        qidx = nb * NC + i
                        if qidx >= 4:
                            s.wait_ge(sq4[qidx % 4].sem, (qidx // 4) * 16)
                        s.copy(qt_stage[:, (qidx % 4) * 512:(qidx % 4) * 512 + 512],
                               ps_a[gq % 4][:, :]).then_inc(se.sem, 1)
                    else:
                        mc = i - 8
                        s.copy(kt_bf[:, mc * T + nb * 512:mc * T + (nb + 1) * 512],
                               ps_a[gq % 4][:, :]).then_inc(se.sem, 1)
                    se.val += 1

        @blk.vector
        def _(v):
            for nb in range(NQB):
                for tt in range(4):
                    gv = nb * 4 + tt
                    v.wait_ge(tv.sem, gv + 1)
                    base = (nb * 4 + tt) * 520
                    v.tensor_copy(
                        bass.AP(v_bf, base, [[32 * 520, 128], [65, 8], [1, 64]]),
                        ps_b[gv % 4][:, :].rearrange("p (g e) -> p g e", g=8)
                        ).then_inc(vv.sem, 1)
                    vv.val += 1

    ctx_a.close()
    en = enp
    qtb = en(nc.sbuf_tensor("p1_qtb", [128, 2 * NC * 512], BF16))
    x0f = en(nc.sbuf_tensor("p1_x0f", [128, NC * 512], F32))
    pt = [en(nc.sbuf_tensor(f"p1_pt{i}", [128, 512], BF16)) for i in range(8)]
    dn_s = en(nc.sbuf_tensor("p1_dn", [128, 512], BF16))
    dn_l = en(nc.sbuf_tensor("p1_dnl", [128, 512], F32))
    bc_s = en(nc.sbuf_tensor("p1_bc", [128, 512], F32))
    zt_bf = en(nc.sbuf_tensor("p1_zt", [128, 2 * NC * 512], BF16))
    mk = en(nc.sbuf_tensor("p1_mk", [128, 128], BF16))
    rb_bf = en(nc.sbuf_tensor("p1_rbb", [128, NC * 512], BF16))
    sq_bf = en(nc.sbuf_tensor("p1_sqb", [128, NC * 512], BF16))
    s_mu = en(nc.sbuf_tensor("p1_mu", [128, 512], F32))
    s_var = en(nc.sbuf_tensor("p1_var", [128, 512], F32))
    s_rs = en(nc.sbuf_tensor("p1_rs", [128, 512], F32))
    s_tmp = en(nc.sbuf_tensor("p1_tmp", [128, 512], F32))
    x1_st = en(nc.sbuf_tensor("p1_x1st", [128, 2 * NC * 512], F32))

    ps_s = [en(nc.psum_tensor(f"p1_pss{i}", [128, 512], F32)) for i in range(3)]
    ps_z = [en(nc.psum_tensor(f"p1_psz{i}", [128, 512], F32)) for i in range(2)]
    ps_bc = [en(nc.psum_tensor(f"p1_psbc{i}", [128, 512], F32)) for i in range(1)]
    ps_t6 = [en(nc.psum_tensor(f"p1_pst6{i}", [128, 512], F32)) for i in range(2)]

    lqq = Ctr.get(en_sem, nc, "p1b_lqq")
    lqq2 = Ctr.get(en_sem, nc, "p1b_lqq2")
    gpx = Ctr.get(en_sem, nc, "p1b_gpx")
    lqx = Ctr.get(en_sem, nc, "p1b_lqx")
    tS = Ctr.get(en_sem, nc, "g2")
    vM = Ctr.get(en_sem, nc, "g3")
    sE = Ctr.get(en_sem, nc, "g4")
    tP = Ctr.get(en_sem, nc, "g5")
    vR = Ctr.get(en_sem, nc, "g6")
    tB = Ctr.get(en_sem, nc, "g7")
    vZ = Ctr.get(en_sem, nc, "g8")
    tO = Ctr.get(en_sem, nc, "g9")
    vRb = Ctr.get(en_sem, nc, "g10")
    sRb = Ctr.get(en_sem, nc, "g11")
    vSq = Ctr.get(en_sem, nc, "g12")
    tT = Ctr.get(en_sem, nc, "g13")
    vVs = Ctr.get(en_sem, nc, "g14")
    sRs = Ctr.get(en_sem, nc, "g15")
    vXn = Ctr.get(en_sem, nc, "g16")
    sXb = Ctr.get(en_sem, nc, "g17")
    stx = Ctr.get(en_sem, nc, "p1b_stx")
    vch = Ctr.get(en_sem, nc, "g19")
    sch = Ctr.get(en_sem, nc, "g20")
    vBC = Ctr.get(en_sem, nc, "g21")

    with nc.Block() as blk:
        @blk.sync
        def _(sy):
            sy.dma_start(mk[:, :], mask_diag_o).then_inc(lqq.sem, 16)
            sy.wait_ge(lqq.sem, 16)

        @blk.vector
        def _(v):
            v.memset(dn_s[:, :], 0.0)
    lqq.val += 16

    nkts = [4 * qb + 4 for qb in range(NQB)]
    cumS = [0]
    for qb in range(NQB):
        cumS.append(cumS[-1] + 16 * nkts[qb])

    def gidx(qb, h, kt):
        return cumS[qb] + h * nkts[qb] + kt

    PAIRS = [(0, 2), (1, 3), (4, 6), (5, 7), (8, 10), (9, 11), (12, 14), (13, 15)]
    SIDX, PIDX, MSIDX, ORD_S = {}, {}, {}, {}
    for qb in range(NQB):
        nkt = nkts[qb]
        order_S, order_PV = [], []
        for hA, hB in PAIRS:
            for kt in range(nkt):
                order_S += [(hA, kt), (hB, kt)]
                if kt >= 1:
                    order_PV += [(hA, kt - 1), (hB, kt - 1)]
            order_PV += [(hA, nkt - 1), (hB, nkt - 1)]
        sidx = {u: cumS[qb] + i + 1 for i, u in enumerate(order_S)}
        pidx = {u: cumS[qb] + i + 1 for i, u in enumerate(order_PV)}
        em_order = []
        for hA, hB in PAIRS:
            for h_ in (hA, hB):
                for kt_ in range(4 * qb + 1, nkt):
                    em_order.append((h_, kt_))
        mpos = {u: i + 48 * qb + 1 for i, u in enumerate(em_order)}
        msidx = {}
        for u in order_S:
            h_, kt_ = u
            if kt_ > 4 * qb:
                msidx[u] = mpos[(h_, kt_)]
            else:
                msidx[u] = 0
        SIDX[qb], PIDX[qb], MSIDX[qb], ORD_S[qb] = sidx, pidx, msidx, order_S
    ZBANK = {}
    for _hA, _hB in PAIRS:
        ZBANK[_hA], ZBANK[_hB] = 0, 1

    NH = 16
    b_lqq, b_lqx, b_stx, b_gpx = lqq.val, lqx.val, stx.val, gpx.val
    b_tS, b_sE, b_tP, b_vM = tS.val, sE.val, tP.val, vM.val
    b_vR, b_tB, b_vZ, b_tO = vR.val, tB.val, vZ.val, tO.val
    b_vRb, b_sRb, b_vSq, b_tT = vRb.val, sRb.val, vSq.val, tT.val
    b_vVs, b_sRs, b_vXn, b_vBC = vVs.val, sRs.val, vXn.val, vBC.val

    with nc.Block() as blk:
        @blk.sync
        def _(sy):
            sy.dma_start(qtb[:, 0:NC * 512].rearrange("p (c j) -> p c j", c=NC),
                         qt_hbm[:, 0:512].rearrange("(c p) j -> p c j", p=128)
                         ).then_inc(lqq.sem, 16)
            lqq.val += 16
            for qb in range(NQB + 2):
                if qb + 1 < NQB:
                    sy.wait_ge(tS.sem, b_tS + cumS[qb])
                    sl_sem = lqq if (qb + 1) % 2 == 0 else lqq2
                    sy.dma_start(
                        qtb[:, ((qb + 1) % 2) * NC * 512:((qb + 1) % 2 + 1) * NC * 512]
                        .rearrange("p (c j) -> p c j", c=NC),
                        qt_hbm[:, (qb + 1) * 512:(qb + 2) * 512]
                        .rearrange("(c p) j -> p c j", p=128)).then_inc(sl_sem.sem, 16)
                    sl_sem.val += 16
                if qb < NQB:
                    sy.wait_ge(vRb.sem, b_vRb + 8 * qb)
                    sy.dma_start(x0f[:, :].rearrange("p (c j) -> p c j", c=NC),
                                 xf0[:, 1 + qb * 512:1 + (qb + 1) * 512]
                                 .rearrange("(c p) j -> p c j", p=128)).then_inc(lqx.sem, 16)
                    lqx.val += 16
                if qb >= 2:
                    q0 = qb - 2
                    for mc in range(NC):
                        sy.wait_ge(vXn.sem, b_vXn + q0 * NC + mc + 1)
                        sy.dma_start(xf1[mc * 128:(mc + 1) * 128,
                                         1 + q0 * 512:1 + (q0 + 1) * 512],
                                     x1_st[:, (q0 % 2) * NC * 512 + mc * 512:
                                           (q0 % 2) * NC * 512 + (mc + 1) * 512]
                                     ).then_inc(stx.sem, 16)
                    stx.val += NC * 16
            sy.wait_ge(stx.sem, stx.val)
            sy.wait_ge(gpx.sem, b_gpx + NQB * NC * 16)

        @blk.gpsimd
        def _(g):
            for qb in range(NQB):
                for mc in range(NC):
                    g.wait_ge(vXn.sem, b_vXn + qb * NC + mc + 1)
                    g.dma_start(xb1[mc * 128:(mc + 1) * 128, 1 + qb * 512:1 + (qb + 1) * 512],
                                x1_st[:, (qb % 2) * NC * 512 + mc * 512:
                                      (qb % 2) * NC * 512 + (mc + 1) * 512]
                                ).then_inc(gpx.sem, 16)
                gpx.val += NC * 16

        @blk.tensor
        def _(t):
            def attn_tensor(qb):
                nkt = nkts[qb]
                qoff = (qb % 2) * NC * 512
                sidx, pidx, msidx = SIDX[qb], PIDX[qb], MSIDX[qb]
                order_S = ORD_S[qb]
                if qb % 2 == 0:
                    t.wait_ge(lqq.sem, b_lqq + 16 * (qb // 2 + 1))
                else:
                    t.wait_ge(lqq2.sem, 16 * ((qb - 1) // 2 + 1))

                def emit_S(h, kt):
                    g = h // 2
                    qch = (h // 4) * 2 + h % 2
                    qhf = (h // 2) % 2
                    GS = sidx[(h, kt)] - 1
                    off = max(0, (kt - 4 * qb) * 128)
                    diag = kt >= 4 * qb
                    inst = t.matmul(ps_s[GS % 3][:, off:512],
                                    kt_bf[(g % 2) * 64:(g % 2) * 64 + 64,
                                          (g // 2) * T + kt * 128:(g // 2) * T + kt * 128 + 128],
                                    qtb[qhf * 64:qhf * 64 + 64,
                                        qoff + qch * 512 + off:qoff + qch * 512 + 512],
                                    start=True, stop=not diag)
                    if diag:
                        inst = t.matmul(ps_s[GS % 3][:, off:off + 128], ident_bf[:, :],
                                        mk[:, :], start=False, stop=True)
                    inst.then_inc(tS.sem, 1)
                    tS.val += 1

                def emit_PV(h, kt):
                    g = h // 2
                    GS = sidx[(h, kt)] - 1
                    t.matmul(ps_z[ZBANK[h]][0:65, 0:512],
                             v_bf[:, kt * 520 + g * 65:kt * 520 + g * 65 + 65],
                             pt[GS % 8][:, 0:512],
                             start=(kt == 0), stop=(kt == nkt - 1)).then_inc(tP.sem, 1)
                    tP.val += 1

                def emit_pair_bcast(p):
                    t.wait_ge(vR.sem, b_vR + 8 * qb + p + 1)
                    if 8 * qb + p >= 1:
                        t.wait_ge(vBC.sem, b_vBC + 8 * qb + p)
                    t.matmul(ps_bc[0][0:128, 0:512], sel65_f[0:65, 0:128],
                             dn_s[0:65, 0:512], start=True, stop=True).then_inc(tB.sem, 1)
                    tB.val += 1

                for p, (hA, hB) in enumerate(PAIRS):
                    for kt in range(nkt):
                        GSb = sidx[(hB, kt)] - 1
                        if GSb >= 3:
                            t.wait_ge(sE.sem, b_sE + GSb - 2)
                        emit_S(hA, kt)
                        emit_S(hB, kt)
                        if p >= 1 and kt == 0:
                            emit_pair_bcast(p - 1)
                        if kt >= 1:
                            t.wait_ge(sE.sem, b_sE + sidx[(hB, kt - 1)])
                            nmz = max(msidx[(hA, kt - 1)], msidx[(hB, kt - 1)])
                            if nmz > 0:
                                t.wait_ge(vM.sem, b_vM + nmz)
                            if kt == 1 and 16 * qb + 2 * p > 0:
                                t.wait_ge(vZ.sem, b_vZ + 16 * qb + 2 * p)
                            emit_PV(hA, kt - 1)
                            emit_PV(hB, kt - 1)
                    t.wait_ge(sE.sem, b_sE + sidx[(hB, nkt - 1)])
                    nmz = max(msidx[(hA, nkt - 1)], msidx[(hB, nkt - 1)])
                    if nmz > 0:
                        t.wait_ge(vM.sem, b_vM + nmz)
                    emit_PV(hA, nkt - 1)
                    emit_PV(hB, nkt - 1)
                emit_pair_bcast(7)

            def TA(qb):
                t.wait_ge(vZ.sem, b_vZ + 16 * (qb + 1))
                if qb >= 2:
                    t.wait_ge(vVs.sem, b_vVs + qb - 1)
                for mc in range(NC):
                    if mc >= 2:
                        t.wait_ge(vRb.sem, b_vRb + 8 * qb + mc - 1)
                    for kc in range(NC):
                        inst = t.matmul(ps_t6[mc % 2][:, :],
                                        wo_bf[:, kc * D + mc * 128:kc * D + mc * 128 + 128],
                                        zt_bf[:, (qb % 2) * NC * 512 + kc * 512:
                                              (qb % 2) * NC * 512 + (kc + 1) * 512],
                                        start=(kc == 0), stop=(kc == NC - 1))
                    inst.then_inc(tO.sem, 1)
                    tO.val += 1

            def TB(qb):
                t.wait_ge(sRb.sem, b_sRb + 8 * (qb + 1))
                t.wait_ge(vRb.sem, b_vRb + 8 * min(qb + 2, NQB))
                if qb >= 1:
                    t.wait_ge(vVs.sem, b_vVs + qb)
                for c in range(NC):
                    t.matmul(ps_t6[0][:, :], ones_bf[:, :],
                             rb_bf[:, c * 512:(c + 1) * 512],
                             start=(c == 0), stop=(c == NC - 1))
                t.wait_ge(vSq.sem, b_vSq + 8 * (qb + 1))
                for c in range(NC):
                    inst = t.matmul(ps_t6[1][:, :], ones_bf[:, :],
                                    sq_bf[:, c * 512:(c + 1) * 512],
                                    start=(c == 0), stop=(c == NC - 1))
                inst.then_inc(tT.sem, 2)
                tT.val += 2

            for qb in range(NQB + 2):
                if qb < NQB:
                    attn_tensor(qb)
                if qb >= 1 and qb - 1 < NQB:
                    TA(qb - 1)
                if qb >= 2:
                    TB(qb - 2)

        @blk.vector
        def _(v):
            def attn_vector(qb):
                nkt = nkts[qb]
                sidx, pidx, msidx = SIDX[qb], PIDX[qb], MSIDX[qb]
                order_S = ORD_S[qb]
                if qb >= 2:
                    v.wait_ge(tO.sem, b_tO + 8 * (qb - 1))

                def msets_h(h_):
                    for kt in range(4 * qb + 1, nkt):
                        GS = sidx[(h_, kt)] - 1
                        off = (kt - 4 * qb) * 128
                        if GS >= 8:
                            u6 = order_S[GS - 8 - cumS[qb]] if GS - 8 >= cumS[qb] else None
                            if u6 is not None:
                                v.wait_ge(tP.sem, b_tP + pidx[u6])
                        v.memset(pt[GS % 8][:, 0:off], 0.0).then_inc(vM.sem, 1)
                        vM.val += 1

                msets_h(PAIRS[0][0])
                msets_h(PAIRS[0][1])
                for p, (hA, hB) in enumerate(PAIRS):
                    v.wait_ge(tB.sem, b_tB + 8 * qb + p + 1)
                    v.wait_ge(vZ.sem, vZ.val)
                    v.tensor_copy(bc_s[0:128, 0:512],
                                  ps_bc[0][0:128, 0:512]).then_inc(vBC.sem, 1)
                    vBC.val += 1
                    v.wait_ge(vBC.sem, vBC.val)
                    for pos, h in enumerate((hA, hB)):
                        qch = (h // 4) * 2 + h % 2
                        qhf = (h // 2) % 2
                        v.tensor_tensor(zt_bf[qhf * 64:qhf * 64 + 64,
                                              (qb % 2) * NC * 512 + qch * 512:
                                              (qb % 2) * NC * 512 + qch * 512 + 512],
                                        ps_z[pos][0:64, 0:512],
                                        bc_s[pos * 64:pos * 64 + 64, 0:512],
                                        ALU.mult).then_inc(vZ.sem, 1)
                        vZ.val += 1
                    if p + 1 < 8:
                        msets_h(PAIRS[p + 1][0])
                        msets_h(PAIRS[p + 1][1])

            def VA(qb):
                xs = (qb % 2) * NC * 512
                v.wait_ge(lqx.sem, b_lqx + 16 * (qb + 1))
                if qb >= 2:
                    v.wait_ge(stx.sem, b_stx + (qb - 1) * NC * 16)
                    v.wait_ge(gpx.sem, b_gpx + (qb - 1) * NC * 16)
                for mc in range(NC):
                    v.wait_ge(tO.sem, b_tO + 8 * qb + mc + 1)
                    v.scalar_tensor_tensor(x1_st[:, xs + mc * 512:xs + (mc + 1) * 512],
                                           ps_t6[mc % 2][:, :], bout_cols[:, mc:mc + 1],
                                           x0f[:, mc * 512:(mc + 1) * 512],
                                           ALU.add, ALU.add).then_inc(vRb.sem, 1)
                    vRb.val += 1
                if qb >= 1:
                    v.wait_ge(tT.sem, b_tT + 2 * qb)
                for mc in range(NC):
                    v.wait_ge(vRb.sem, b_vRb + 8 * qb + mc + 1)
                    v.tensor_mul(sq_bf[:, mc * 512:(mc + 1) * 512],
                                 x1_st[:, xs + mc * 512:xs + (mc + 1) * 512],
                                 x1_st[:, xs + mc * 512:xs + (mc + 1) * 512]).then_inc(vSq.sem, 1)
                    vSq.val += 1

            def VB(qb):
                v.wait_ge(tT.sem, b_tT + 2 * (qb + 1))
                v.tensor_single_scalar(s_mu[:, :], ps_t6[0][:, :], 1.0 / D,
                                       ALU.mult).then_inc(vch.sem, 1)
                vch.val += 1
                v.wait_ge(vch.sem, vch.val)
                v.tensor_tensor(s_var[:, :], s_mu[:, :], s_mu[:, :], ALU.mult)
                v.tensor_scalar(s_tmp[:, :], ps_t6[1][:, :], 1.0 / D, EPS, ALU.mult,
                                ALU.add).then_inc(vch.sem, 1)
                vch.val += 1
                v.wait_ge(vch.sem, vch.val)
                v.tensor_tensor(s_var[:, :], s_tmp[:, :], s_var[:, :],
                                ALU.subtract).then_inc(vVs.sem, 1)
                vVs.val += 1

            def VC(qb):
                xs = (qb % 2) * NC * 512
                v.wait_ge(sRs.sem, b_sRs + 2 * (qb + 1))
                for mc in range(NC):
                    sl = slice(xs + mc * 512, xs + (mc + 1) * 512)
                    v.tensor_tensor(x1_st[:, sl], x1_st[:, sl], s_mu[:, :],
                                    ALU.subtract).then_inc(vch.sem, 1)
                    vch.val += 1
                    v.wait_ge(vch.sem, vch.val)
                    v.tensor_tensor(x1_st[:, sl], x1_st[:, sl], s_rs[:, :],
                                    ALU.mult).then_inc(vXn.sem, 1)
                    vXn.val += 1

            for qb in range(NQB + 2):
                if qb < NQB:
                    attn_vector(qb)
                if qb >= 1 and qb - 1 < NQB:
                    VA(qb - 1)
                if qb >= 2:
                    VB(qb - 2)
                    VC(qb - 2)

        @blk.scalar
        def _(s):
            def den_pair(qb, p, hA, hB):
                pidx = PIDX[qb]
                if 8 * qb + p >= 1:
                    s.wait_ge(tB.sem, b_tB + 8 * qb + p)
                s.wait_ge(tP.sem, b_tP + pidx[(hB, nkts[qb] - 1)])
                s.activation(dn_l[0:1, 0:512], ps_z[0][64:65, 0:512],
                             AF.Ln).then_inc(sch.sem, 1)
                sch.val += 1
                s.activation(dn_l[64:65, 0:512], ps_z[1][64:65, 0:512],
                             AF.Ln).then_inc(sch.sem, 1)
                sch.val += 1
                s.wait_ge(sch.sem, sch.val)
                s.activation(dn_s[0:1, 0:512], dn_l[0:1, 0:512],
                             AF.Exp, scale=-1.0)
                s.activation(dn_s[64:65, 0:512], dn_l[64:65, 0:512],
                             AF.Exp, scale=-1.0).then_inc(vR.sem, 1)
                vR.val += 1

            def attn_scalar(qb):
                nkt = nkts[qb]
                sidx, pidx = SIDX[qb], PIDX[qb]
                order_S = ORD_S[qb]
                for p, (hA, hB) in enumerate(PAIRS):
                    for kt in range(nkt):
                        GSb = sidx[(hB, kt)] - 1
                        if GSb >= 8 and GSb - 8 >= cumS[qb]:
                            u6 = order_S[GSb - 8 - cumS[qb]]
                            s.wait_ge(tP.sem, b_tP + pidx[u6])
                        s.wait_ge(tS.sem, b_tS + GSb + 1)
                        for h in (hA, hB):
                            GS = sidx[(h, kt)] - 1
                            off = max(0, (kt - 4 * qb) * 128)
                            s.activation(pt[GS % 8][:, off:512], ps_s[GS % 3][:, off:512],
                                         AF.Exp, scale=0.125).then_inc(sE.sem, 1)
                            sE.val += 1
                    den_pair(qb, p, hA, hB)

            def SA(qb):
                xs = (qb % 2) * NC * 512
                if qb >= 1:
                    s.wait_ge(tT.sem, b_tT + 2 * qb)
                for mc in range(NC):
                    s.wait_ge(vRb.sem, b_vRb + 8 * qb + mc + 1)
                    s.copy(rb_bf[:, mc * 512:(mc + 1) * 512],
                           x1_st[:, xs + mc * 512:xs + (mc + 1) * 512]).then_inc(sRb.sem, 1)
                    sRb.val += 1

            def SB(qb):
                s.wait_ge(vVs.sem, b_vVs + qb + 1)
                s.activation(s_rs[:, :], s_var[:, :], AF.Ln).then_inc(sRs.sem, 1)
                sRs.val += 1
                s.wait_ge(sRs.sem, sRs.val)
                s.activation(s_rs[:, :], s_rs[:, :], AF.Exp, scale=-0.5).then_inc(sRs.sem, 1)
                sRs.val += 1

            for qb in range(NQB + 2):
                if qb < NQB:
                    attn_scalar(qb)
                if qb >= 1 and qb - 1 < NQB:
                    SA(qb - 1)
                if qb >= 2:
                    SB(qb - 2)
    ctx_p1.close()


def _p2_inner_attn(nc, en_sem, lid, xf_in, xb_in, xf_out, xb_out,
                   wqkv_d, wo_d, scol_d,
                   tab_cs, tab_sn, tab_ci, tab_si, r2_d, bdsel_d, selbc_d,
                   mask_di, mask_ai, sel65_f, ident_bf):
    NHI = 8
    ctx = ExitStack()
    en = ctx.enter_context
    L = str(lid)

    wqkv_bf = en(nc.sbuf_tensor("p2_wqkv" + L, [128, NC * 1536], BF16))
    wo_bf = en(nc.sbuf_tensor("p2_wo" + L, [128, 4 * D], BF16))
    r2_f = en(nc.sbuf_tensor("p2_r2" + L, [128, 128], F32))
    scol = en(nc.sbuf_tensor("p2_sc" + L, [128, 4], F32))
    bdsel = en(nc.sbuf_tensor("p2_bd" + L, [128, 4 * 8], BF16))
    selbc = en(nc.sbuf_tensor("p2_sb" + L, [128, 4 * 128], BF16))
    mk_d = en(nc.sbuf_tensor("p2_mkd" + L, [128, 128], BF16))
    mk_a = en(nc.sbuf_tensor("p2_mka" + L, [128, 128], BF16))

    kr = en(nc.sbuf_tensor("p2_kr" + L, [128, 2 * 4 * 512], BF16))
    vr = en(nc.sbuf_tensor("p2_vr" + L, [128, 2 * 4 * 520], BF16))
    qfin = en(nc.sbuf_tensor("p2_qf" + L, [128, 4 * 512], BF16))
    xq_in = en(nc.sbuf_tensor("p2_xq" + L, [128, NC * 512], BF16))
    xf_cur = en(nc.sbuf_tensor("p2_xf" + L, [128, NC * 512], F32))
    tabs = en(nc.sbuf_tensor("p2_tab" + L, [128, 4 * 512], F32))
    ctabs = en(nc.sbuf_tensor("p2_ctab" + L, [128, 4 * 512], F32))
    qt_f = en(nc.sbuf_tensor("p2_qtf" + L, [128, NC * 512], F32))
    t1b = en(nc.sbuf_tensor("p2_t1" + L, [128, NC * 512], F32))
    t3b = en(nc.sbuf_tensor("p2_t3" + L, [128, 2 * 512], F32))
    sqt = en(nc.sbuf_tensor("p2_sqt" + L, [128, 8 * 512], BF16))
    ssrow = en(nc.sbuf_tensor("p2_ss" + L, [128, 2 * 512], F32))
    invr = en(nc.sbuf_tensor("p2_inv" + L, [128, 2 * 512], BF16))
    pt = [en(nc.sbuf_tensor(f"p2_pt{i}" + L, [128, 512], BF16)) for i in range(8)]
    dn_s = en(nc.sbuf_tensor("p2_dn" + L, [128, 512], BF16))
    dn_l = en(nc.sbuf_tensor("p2_dnl" + L, [128, 512], F32))
    bc_s = en(nc.sbuf_tensor("p2_bc" + L, [128, 512], F32))
    zin = en(nc.sbuf_tensor("p2_zin" + L, [128, 4 * 512], BF16))
    x2_st = en(nc.sbuf_tensor("p2_x2" + L, [128, NC * 512], F32))

    pb = [en(nc.psum_tensor(f"p2_pb{i}" + L, [128, 512], F32)) for i in range(8)]

    wl = Ctr.get(en_sem, nc, "p2_wl")
    lxq = Ctr.get(en_sem, nc, "p2_lxq")
    lxf = Ctr.get(en_sem, nc, "p2_lxf")
    ltb = Ctr.get(en_sem, nc, "p2_ltb")
    tQ = Ctr.get(en_sem, nc, "g4")
    vQ = Ctr.get(en_sem, nc, "g5")
    tV = Ctr.get(en_sem, nc, "g6")
    vV = Ctr.get(en_sem, nc, "g7")
    sSq = Ctr.get(en_sem, nc, "g8")
    tSS = Ctr.get(en_sem, nc, "g9")
    vSS = Ctr.get(en_sem, nc, "g10")
    sIv = Ctr.get(en_sem, nc, "g11")
    tBc = Ctr.get(en_sem, nc, "g12")
    tRo = Ctr.get(en_sem, nc, "g13")
    vT3 = Ctr.get(en_sem, nc, "g14")
    vQf = Ctr.get(en_sem, nc, "g15")
    tS = Ctr.get(en_sem, nc, "g16")
    vM = Ctr.get(en_sem, nc, "g17")
    sE = Ctr.get(en_sem, nc, "g18")
    tP = Ctr.get(en_sem, nc, "g19")
    vR = Ctr.get(en_sem, nc, "g20")
    tB = Ctr.get(en_sem, nc, "g21")
    vZ = Ctr.get(en_sem, nc, "g22")
    tO = Ctr.get(en_sem, nc, "g23")
    vO = Ctr.get(en_sem, nc, "g24")
    stg = Ctr.get(en_sem, nc, "p2_stg")
    vch = Ctr.get(en_sem, nc, "g27")
    sch = Ctr.get(en_sem, nc, "g28")
    vBC = Ctr.get(en_sem, nc, "g29")

    with nc.Block() as blk:
        @blk.gpsimd
        def _(g):
            for c in range(NC):
                g.dma_start(wqkv_bf[:, c * 1536:(c + 1) * 1536],
                            wqkv_d[c * 128:(c + 1) * 128, :]).then_inc(wl.sem, 16)
            for c in range(4):
                g.dma_start(wo_bf[:, c * D:(c + 1) * D],
                            wo_d[c * 128:(c + 1) * 128, :]).then_inc(wl.sem, 16)
            g.dma_start(r2_f[:, :], r2_d).then_inc(wl.sem, 16)
            g.dma_start(scol[:, :], scol_d).then_inc(wl.sem, 16)
            g.dma_start(bdsel[:, :], bdsel_d).then_inc(wl.sem, 16)
            g.dma_start(selbc[:, :], selbc_d).then_inc(wl.sem, 16)
            g.dma_start(mk_d[:, :], mask_di).then_inc(wl.sem, 16)
            g.dma_start(mk_a[:, :], mask_ai).then_inc(wl.sem, 16)
            wl.val += (NC + 4 + 6) * 16
            g.memset(bass.AP(vr, 64, [[2 * 4 * 520, 128], [520, 8], [65, 8], [1, 1]]), 1.0)
            g.wait_ge(wl.sem, wl.val)

        @blk.vector
        def _(v):
            v.memset(dn_s[:, :], 0.0)
            v.memset(invr[:, :], 0.0)

    for tb in range(NQB):
        kts = list(range(max(0, 4 * tb - 4), 4 * tb + 4))
        nkt = len(kts)

        def span(kt):
            if kt < 4 * tb:
                j = kt - 4 * (tb - 1)
                return 0, 128 * (j + 1), ("anti", 128 * j)
            j = kt - 4 * tb
            return 128 * j, 512, ("diag", 128 * j)

        order_S, order_PV = [], []
        for hp_ in range(4):
            h0_, h1_ = 2 * hp_, 2 * hp_ + 1
            for ki_ in range(nkt):
                order_S += [(h0_, ki_), (h1_, ki_)]
                if ki_ >= 1:
                    order_PV += [(h0_, ki_ - 1), (h1_, ki_ - 1)]
            order_PV += [(h0_, nkt - 1), (h1_, nkt - 1)]
        sidx = {u: i + 1 for i, u in enumerate(order_S)}
        pidx = {u: i + 1 for i, u in enumerate(order_PV)}
        em_order = []
        for hp_ in range(4):
            for ki_ in range(nkt):
                lo_, hi_, _mm = span(kts[ki_])
                if lo_ > 0 or hi_ < 512:
                    em_order.append((2 * hp_, ki_))
                    em_order.append((2 * hp_ + 1, ki_))
        mpos = {u: i + 1 for i, u in enumerate(em_order)}
        msidx = {}
        for u in order_S:
            msidx[u] = mpos.get(u, 0)

        lxq0, lxf0, ltb0 = lxq.val, lxf.val, ltb.val
        tQ0, vQ0, tV0, vV0, sSq0 = tQ.val, vQ.val, tV.val, vV.val, sSq.val
        tSS0, vSS0, sIv0, tBc0, tRo0, vT30, vQf0 = (tSS.val, vSS.val, sIv.val, tBc.val,
                                                    tRo.val, vT3.val, vQf.val)
        tS0, vM0, sE0, tP0 = tS.val, vM.val, sE.val, tP.val
        vR0, tB0, vZ0, tO0, vO0, stg0 = (vR.val, tB.val, vZ.val, tO.val, vO.val,
                                         stg.val)
        vBC0 = vBC.val
        slot = tb % 2

        with nc.Block() as blk:
            @blk.sync
            def _(sy):
                sy.dma_start(xq_in[:, 0:4 * 512].rearrange("p (c j) -> p c j", c=4),
                             xb_in[0:512, 1 + tb * 512:1 + (tb + 1) * 512]
                             .rearrange("(c p) j -> p c j", p=128)).then_inc(lxq.sem, 16)
                sy.dma_start(xq_in[:, 4 * 512:NC * 512].rearrange("p (c j) -> p c j", c=4),
                             xb_in[512:1024, 0 + tb * 512:0 + (tb + 1) * 512]
                             .rearrange("(c p) j -> p c j", p=128)).then_inc(lxq.sem, 16)
                for i, tab in enumerate([tab_cs, tab_sn, tab_ci, tab_si]):
                    sy.dma_start(tabs[:, i * 512:(i + 1) * 512],
                                 tab[:, tb * 512:(tb + 1) * 512]).then_inc(ltb.sem, 16)
                sy.dma_start(xf_cur[:, :].rearrange("p (c j) -> p c j", c=NC),
                             xf_in[:, 1 + tb * 512:1 + (tb + 1) * 512]
                             .rearrange("(c p) j -> p c j", p=128)).then_inc(lxf.sem, 16)
                lxq.val += 32
                lxf.val += 16
                ltb.val += 64

            @blk.gpsimd
            def _(g):
                for mc in range(NC):
                    g.wait_ge(vO.sem, vO0 + mc + 1)
                    g.dma_start(xf_out[mc * 128:(mc + 1) * 128,
                                       1 + tb * 512:1 + (tb + 1) * 512],
                                x2_st[:, mc * 512:(mc + 1) * 512]).then_inc(stg.sem, 16)
                    g.dma_start(xb_out[mc * 128:(mc + 1) * 128,
                                       1 + tb * 512:1 + (tb + 1) * 512],
                                x2_st[:, mc * 512:(mc + 1) * 512]).then_inc(stg.sem, 16)
                stg.val += NC * 32
                if tb == NQB - 1:
                    g.wait_ge(stg.sem, stg.val)

            @blk.tensor
            def _(t):
                t.wait_ge(lxq.sem, lxq0 + 32)
                for mc in range(NC):
                    if mc >= 2:
                        t.wait_ge(vQ.sem, vQ0 + mc - 1)
                    for kc in range(NC):
                        inst = t.matmul(pb[mc % 2][:, :],
                                        wqkv_bf[:, kc * 1536 + mc * 128:kc * 1536 + mc * 128 + 128],
                                        xq_in[:, kc * 512:(kc + 1) * 512],
                                        start=(kc == 0), stop=(kc == NC - 1))
                    inst.then_inc(tQ.sem, 1)
                    tQ.val += 1
                for tt in range(4):
                    if tt >= 2:
                        t.wait_ge(vV.sem, vV0 + tt - 1)
                    for kc in range(NC):
                        inst = t.matmul(pb[2 + tt % 2][:, :],
                                        xq_in[:, kc * 512 + tt * 128:kc * 512 + tt * 128 + 128],
                                        wqkv_bf[:, kc * 1536 + 1024:kc * 1536 + 1536],
                                        start=(kc == 0), stop=(kc == NC - 1))
                    inst.then_inc(tV.sem, 1)
                    tV.val += 1
                for mc in range(NC if P2_STAGE >= 2 else 0):
                    t.wait_ge(vQ.sem, vQ0 + mc + 1)
                    if mc >= 2:
                        t.wait_ge(vT3.sem, vT30 + mc - 1)
                    t.matmul(pb[4 + mc % 2][:, :], r2_f[:, :],
                             qt_f[:, mc * 512:(mc + 1) * 512],
                             start=True, stop=True).then_inc(tRo.sem, 1)
                    tRo.val += 1
                for side in range(2 if P2_STAGE >= 2 else 0):
                    for c in range(4):
                        t.wait_ge(sSq.sem, sSq0 + side * 4 + c + 1)
                        inst = t.matmul(pb[6 + side][0:8, :], bdsel[:, c * 8:(c + 1) * 8],
                                        sqt[:, (side * 4 + c) * 512:(side * 4 + c + 1) * 512],
                                        start=(c == 0), stop=(c == 3))
                    inst.then_inc(tSS.sem, 1)
                    tSS.val += 1
                if P2_STAGE >= 2:
                    t.wait_ge(vSS.sem, vSS0 + 2)
                for idx in range(NC if P2_STAGE >= 2 else 0):
                    side, c = idx // 4, idx % 4
                    t.wait_ge(sIv.sem, sIv0 + side + 1)
                    if idx >= 2:
                        t.wait_ge(vQf.sem, vQf0 + idx - 1)
                    t.matmul(pb[6 + idx % 2][0:128, :],
                             selbc[0:65, c * 128:(c + 1) * 128],
                             invr[0:65, side * 512:side * 512 + 512],
                             start=True, stop=True).then_inc(tBc.sem, 1)
                    tBc.val += 1
                if P2_STAGE >= 3:
                    t.wait_ge(vQf.sem, vQf0 + NC)

                def emit_S(h, ki):
                    hc, hf = h // 2, (h % 2) * 64
                    kt = kts[ki]
                    GS = sidx[(h, ki)] - 1
                    lo, hi, (mtyp, moff) = span(kt)
                    kslot = (kt // 4) % 2
                    ktt = kt % 4
                    t.matmul(pb[GS % 3][:, lo:hi],
                             kr[hf:hf + 64, kslot * 2048 + hc * 512 + ktt * 128:
                                kslot * 2048 + hc * 512 + ktt * 128 + 128],
                             qfin[hf:hf + 64, hc * 512 + lo:hc * 512 + hi],
                             start=True, stop=False)
                    mten = mk_a if mtyp == "anti" else mk_d
                    t.matmul(pb[GS % 3][:, moff:moff + 128], ident_bf[:, :], mten[:, :],
                             start=False, stop=True).then_inc(tS.sem, 1)
                    tS.val += 1

                def emit_PV(h, ki):
                    kt = kts[ki]
                    GS = sidx[(h, ki)] - 1
                    t.matmul(pb[3 + h % 3][0:65, 0:512],
                             vr[:, ((kt // 4) % 2) * 2080 + (kt % 4) * 520 + h * 65:
                                ((kt // 4) % 2) * 2080 + (kt % 4) * 520 + h * 65 + 65],
                             pt[GS % 8][:, 0:512],
                             start=(ki == 0), stop=(ki == nkt - 1)).then_inc(tP.sem, 1)
                    tP.val += 1

                def emit_pair_bcast(p):
                    t.wait_ge(vR.sem, vR0 + p + 1)
                    if p >= 2:
                        t.wait_ge(vBC.sem, vBC0 + p - 1)
                    t.matmul(pb[6 + p % 2][0:128, 0:512], sel65_f[0:65, 0:128],
                             dn_s[0:65, 0:512], start=True, stop=True).then_inc(tB.sem, 1)
                    tB.val += 1

                for hp in range(NHI // 2 if P2_STAGE >= 3 else 0):
                    h0, h1 = 2 * hp, 2 * hp + 1
                    for ki in range(nkt):
                        GSb = sidx[(h1, ki)] - 1
                        if GSb >= 3:
                            t.wait_ge(sE.sem, sE0 + GSb - 2)
                        emit_S(h0, ki)
                        emit_S(h1, ki)
                        if hp >= 1 and ki == 0:
                            emit_pair_bcast(hp - 1)
                        if ki >= 1:
                            t.wait_ge(sE.sem, sE0 + sidx[(h1, ki - 1)])
                            nmz = max(msidx[(h0, ki - 1)], msidx[(h1, ki - 1)])
                            if nmz > 0:
                                t.wait_ge(vM.sem, vM0 + nmz)
                            if ki == 1 and h1 >= 3:
                                t.wait_ge(vZ.sem, vZ0 + h1 - 2)
                            emit_PV(h0, ki - 1)
                            emit_PV(h1, ki - 1)
                    t.wait_ge(sE.sem, sE0 + sidx[(h1, nkt - 1)])
                    nmz = max(msidx[(h0, nkt - 1)], msidx[(h1, nkt - 1)])
                    if nmz > 0:
                        t.wait_ge(vM.sem, vM0 + nmz)
                    emit_PV(h0, nkt - 1)
                    emit_PV(h1, nkt - 1)
                if P2_STAGE >= 3:
                    emit_pair_bcast(NHI // 2 - 1)
                t.wait_ge(vZ.sem, vZ0 + NHI)
                for mc in range(NC if P2_STAGE >= 4 else 0):
                    if mc >= 2:
                        t.wait_ge(vO.sem, vO0 + mc - 1)
                    for kc in range(4):
                        inst = t.matmul(pb[mc % 2][:, :],
                                        wo_bf[:, kc * D + mc * 128:kc * D + mc * 128 + 128],
                                        zin[:, kc * 512:(kc + 1) * 512],
                                        start=(kc == 0), stop=(kc == 3))
                    inst.then_inc(tO.sem, 1)
                    tO.val += 1

            @blk.vector
            def _(v):
                v.wait_ge(ltb.sem, ltb0 + 64)
                for i in range(4):
                    v.tensor_single_scalar(ctabs[:, i * 512:(i + 1) * 512],
                                           tabs[:, i * 512:(i + 1) * 512],
                                           scol[:, i:i + 1], ALU.mult).then_inc(vch.sem, 1)
                    vch.val += 1
                v.wait_ge(vch.sem, vch.val)
                for mc in range(NC):
                    v.wait_ge(tQ.sem, tQ0 + mc + 1)
                    v.tensor_copy(qt_f[:, mc * 512:(mc + 1) * 512],
                                  pb[mc % 2][:, :]).then_inc(vQ.sem, 1)
                    vQ.val += 1
                    v.wait_ge(vQ.sem, vQ.val)
                    cs_off = (0 if mc < 4 else 2) * 512
                    v.tensor_tensor(t1b[:, mc * 512:(mc + 1) * 512],
                                    qt_f[:, mc * 512:(mc + 1) * 512],
                                    ctabs[:, cs_off:cs_off + 512], ALU.mult)
                for tt in range(4):
                    v.wait_ge(tV.sem, tV0 + tt + 1)
                    v.tensor_copy(
                        bass.AP(vr, slot * 2080 + tt * 520,
                                [[2 * 4 * 520, 128], [65, 8], [1, 64]]),
                        pb[2 + tt % 2][:, :].rearrange("p (g e) -> p g e", g=8)
                        ).then_inc(vV.sem, 1)
                    vV.val += 1
                for mc in range(NC if P2_STAGE >= 2 else 0):
                    sn_off = (1 if mc < 4 else 3) * 512
                    v.wait_ge(tRo.sem, tRo0 + mc + 1)
                    v.tensor_tensor(t3b[:, (mc % 2) * 512:(mc % 2) * 512 + 512],
                                    pb[4 + mc % 2][:, :],
                                    ctabs[:, sn_off:sn_off + 512], ALU.mult).then_inc(vT3.sem, 1)
                    vT3.val += 1
                    v.wait_ge(vT3.sem, vT3.val)
                    v.tensor_tensor(t1b[:, mc * 512:(mc + 1) * 512],
                                    t1b[:, mc * 512:(mc + 1) * 512],
                                    t3b[:, (mc % 2) * 512:(mc % 2) * 512 + 512],
                                    ALU.add).then_inc(vch.sem, 1)
                    vch.val += 1
                for side in range(2 if P2_STAGE >= 2 else 0):
                    v.wait_ge(tSS.sem, tSS0 + side + 1)
                    v.tensor_single_scalar(ssrow[0:8, side * 512:side * 512 + 512],
                                           pb[6 + side][0:8, :], 1e-24,
                                           ALU.max).then_inc(vSS.sem, 1)
                    vSS.val += 1
                if P2_STAGE >= 2:
                    v.wait_ge(vch.sem, vch.val)
                for mc in range(NC if P2_STAGE >= 2 else 0):
                    v.wait_ge(tBc.sem, tBc0 + mc + 1)
                    dest = qfin if mc < 4 else kr
                    doff = (mc % 4) * 512 if mc < 4 else slot * 2048 + (mc - 4) * 512
                    v.tensor_tensor(dest[:, doff:doff + 512],
                                    t1b[:, mc * 512:(mc + 1) * 512],
                                    pb[6 + mc % 2][:, :], ALU.mult).then_inc(vQf.sem, 1)
                    vQf.val += 1
                def msets_pair(hp_):
                    for ki_ in range(nkt):
                        kt_ = kts[ki_]
                        lo_, hi_, _mm = span(kt_)
                        if lo_ > 0 or hi_ < 512:
                            for h_ in (2 * hp_, 2 * hp_ + 1):
                                GS = sidx[(h_, ki_)] - 1
                                if GS >= 8:
                                    u6 = order_S[GS - 8]
                                    v.wait_ge(tP.sem, tP0 + pidx[u6])
                                if lo_ > 0 and hi_ < 512:
                                    v.memset(pt[GS % 8][:, 0:lo_], 0.0)
                                    v.memset(pt[GS % 8][:, hi_:512], 0.0).then_inc(vM.sem, 1)
                                elif lo_ > 0:
                                    v.memset(pt[GS % 8][:, 0:lo_], 0.0).then_inc(vM.sem, 1)
                                else:
                                    v.memset(pt[GS % 8][:, hi_:512], 0.0).then_inc(vM.sem, 1)
                                vM.val += 1

                if P2_STAGE >= 3:
                    msets_pair(0)
                for hp in range(NHI // 2 if P2_STAGE >= 3 else 0):
                    v.wait_ge(tB.sem, tB0 + hp + 1)
                    v.wait_ge(vZ.sem, vZ.val)
                    v.tensor_copy(bc_s[0:128, 0:512],
                                  pb[6 + hp % 2][0:128, 0:512]).then_inc(vBC.sem, 1)
                    vBC.val += 1
                    v.wait_ge(vBC.sem, vBC.val)
                    for pos, h in enumerate((2 * hp, 2 * hp + 1)):
                        hc, hf = h // 2, (h % 2) * 64
                        v.tensor_tensor(zin[hf:hf + 64, hc * 512:hc * 512 + 512],
                                        pb[3 + h % 3][0:64, 0:512],
                                        bc_s[pos * 64:pos * 64 + 64, 0:512],
                                        ALU.mult).then_inc(vZ.sem, 1)
                        vZ.val += 1
                    if 2 * hp + 2 < NHI:
                        msets_pair(hp + 1)
                v.wait_ge(lxf.sem, lxf0 + 16)
                v.wait_ge(stg.sem, stg0)
                for mc in range(NC):
                    if P2_STAGE >= 4:
                        v.wait_ge(tO.sem, tO0 + mc + 1)
                        v.tensor_tensor(x2_st[:, mc * 512:(mc + 1) * 512], pb[mc % 2][:, :],
                                        xf_cur[:, mc * 512:(mc + 1) * 512],
                                        ALU.add).then_inc(vO.sem, 1)
                    else:
                        v.wait_ge(vQ.sem, vQ0 + NC)
                        v.tensor_copy(x2_st[:, mc * 512:(mc + 1) * 512],
                                      xf_cur[:, mc * 512:(mc + 1) * 512]).then_inc(vO.sem, 1)
                    vO.val += 1

            @blk.scalar
            def _(s):
                for mc in range(NC if P2_STAGE >= 2 else 0):
                    s.wait_ge(vQ.sem, vQ0 + mc + 1)
                    s.activation(sqt[:, mc * 512:(mc + 1) * 512],
                                 qt_f[:, mc * 512:(mc + 1) * 512],
                                 AF.Square).then_inc(sSq.sem, 1)
                    sSq.val += 1
                for side in range(2 if P2_STAGE >= 2 else 0):
                    s.wait_ge(vSS.sem, vSS0 + side + 1)
                    s.activation(ssrow[0:8, side * 512:side * 512 + 512],
                                 ssrow[0:8, side * 512:side * 512 + 512],
                                 AF.Ln).then_inc(sch.sem, 1)
                    sch.val += 1
                    s.wait_ge(sch.sem, sch.val)
                    s.activation(invr[0:8, side * 512:side * 512 + 512],
                                 ssrow[0:8, side * 512:side * 512 + 512],
                                 AF.Exp, scale=-0.5).then_inc(sIv.sem, 1)
                    sIv.val += 1
                def den_pair_i(hp):
                    h0, h1 = 2 * hp, 2 * hp + 1
                    if hp >= 1:
                        s.wait_ge(tB.sem, tB0 + hp)
                    s.wait_ge(tP.sem, tP0 + pidx[(h1, nkt - 1)])
                    s.activation(dn_l[0:1, 0:512], pb[3 + h0 % 3][64:65, 0:512],
                                 AF.Ln).then_inc(sch.sem, 1)
                    sch.val += 1
                    s.activation(dn_l[64:65, 0:512], pb[3 + h1 % 3][64:65, 0:512],
                                 AF.Ln).then_inc(sch.sem, 1)
                    sch.val += 1
                    s.wait_ge(sch.sem, sch.val)
                    s.activation(dn_s[0:1, 0:512], dn_l[0:1, 0:512],
                                 AF.Exp, scale=-1.0)
                    s.activation(dn_s[64:65, 0:512], dn_l[64:65, 0:512],
                                 AF.Exp, scale=-1.0).then_inc(vR.sem, 1)
                    vR.val += 1

                for hp in range(NHI // 2 if P2_STAGE >= 3 else 0):
                    h0, h1 = 2 * hp, 2 * hp + 1
                    for ki in range(nkt):
                        GSb = sidx[(h1, ki)] - 1
                        kt = kts[ki]
                        lo, hi, _m = span(kt)
                        if GSb >= 8:
                            u6 = order_S[GSb - 8]
                            s.wait_ge(tP.sem, tP0 + pidx[u6])
                        s.wait_ge(tS.sem, tS0 + GSb + 1)
                        for h in (h0, h1):
                            GS = sidx[(h, ki)] - 1
                            s.activation(pt[GS % 8][:, lo:hi], pb[GS % 3][:, lo:hi],
                                         AF.Exp, scale=8.0).then_inc(sE.sem, 1)
                            sE.val += 1
                    den_pair_i(hp)
    ctx.close()


def _p3_inner_ffn(nc, en_sem, lid, xf_in, xb_in, xf_out, xb_out,
                  w1_d, w2_d, b1a_d, b1g_d, b2_d, lng_d, lnb_d, ones_bf):
    ctx = ExitStack()
    en = ctx.enter_context
    L = str(lid)
    NJ = 22
    JW = [128] * 21 + [42]

    w2_bf = en(nc.sbuf_tensor("p3_w2" + L, [128, NJ * D], BF16))
    h2q = en(nc.sbuf_tensor("p3_h2" + L, [128, NJ * 1024], BF16))
    w1buf = en(nc.sbuf_tensor("p3_w1" + L, [128, 4 * NC * 128], BF16))
    hn_q = en(nc.sbuf_tensor("p3_hn" + L, [128, NC * 1024], BF16))
    hf = en(nc.sbuf_tensor("p3_hf" + L, [128, NC * 512], F32))
    hb = en(nc.sbuf_tensor("p3_hb" + L, [128, NC * 512], BF16))
    sqb = en(nc.sbuf_tensor("p3_sq" + L, [128, NC * 512], BF16))
    s_mu = en(nc.sbuf_tensor("p3_mu" + L, [128, 512], F32))
    s_var = en(nc.sbuf_tensor("p3_var" + L, [128, 512], F32))
    s_rs = en(nc.sbuf_tensor("p3_rs" + L, [128, 512], F32))
    s_tmp = en(nc.sbuf_tensor("p3_tmp" + L, [128, 512], F32))
    glu = en(nc.sbuf_tensor("p3_glu" + L, [128, 2 * 512], BF16))
    bcols = en(nc.sbuf_tensor("p3_bc" + L, [128, 2 * NJ + NC * 3], F32))
    xres = en(nc.sbuf_tensor("p3_xr" + L, [128, NC * 512], F32))
    x3_st = en(nc.sbuf_tensor("p3_x3" + L, [128, NC * 512], F32))

    pb = [en(nc.psum_tensor(f"p3_pb{i}" + L, [128, 512], F32)) for i in range(6)]

    wl = Ctr.get(en_sem, nc, "p3_wl")
    lhf = Ctr.get(en_sem, nc, "p3_lhf")
    lhb = Ctr.get(en_sem, nc, "p3_lhb")
    lw1 = [Ctr.get(en_sem, nc, f"p3_lw1{i}") for i in range(4)]
    lxr = Ctr.get(en_sem, nc, "p3_lxr")
    vSq = Ctr.get(en_sem, nc, "g5")
    tLn = Ctr.get(en_sem, nc, "g6")
    vMu = Ctr.get(en_sem, nc, "g7")
    sRs = Ctr.get(en_sem, nc, "g8")
    vHn = Ctr.get(en_sem, nc, "g9")
    tY = Ctr.get(en_sem, nc, "g10")
    sGl = Ctr.get(en_sem, nc, "g11")
    vH2 = Ctr.get(en_sem, nc, "g12")
    tW2 = Ctr.get(en_sem, nc, "g13")
    vX3 = Ctr.get(en_sem, nc, "g14")
    st3 = Ctr.get(en_sem, nc, "p3_st3")
    stb = Ctr.get(en_sem, nc, "p3_stb")
    vch = Ctr.get(en_sem, nc, "g17")

    with nc.Block() as blk:
        @blk.gpsimd
        def _(g):
            g.dma_start(w2_bf[:, 0:21 * D].rearrange("p (c j) -> p c j", c=21),
                        w2_d[0:21 * 128, :].rearrange("(c p) j -> p c j", p=128)
                        ).then_inc(wl.sem, 16)
            g.dma_start(w2_bf[0:42, 21 * D:22 * D], w2_d[21 * 128:2730, :]).then_inc(wl.sem, 16)
            g.dma_start(bcols[:, 0:NJ], b1a_d).then_inc(wl.sem, 16)
            g.dma_start(bcols[:, NJ:2 * NJ], b1g_d).then_inc(wl.sem, 16)
            g.dma_start(bcols[:, 2 * NJ:2 * NJ + NC], b2_d).then_inc(wl.sem, 16)
            g.dma_start(bcols[:, 2 * NJ + NC:2 * NJ + 2 * NC], lng_d).then_inc(wl.sem, 16)
            g.dma_start(bcols[:, 2 * NJ + 2 * NC:2 * NJ + 3 * NC], lnb_d).then_inc(wl.sem, 16)
            wl.val += 7 * 16
            g.wait_ge(wl.sem, wl.val)

    for qt in range(4):
        lhf0, lhb0, lxr0 = lhf.val, lhb.val, lxr.val
        vSq0, tLn0, vMu0, sRs0, vHn0 = vSq.val, tLn.val, vMu.val, sRs.val, vHn.val
        tY0, sGl0, vH20, tW20 = tY.val, sGl.val, vH2.val, tW2.val
        vX30, st30, stb0 = vX3.val, st3.val, stb.val
        lw10 = [c.val for c in lw1]

        with nc.Block() as blk:
            @blk.gpsimd
            def _(g):
                for seq in range(2 * NJ):
                    j, which = seq // 2, seq % 2
                    half = 1 - which
                    jw = JW[j]
                    j0 = half * 2730 + j * 128
                    if seq >= 4:
                        g.wait_ge(tY.sem, tY0 + (seq - 3) * 2)
                    g.dma_start(
                        w1buf[:, (seq % 4) * NC * 128:(seq % 4) * NC * 128 + NC * jw]
                        .rearrange("p (c j) -> p c j", c=NC),
                        w1_d[:, j0:j0 + jw].rearrange("(c p) j -> p c j", p=128)
                        ).then_inc(lw1[seq % 4].sem, 16)
                    lw1[seq % 4].val += 16
                for sub in range(2):
                    t0 = qt * 1024 + sub * 512
                    for mc in range(NC):
                        g.wait_ge(vX3.sem, vX30 + sub * NC + mc + 1)
                        g.dma_start(xb_out[mc * 128:(mc + 1) * 128, 1 + t0:1 + t0 + 512],
                                    x3_st[:, mc * 512:(mc + 1) * 512]).then_inc(stb.sem, 16)
                    stb.val += NC * 16
                if qt == 3:
                    g.wait_ge(stb.sem, stb.val)

            @blk.sync
            def _(sy):
                for sub in range(2):
                    t0 = qt * 1024 + sub * 512
                    if sub == 1:
                        sy.wait_ge(tLn.sem, tLn0 + 2)
                    sy.dma_start(hb[:, 0:4 * 512].rearrange("p (c j) -> p c j", c=4),
                                 xb_in[0:512, 1 + t0:1 + t0 + 512]
                                 .rearrange("(c p) j -> p c j", p=128)).then_inc(lhb.sem, 16)
                    sy.dma_start(hb[:, 4 * 512:NC * 512].rearrange("p (c j) -> p c j", c=4),
                                 xb_in[512:1024, 0 + t0:0 + t0 + 512]
                                 .rearrange("(c p) j -> p c j", p=128)).then_inc(lhb.sem, 16)
                    if sub == 1:
                        sy.wait_ge(vHn.sem, vHn0 + NC)
                    sy.dma_start(hf[:, 0:4 * 512].rearrange("p (c j) -> p c j", c=4),
                                 xf_in[0:512, 1 + t0:1 + t0 + 512]
                                 .rearrange("(c p) j -> p c j", p=128)).then_inc(lhf.sem, 16)
                    sy.dma_start(hf[:, 4 * 512:NC * 512].rearrange("p (c j) -> p c j", c=4),
                                 xf_in[512:1024, 0 + t0:0 + t0 + 512]
                                 .rearrange("(c p) j -> p c j", p=128)).then_inc(lhf.sem, 16)
                    lhf.val += 32
                    lhb.val += 32
                sy.wait_ge(vHn.sem, vHn0 + 2 * NC)
                for sub in range(2):
                    t0 = qt * 1024 + sub * 512
                    if sub == 1:
                        sy.wait_ge(vX3.sem, vX30 + NC)
                    sy.dma_start(xres[:, :].rearrange("p (c j) -> p c j", c=NC),
                                 xf_in[:, 1 + t0:1 + t0 + 512]
                                 .rearrange("(c p) j -> p c j", p=128)).then_inc(lxr.sem, 16)
                    lxr.val += 16
                    for mc in range(NC):
                        sy.wait_ge(vX3.sem, vX30 + sub * NC + mc + 1)
                        sy.dma_start(xf_out[mc * 128:(mc + 1) * 128, 1 + t0:1 + t0 + 512],
                                     x3_st[:, mc * 512:(mc + 1) * 512]).then_inc(st3.sem, 16)
                    st3.val += NC * 16
                if qt == 3:
                    sy.wait_ge(st3.sem, st3.val)

            @blk.vector
            def _(v):
                v.wait_ge(stb.sem, stb0)
                v.wait_ge(st3.sem, st30)
                for sub in range(2):
                    v.wait_ge(tLn.sem, tLn0 + (sub + 1) * 2)
                    v.tensor_single_scalar(s_mu[:, :], pb[4][:, :], 1.0 / D,
                                           ALU.mult).then_inc(vch.sem, 1)
                    vch.val += 1
                    v.wait_ge(vch.sem, vch.val)
                    v.tensor_tensor(s_var[:, :], s_mu[:, :], s_mu[:, :], ALU.mult)
                    v.tensor_scalar(s_tmp[:, :], pb[5][:, :], 1.0 / D, EPS, ALU.mult,
                                    ALU.add).then_inc(vch.sem, 1)
                    vch.val += 1
                    v.wait_ge(vch.sem, vch.val)
                    v.tensor_tensor(s_var[:, :], s_tmp[:, :], s_var[:, :],
                                    ALU.subtract).then_inc(vMu.sem, 1)
                    vMu.val += 1
                    v.wait_ge(sRs.sem, sRs0 + 2 * (sub + 1))
                    v.wait_ge(lhf.sem, lhf0 + (sub + 1) * 32)
                    for c in range(NC):
                        sl = slice(c * 512, (c + 1) * 512)
                        dst = slice(c * 1024 + sub * 512, c * 1024 + sub * 512 + 512)
                        v.tensor_tensor(x3_st[:, sl], hf[:, sl], s_mu[:, :],
                                        ALU.subtract).then_inc(vch.sem, 1)
                        vch.val += 1
                        v.wait_ge(vch.sem, vch.val)
                        v.tensor_tensor(x3_st[:, sl], x3_st[:, sl], s_rs[:, :],
                                        ALU.mult).then_inc(vch.sem, 1)
                        vch.val += 1
                        v.wait_ge(vch.sem, vch.val)
                        v.tensor_scalar(hn_q[:, dst], x3_st[:, sl],
                                        bcols[:, 2 * NJ + NC + c:2 * NJ + NC + c + 1],
                                        bcols[:, 2 * NJ + 2 * NC + c:2 * NJ + 2 * NC + c + 1],
                                        ALU.mult, ALU.add).then_inc(vHn.sem, 1)
                        vHn.val += 1
                for j in range(NJ):
                    for nb in range(2):
                        gi = j * 2 + nb
                        v.wait_ge(sGl.sem, sGl0 + gi + 1)
                        v.wait_ge(tY.sem, tY0 + j * 4 + 3 + nb)
                        v.scalar_tensor_tensor(
                            h2q[:, j * 1024 + nb * 512:j * 1024 + nb * 512 + 512],
                            pb[2 + gi % 2][:, :], bcols[:, j:j + 1],
                            glu[:, (gi % 2) * 512:(gi % 2) * 512 + 512],
                            ALU.add, ALU.mult).then_inc(vH2.sem, 1)
                        vH2.val += 1
                for sub in range(2):
                    if sub == 1:
                        v.wait_ge(stb.sem, stb0 + NC * 16)
                        v.wait_ge(st3.sem, st30 + NC * 16)
                    v.wait_ge(lxr.sem, lxr0 + (sub + 1) * 16)
                    for mc in range(NC):
                        v.wait_ge(tW2.sem, tW20 + sub * NC + mc + 1)
                        v.scalar_tensor_tensor(x3_st[:, mc * 512:(mc + 1) * 512],
                                               pb[4 + mc % 2][:, :],
                                               bcols[:, 2 * NJ + mc:2 * NJ + mc + 1],
                                               xres[:, mc * 512:(mc + 1) * 512],
                                               ALU.add, ALU.add).then_inc(vX3.sem, 1)
                        vX3.val += 1

            @blk.scalar
            def _(s):
                for sub in range(2):
                    s.wait_ge(lhb.sem, lhb0 + (sub + 1) * 32)
                    for c in range(NC):
                        s.activation(sqb[:, c * 512:(c + 1) * 512],
                                     hb[:, c * 512:(c + 1) * 512],
                                     AF.Square).then_inc(vSq.sem, 1)
                        vSq.val += 1
                    s.wait_ge(vMu.sem, vMu0 + sub + 1)
                    s.activation(s_rs[:, :], s_var[:, :], AF.Ln).then_inc(sRs.sem, 1)
                    sRs.val += 1
                    s.wait_ge(sRs.sem, sRs.val)
                    s.activation(s_rs[:, :], s_rs[:, :], AF.Exp, scale=-0.5).then_inc(sRs.sem, 1)
                    sRs.val += 1
                for j in range(NJ):
                    for nb in range(2):
                        gi = j * 2 + nb
                        s.wait_ge(tY.sem, tY0 + j * 4 + nb + 1)
                        if gi >= 2:
                            s.wait_ge(vH2.sem, vH20 + gi - 1)
                        s.activation(glu[:, (gi % 2) * 512:(gi % 2) * 512 + 512],
                                     pb[gi % 2][:, :], AF.Gelu,
                                     bias=bcols[:, NJ + j:NJ + j + 1]).then_inc(sGl.sem, 1)
                        sGl.val += 1

            @blk.tensor
            def _(t):
                for sub in range(2):
                    t.wait_ge(lhb.sem, lhb0 + (sub + 1) * 32)
                    for c in range(NC):
                        t.matmul(pb[4][:, :], ones_bf[:, :], hb[:, c * 512:(c + 1) * 512],
                                 start=(c == 0), stop=(c == NC - 1))
                    t.wait_ge(vSq.sem, vSq0 + (sub + 1) * NC)
                    for c in range(NC):
                        inst = t.matmul(pb[5][:, :], ones_bf[:, :], sqb[:, c * 512:(c + 1) * 512],
                                        start=(c == 0), stop=(c == NC - 1))
                    inst.then_inc(tLn.sem, 2)
                    tLn.val += 2
                    t.wait_ge(vMu.sem, vMu0 + sub + 1)
                for j in range(NJ):
                    jw = JW[j]
                    for which, bank in ((0, 0), (1, 2)):
                        seq = j * 2 + which
                        t.wait_ge(lw1[seq % 4].sem, lw10[seq % 4] + (seq // 4 + 1) * 16)
                        for nb in range(2):
                            gi = j * 2 + nb
                            t.wait_ge(vHn.sem, vHn0 + (nb + 1) * NC)
                            if gi >= 2:
                                if bank == 0:
                                    t.wait_ge(sGl.sem, sGl0 + gi - 1)
                                else:
                                    t.wait_ge(vH2.sem, vH20 + gi - 1)
                            for kc in range(NC):
                                inst = t.matmul(
                                    pb[bank + gi % 2][0:jw, :] if jw < 128 else pb[bank + gi % 2][:, :],
                                    w1buf[:, (seq % 4) * NC * 128 + kc * jw:
                                          (seq % 4) * NC * 128 + kc * jw + jw],
                                    hn_q[:, kc * 1024 + nb * 512:kc * 1024 + nb * 512 + 512],
                                    start=(kc == 0), stop=(kc == NC - 1))
                            inst.then_inc(tY.sem, 1)
                            tY.val += 1
                t.wait_ge(vH2.sem, vH20 + NJ * 2)
                for sub in range(2):
                    for mc in range(NC):
                        if sub * NC + mc >= 2:
                            t.wait_ge(vX3.sem, vX30 + sub * NC + mc - 1)
                        for j in range(NJ):
                            jw = JW[j]
                            inst = t.matmul(pb[4 + mc % 2][:, :],
                                            w2_bf[0:jw, j * D + mc * 128:j * D + mc * 128 + 128],
                                            h2q[0:jw, j * 1024 + sub * 512:j * 1024 + sub * 512 + 512],
                                            start=(j == 0), stop=(j == NJ - 1))
                        inst.then_inc(tW2.sem, 1)
                        tW2.val += 1
    ctx.close()


def build_program():
    Ctr.reset_registry()
    nc = bass.Bass("TRN2", target_bir_lowering=False)

    resid = nc.dram_tensor("resid", [T, D], F32, kind="ExternalInput").ap()
    ident = nc.dram_tensor("ident", [128, 128], F32, kind="ExternalInput").ap()
    out = nc.dram_tensor("out", [T, D], F32, kind="ExternalOutput").ap()

    xf0 = nc.dram_tensor("xf0", [D, W], F32, kind="Internal").ap()
    xb0 = nc.dram_tensor("xb0", [D, W], BF16, kind="Internal").ap()
    xf1 = nc.dram_tensor("xf1", [D, W], F32, kind="Internal").ap()
    xb1 = nc.dram_tensor("xb1", [D, W], BF16, kind="Internal").ap()
    qt_hbm = nc.dram_tensor("qt_hbm", [D, T], BF16, kind="Internal").ap()

    dbg_zt = nc.dram_tensor("dbg_zt", [128, NC * 512], BF16, kind="ExternalOutput").ap() \
        if DEBUG_ZT else None
    wq = nc.dram_tensor("wq", [D, D], F32, kind="ExternalInput").ap()
    wk = nc.dram_tensor("wk", [D, 512], F32, kind="ExternalInput").ap()
    wv = nc.dram_tensor("wv", [D, 512], F32, kind="ExternalInput").ap()
    wo = nc.dram_tensor("wo", [D, D], F32, kind="ExternalInput").ap()
    bout_d = nc.dram_tensor("bout", [128, NC], F32, kind="ExternalInput").ap()
    mask_diag_o = nc.dram_tensor("mask_diag_o", [128, 128], BF16, kind="ExternalInput").ap()
    wqkv_l = [nc.dram_tensor(f"wqkv{l}", [D, 1536], F32, kind="ExternalInput").ap()
              for l in range(2)]
    wo_il = [nc.dram_tensor(f"wo_i{l}", [512, D], F32, kind="ExternalInput").ap()
             for l in range(2)]
    scol_l = [nc.dram_tensor(f"scol{l}", [128, 4], F32, kind="ExternalInput").ap()
              for l in range(2)]
    tab_cs = nc.dram_tensor("tab_cs", [128, T], F32, kind="ExternalInput").ap()
    tab_sn = nc.dram_tensor("tab_sn", [128, T], F32, kind="ExternalInput").ap()
    tab_ci = nc.dram_tensor("tab_ci", [128, T], F32, kind="ExternalInput").ap()
    tab_si = nc.dram_tensor("tab_si", [128, T], F32, kind="ExternalInput").ap()
    r2_d = nc.dram_tensor("r2", [128, 128], F32, kind="ExternalInput").ap()
    bdsel_d = nc.dram_tensor("bdsel", [128, 32], BF16, kind="ExternalInput").ap()
    selbc_d = nc.dram_tensor("selbc", [128, 512], F32, kind="ExternalInput").ap()
    mask_di = nc.dram_tensor("mask_di", [128, 128], BF16, kind="ExternalInput").ap()
    mask_ai = nc.dram_tensor("mask_ai", [128, 128], BF16, kind="ExternalInput").ap()
    xf2 = nc.dram_tensor("xf2", [D, W], F32, kind="Internal").ap()
    xb2 = nc.dram_tensor("xb2", [D, W], BF16, kind="Internal").ap()
    xf3 = nc.dram_tensor("xf3", [D, W], F32, kind="Internal").ap()
    xb3 = nc.dram_tensor("xb3", [D, W], BF16, kind="Internal").ap()
    w1_l = [nc.dram_tensor(f"w1_{l}", [D, 5460], F32, kind="ExternalInput").ap()
            for l in range(2)]
    w2_l = [nc.dram_tensor(f"w2_{l}", [2730, D], F32, kind="ExternalInput").ap()
            for l in range(2)]
    b1a_l = [nc.dram_tensor(f"b1a_{l}", [128, 22], F32, kind="ExternalInput").ap()
             for l in range(2)]
    b1g_l = [nc.dram_tensor(f"b1g_{l}", [128, 22], F32, kind="ExternalInput").ap()
             for l in range(2)]
    b2_l = [nc.dram_tensor(f"b2_{l}", [128, NC], F32, kind="ExternalInput").ap()
            for l in range(2)]
    lng_l = [nc.dram_tensor(f"lng_{l}", [128, NC], F32, kind="ExternalInput").ap()
             for l in range(2)]
    lnb_l = [nc.dram_tensor(f"lnb_{l}", [128, NC], F32, kind="ExternalInput").ap()
             for l in range(2)]

    with ExitStack() as ctx:
        en = ctx.enter_context
        ident_f32 = en(nc.sbuf_tensor("ident_f32", [128, 128], F32))
        ident_bf = en(nc.sbuf_tensor("ident_bf", [128, 128], BF16))
        ones_bf = en(nc.sbuf_tensor("ones_bf", [128, 128], BF16))
        ones64f = en(nc.sbuf_tensor("ones64f", [128, 64], F32))
        sel65_f = en(nc.sbuf_tensor("sel65_f", [128, 128], BF16))
        boot = Ctr(en, nc, "boot")
        boot2 = Ctr(en, nc, "boot2")
        with nc.Block() as blk:
            @blk.gpsimd
            def _(g):
                g.dma_start(ident_f32[:, :], ident).then_inc(boot.sem, 16)
                g.dma_start(ident_bf[:, :], ident).then_inc(boot.sem, 16)
                g.memset(ones_bf[:, :], 1.0)
                g.memset(ones64f[:, :], 1.0)
                g.wait_ge(boot.sem, 32)

            @blk.vector
            def _(v):
                v.memset(sel65_f[:, :], 0.0).then_inc(boot2.sem, 1)
                v.wait_ge(boot2.sem, 1)
                v.memset(sel65_f[0:1, 0:64], 1.0)
                v.memset(sel65_f[64:65, 64:128], 1.0)

        _p0_transpose_in(nc, en, resid, xf0, xb0, ident_f32,
                         [xf1, xf2, xf3], [xb1, xb2, xb3])
        if LEVEL >= 1:
            _p1_outer_attn(nc, en, xf0, xb0, xf1, xb1, qt_hbm, wq, wk, wv, wo, bout_d,
                           mask_diag_o, ident_f32, ident_bf, ones_bf, ones64f,
                           sel65_f, dbg_zt)
            if LEVEL >= 2:
                _p2_inner_attn(nc, en, 0, xf1, xb1, xf2, xb2,
                               wqkv_l[0], wo_il[0], scol_l[0],
                               tab_cs, tab_sn, tab_ci, tab_si, r2_d, bdsel_d, selbc_d,
                               mask_di, mask_ai, sel65_f, ident_bf)
                if LEVEL >= 3:
                    _p3_inner_ffn(nc, en, 0, xf2, xb2, xf3, xb3,
                                  w1_l[0], w2_l[0], b1a_l[0], b1g_l[0], b2_l[0],
                                  lng_l[0], lnb_l[0], ones_bf)
                    if LEVEL >= 4:
                        _p2_inner_attn(nc, en, 1, xf3, xb3, xf2, xb2,
                                       wqkv_l[1], wo_il[1], scol_l[1],
                                       tab_cs, tab_sn, tab_ci, tab_si, r2_d, bdsel_d,
                                       selbc_d, mask_di, mask_ai, sel65_f, ident_bf)
                        _p3_inner_ffn(nc, en, 1, xf2, xb2, xf3, xb3,
                                      w1_l[1], w2_l[1], b1a_l[1], b1g_l[1], b2_l[1],
                                      lng_l[1], lnb_l[1], ones_bf)
                        _p6_final(nc, en, xf1, xf3, out, ident_f32, ones_bf)
                    else:
                        _p6_final(nc, en, xf3, xf3, out, ident_f32, ones_bf)
                else:
                    _p6_final(nc, en, xf2, xf2, out, ident_f32, ones_bf)
            else:
                _p6_final(nc, en, xf1, xf1, out, ident_f32, ones_bf)
        else:
            _p6_final(nc, en, xf0, xf0, out, ident_f32, ones_bf)

    return nc


def _p0_transpose_in(nc, en_sem, resid, xf0, xb0, ident_f32, xf_more=(), xb_more=()):
    ctx = ExitStack()
    en = ctx.enter_context
    t_in = en(nc.sbuf_tensor("p0_in", [128, 4 * D], F32))
    t_st = en(nc.sbuf_tensor("p0_stage", [128, 4 * D], F32))
    t_z = en(nc.sbuf_tensor("p0_z", [128, NC], F32))
    t_zb = en(nc.sbuf_tensor("p0_zb", [128, NC], BF16))
    ps = [en(nc.psum_tensor(f"p0_ps{i}", [128, 512], F32)) for i in range(4)]

    ld4 = [Ctr.get(en_sem, nc, f"p0_ld{i}") for i in range(4)]
    tp = Ctr.get(en_sem, nc, "p0_tp")
    evv = Ctr.get(en_sem, nc, "p0_evv")
    stf4 = [Ctr.get(en_sem, nc, f"p0_stf{i}") for i in range(4)]
    stb4 = [Ctr.get(en_sem, nc, f"p0_stb{i}") for i in range(4)]
    stz = Ctr.get(en_sem, nc, "p0_stz")

    NT = T // 128

    mz = Ctr.get(en_sem, nc, "p0_mz")
    with nc.Block() as blk:
        @blk.gpsimd
        def _(g):
            g.memset(t_z[:, :], 0.0)
            g.memset(t_zb[:, :], 0.0).then_inc(mz.sem, 1)
            for tt in range(NT):
                if tt % 2 == 0:
                    if tt >= 4:
                        g.wait_ge(tp.sem, (tt - 3) * NC)
                    g.dma_start(t_in[:, (tt % 4) * D:(tt % 4) * D + D],
                                resid[tt * 128:(tt + 1) * 128, :]).then_inc(ld4[tt % 4].sem, 16)
                    ld4[tt % 4].val += 16
                if tt >= 1:
                    g.wait_ge(evv.sem, tt * NC)
                    g.dma_start(xb0[:, 1 + (tt - 1) * 128:1 + tt * 128]
                                .rearrange("(c p) j -> p c j", p=128),
                                t_st[:, ((tt - 1) % 4) * D:((tt - 1) % 4) * D + D]
                                .rearrange("p (c j) -> p c j", c=NC)
                                ).then_inc(stb4[(tt - 1) % 4].sem, 16)
                    stb4[(tt - 1) % 4].val += 16
            g.wait_ge(evv.sem, NT * NC)
            g.dma_start(xb0[:, 1 + (NT - 1) * 128:1 + NT * 128]
                        .rearrange("(c p) j -> p c j", p=128),
                        t_st[:, ((NT - 1) % 4) * D:((NT - 1) % 4) * D + D]
                        .rearrange("p (c j) -> p c j", c=NC)).then_inc(stb4[(NT - 1) % 4].sem, 16)
            stb4[(NT - 1) % 4].val += 16
            for i in range(4):
                g.wait_ge(stb4[i].sem, stb4[i].val)

        @blk.tensor
        def _(t):
            for tt in range(NT):
                t.wait_ge(ld4[tt % 4].sem, (tt // 4 + 1) * 16)
                for c in range(NC):
                    gi = tt * NC + c
                    if gi >= 4:
                        t.wait_ge(evv.sem, gi - 3)
                    t.transpose(ps[gi % 4][:, 0:128],
                                t_in[:, (tt % 4) * D + c * 128:(tt % 4) * D + c * 128 + 128],
                                ident_f32[:, :]).then_inc(tp.sem, 1)
            tp.val += NT * NC

        @blk.vector
        def _(v):
            for tt in range(NT):
                if tt >= 4:
                    v.wait_ge(stf4[tt % 4].sem, ((tt - 4) // 4 + 1) * 16)
                    v.wait_ge(stb4[tt % 4].sem, ((tt - 4) // 4 + 1) * 16)
                for c in range(NC):
                    gi = tt * NC + c
                    v.wait_ge(tp.sem, gi + 1)
                    v.tensor_copy(t_st[:, (tt % 4) * D + c * 128:(tt % 4) * D + c * 128 + 128],
                                  ps[gi % 4][:, 0:128]).then_inc(evv.sem, 1)
            evv.val += NT * NC

        @blk.sync
        def _(sy):
            def load_odd(tt):
                if tt >= 4:
                    sy.wait_ge(tp.sem, (tt - 3) * NC)
                sy.dma_start(t_in[:, (tt % 4) * D:(tt % 4) * D + D],
                             resid[tt * 128:(tt + 1) * 128, :]).then_inc(ld4[tt % 4].sem, 16)
                ld4[tt % 4].val += 16

            load_odd(1)
            load_odd(3)
            sy.wait_ge(mz.sem, 1)
            for xf_t in [xf0] + list(xf_more):
                for c in range(NC):
                    sy.dma_start(xf_t[c * 128:(c + 1) * 128, 0:1], t_z[:, c:c + 1],
                                 allow_slow_non_contiguous=True).then_inc(stz.sem, 16)
                    stz.val += 16
            for xb_t in [xb0] + list(xb_more):
                for c in range(NC):
                    sy.dma_start(xb_t[c * 128:(c + 1) * 128, 0:1], t_zb[:, c:c + 1],
                                 allow_slow_non_contiguous=True).then_inc(stz.sem, 16)
                    stz.val += 16
            for tt in range(NT):
                nxt = tt + 4
                if nxt % 2 == 1 and nxt < NT:
                    load_odd(nxt)
                sy.wait_ge(evv.sem, (tt + 1) * NC)
                sy.dma_start(
                    xf0[:, 1 + tt * 128:1 + (tt + 1) * 128].rearrange("(c p) j -> p c j", p=128),
                    t_st[:, (tt % 4) * D:(tt % 4) * D + D].rearrange("p (c j) -> p c j", c=NC)
                    ).then_inc(stf4[tt % 4].sem, 16)
                stf4[tt % 4].val += 16
            sy.wait_ge(stz.sem, stz.val)
            for i in range(4):
                sy.wait_ge(stf4[i].sem, stf4[i].val)
    ctx.close()


def _p6_final(nc, en_sem, xfa, xfb, out, ident_f32, ones_bf):
    ctx = ExitStack()
    en = ctx.enter_context
    s_a = en(nc.sbuf_tensor("p6_a", [128, 2 * NC * QB], F32))
    s_b = en(nc.sbuf_tensor("p6_b", [128, 2 * NC * QB], F32))
    s_r = en(nc.sbuf_tensor("p6_r", [128, 2 * NC * QB], F32))
    s_rb = en(nc.sbuf_tensor("p6_rb", [128, 2 * NC * QB], BF16))
    s_sq = en(nc.sbuf_tensor("p6_sq", [128, 2 * NC * QB], BF16))
    s_mu = en(nc.sbuf_tensor("p6_mu", [128, 2 * QB], F32))
    s_var = en(nc.sbuf_tensor("p6_var", [128, 2 * QB], F32))
    s_rs = en(nc.sbuf_tensor("p6_rs", [128, 2 * QB], F32))
    s_ot = en(nc.sbuf_tensor("p6_ot", [128, 4 * D], F32))
    ps_s = [en(nc.psum_tensor(f"p6_pss{i}", [128, 512], F32)) for i in range(2)]
    ps_q = [en(nc.psum_tensor(f"p6_psq{i}", [128, 512], F32)) for i in range(2)]
    ps_t = [en(nc.psum_tensor(f"p6_pst{i}", [128, 512], F32)) for i in range(4)]

    ld2 = [Ctr.get(en_sem, nc, f"p6_ld{i}") for i in range(2)]
    vrr = Ctr.get(en_sem, nc, "g1")
    ssq = Ctr.get(en_sem, nc, "g2")
    tst = Ctr.get(en_sem, nc, "g3")
    vvs = Ctr.get(en_sem, nc, "g4")
    srs = Ctr.get(en_sem, nc, "g5")
    vxn = Ctr.get(en_sem, nc, "g6")
    ttp = Ctr.get(en_sem, nc, "g7")
    vev = Ctr.get(en_sem, nc, "g8")
    st = Ctr.get(en_sem, nc, "p6_st")
    vch = Ctr.get(en_sem, nc, "g10")
    sch = Ctr.get(en_sem, nc, "g11")
    b_ld2 = [c.val for c in ld2]
    b_vrr, b_ssq, b_tst = vrr.val, ssq.val, tst.val
    b_vvs, b_srs, b_vxn, b_ttp, b_vev, b_st = (vvs.val, srs.val, vxn.val, ttp.val,
                                               vev.val, st.val)

    with nc.Block() as blk:
        @blk.sync
        def _(sy):
            for qb in range(NQB + 1):
                if qb < NQB:
                    if qb >= 2:
                        sy.wait_ge(vrr.sem, b_vrr + (qb - 1) * NC)
                    xs = (qb % 2) * NC * QB
                    sy.dma_start(s_a[:, xs:xs + NC * QB].rearrange("p (c j) -> p c j", c=NC),
                                 xfa[:, 1 + qb * QB:1 + (qb + 1) * QB].rearrange(
                                     "(c p) j -> p c j", p=128)).then_inc(ld2[qb % 2].sem, 16)
                    sy.dma_start(s_b[:, xs:xs + NC * QB].rearrange("p (c j) -> p c j", c=NC),
                                 xfb[:, 1 + qb * QB:1 + (qb + 1) * QB].rearrange(
                                     "(c p) j -> p c j", p=128)).then_inc(ld2[qb % 2].sem, 16)
                    ld2[qb % 2].val += 32
                if qb >= 1:
                    q0 = qb - 1
                    for half in range(2):
                        sy.wait_ge(vev.sem, b_vev + q0 * 4 * NC + (half + 1) * 2 * NC)
                        sy.dma_start(out[q0 * QB + half * 256:q0 * QB + half * 256 + 256, :]
                                     .rearrange("(i p) j -> p i j", p=128),
                                     s_ot[:, half * 2 * D:(half * 2 + 2) * D]
                                     .rearrange("p (i j) -> p i j", i=2)
                                     ).then_inc(st.sem, 16)
                        st.val += 16
            sy.wait_ge(st.sem, st.val)

        @blk.vector
        def _(v):
            for qb in range(NQB):
                xs = (qb % 2) * NC * QB
                ms = (qb % 2) * QB
                v.wait_ge(ld2[qb % 2].sem, b_ld2[qb % 2] + 32 * (qb // 2 + 1))
                if qb >= 2:
                    v.wait_ge(ttp.sem, b_ttp + (qb - 1) * 4 * NC)
                    v.wait_ge(tst.sem, b_tst + 2 * (qb - 1))
                    v.wait_ge(srs.sem, b_srs + qb - 1)
                for c in range(NC):
                    sl = slice(xs + c * QB, xs + (c + 1) * QB)
                    v.tensor_tensor(s_r[:, sl], s_a[:, sl], s_b[:, sl],
                                    ALU.add).then_inc(vch.sem, 1)
                    vch.val += 1
                    v.wait_ge(vch.sem, vch.val)
                    v.tensor_copy(s_rb[:, sl], s_r[:, sl]).then_inc(vrr.sem, 1)
                    vrr.val += 1
                v.wait_ge(tst.sem, b_tst + 2 * (qb + 1))
                v.tensor_single_scalar(s_mu[:, ms:ms + QB], ps_s[qb % 2][:, :], 1.0 / D,
                                       ALU.mult).then_inc(vch.sem, 1)
                vch.val += 1
                v.wait_ge(vch.sem, vch.val)
                v.tensor_tensor(s_var[:, ms:ms + QB], s_mu[:, ms:ms + QB],
                                s_mu[:, ms:ms + QB], ALU.mult)
                v.tensor_scalar(s_rs[:, ms:ms + QB], ps_q[qb % 2][:, :], 1.0 / D, EPS,
                                ALU.mult, ALU.add).then_inc(vch.sem, 1)
                vch.val += 1
                v.wait_ge(vch.sem, vch.val)
                v.tensor_tensor(s_var[:, ms:ms + QB], s_rs[:, ms:ms + QB],
                                s_var[:, ms:ms + QB], ALU.subtract).then_inc(vvs.sem, 1)
                vvs.val += 1
                v.wait_ge(srs.sem, b_srs + qb + 1)
                for c in range(NC):
                    sl = slice(xs + c * QB, xs + (c + 1) * QB)
                    v.tensor_tensor(s_r[:, sl], s_r[:, sl], s_mu[:, ms:ms + QB],
                                    ALU.subtract).then_inc(vch.sem, 1)
                    vch.val += 1
                    v.wait_ge(vch.sem, vch.val)
                    v.tensor_tensor(s_r[:, sl], s_r[:, sl], s_rs[:, ms:ms + QB],
                                    ALU.mult).then_inc(vxn.sem, 1)
                    vxn.val += 1
                for ttile in range(4):
                    if qb >= 1 and ttile == 0:
                        v.wait_ge(st.sem, b_st + qb * 32)
                    for c in range(NC):
                        g = qb * 4 * NC + ttile * NC + c
                        v.wait_ge(ttp.sem, b_ttp + g + 1)
                        v.tensor_copy(s_ot[:, ttile * D + c * 128:ttile * D + c * 128 + 128],
                                      ps_t[g % 4][:, 0:128]).then_inc(vev.sem, 1)
                        vev.val += 1

        @blk.scalar
        def _(s):
            for qb in range(NQB):
                xs = (qb % 2) * NC * QB
                ms = (qb % 2) * QB
                if qb >= 2:
                    s.wait_ge(tst.sem, b_tst + 2 * (qb - 1))
                for c in range(NC):
                    sl = slice(xs + c * QB, xs + (c + 1) * QB)
                    s.wait_ge(vrr.sem, b_vrr + qb * NC + c + 1)
                    s.activation(s_sq[:, sl], s_rb[:, sl], AF.Square).then_inc(ssq.sem, 1)
                    ssq.val += 1
                s.wait_ge(vvs.sem, b_vvs + qb + 1)
                s.activation(s_rs[:, ms:ms + QB], s_var[:, ms:ms + QB],
                             AF.Ln).then_inc(sch.sem, 1)
                sch.val += 1
                s.wait_ge(sch.sem, sch.val)
                s.activation(s_rs[:, ms:ms + QB], s_rs[:, ms:ms + QB],
                             AF.Exp, scale=-0.5).then_inc(srs.sem, 1)
                srs.val += 1

        @blk.tensor
        def _(t):
            for qb in range(NQB):
                xs = (qb % 2) * NC * QB
                t.wait_ge(vrr.sem, b_vrr + (qb + 1) * NC)
                if qb >= 2:
                    t.wait_ge(vvs.sem, b_vvs + qb - 1)
                for c in range(NC):
                    t.matmul(ps_s[qb % 2][:, :], ones_bf[:, :],
                             s_rb[:, xs + c * QB:xs + (c + 1) * QB],
                             start=(c == 0), stop=(c == NC - 1))
                t.wait_ge(ssq.sem, b_ssq + (qb + 1) * NC)
                for c in range(NC):
                    inst = t.matmul(ps_q[qb % 2][:, :], ones_bf[:, :],
                                    s_sq[:, xs + c * QB:xs + (c + 1) * QB],
                                    start=(c == 0), stop=(c == NC - 1))
                inst.then_inc(tst.sem, 2)
                tst.val += 2
                for ttile in range(4):
                    for c in range(NC):
                        g = qb * 4 * NC + ttile * NC + c
                        if g >= 4:
                            t.wait_ge(vev.sem, b_vev + g - 3)
                        if ttile == 0 and c == 0:
                            t.wait_ge(vxn.sem, b_vxn + (qb + 1) * NC)
                        t.transpose(ps_t[g % 4][:, 0:128],
                                    s_r[:, xs + c * QB + ttile * 128:
                                        xs + c * QB + ttile * 128 + 128],
                                    ident_f32[:, :]).then_inc(ttp.sem, 1)
                        ttp.val += 1
    ctx.close()


_cache = {}


def kernel(**inputs):
    resid_full = np.ascontiguousarray(np.asarray(inputs["L0_attn_resid"], np.float32))
    B = resid_full.shape[0]
    ident = np.eye(128, dtype=np.float32)

    hperm = np.zeros(D, np.int64)
    for h in range(16):
        qch, qhf = (h // 4) * 2 + h % 2, (h // 2) % 2
        hperm[qch * 128 + qhf * 64: qch * 128 + qhf * 64 + 64] = np.arange(h * 64, h * 64 + 64)
    wq = np.ascontiguousarray(np.asarray(inputs["L0_attn_W_Q"], np.float32)
                              .reshape(D, D)[:, hperm])
    wk = np.ascontiguousarray(np.asarray(inputs["L0_attn_W_K"], np.float32).reshape(D, 512))
    wv = np.ascontiguousarray(np.asarray(inputs["L0_attn_W_V"], np.float32).reshape(D, 512))
    wo = np.ascontiguousarray(
        np.asarray(inputs["L0_attn_W_out"], np.float32).transpose(1, 0, 2).reshape(D, D)[hperm, :])
    bout = np.ascontiguousarray(
        np.asarray(inputs["L0_attn_b_out"], np.float32).reshape(NC, 128).T)
    ii = np.arange(128)
    mask_do = np.where(ii[:, None] <= ii[None, :], 0.0, -1e6).astype(ml_dtypes.bfloat16)
    mask_di = np.where(ii[:, None] <= ii[None, :], 0.0, -1e36).astype(ml_dtypes.bfloat16)
    mask_ai = np.where(ii[:, None] >= ii[None, :], 0.0, -1e36).astype(ml_dtypes.bfloat16)

    half = np.arange(0, 64, 2, dtype=np.float64)
    inv_freq = 1.0 / 10000.0 ** (half / 64.0)
    tt_ = np.arange(T, dtype=np.float64)
    freqs = np.concatenate([tt_[:, None] * inv_freq, tt_[:, None] * inv_freq], axis=1)
    base = np.concatenate([(half + 25.6) / 89.6] * 2)
    power = (tt_ - T // 2) / 256.0
    scale = base[None, :] ** power[:, None]
    cosf, sinf = np.cos(freqs), np.sin(freqs)
    t2 = lambda a: np.ascontiguousarray(np.tile(a.T, (2, 1)).astype(np.float32))
    tab_cs = t2(cosf * scale)
    tab_sn = t2(sinf * scale)
    tab_ci = t2(cosf / scale)
    tab_si = t2(sinf / scale)

    def scol_for(l):
        qs = np.asarray(inputs["L0_ffn_q_scale"], np.float32)[l]
        ks = np.asarray(inputs["L0_ffn_k_scale"], np.float32)[l]
        rot = lambda a: np.concatenate([a[32:], a[:32]])
        return np.ascontiguousarray(np.stack(
            [np.tile(qs, 2), np.tile(rot(qs), 2), np.tile(ks, 2), np.tile(rot(ks), 2)],
            axis=1))

    r2 = np.zeros((128, 128), np.float32)
    for b0 in (0, 64):
        for i in range(32):
            r2[b0 + 32 + i, b0 + i] = -1.0
            r2[b0 + i, b0 + 32 + i] = 1.0
    bdsel = np.zeros((128, 32), np.float32)
    for c in range(4):
        for p in range(128):
            bdsel[p, c * 8 + 2 * c + p // 64] = 1.0
    bdsel = bdsel.astype(ml_dtypes.bfloat16)
    selbc = np.zeros((128, 512), np.float32)
    for c in range(4):
        for p in range(128):
            selbc[2 * c + p // 64, c * 128 + p] = 1.0

    def colpad(v):
        out = np.zeros((128, 22), np.float32)
        for j in range(22):
            w = min(128, 2730 - j * 128)
            out[:w, j] = v[j * 128:j * 128 + w]
        return out

    nc = build_program()
    in_maps = []
    for b in range(B):
        in_maps.append(dict(
            resid=np.ascontiguousarray(resid_full[b]),
            ident=ident,
            wq=wq, wk=wk, wv=wv, wo=wo, bout=bout, mask_diag_o=mask_do,
            wqkv0=np.ascontiguousarray(np.asarray(inputs["L0_ffn_wqkv"], np.float32)[0]),
            wqkv1=np.ascontiguousarray(np.asarray(inputs["L0_ffn_wqkv"], np.float32)[1]),
            wo_i0=np.ascontiguousarray(np.asarray(inputs["L0_ffn_wo"], np.float32)[0]),
            wo_i1=np.ascontiguousarray(np.asarray(inputs["L0_ffn_wo"], np.float32)[1]),
            scol0=scol_for(0), scol1=scol_for(1),
            tab_cs=tab_cs, tab_sn=tab_sn, tab_ci=tab_ci, tab_si=tab_si,
            r2=r2, bdsel=bdsel, selbc=selbc, mask_di=mask_di, mask_ai=mask_ai,
            **{f"w1_{l}": np.ascontiguousarray(np.asarray(inputs["L0_ffn_w1"], np.float32)[l])
               for l in range(2)},
            **{f"w2_{l}": np.ascontiguousarray(np.asarray(inputs["L0_ffn_w2"], np.float32)[l])
               for l in range(2)},
            **{f"b1a_{l}": colpad(np.asarray(inputs["L0_ffn_b1"], np.float32)[l][:2730])
               for l in range(2)},
            **{f"b1g_{l}": colpad(np.asarray(inputs["L0_ffn_b1"], np.float32)[l][2730:])
               for l in range(2)},
            **{f"b2_{l}": np.ascontiguousarray(
                np.asarray(inputs["L0_ffn_b2"], np.float32)[l].reshape(NC, 128).T)
               for l in range(2)},
            **{f"lng_{l}": np.ascontiguousarray(
                np.asarray(inputs["L0_ffn_ln_g"], np.float32)[l].reshape(NC, 128).T)
               for l in range(2)},
            **{f"lnb_{l}": np.ascontiguousarray(
                np.asarray(inputs["L0_ffn_ln_b"], np.float32)[l].reshape(NC, 128).T)
               for l in range(2)},
        ))
    res = run_bass_kernel_spmd(nc, in_maps, core_ids=list(range(B)))
    if DEBUG_ZT:
        kernel._dbg_zt = np.stack([np.asarray(r["dbg_zt"], np.float32) for r in res.results])
    return np.stack([np.asarray(r["out"], np.float32) for r in res.results], axis=0)


# version 18
# speedup vs baseline: 1.1247x; 1.0311x over previous
import numpy as np
import ml_dtypes
from contextlib import ExitStack

import concourse.bass as bass
import concourse.mybir as mybir
from concourse.bass_utils import run_bass_kernel_spmd

F32 = mybir.dt.float32
BF16 = mybir.dt.bfloat16
AF = mybir.ActivationFunctionType
ALU = mybir.AluOpType

T = 4096
D = 1024
NC = 8
W = T + 1
QB = 512
NQB = T // QB
EPS = 1e-5

LEVEL = 4
P2_STAGE = 4
DEBUG_ZT = False
DBG_QB = 0


class Ctr:
    _registry = {}

    def __init__(self, en, nc, name):
        self.sem = en(nc.semaphore(name))
        self.val = 0

    @classmethod
    def get(cls, en, nc, name):
        if name not in cls._registry:
            cls._registry[name] = Ctr(en, nc, name)
        return cls._registry[name]

    @classmethod
    def reset_registry(cls):
        cls._registry = {}


def _ln_stats(nc, v, s, t, ones_bf, rb_slices, sq_slices, ps_s, ps_q, s_mu, s_var, s_rs,
              c_rdy, c_sq, c_st, c_vs, c_rs):
    n = len(rb_slices)
    for c in range(n):
        t.matmul(ps_s, ones_bf, rb_slices[c], start=(c == 0), stop=(c == n - 1))
    for c in range(n):
        inst = t.matmul(ps_q, ones_bf, sq_slices[c], start=(c == 0), stop=(c == n - 1))
    inst.then_inc(c_st.sem, 2)
    c_st.val += 2




def _p1_outer_attn(nc, en_sem, xf0, xb0, xf1, xb1, qt_hbm, wq, wk, wv, wo, bout_d,
                   mask_diag_o, ident_f32, ident_bf, ones_bf, ones64f, sel65_f,
                   dbg_zt=None):
    NH = 16
    ctx_p1 = ExitStack()
    ctx_a = ExitStack()
    enp = ctx_p1.enter_context
    ena = ctx_a.enter_context

    wo_bf = enp(nc.sbuf_tensor("p1_wo", [128, NC * D], BF16))
    bout_cols = enp(nc.sbuf_tensor("p1_bout", [128, NC], F32))
    kt_bf = enp(nc.sbuf_tensor("p1_kt", [128, 4 * T], BF16))
    v_bf = enp(nc.sbuf_tensor("p1_v", [128, 32 * 520], BF16))
    wq_bf = ena(nc.sbuf_tensor("p1_wq", [128, NC * D], BF16))
    wk_bf = ena(nc.sbuf_tensor("p1_wk", [128, NC * 512], BF16))
    wv_bf = ena(nc.sbuf_tensor("p1_wv", [128, NC * 512], BF16))
    xq = ena(nc.sbuf_tensor("p1_xq", [128, 2 * NC * 512], BF16))
    qt_stage = ena(nc.sbuf_tensor("p1_qst", [128, 4 * 512], BF16))

    ps_a = [ena(nc.psum_tensor(f"p1_psa{i}", [128, 512], F32)) for i in range(4)]
    ps_b = [ena(nc.psum_tensor(f"p1_psb{i}", [128, 512], F32)) for i in range(4)]

    wl = Ctr(en_sem, nc, "p1_wl")
    ld2 = [Ctr(en_sem, nc, f"p1_ld{i}") for i in range(2)]
    tq = Ctr(en_sem, nc, "p1_tq")
    tv = Ctr(en_sem, nc, "p1_tv")
    se = Ctr(en_sem, nc, "p1_se")
    vv = Ctr(en_sem, nc, "p1_vv")
    sq4 = [Ctr(en_sem, nc, f"p1_sq{i}") for i in range(4)]

    with nc.Block() as blk:
        @blk.gpsimd
        def _(g):
            for c in range(NC):
                g.dma_start(wq_bf[:, c * D:(c + 1) * D],
                            wq[c * 128:(c + 1) * 128, :]).then_inc(wl.sem, 16)
                g.dma_start(wk_bf[:, c * 512:(c + 1) * 512],
                            wk[c * 128:(c + 1) * 128, :]).then_inc(wl.sem, 16)
                g.dma_start(wv_bf[:, c * 512:(c + 1) * 512],
                            wv[c * 128:(c + 1) * 128, :]).then_inc(wl.sem, 16)
                g.dma_start(wo_bf[:, c * D:(c + 1) * D],
                            wo[c * 128:(c + 1) * 128, :]).then_inc(wl.sem, 16)
            g.dma_start(bout_cols[:, :], bout_d).then_inc(wl.sem, 16)
            wl.val += (4 * NC + 1) * 16
            g.memset(bass.AP(v_bf, 64, [[32 * 520, 128], [520, 32], [65, 8], [1, 1]]), 1.0)

        @blk.sync
        def _(sy):
            for nb in range(NQB):
                if nb >= 2:
                    sy.wait_ge(tq.sem, (nb - 1) * 12)
                    sy.wait_ge(tv.sem, (nb - 1) * 4)
                sy.dma_start(xq[:, (nb % 2) * NC * 512:(nb % 2) * NC * 512 + NC * 512]
                             .rearrange("p (c j) -> p c j", c=NC),
                             xb0[:, 1 + nb * 512:1 + (nb + 1) * 512]
                             .rearrange("(c p) j -> p c j", p=128)).then_inc(ld2[nb % 2].sem, 16)
                ld2[nb % 2].val += 16
                for mc in range(NC):
                    qidx = nb * NC + mc
                    sy.wait_ge(se.sem, nb * 12 + mc + 1)
                    sy.dma_start(qt_hbm[mc * 128:(mc + 1) * 128, nb * 512:(nb + 1) * 512],
                                 qt_stage[:, (qidx % 4) * 512:(qidx % 4) * 512 + 512]
                                 ).then_inc(sq4[qidx % 4].sem, 16)
                    sq4[qidx % 4].val += 16
            for i in range(4):
                sy.wait_ge(sq4[i].sem, sq4[i].val)

        @blk.tensor
        def _(t):
            t.wait_ge(wl.sem, wl.val)
            for nb in range(NQB):
                t.wait_ge(ld2[nb % 2].sem, (nb // 2 + 1) * 16)
                xs = (nb % 2) * NC * 512
                for mc in range(8):
                    gq = nb * 12 + mc
                    if gq >= 4:
                        t.wait_ge(se.sem, gq - 3)
                    for kc in range(NC):
                        inst = t.matmul(ps_a[gq % 4][:, :],
                                        wq_bf[:, kc * D + mc * 128:kc * D + mc * 128 + 128],
                                        xq[:, xs + kc * 512:xs + (kc + 1) * 512],
                                        start=(kc == 0), stop=(kc == NC - 1))
                    inst.then_inc(tq.sem, 1)
                    tq.val += 1
                for mc in range(4):
                    gq = nb * 12 + 8 + mc
                    if gq >= 4:
                        t.wait_ge(se.sem, gq - 3)
                    for kc in range(NC):
                        inst = t.matmul(ps_a[gq % 4][:, :],
                                        wk_bf[:, kc * 512 + mc * 128:kc * 512 + mc * 128 + 128],
                                        xq[:, xs + kc * 512:xs + (kc + 1) * 512],
                                        start=(kc == 0), stop=(kc == NC - 1))
                    inst.then_inc(tq.sem, 1)
                    tq.val += 1
                for tt in range(4):
                    gv = nb * 4 + tt
                    if gv >= 4:
                        t.wait_ge(vv.sem, gv - 3)
                    for kc in range(NC):
                        inst = t.matmul(ps_b[gv % 4][:, :],
                                        xq[:, xs + kc * 512 + tt * 128:xs + kc * 512 + tt * 128 + 128],
                                        wv_bf[:, kc * 512:(kc + 1) * 512],
                                        start=(kc == 0), stop=(kc == NC - 1))
                    inst.then_inc(tv.sem, 1)
                    tv.val += 1

        @blk.scalar
        def _(s):
            for nb in range(NQB):
                for i in range(12):
                    gq = nb * 12 + i
                    s.wait_ge(tq.sem, gq + 1)
                    if i < 8:
                        qidx = nb * NC + i
                        if qidx >= 4:
                            s.wait_ge(sq4[qidx % 4].sem, (qidx // 4) * 16)
                        s.copy(qt_stage[:, (qidx % 4) * 512:(qidx % 4) * 512 + 512],
                               ps_a[gq % 4][:, :]).then_inc(se.sem, 1)
                    else:
                        mc = i - 8
                        s.copy(kt_bf[:, mc * T + nb * 512:mc * T + (nb + 1) * 512],
                               ps_a[gq % 4][:, :]).then_inc(se.sem, 1)
                    se.val += 1

        @blk.vector
        def _(v):
            for nb in range(NQB):
                for tt in range(4):
                    gv = nb * 4 + tt
                    v.wait_ge(tv.sem, gv + 1)
                    base = (nb * 4 + tt) * 520
                    v.tensor_copy(
                        bass.AP(v_bf, base, [[32 * 520, 128], [65, 8], [1, 64]]),
                        ps_b[gv % 4][:, :].rearrange("p (g e) -> p g e", g=8)
                        ).then_inc(vv.sem, 1)
                    vv.val += 1

    ctx_a.close()
    en = enp
    qtb = en(nc.sbuf_tensor("p1_qtb", [128, 2 * NC * 512], BF16))
    x0f = en(nc.sbuf_tensor("p1_x0f", [128, NC * 512], F32))
    pt = [en(nc.sbuf_tensor(f"p1_pt{i}", [128, 512], BF16)) for i in range(8)]
    dn_s = en(nc.sbuf_tensor("p1_dn", [128, 512], BF16))
    dn_l = en(nc.sbuf_tensor("p1_dnl", [128, 512], F32))
    bc_s = en(nc.sbuf_tensor("p1_bc", [128, 512], F32))
    zt_bf = en(nc.sbuf_tensor("p1_zt", [128, 2 * NC * 512], BF16))
    mk = en(nc.sbuf_tensor("p1_mk", [128, 128], BF16))
    rb_bf = en(nc.sbuf_tensor("p1_rbb", [128, NC * 512], BF16))
    sq_bf = en(nc.sbuf_tensor("p1_sqb", [128, NC * 512], BF16))
    s_mu = en(nc.sbuf_tensor("p1_mu", [128, 512], F32))
    s_var = en(nc.sbuf_tensor("p1_var", [128, 512], F32))
    s_rs = en(nc.sbuf_tensor("p1_rs", [128, 512], F32))
    s_tmp = en(nc.sbuf_tensor("p1_tmp", [128, 512], F32))
    x1_st = en(nc.sbuf_tensor("p1_x1st", [128, 2 * NC * 512], F32))

    ps_s = [en(nc.psum_tensor(f"p1_pss{i}", [128, 512], F32)) for i in range(3)]
    ps_z = [en(nc.psum_tensor(f"p1_psz{i}", [128, 512], F32)) for i in range(2)]
    ps_bc = [en(nc.psum_tensor(f"p1_psbc{i}", [128, 512], F32)) for i in range(1)]
    ps_t6 = [en(nc.psum_tensor(f"p1_pst6{i}", [128, 512], F32)) for i in range(2)]

    lqq = Ctr.get(en_sem, nc, "p1b_lqq")
    lqq2 = Ctr.get(en_sem, nc, "p1b_lqq2")
    gpx = Ctr.get(en_sem, nc, "p1b_gpx")
    lqx = Ctr.get(en_sem, nc, "p1b_lqx")
    tS = Ctr.get(en_sem, nc, "g2")
    vM = Ctr.get(en_sem, nc, "g3")
    sE = Ctr.get(en_sem, nc, "g4")
    tP = Ctr.get(en_sem, nc, "g5")
    vR = Ctr.get(en_sem, nc, "g6")
    tB = Ctr.get(en_sem, nc, "g7")
    vZ = Ctr.get(en_sem, nc, "g8")
    tO = Ctr.get(en_sem, nc, "g9")
    vRb = Ctr.get(en_sem, nc, "g10")
    sRb = Ctr.get(en_sem, nc, "g11")
    vSq = Ctr.get(en_sem, nc, "g12")
    tT = Ctr.get(en_sem, nc, "g13")
    vVs = Ctr.get(en_sem, nc, "g14")
    sRs = Ctr.get(en_sem, nc, "g15")
    vXn = Ctr.get(en_sem, nc, "g16")
    sXb = Ctr.get(en_sem, nc, "g17")
    stx = Ctr.get(en_sem, nc, "p1b_stx")
    vch = Ctr.get(en_sem, nc, "g19")
    sch = Ctr.get(en_sem, nc, "g20")
    vBC = Ctr.get(en_sem, nc, "g21")

    with nc.Block() as blk:
        @blk.sync
        def _(sy):
            sy.dma_start(mk[:, :], mask_diag_o).then_inc(lqq.sem, 16)
            sy.wait_ge(lqq.sem, 16)

        @blk.vector
        def _(v):
            v.memset(dn_s[:, :], 0.0)
    lqq.val += 16

    nkts = [4 * qb + 4 for qb in range(NQB)]
    cumS = [0]
    for qb in range(NQB):
        cumS.append(cumS[-1] + 16 * nkts[qb])

    def gidx(qb, h, kt):
        return cumS[qb] + h * nkts[qb] + kt

    PAIRS = [(0, 2), (1, 3), (4, 6), (5, 7), (8, 10), (9, 11), (12, 14), (13, 15)]
    SIDX, PIDX, MSIDX, ORD_S = {}, {}, {}, {}
    for qb in range(NQB):
        nkt = nkts[qb]
        order_S, order_PV = [], []
        for hA, hB in PAIRS:
            for kt in range(nkt):
                order_S += [(hA, kt), (hB, kt)]
                if kt >= 1:
                    order_PV += [(hA, kt - 1), (hB, kt - 1)]
            order_PV += [(hA, nkt - 1), (hB, nkt - 1)]
        sidx = {u: cumS[qb] + i + 1 for i, u in enumerate(order_S)}
        pidx = {u: cumS[qb] + i + 1 for i, u in enumerate(order_PV)}
        em_order = []
        for hA, hB in PAIRS:
            for h_ in (hA, hB):
                for kt_ in range(4 * qb + 1, nkt):
                    em_order.append((h_, kt_))
        mpos = {u: i + 48 * qb + 1 for i, u in enumerate(em_order)}
        msidx = {}
        for u in order_S:
            h_, kt_ = u
            if kt_ > 4 * qb:
                msidx[u] = mpos[(h_, kt_)]
            else:
                msidx[u] = 0
        SIDX[qb], PIDX[qb], MSIDX[qb], ORD_S[qb] = sidx, pidx, msidx, order_S
    ZBANK = {}
    for _hA, _hB in PAIRS:
        ZBANK[_hA], ZBANK[_hB] = 0, 1

    NH = 16
    b_lqq, b_lqx, b_stx, b_gpx = lqq.val, lqx.val, stx.val, gpx.val
    b_tS, b_sE, b_tP, b_vM = tS.val, sE.val, tP.val, vM.val
    b_vR, b_tB, b_vZ, b_tO = vR.val, tB.val, vZ.val, tO.val
    b_vRb, b_sRb, b_vSq, b_tT = vRb.val, sRb.val, vSq.val, tT.val
    b_vVs, b_sRs, b_vXn, b_vBC = vVs.val, sRs.val, vXn.val, vBC.val

    with nc.Block() as blk:
        @blk.sync
        def _(sy):
            sy.dma_start(qtb[:, 0:NC * 512].rearrange("p (c j) -> p c j", c=NC),
                         qt_hbm[:, 0:512].rearrange("(c p) j -> p c j", p=128)
                         ).then_inc(lqq.sem, 16)
            lqq.val += 16
            for qb in range(NQB + 2):
                if qb + 1 < NQB:
                    sy.wait_ge(tS.sem, b_tS + cumS[qb])
                    sl_sem = lqq if (qb + 1) % 2 == 0 else lqq2
                    sy.dma_start(
                        qtb[:, ((qb + 1) % 2) * NC * 512:((qb + 1) % 2 + 1) * NC * 512]
                        .rearrange("p (c j) -> p c j", c=NC),
                        qt_hbm[:, (qb + 1) * 512:(qb + 2) * 512]
                        .rearrange("(c p) j -> p c j", p=128)).then_inc(sl_sem.sem, 16)
                    sl_sem.val += 16
                if qb < NQB:
                    sy.wait_ge(vRb.sem, b_vRb + 8 * qb)
                    sy.dma_start(x0f[:, :].rearrange("p (c j) -> p c j", c=NC),
                                 xf0[:, 1 + qb * 512:1 + (qb + 1) * 512]
                                 .rearrange("(c p) j -> p c j", p=128)).then_inc(lqx.sem, 16)
                    lqx.val += 16
                if qb >= 2:
                    q0 = qb - 2
                    for mc in range(NC):
                        sy.wait_ge(vXn.sem, b_vXn + q0 * NC + mc + 1)
                        sy.dma_start(xf1[mc * 128:(mc + 1) * 128,
                                         1 + q0 * 512:1 + (q0 + 1) * 512],
                                     x1_st[:, (q0 % 2) * NC * 512 + mc * 512:
                                           (q0 % 2) * NC * 512 + (mc + 1) * 512]
                                     ).then_inc(stx.sem, 16)
                    stx.val += NC * 16
            sy.wait_ge(stx.sem, stx.val)
            sy.wait_ge(gpx.sem, b_gpx + NQB * NC * 16)

        @blk.gpsimd
        def _(g):
            for qb in range(NQB):
                for mc in range(NC):
                    g.wait_ge(vXn.sem, b_vXn + qb * NC + mc + 1)
                    g.dma_start(xb1[mc * 128:(mc + 1) * 128, 1 + qb * 512:1 + (qb + 1) * 512],
                                x1_st[:, (qb % 2) * NC * 512 + mc * 512:
                                      (qb % 2) * NC * 512 + (mc + 1) * 512]
                                ).then_inc(gpx.sem, 16)
                gpx.val += NC * 16

        @blk.tensor
        def _(t):
            def attn_tensor(qb):
                nkt = nkts[qb]
                qoff = (qb % 2) * NC * 512
                sidx, pidx, msidx = SIDX[qb], PIDX[qb], MSIDX[qb]
                order_S = ORD_S[qb]
                if qb % 2 == 0:
                    t.wait_ge(lqq.sem, b_lqq + 16 * (qb // 2 + 1))
                else:
                    t.wait_ge(lqq2.sem, 16 * ((qb - 1) // 2 + 1))

                def emit_S(h, kt):
                    g = h // 2
                    qch = (h // 4) * 2 + h % 2
                    qhf = (h // 2) % 2
                    GS = sidx[(h, kt)] - 1
                    off = max(0, (kt - 4 * qb) * 128)
                    diag = kt >= 4 * qb
                    inst = t.matmul(ps_s[GS % 3][:, off:512],
                                    kt_bf[(g % 2) * 64:(g % 2) * 64 + 64,
                                          (g // 2) * T + kt * 128:(g // 2) * T + kt * 128 + 128],
                                    qtb[qhf * 64:qhf * 64 + 64,
                                        qoff + qch * 512 + off:qoff + qch * 512 + 512],
                                    start=True, stop=not diag)
                    if diag:
                        inst = t.matmul(ps_s[GS % 3][:, off:off + 128], ident_bf[:, :],
                                        mk[:, :], start=False, stop=True)
                    inst.then_inc(tS.sem, 1)
                    tS.val += 1

                def emit_PV(h, kt):
                    g = h // 2
                    GS = sidx[(h, kt)] - 1
                    t.matmul(ps_z[ZBANK[h]][0:65, 0:512],
                             v_bf[:, kt * 520 + g * 65:kt * 520 + g * 65 + 65],
                             pt[GS % 8][:, 0:512],
                             start=(kt == 0), stop=(kt == nkt - 1)).then_inc(tP.sem, 1)
                    tP.val += 1

                def emit_pair_bcast(p):
                    t.wait_ge(vR.sem, b_vR + 8 * qb + p + 1)
                    if 8 * qb + p >= 1:
                        t.wait_ge(vBC.sem, b_vBC + 8 * qb + p)
                    t.matmul(ps_bc[0][0:128, 0:512], sel65_f[0:65, 0:128],
                             dn_s[0:65, 0:512], start=True, stop=True).then_inc(tB.sem, 1)
                    tB.val += 1

                for p, (hA, hB) in enumerate(PAIRS):
                    for kt in range(nkt):
                        GSb = sidx[(hB, kt)] - 1
                        if GSb >= 3:
                            t.wait_ge(sE.sem, b_sE + GSb - 2)
                        emit_S(hA, kt)
                        emit_S(hB, kt)
                        if p >= 1 and kt == 0:
                            emit_pair_bcast(p - 1)
                        if kt >= 1:
                            t.wait_ge(sE.sem, b_sE + sidx[(hB, kt - 1)])
                            nmz = max(msidx[(hA, kt - 1)], msidx[(hB, kt - 1)])
                            if nmz > 0:
                                t.wait_ge(vM.sem, b_vM + nmz)
                            if kt == 1 and 16 * qb + 2 * p > 0:
                                t.wait_ge(vZ.sem, b_vZ + 16 * qb + 2 * p)
                            emit_PV(hA, kt - 1)
                            emit_PV(hB, kt - 1)
                    t.wait_ge(sE.sem, b_sE + sidx[(hB, nkt - 1)])
                    nmz = max(msidx[(hA, nkt - 1)], msidx[(hB, nkt - 1)])
                    if nmz > 0:
                        t.wait_ge(vM.sem, b_vM + nmz)
                    emit_PV(hA, nkt - 1)
                    emit_PV(hB, nkt - 1)
                emit_pair_bcast(7)

            def TA(qb):
                t.wait_ge(vZ.sem, b_vZ + 16 * (qb + 1))
                if qb >= 2:
                    t.wait_ge(vVs.sem, b_vVs + qb - 1)
                for mc in range(NC):
                    if mc >= 2:
                        t.wait_ge(vRb.sem, b_vRb + 8 * qb + mc - 1)
                    for kc in range(NC):
                        inst = t.matmul(ps_t6[mc % 2][:, :],
                                        wo_bf[:, kc * D + mc * 128:kc * D + mc * 128 + 128],
                                        zt_bf[:, (qb % 2) * NC * 512 + kc * 512:
                                              (qb % 2) * NC * 512 + (kc + 1) * 512],
                                        start=(kc == 0), stop=(kc == NC - 1))
                    inst.then_inc(tO.sem, 1)
                    tO.val += 1

            def TB(qb):
                t.wait_ge(sRb.sem, b_sRb + 8 * (qb + 1))
                t.wait_ge(vRb.sem, b_vRb + 8 * min(qb + 2, NQB))
                if qb >= 1:
                    t.wait_ge(vVs.sem, b_vVs + qb)
                for c in range(NC):
                    t.matmul(ps_t6[0][:, :], ones_bf[:, :],
                             rb_bf[:, c * 512:(c + 1) * 512],
                             start=(c == 0), stop=(c == NC - 1))
                t.wait_ge(vSq.sem, b_vSq + 8 * (qb + 1))
                for c in range(NC):
                    inst = t.matmul(ps_t6[1][:, :], ones_bf[:, :],
                                    sq_bf[:, c * 512:(c + 1) * 512],
                                    start=(c == 0), stop=(c == NC - 1))
                inst.then_inc(tT.sem, 2)
                tT.val += 2

            for qb in range(NQB + 2):
                if qb < NQB:
                    attn_tensor(qb)
                if qb >= 1 and qb - 1 < NQB:
                    TA(qb - 1)
                if qb >= 2:
                    TB(qb - 2)

        @blk.vector
        def _(v):
            def attn_vector(qb):
                nkt = nkts[qb]
                sidx, pidx, msidx = SIDX[qb], PIDX[qb], MSIDX[qb]
                order_S = ORD_S[qb]
                if qb >= 2:
                    v.wait_ge(tO.sem, b_tO + 8 * (qb - 1))

                def msets_h(h_):
                    for kt in range(4 * qb + 1, nkt):
                        GS = sidx[(h_, kt)] - 1
                        off = (kt - 4 * qb) * 128
                        if GS >= 8:
                            u6 = order_S[GS - 8 - cumS[qb]] if GS - 8 >= cumS[qb] else None
                            if u6 is not None:
                                v.wait_ge(tP.sem, b_tP + pidx[u6])
                        v.memset(pt[GS % 8][:, 0:off], 0.0).then_inc(vM.sem, 1)
                        vM.val += 1

                msets_h(PAIRS[0][0])
                msets_h(PAIRS[0][1])
                for p, (hA, hB) in enumerate(PAIRS):
                    v.wait_ge(tB.sem, b_tB + 8 * qb + p + 1)
                    v.wait_ge(vZ.sem, vZ.val)
                    v.tensor_copy(bc_s[0:128, 0:512],
                                  ps_bc[0][0:128, 0:512]).then_inc(vBC.sem, 1)
                    vBC.val += 1
                    v.wait_ge(vBC.sem, vBC.val)
                    for pos, h in enumerate((hA, hB)):
                        qch = (h // 4) * 2 + h % 2
                        qhf = (h // 2) % 2
                        v.tensor_tensor(zt_bf[qhf * 64:qhf * 64 + 64,
                                              (qb % 2) * NC * 512 + qch * 512:
                                              (qb % 2) * NC * 512 + qch * 512 + 512],
                                        ps_z[pos][0:64, 0:512],
                                        bc_s[pos * 64:pos * 64 + 64, 0:512],
                                        ALU.mult).then_inc(vZ.sem, 1)
                        vZ.val += 1
                    if p + 1 < 8:
                        msets_h(PAIRS[p + 1][0])
                        msets_h(PAIRS[p + 1][1])

            def VA(qb):
                xs = (qb % 2) * NC * 512
                v.wait_ge(lqx.sem, b_lqx + 16 * (qb + 1))
                if qb >= 2:
                    v.wait_ge(stx.sem, b_stx + (qb - 1) * NC * 16)
                    v.wait_ge(gpx.sem, b_gpx + (qb - 1) * NC * 16)
                for mc in range(NC):
                    v.wait_ge(tO.sem, b_tO + 8 * qb + mc + 1)
                    v.scalar_tensor_tensor(x1_st[:, xs + mc * 512:xs + (mc + 1) * 512],
                                           ps_t6[mc % 2][:, :], bout_cols[:, mc:mc + 1],
                                           x0f[:, mc * 512:(mc + 1) * 512],
                                           ALU.add, ALU.add).then_inc(vRb.sem, 1)
                    vRb.val += 1
                if qb >= 1:
                    v.wait_ge(tT.sem, b_tT + 2 * qb)
                for mc in range(NC):
                    v.wait_ge(vRb.sem, b_vRb + 8 * qb + mc + 1)
                    v.tensor_mul(sq_bf[:, mc * 512:(mc + 1) * 512],
                                 x1_st[:, xs + mc * 512:xs + (mc + 1) * 512],
                                 x1_st[:, xs + mc * 512:xs + (mc + 1) * 512]).then_inc(vSq.sem, 1)
                    vSq.val += 1

            def VB(qb):
                v.wait_ge(tT.sem, b_tT + 2 * (qb + 1))
                v.tensor_single_scalar(s_mu[:, :], ps_t6[0][:, :], 1.0 / D,
                                       ALU.mult).then_inc(vch.sem, 1)
                vch.val += 1
                v.wait_ge(vch.sem, vch.val)
                v.tensor_tensor(s_var[:, :], s_mu[:, :], s_mu[:, :], ALU.mult)
                v.tensor_scalar(s_tmp[:, :], ps_t6[1][:, :], 1.0 / D, EPS, ALU.mult,
                                ALU.add).then_inc(vch.sem, 1)
                vch.val += 1
                v.wait_ge(vch.sem, vch.val)
                v.tensor_tensor(s_var[:, :], s_tmp[:, :], s_var[:, :],
                                ALU.subtract).then_inc(vVs.sem, 1)
                vVs.val += 1

            def VC(qb):
                xs = (qb % 2) * NC * 512
                v.wait_ge(sRs.sem, b_sRs + 2 * (qb + 1))
                for mc in range(NC):
                    sl = slice(xs + mc * 512, xs + (mc + 1) * 512)
                    v.tensor_tensor(x1_st[:, sl], x1_st[:, sl], s_mu[:, :],
                                    ALU.subtract).then_inc(vch.sem, 1)
                    vch.val += 1
                    v.wait_ge(vch.sem, vch.val)
                    v.tensor_tensor(x1_st[:, sl], x1_st[:, sl], s_rs[:, :],
                                    ALU.mult).then_inc(vXn.sem, 1)
                    vXn.val += 1

            for qb in range(NQB + 2):
                if qb < NQB:
                    attn_vector(qb)
                if qb >= 1 and qb - 1 < NQB:
                    VA(qb - 1)
                if qb >= 2:
                    VB(qb - 2)
                    VC(qb - 2)

        @blk.scalar
        def _(s):
            def den_pair(qb, p, hA, hB):
                pidx = PIDX[qb]
                if 8 * qb + p >= 1:
                    s.wait_ge(tB.sem, b_tB + 8 * qb + p)
                s.wait_ge(tP.sem, b_tP + pidx[(hB, nkts[qb] - 1)])
                s.activation(dn_l[0:1, 0:512], ps_z[0][64:65, 0:512],
                             AF.Ln).then_inc(sch.sem, 1)
                sch.val += 1
                s.activation(dn_l[64:65, 0:512], ps_z[1][64:65, 0:512],
                             AF.Ln).then_inc(sch.sem, 1)
                sch.val += 1
                s.wait_ge(sch.sem, sch.val)
                s.activation(dn_s[0:1, 0:512], dn_l[0:1, 0:512],
                             AF.Exp, scale=-1.0)
                s.activation(dn_s[64:65, 0:512], dn_l[64:65, 0:512],
                             AF.Exp, scale=-1.0).then_inc(vR.sem, 1)
                vR.val += 1

            def attn_scalar(qb):
                nkt = nkts[qb]
                sidx, pidx = SIDX[qb], PIDX[qb]
                order_S = ORD_S[qb]
                for p, (hA, hB) in enumerate(PAIRS):
                    for kt in range(nkt):
                        GSb = sidx[(hB, kt)] - 1
                        if GSb >= 8 and GSb - 8 >= cumS[qb]:
                            u6 = order_S[GSb - 8 - cumS[qb]]
                            s.wait_ge(tP.sem, b_tP + pidx[u6])
                        s.wait_ge(tS.sem, b_tS + GSb + 1)
                        for h in (hA, hB):
                            GS = sidx[(h, kt)] - 1
                            off = max(0, (kt - 4 * qb) * 128)
                            s.activation(pt[GS % 8][:, off:512], ps_s[GS % 3][:, off:512],
                                         AF.Exp, scale=0.125).then_inc(sE.sem, 1)
                            sE.val += 1
                    den_pair(qb, p, hA, hB)

            def SA(qb):
                xs = (qb % 2) * NC * 512
                if qb >= 1:
                    s.wait_ge(tT.sem, b_tT + 2 * qb)
                for mc in range(NC):
                    s.wait_ge(vRb.sem, b_vRb + 8 * qb + mc + 1)
                    s.copy(rb_bf[:, mc * 512:(mc + 1) * 512],
                           x1_st[:, xs + mc * 512:xs + (mc + 1) * 512]).then_inc(sRb.sem, 1)
                    sRb.val += 1

            def SB(qb):
                s.wait_ge(vVs.sem, b_vVs + qb + 1)
                s.activation(s_rs[:, :], s_var[:, :], AF.Ln).then_inc(sRs.sem, 1)
                sRs.val += 1
                s.wait_ge(sRs.sem, sRs.val)
                s.activation(s_rs[:, :], s_rs[:, :], AF.Exp, scale=-0.5).then_inc(sRs.sem, 1)
                sRs.val += 1

            for qb in range(NQB + 2):
                if qb < NQB:
                    attn_scalar(qb)
                if qb >= 1 and qb - 1 < NQB:
                    SA(qb - 1)
                if qb >= 2:
                    SB(qb - 2)
    ctx_p1.close()


def _p2_inner_attn(nc, en_sem, lid, xf_in, xb_in, xf_out, xb_out,
                   wqkv_d, wo_d, scol_d,
                   tab_cs, tab_sn, tab_ci, tab_si, r2_d, bdsel_d, selbc_d,
                   mask_di, mask_ai, sel65_f, ident_bf):
    NHI = 8
    ctx = ExitStack()
    en = ctx.enter_context
    L = str(lid)

    wqkv_bf = en(nc.sbuf_tensor("p2_wqkv" + L, [128, NC * 1536], BF16))
    wo_bf = en(nc.sbuf_tensor("p2_wo" + L, [128, 4 * D], BF16))
    r2_f = en(nc.sbuf_tensor("p2_r2" + L, [128, 128], F32))
    scol = en(nc.sbuf_tensor("p2_sc" + L, [128, 4], F32))
    bdsel = en(nc.sbuf_tensor("p2_bd" + L, [128, 4 * 8], BF16))
    selbc = en(nc.sbuf_tensor("p2_sb" + L, [128, 4 * 128], BF16))
    mk_d = en(nc.sbuf_tensor("p2_mkd" + L, [128, 128], BF16))
    mk_a = en(nc.sbuf_tensor("p2_mka" + L, [128, 128], BF16))

    kr = en(nc.sbuf_tensor("p2_kr" + L, [128, 2 * 4 * 512], BF16))
    vr = en(nc.sbuf_tensor("p2_vr" + L, [128, 2 * 4 * 520], BF16))
    qfin = en(nc.sbuf_tensor("p2_qf" + L, [128, 4 * 512], BF16))
    xq_in = en(nc.sbuf_tensor("p2_xq" + L, [128, 2 * NC * 512], BF16))
    xf_cur = en(nc.sbuf_tensor("p2_xf" + L, [128, NC * 512], F32))
    tabs = en(nc.sbuf_tensor("p2_tab" + L, [128, 2 * 4 * 512], F32))
    ctabs = en(nc.sbuf_tensor("p2_ctab" + L, [128, 4 * 512], F32))
    qt_f = en(nc.sbuf_tensor("p2_qtf" + L, [128, NC * 512], F32))
    t1b = en(nc.sbuf_tensor("p2_t1" + L, [128, NC * 512], F32))
    t3b = en(nc.sbuf_tensor("p2_t3" + L, [128, 2 * 512], F32))
    sqt = en(nc.sbuf_tensor("p2_sqt" + L, [128, 8 * 512], BF16))
    ssrow = en(nc.sbuf_tensor("p2_ss" + L, [128, 2 * 512], F32))
    invr = en(nc.sbuf_tensor("p2_inv" + L, [128, 2 * 512], BF16))
    pt = [en(nc.sbuf_tensor(f"p2_pt{i}" + L, [128, 512], BF16)) for i in range(8)]
    dn_s = en(nc.sbuf_tensor("p2_dn" + L, [128, 512], BF16))
    dn_l = en(nc.sbuf_tensor("p2_dnl" + L, [128, 512], F32))
    bc_s = en(nc.sbuf_tensor("p2_bc" + L, [128, 512], F32))
    zin = en(nc.sbuf_tensor("p2_zin" + L, [128, 4 * 512], BF16))
    x2_st = en(nc.sbuf_tensor("p2_x2" + L, [128, NC * 512], F32))

    pb = [en(nc.psum_tensor(f"p2_pb{i}" + L, [128, 512], F32)) for i in range(8)]

    wl = Ctr.get(en_sem, nc, "p2_wl")
    lxq2 = [Ctr.get(en_sem, nc, f"p2_lxq{i}") for i in range(2)]
    lxf = Ctr.get(en_sem, nc, "p2_lxf")
    ltb2 = [Ctr.get(en_sem, nc, f"p2_ltb{i}") for i in range(2)]
    tQ = Ctr.get(en_sem, nc, "g4")
    vQ = Ctr.get(en_sem, nc, "g5")
    tV = Ctr.get(en_sem, nc, "g6")
    vV = Ctr.get(en_sem, nc, "g7")
    sSq = Ctr.get(en_sem, nc, "g8")
    tSS = Ctr.get(en_sem, nc, "g9")
    vSS = Ctr.get(en_sem, nc, "g10")
    sIv = Ctr.get(en_sem, nc, "g11")
    tBc = Ctr.get(en_sem, nc, "g12")
    tRo = Ctr.get(en_sem, nc, "g13")
    vT3 = Ctr.get(en_sem, nc, "g14")
    vQf = Ctr.get(en_sem, nc, "g15")
    tS = Ctr.get(en_sem, nc, "g16")
    vM = Ctr.get(en_sem, nc, "g17")
    sE = Ctr.get(en_sem, nc, "g18")
    tP = Ctr.get(en_sem, nc, "g19")
    vR = Ctr.get(en_sem, nc, "g20")
    tB = Ctr.get(en_sem, nc, "g21")
    vZ = Ctr.get(en_sem, nc, "g22")
    tO = Ctr.get(en_sem, nc, "g23")
    vO = Ctr.get(en_sem, nc, "g24")
    stg = Ctr.get(en_sem, nc, "p2_stg")
    vch = Ctr.get(en_sem, nc, "g27")
    sch = Ctr.get(en_sem, nc, "g28")
    vBC = Ctr.get(en_sem, nc, "g29")

    with nc.Block() as blk:
        @blk.gpsimd
        def _(g):
            for c in range(NC):
                g.dma_start(wqkv_bf[:, c * 1536:(c + 1) * 1536],
                            wqkv_d[c * 128:(c + 1) * 128, :]).then_inc(wl.sem, 16)
            for c in range(4):
                g.dma_start(wo_bf[:, c * D:(c + 1) * D],
                            wo_d[c * 128:(c + 1) * 128, :]).then_inc(wl.sem, 16)
            g.dma_start(r2_f[:, :], r2_d).then_inc(wl.sem, 16)
            g.dma_start(scol[:, :], scol_d).then_inc(wl.sem, 16)
            g.dma_start(bdsel[:, :], bdsel_d).then_inc(wl.sem, 16)
            g.dma_start(selbc[:, :], selbc_d).then_inc(wl.sem, 16)
            g.dma_start(mk_d[:, :], mask_di).then_inc(wl.sem, 16)
            g.dma_start(mk_a[:, :], mask_ai).then_inc(wl.sem, 16)
            wl.val += (NC + 4 + 6) * 16
            g.memset(bass.AP(vr, 64, [[2 * 4 * 520, 128], [520, 8], [65, 8], [1, 1]]), 1.0)
            g.wait_ge(wl.sem, wl.val)

        @blk.vector
        def _(v):
            v.memset(dn_s[:, :], 0.0)
            v.memset(invr[:, :], 0.0)

    for tb in range(NQB):
        kts = list(range(max(0, 4 * tb - 4), 4 * tb + 4))
        nkt = len(kts)

        def span(kt):
            if kt < 4 * tb:
                j = kt - 4 * (tb - 1)
                return 0, 128 * (j + 1), ("anti", 128 * j)
            j = kt - 4 * tb
            return 128 * j, 512, ("diag", 128 * j)

        order_S, order_PV = [], []
        for hp_ in range(4):
            h0_, h1_ = 2 * hp_, 2 * hp_ + 1
            for ki_ in range(nkt):
                order_S += [(h0_, ki_), (h1_, ki_)]
                if ki_ >= 1:
                    order_PV += [(h0_, ki_ - 1), (h1_, ki_ - 1)]
            order_PV += [(h0_, nkt - 1), (h1_, nkt - 1)]
        sidx = {u: i + 1 for i, u in enumerate(order_S)}
        pidx = {u: i + 1 for i, u in enumerate(order_PV)}
        em_order = []
        for hp_ in range(4):
            for ki_ in range(nkt):
                lo_, hi_, _mm = span(kts[ki_])
                if lo_ > 0 or hi_ < 512:
                    em_order.append((2 * hp_, ki_))
                    em_order.append((2 * hp_ + 1, ki_))
        mpos = {u: i + 1 for i, u in enumerate(em_order)}
        msidx = {}
        for u in order_S:
            msidx[u] = mpos.get(u, 0)

        lxf0 = lxf.val
        lxq20 = [c.val for c in lxq2]
        ltb20 = [c.val for c in ltb2]
        tQ0, vQ0, tV0, vV0, sSq0 = tQ.val, vQ.val, tV.val, vV.val, sSq.val
        tSS0, vSS0, sIv0, tBc0, tRo0, vT30, vQf0 = (tSS.val, vSS.val, sIv.val, tBc.val,
                                                    tRo.val, vT3.val, vQf.val)
        tS0, vM0, sE0, tP0 = tS.val, vM.val, sE.val, tP.val
        vR0, tB0, vZ0, tO0, vO0, stg0 = (vR.val, tB.val, vZ.val, tO.val, vO.val,
                                         stg.val)
        vBC0 = vBC.val
        slot = tb % 2

        with nc.Block() as blk:
            @blk.sync
            def _(sy):
                def load_tb(tbl):
                    xo = (tbl % 2) * NC * 512
                    to = (tbl % 2) * 4 * 512
                    sy.dma_start(xq_in[:, xo:xo + 4 * 512].rearrange("p (c j) -> p c j", c=4),
                                 xb_in[0:512, 1 + tbl * 512:1 + (tbl + 1) * 512]
                                 .rearrange("(c p) j -> p c j", p=128)
                                 ).then_inc(lxq2[tbl % 2].sem, 16)
                    sy.dma_start(xq_in[:, xo + 4 * 512:xo + NC * 512]
                                 .rearrange("p (c j) -> p c j", c=4),
                                 xb_in[512:1024, 0 + tbl * 512:0 + (tbl + 1) * 512]
                                 .rearrange("(c p) j -> p c j", p=128)
                                 ).then_inc(lxq2[tbl % 2].sem, 16)
                    lxq2[tbl % 2].val += 32
                    for i, tab in enumerate([tab_cs, tab_sn, tab_ci, tab_si]):
                        sy.dma_start(tabs[:, to + i * 512:to + (i + 1) * 512],
                                     tab[:, tbl * 512:(tbl + 1) * 512]
                                     ).then_inc(ltb2[tbl % 2].sem, 16)
                    ltb2[tbl % 2].val += 64

                if tb == 0:
                    load_tb(0)
                if tb + 1 < NQB:
                    if tb >= 1:
                        sy.wait_ge(tV.sem, tV0)
                        sy.wait_ge(vQ.sem, vQ0)
                    load_tb(tb + 1)
                sy.dma_start(xf_cur[:, :].rearrange("p (c j) -> p c j", c=NC),
                             xf_in[:, 1 + tb * 512:1 + (tb + 1) * 512]
                             .rearrange("(c p) j -> p c j", p=128)).then_inc(lxf.sem, 16)
                lxf.val += 16

            @blk.gpsimd
            def _(g):
                for mc in range(NC):
                    g.wait_ge(vO.sem, vO0 + mc + 1)
                    g.dma_start(xf_out[mc * 128:(mc + 1) * 128,
                                       1 + tb * 512:1 + (tb + 1) * 512],
                                x2_st[:, mc * 512:(mc + 1) * 512]).then_inc(stg.sem, 16)
                    g.dma_start(xb_out[mc * 128:(mc + 1) * 128,
                                       1 + tb * 512:1 + (tb + 1) * 512],
                                x2_st[:, mc * 512:(mc + 1) * 512]).then_inc(stg.sem, 16)
                stg.val += NC * 32
                if tb == NQB - 1:
                    g.wait_ge(stg.sem, stg.val)

            @blk.tensor
            def _(t):
                xo = (tb % 2) * NC * 512
                t.wait_ge(lxq2[tb % 2].sem, lxq20[tb % 2] + (32 if tb == 0 else 0))
                for mc in range(NC):
                    if mc >= 2:
                        t.wait_ge(vQ.sem, vQ0 + mc - 1)
                    for kc in range(NC):
                        inst = t.matmul(pb[mc % 2][:, :],
                                        wqkv_bf[:, kc * 1536 + mc * 128:kc * 1536 + mc * 128 + 128],
                                        xq_in[:, xo + kc * 512:xo + (kc + 1) * 512],
                                        start=(kc == 0), stop=(kc == NC - 1))
                    inst.then_inc(tQ.sem, 1)
                    tQ.val += 1
                for tt in range(4):
                    if tt >= 2:
                        t.wait_ge(vV.sem, vV0 + tt - 1)
                    for kc in range(NC):
                        inst = t.matmul(pb[2 + tt % 2][:, :],
                                        xq_in[:, xo + kc * 512 + tt * 128:
                                              xo + kc * 512 + tt * 128 + 128],
                                        wqkv_bf[:, kc * 1536 + 1024:kc * 1536 + 1536],
                                        start=(kc == 0), stop=(kc == NC - 1))
                    inst.then_inc(tV.sem, 1)
                    tV.val += 1
                for mc in range(NC if P2_STAGE >= 2 else 0):
                    t.wait_ge(vQ.sem, vQ0 + mc + 1)
                    if mc >= 2:
                        t.wait_ge(vT3.sem, vT30 + mc - 1)
                    t.matmul(pb[4 + mc % 2][:, :], r2_f[:, :],
                             qt_f[:, mc * 512:(mc + 1) * 512],
                             start=True, stop=True).then_inc(tRo.sem, 1)
                    tRo.val += 1
                for side in range(2 if P2_STAGE >= 2 else 0):
                    for c in range(4):
                        t.wait_ge(sSq.sem, sSq0 + side * 4 + c + 1)
                        inst = t.matmul(pb[6 + side][0:8, :], bdsel[:, c * 8:(c + 1) * 8],
                                        sqt[:, (side * 4 + c) * 512:(side * 4 + c + 1) * 512],
                                        start=(c == 0), stop=(c == 3))
                    inst.then_inc(tSS.sem, 1)
                    tSS.val += 1
                if P2_STAGE >= 2:
                    t.wait_ge(vSS.sem, vSS0 + 2)
                for idx in range(NC if P2_STAGE >= 2 else 0):
                    side, c = idx // 4, idx % 4
                    t.wait_ge(sIv.sem, sIv0 + side + 1)
                    if idx >= 2:
                        t.wait_ge(vQf.sem, vQf0 + idx - 1)
                    t.matmul(pb[6 + idx % 2][0:128, :],
                             selbc[0:65, c * 128:(c + 1) * 128],
                             invr[0:65, side * 512:side * 512 + 512],
                             start=True, stop=True).then_inc(tBc.sem, 1)
                    tBc.val += 1
                if P2_STAGE >= 3:
                    t.wait_ge(vQf.sem, vQf0 + NC)

                def emit_S(h, ki):
                    hc, hf = h // 2, (h % 2) * 64
                    kt = kts[ki]
                    GS = sidx[(h, ki)] - 1
                    lo, hi, (mtyp, moff) = span(kt)
                    kslot = (kt // 4) % 2
                    ktt = kt % 4
                    t.matmul(pb[GS % 3][:, lo:hi],
                             kr[hf:hf + 64, kslot * 2048 + hc * 512 + ktt * 128:
                                kslot * 2048 + hc * 512 + ktt * 128 + 128],
                             qfin[hf:hf + 64, hc * 512 + lo:hc * 512 + hi],
                             start=True, stop=False)
                    mten = mk_a if mtyp == "anti" else mk_d
                    t.matmul(pb[GS % 3][:, moff:moff + 128], ident_bf[:, :], mten[:, :],
                             start=False, stop=True).then_inc(tS.sem, 1)
                    tS.val += 1

                def emit_PV(h, ki):
                    kt = kts[ki]
                    GS = sidx[(h, ki)] - 1
                    t.matmul(pb[3 + h % 3][0:65, 0:512],
                             vr[:, ((kt // 4) % 2) * 2080 + (kt % 4) * 520 + h * 65:
                                ((kt // 4) % 2) * 2080 + (kt % 4) * 520 + h * 65 + 65],
                             pt[GS % 8][:, 0:512],
                             start=(ki == 0), stop=(ki == nkt - 1)).then_inc(tP.sem, 1)
                    tP.val += 1

                def emit_pair_bcast(p):
                    t.wait_ge(vR.sem, vR0 + p + 1)
                    if p >= 2:
                        t.wait_ge(vBC.sem, vBC0 + p - 1)
                    t.matmul(pb[6 + p % 2][0:128, 0:512], sel65_f[0:65, 0:128],
                             dn_s[0:65, 0:512], start=True, stop=True).then_inc(tB.sem, 1)
                    tB.val += 1

                for hp in range(NHI // 2 if P2_STAGE >= 3 else 0):
                    h0, h1 = 2 * hp, 2 * hp + 1
                    for ki in range(nkt):
                        GSb = sidx[(h1, ki)] - 1
                        if GSb >= 3:
                            t.wait_ge(sE.sem, sE0 + GSb - 2)
                        emit_S(h0, ki)
                        emit_S(h1, ki)
                        if hp >= 1 and ki == 0:
                            emit_pair_bcast(hp - 1)
                        if ki >= 1:
                            t.wait_ge(sE.sem, sE0 + sidx[(h1, ki - 1)])
                            nmz = max(msidx[(h0, ki - 1)], msidx[(h1, ki - 1)])
                            if nmz > 0:
                                t.wait_ge(vM.sem, vM0 + nmz)
                            if ki == 1 and h1 >= 3:
                                t.wait_ge(vZ.sem, vZ0 + h1 - 2)
                            emit_PV(h0, ki - 1)
                            emit_PV(h1, ki - 1)
                    t.wait_ge(sE.sem, sE0 + sidx[(h1, nkt - 1)])
                    nmz = max(msidx[(h0, nkt - 1)], msidx[(h1, nkt - 1)])
                    if nmz > 0:
                        t.wait_ge(vM.sem, vM0 + nmz)
                    emit_PV(h0, nkt - 1)
                    emit_PV(h1, nkt - 1)
                if P2_STAGE >= 3:
                    emit_pair_bcast(NHI // 2 - 1)
                t.wait_ge(vZ.sem, vZ0 + NHI)
                for mc in range(NC if P2_STAGE >= 4 else 0):
                    if mc >= 2:
                        t.wait_ge(vO.sem, vO0 + mc - 1)
                    for kc in range(4):
                        inst = t.matmul(pb[mc % 2][:, :],
                                        wo_bf[:, kc * D + mc * 128:kc * D + mc * 128 + 128],
                                        zin[:, kc * 512:(kc + 1) * 512],
                                        start=(kc == 0), stop=(kc == 3))
                    inst.then_inc(tO.sem, 1)
                    tO.val += 1

            @blk.vector
            def _(v):
                to = (tb % 2) * 4 * 512
                v.wait_ge(ltb2[tb % 2].sem, ltb20[tb % 2] + (64 if tb == 0 else 0))
                for i in range(4):
                    v.tensor_single_scalar(ctabs[:, i * 512:(i + 1) * 512],
                                           tabs[:, to + i * 512:to + (i + 1) * 512],
                                           scol[:, i:i + 1], ALU.mult).then_inc(vch.sem, 1)
                    vch.val += 1
                v.wait_ge(vch.sem, vch.val)
                for mc in range(NC):
                    v.wait_ge(tQ.sem, tQ0 + mc + 1)
                    v.tensor_copy(qt_f[:, mc * 512:(mc + 1) * 512],
                                  pb[mc % 2][:, :]).then_inc(vQ.sem, 1)
                    vQ.val += 1
                    v.wait_ge(vQ.sem, vQ.val)
                    cs_off = (0 if mc < 4 else 2) * 512
                    v.tensor_tensor(t1b[:, mc * 512:(mc + 1) * 512],
                                    qt_f[:, mc * 512:(mc + 1) * 512],
                                    ctabs[:, cs_off:cs_off + 512], ALU.mult)
                for tt in range(4):
                    v.wait_ge(tV.sem, tV0 + tt + 1)
                    v.tensor_copy(
                        bass.AP(vr, slot * 2080 + tt * 520,
                                [[2 * 4 * 520, 128], [65, 8], [1, 64]]),
                        pb[2 + tt % 2][:, :].rearrange("p (g e) -> p g e", g=8)
                        ).then_inc(vV.sem, 1)
                    vV.val += 1
                for mc in range(NC if P2_STAGE >= 2 else 0):
                    sn_off = (1 if mc < 4 else 3) * 512
                    v.wait_ge(tRo.sem, tRo0 + mc + 1)
                    v.tensor_tensor(t3b[:, (mc % 2) * 512:(mc % 2) * 512 + 512],
                                    pb[4 + mc % 2][:, :],
                                    ctabs[:, sn_off:sn_off + 512], ALU.mult).then_inc(vT3.sem, 1)
                    vT3.val += 1
                    v.wait_ge(vT3.sem, vT3.val)
                    v.tensor_tensor(t1b[:, mc * 512:(mc + 1) * 512],
                                    t1b[:, mc * 512:(mc + 1) * 512],
                                    t3b[:, (mc % 2) * 512:(mc % 2) * 512 + 512],
                                    ALU.add).then_inc(vch.sem, 1)
                    vch.val += 1
                for side in range(2 if P2_STAGE >= 2 else 0):
                    v.wait_ge(tSS.sem, tSS0 + side + 1)
                    v.tensor_single_scalar(ssrow[0:8, side * 512:side * 512 + 512],
                                           pb[6 + side][0:8, :], 1e-24,
                                           ALU.max).then_inc(vSS.sem, 1)
                    vSS.val += 1
                if P2_STAGE >= 2:
                    v.wait_ge(vch.sem, vch.val)
                for mc in range(NC if P2_STAGE >= 2 else 0):
                    v.wait_ge(tBc.sem, tBc0 + mc + 1)
                    dest = qfin if mc < 4 else kr
                    doff = (mc % 4) * 512 if mc < 4 else slot * 2048 + (mc - 4) * 512
                    v.tensor_tensor(dest[:, doff:doff + 512],
                                    t1b[:, mc * 512:(mc + 1) * 512],
                                    pb[6 + mc % 2][:, :], ALU.mult).then_inc(vQf.sem, 1)
                    vQf.val += 1
                def msets_pair(hp_):
                    for ki_ in range(nkt):
                        kt_ = kts[ki_]
                        lo_, hi_, _mm = span(kt_)
                        if lo_ > 0 or hi_ < 512:
                            for h_ in (2 * hp_, 2 * hp_ + 1):
                                GS = sidx[(h_, ki_)] - 1
                                if GS >= 8:
                                    u6 = order_S[GS - 8]
                                    v.wait_ge(tP.sem, tP0 + pidx[u6])
                                if lo_ > 0 and hi_ < 512:
                                    v.memset(pt[GS % 8][:, 0:lo_], 0.0)
                                    v.memset(pt[GS % 8][:, hi_:512], 0.0).then_inc(vM.sem, 1)
                                elif lo_ > 0:
                                    v.memset(pt[GS % 8][:, 0:lo_], 0.0).then_inc(vM.sem, 1)
                                else:
                                    v.memset(pt[GS % 8][:, hi_:512], 0.0).then_inc(vM.sem, 1)
                                vM.val += 1

                if P2_STAGE >= 3:
                    msets_pair(0)
                for hp in range(NHI // 2 if P2_STAGE >= 3 else 0):
                    v.wait_ge(tB.sem, tB0 + hp + 1)
                    v.wait_ge(vZ.sem, vZ.val)
                    v.tensor_copy(bc_s[0:128, 0:512],
                                  pb[6 + hp % 2][0:128, 0:512]).then_inc(vBC.sem, 1)
                    vBC.val += 1
                    v.wait_ge(vBC.sem, vBC.val)
                    for pos, h in enumerate((2 * hp, 2 * hp + 1)):
                        hc, hf = h // 2, (h % 2) * 64
                        v.tensor_tensor(zin[hf:hf + 64, hc * 512:hc * 512 + 512],
                                        pb[3 + h % 3][0:64, 0:512],
                                        bc_s[pos * 64:pos * 64 + 64, 0:512],
                                        ALU.mult).then_inc(vZ.sem, 1)
                        vZ.val += 1
                    if 2 * hp + 2 < NHI:
                        msets_pair(hp + 1)
                v.wait_ge(lxf.sem, lxf0 + 16)
                v.wait_ge(stg.sem, stg0)
                for mc in range(NC):
                    if P2_STAGE >= 4:
                        v.wait_ge(tO.sem, tO0 + mc + 1)
                        v.tensor_tensor(x2_st[:, mc * 512:(mc + 1) * 512], pb[mc % 2][:, :],
                                        xf_cur[:, mc * 512:(mc + 1) * 512],
                                        ALU.add).then_inc(vO.sem, 1)
                    else:
                        v.wait_ge(vQ.sem, vQ0 + NC)
                        v.tensor_copy(x2_st[:, mc * 512:(mc + 1) * 512],
                                      xf_cur[:, mc * 512:(mc + 1) * 512]).then_inc(vO.sem, 1)
                    vO.val += 1

            @blk.scalar
            def _(s):
                for mc in range(NC if P2_STAGE >= 2 else 0):
                    s.wait_ge(vQ.sem, vQ0 + mc + 1)
                    s.activation(sqt[:, mc * 512:(mc + 1) * 512],
                                 qt_f[:, mc * 512:(mc + 1) * 512],
                                 AF.Square).then_inc(sSq.sem, 1)
                    sSq.val += 1
                for side in range(2 if P2_STAGE >= 2 else 0):
                    s.wait_ge(vSS.sem, vSS0 + side + 1)
                    s.activation(ssrow[0:8, side * 512:side * 512 + 512],
                                 ssrow[0:8, side * 512:side * 512 + 512],
                                 AF.Ln).then_inc(sch.sem, 1)
                    sch.val += 1
                    s.wait_ge(sch.sem, sch.val)
                    s.activation(invr[0:8, side * 512:side * 512 + 512],
                                 ssrow[0:8, side * 512:side * 512 + 512],
                                 AF.Exp, scale=-0.5).then_inc(sIv.sem, 1)
                    sIv.val += 1
                def den_pair_i(hp):
                    h0, h1 = 2 * hp, 2 * hp + 1
                    if hp >= 1:
                        s.wait_ge(tB.sem, tB0 + hp)
                    s.wait_ge(tP.sem, tP0 + pidx[(h1, nkt - 1)])
                    s.activation(dn_l[0:1, 0:512], pb[3 + h0 % 3][64:65, 0:512],
                                 AF.Ln).then_inc(sch.sem, 1)
                    sch.val += 1
                    s.activation(dn_l[64:65, 0:512], pb[3 + h1 % 3][64:65, 0:512],
                                 AF.Ln).then_inc(sch.sem, 1)
                    sch.val += 1
                    s.wait_ge(sch.sem, sch.val)
                    s.activation(dn_s[0:1, 0:512], dn_l[0:1, 0:512],
                                 AF.Exp, scale=-1.0)
                    s.activation(dn_s[64:65, 0:512], dn_l[64:65, 0:512],
                                 AF.Exp, scale=-1.0).then_inc(vR.sem, 1)
                    vR.val += 1

                for hp in range(NHI // 2 if P2_STAGE >= 3 else 0):
                    h0, h1 = 2 * hp, 2 * hp + 1
                    for ki in range(nkt):
                        GSb = sidx[(h1, ki)] - 1
                        kt = kts[ki]
                        lo, hi, _m = span(kt)
                        if GSb >= 8:
                            u6 = order_S[GSb - 8]
                            s.wait_ge(tP.sem, tP0 + pidx[u6])
                        s.wait_ge(tS.sem, tS0 + GSb + 1)
                        for h in (h0, h1):
                            GS = sidx[(h, ki)] - 1
                            s.activation(pt[GS % 8][:, lo:hi], pb[GS % 3][:, lo:hi],
                                         AF.Exp, scale=8.0).then_inc(sE.sem, 1)
                            sE.val += 1
                    den_pair_i(hp)
    ctx.close()


def _p3_inner_ffn(nc, en_sem, lid, xf_in, xb_in, xf_out, xb_out,
                  w1_d, w2_d, b1a_d, b1g_d, b2_d, lng_d, lnb_d, ones_bf):
    ctx = ExitStack()
    en = ctx.enter_context
    L = str(lid)
    NJ = 22
    JW = [128] * 21 + [42]

    w2_bf = en(nc.sbuf_tensor("p3_w2" + L, [128, NJ * D], BF16))
    h2q = en(nc.sbuf_tensor("p3_h2" + L, [128, NJ * 1024], BF16))
    w1buf = en(nc.sbuf_tensor("p3_w1" + L, [128, 4 * NC * 128], BF16))
    hn_q = en(nc.sbuf_tensor("p3_hn" + L, [128, NC * 1024], BF16))
    hf = en(nc.sbuf_tensor("p3_hf" + L, [128, NC * 512], F32))
    hb = en(nc.sbuf_tensor("p3_hb" + L, [128, NC * 512], BF16))
    sqb = en(nc.sbuf_tensor("p3_sq" + L, [128, NC * 512], BF16))
    s_mu = en(nc.sbuf_tensor("p3_mu" + L, [128, 512], F32))
    s_var = en(nc.sbuf_tensor("p3_var" + L, [128, 512], F32))
    s_rs = en(nc.sbuf_tensor("p3_rs" + L, [128, 512], F32))
    s_tmp = en(nc.sbuf_tensor("p3_tmp" + L, [128, 512], F32))
    glu = en(nc.sbuf_tensor("p3_glu" + L, [128, 2 * 512], BF16))
    bcols = en(nc.sbuf_tensor("p3_bc" + L, [128, 2 * NJ + NC * 3], F32))
    xres = en(nc.sbuf_tensor("p3_xr" + L, [128, NC * 512], F32))
    x3_st = en(nc.sbuf_tensor("p3_x3" + L, [128, NC * 512], F32))

    pb = [en(nc.psum_tensor(f"p3_pb{i}" + L, [128, 512], F32)) for i in range(6)]

    wl = Ctr.get(en_sem, nc, "p3_wl")
    lhf = Ctr.get(en_sem, nc, "p3_lhf")
    lhb = Ctr.get(en_sem, nc, "p3_lhb")
    lw1 = [Ctr.get(en_sem, nc, f"p3_lw1{i}") for i in range(4)]
    lxr = Ctr.get(en_sem, nc, "p3_lxr")
    vSq = Ctr.get(en_sem, nc, "g5")
    tLn = Ctr.get(en_sem, nc, "g6")
    vMu = Ctr.get(en_sem, nc, "g7")
    sRs = Ctr.get(en_sem, nc, "g8")
    vHn = Ctr.get(en_sem, nc, "g9")
    tY = Ctr.get(en_sem, nc, "g10")
    sGl = Ctr.get(en_sem, nc, "g11")
    vH2 = Ctr.get(en_sem, nc, "g12")
    tW2 = Ctr.get(en_sem, nc, "g13")
    vX3 = Ctr.get(en_sem, nc, "g14")
    st3 = Ctr.get(en_sem, nc, "p3_st3")
    stb = Ctr.get(en_sem, nc, "p3_stb")
    vch = Ctr.get(en_sem, nc, "g17")

    with nc.Block() as blk:
        @blk.gpsimd
        def _(g):
            g.dma_start(w2_bf[:, 0:21 * D].rearrange("p (c j) -> p c j", c=21),
                        w2_d[0:21 * 128, :].rearrange("(c p) j -> p c j", p=128)
                        ).then_inc(wl.sem, 16)
            g.dma_start(w2_bf[0:42, 21 * D:22 * D], w2_d[21 * 128:2730, :]).then_inc(wl.sem, 16)
            g.dma_start(bcols[:, 0:NJ], b1a_d).then_inc(wl.sem, 16)
            g.dma_start(bcols[:, NJ:2 * NJ], b1g_d).then_inc(wl.sem, 16)
            g.dma_start(bcols[:, 2 * NJ:2 * NJ + NC], b2_d).then_inc(wl.sem, 16)
            g.dma_start(bcols[:, 2 * NJ + NC:2 * NJ + 2 * NC], lng_d).then_inc(wl.sem, 16)
            g.dma_start(bcols[:, 2 * NJ + 2 * NC:2 * NJ + 3 * NC], lnb_d).then_inc(wl.sem, 16)
            wl.val += 7 * 16
            g.wait_ge(wl.sem, wl.val)

    for qt in range(4):
        lhf0, lhb0, lxr0 = lhf.val, lhb.val, lxr.val
        vSq0, tLn0, vMu0, sRs0, vHn0 = vSq.val, tLn.val, vMu.val, sRs.val, vHn.val
        tY0, sGl0, vH20, tW20 = tY.val, sGl.val, vH2.val, tW2.val
        vX30, st30, stb0 = vX3.val, st3.val, stb.val
        lw10 = [c.val for c in lw1]

        with nc.Block() as blk:
            @blk.gpsimd
            def _(g):
                for seq in range(2 * NJ):
                    j, which = seq // 2, seq % 2
                    half = 1 - which
                    jw = JW[j]
                    j0 = half * 2730 + j * 128
                    if seq >= 4:
                        g.wait_ge(tY.sem, tY0 + (seq - 3) * 2)
                    g.dma_start(
                        w1buf[:, (seq % 4) * NC * 128:(seq % 4) * NC * 128 + NC * jw]
                        .rearrange("p (c j) -> p c j", c=NC),
                        w1_d[:, j0:j0 + jw].rearrange("(c p) j -> p c j", p=128)
                        ).then_inc(lw1[seq % 4].sem, 16)
                    lw1[seq % 4].val += 16
                for sub in range(2):
                    t0 = qt * 1024 + sub * 512
                    for mc in range(NC):
                        g.wait_ge(vX3.sem, vX30 + sub * NC + mc + 1)
                        g.dma_start(xb_out[mc * 128:(mc + 1) * 128, 1 + t0:1 + t0 + 512],
                                    x3_st[:, mc * 512:(mc + 1) * 512]).then_inc(stb.sem, 16)
                    stb.val += NC * 16
                if qt == 3:
                    g.wait_ge(stb.sem, stb.val)

            @blk.sync
            def _(sy):
                for sub in range(2):
                    t0 = qt * 1024 + sub * 512
                    if sub == 1:
                        sy.wait_ge(tLn.sem, tLn0 + 2)
                    sy.dma_start(hb[:, 0:4 * 512].rearrange("p (c j) -> p c j", c=4),
                                 xb_in[0:512, 1 + t0:1 + t0 + 512]
                                 .rearrange("(c p) j -> p c j", p=128)).then_inc(lhb.sem, 16)
                    sy.dma_start(hb[:, 4 * 512:NC * 512].rearrange("p (c j) -> p c j", c=4),
                                 xb_in[512:1024, 0 + t0:0 + t0 + 512]
                                 .rearrange("(c p) j -> p c j", p=128)).then_inc(lhb.sem, 16)
                    if sub == 1:
                        sy.wait_ge(vHn.sem, vHn0 + NC)
                    sy.dma_start(hf[:, 0:4 * 512].rearrange("p (c j) -> p c j", c=4),
                                 xf_in[0:512, 1 + t0:1 + t0 + 512]
                                 .rearrange("(c p) j -> p c j", p=128)).then_inc(lhf.sem, 16)
                    sy.dma_start(hf[:, 4 * 512:NC * 512].rearrange("p (c j) -> p c j", c=4),
                                 xf_in[512:1024, 0 + t0:0 + t0 + 512]
                                 .rearrange("(c p) j -> p c j", p=128)).then_inc(lhf.sem, 16)
                    lhf.val += 32
                    lhb.val += 32
                sy.wait_ge(vHn.sem, vHn0 + 2 * NC)
                for sub in range(2):
                    t0 = qt * 1024 + sub * 512
                    if sub == 1:
                        sy.wait_ge(vX3.sem, vX30 + NC)
                    sy.dma_start(xres[:, :].rearrange("p (c j) -> p c j", c=NC),
                                 xf_in[:, 1 + t0:1 + t0 + 512]
                                 .rearrange("(c p) j -> p c j", p=128)).then_inc(lxr.sem, 16)
                    lxr.val += 16
                    for mc in range(NC):
                        sy.wait_ge(vX3.sem, vX30 + sub * NC + mc + 1)
                        sy.dma_start(xf_out[mc * 128:(mc + 1) * 128, 1 + t0:1 + t0 + 512],
                                     x3_st[:, mc * 512:(mc + 1) * 512]).then_inc(st3.sem, 16)
                    st3.val += NC * 16
                if qt == 3:
                    sy.wait_ge(st3.sem, st3.val)

            @blk.vector
            def _(v):
                v.wait_ge(stb.sem, stb0)
                v.wait_ge(st3.sem, st30)
                for sub in range(2):
                    v.wait_ge(tLn.sem, tLn0 + (sub + 1) * 2)
                    v.tensor_single_scalar(s_mu[:, :], pb[4][:, :], 1.0 / D,
                                           ALU.mult).then_inc(vch.sem, 1)
                    vch.val += 1
                    v.wait_ge(vch.sem, vch.val)
                    v.tensor_tensor(s_var[:, :], s_mu[:, :], s_mu[:, :], ALU.mult)
                    v.tensor_scalar(s_tmp[:, :], pb[5][:, :], 1.0 / D, EPS, ALU.mult,
                                    ALU.add).then_inc(vch.sem, 1)
                    vch.val += 1
                    v.wait_ge(vch.sem, vch.val)
                    v.tensor_tensor(s_var[:, :], s_tmp[:, :], s_var[:, :],
                                    ALU.subtract).then_inc(vMu.sem, 1)
                    vMu.val += 1
                    v.wait_ge(sRs.sem, sRs0 + 2 * (sub + 1))
                    v.wait_ge(lhf.sem, lhf0 + (sub + 1) * 32)
                    for c in range(NC):
                        sl = slice(c * 512, (c + 1) * 512)
                        dst = slice(c * 1024 + sub * 512, c * 1024 + sub * 512 + 512)
                        v.tensor_tensor(x3_st[:, sl], hf[:, sl], s_mu[:, :],
                                        ALU.subtract).then_inc(vch.sem, 1)
                        vch.val += 1
                        v.wait_ge(vch.sem, vch.val)
                        v.tensor_tensor(x3_st[:, sl], x3_st[:, sl], s_rs[:, :],
                                        ALU.mult).then_inc(vch.sem, 1)
                        vch.val += 1
                        v.wait_ge(vch.sem, vch.val)
                        v.tensor_scalar(hn_q[:, dst], x3_st[:, sl],
                                        bcols[:, 2 * NJ + NC + c:2 * NJ + NC + c + 1],
                                        bcols[:, 2 * NJ + 2 * NC + c:2 * NJ + 2 * NC + c + 1],
                                        ALU.mult, ALU.add).then_inc(vHn.sem, 1)
                        vHn.val += 1
                for j in range(NJ):
                    for nb in range(2):
                        gi = j * 2 + nb
                        v.wait_ge(sGl.sem, sGl0 + gi + 1)
                        v.wait_ge(tY.sem, tY0 + j * 4 + 3 + nb)
                        v.scalar_tensor_tensor(
                            h2q[:, j * 1024 + nb * 512:j * 1024 + nb * 512 + 512],
                            pb[2 + gi % 2][:, :], bcols[:, j:j + 1],
                            glu[:, (gi % 2) * 512:(gi % 2) * 512 + 512],
                            ALU.add, ALU.mult).then_inc(vH2.sem, 1)
                        vH2.val += 1
                for sub in range(2):
                    if sub == 1:
                        v.wait_ge(stb.sem, stb0 + NC * 16)
                        v.wait_ge(st3.sem, st30 + NC * 16)
                    v.wait_ge(lxr.sem, lxr0 + (sub + 1) * 16)
                    for mc in range(NC):
                        v.wait_ge(tW2.sem, tW20 + sub * NC + mc + 1)
                        v.scalar_tensor_tensor(x3_st[:, mc * 512:(mc + 1) * 512],
                                               pb[4 + mc % 2][:, :],
                                               bcols[:, 2 * NJ + mc:2 * NJ + mc + 1],
                                               xres[:, mc * 512:(mc + 1) * 512],
                                               ALU.add, ALU.add).then_inc(vX3.sem, 1)
                        vX3.val += 1

            @blk.scalar
            def _(s):
                for sub in range(2):
                    s.wait_ge(lhb.sem, lhb0 + (sub + 1) * 32)
                    for c in range(NC):
                        s.activation(sqb[:, c * 512:(c + 1) * 512],
                                     hb[:, c * 512:(c + 1) * 512],
                                     AF.Square).then_inc(vSq.sem, 1)
                        vSq.val += 1
                    s.wait_ge(vMu.sem, vMu0 + sub + 1)
                    s.activation(s_rs[:, :], s_var[:, :], AF.Ln).then_inc(sRs.sem, 1)
                    sRs.val += 1
                    s.wait_ge(sRs.sem, sRs.val)
                    s.activation(s_rs[:, :], s_rs[:, :], AF.Exp, scale=-0.5).then_inc(sRs.sem, 1)
                    sRs.val += 1
                for j in range(NJ):
                    for nb in range(2):
                        gi = j * 2 + nb
                        s.wait_ge(tY.sem, tY0 + j * 4 + nb + 1)
                        if gi >= 2:
                            s.wait_ge(vH2.sem, vH20 + gi - 1)
                        s.activation(glu[:, (gi % 2) * 512:(gi % 2) * 512 + 512],
                                     pb[gi % 2][:, :], AF.Gelu,
                                     bias=bcols[:, NJ + j:NJ + j + 1]).then_inc(sGl.sem, 1)
                        sGl.val += 1

            @blk.tensor
            def _(t):
                for sub in range(2):
                    t.wait_ge(lhb.sem, lhb0 + (sub + 1) * 32)
                    for c in range(NC):
                        t.matmul(pb[4][:, :], ones_bf[:, :], hb[:, c * 512:(c + 1) * 512],
                                 start=(c == 0), stop=(c == NC - 1))
                    t.wait_ge(vSq.sem, vSq0 + (sub + 1) * NC)
                    for c in range(NC):
                        inst = t.matmul(pb[5][:, :], ones_bf[:, :], sqb[:, c * 512:(c + 1) * 512],
                                        start=(c == 0), stop=(c == NC - 1))
                    inst.then_inc(tLn.sem, 2)
                    tLn.val += 2
                    t.wait_ge(vMu.sem, vMu0 + sub + 1)
                for j in range(NJ):
                    jw = JW[j]
                    for which, bank in ((0, 0), (1, 2)):
                        seq = j * 2 + which
                        t.wait_ge(lw1[seq % 4].sem, lw10[seq % 4] + (seq // 4 + 1) * 16)
                        for nb in range(2):
                            gi = j * 2 + nb
                            t.wait_ge(vHn.sem, vHn0 + (nb + 1) * NC)
                            if gi >= 2:
                                if bank == 0:
                                    t.wait_ge(sGl.sem, sGl0 + gi - 1)
                                else:
                                    t.wait_ge(vH2.sem, vH20 + gi - 1)
                            for kc in range(NC):
                                inst = t.matmul(
                                    pb[bank + gi % 2][0:jw, :] if jw < 128 else pb[bank + gi % 2][:, :],
                                    w1buf[:, (seq % 4) * NC * 128 + kc * jw:
                                          (seq % 4) * NC * 128 + kc * jw + jw],
                                    hn_q[:, kc * 1024 + nb * 512:kc * 1024 + nb * 512 + 512],
                                    start=(kc == 0), stop=(kc == NC - 1))
                            inst.then_inc(tY.sem, 1)
                            tY.val += 1
                t.wait_ge(vH2.sem, vH20 + NJ * 2)
                for sub in range(2):
                    for mc in range(NC):
                        if sub * NC + mc >= 2:
                            t.wait_ge(vX3.sem, vX30 + sub * NC + mc - 1)
                        for j in range(NJ):
                            jw = JW[j]
                            inst = t.matmul(pb[4 + mc % 2][:, :],
                                            w2_bf[0:jw, j * D + mc * 128:j * D + mc * 128 + 128],
                                            h2q[0:jw, j * 1024 + sub * 512:j * 1024 + sub * 512 + 512],
                                            start=(j == 0), stop=(j == NJ - 1))
                        inst.then_inc(tW2.sem, 1)
                        tW2.val += 1
    ctx.close()


def build_program():
    Ctr.reset_registry()
    nc = bass.Bass("TRN2", target_bir_lowering=False)

    resid = nc.dram_tensor("resid", [T, D], F32, kind="ExternalInput").ap()
    ident = nc.dram_tensor("ident", [128, 128], F32, kind="ExternalInput").ap()
    out = nc.dram_tensor("out", [T, D], F32, kind="ExternalOutput").ap()

    xf0 = nc.dram_tensor("xf0", [D, W], F32, kind="Internal").ap()
    xb0 = nc.dram_tensor("xb0", [D, W], BF16, kind="Internal").ap()
    xf1 = nc.dram_tensor("xf1", [D, W], F32, kind="Internal").ap()
    xb1 = nc.dram_tensor("xb1", [D, W], BF16, kind="Internal").ap()
    qt_hbm = nc.dram_tensor("qt_hbm", [D, T], BF16, kind="Internal").ap()

    dbg_zt = nc.dram_tensor("dbg_zt", [128, NC * 512], BF16, kind="ExternalOutput").ap() \
        if DEBUG_ZT else None
    wq = nc.dram_tensor("wq", [D, D], F32, kind="ExternalInput").ap()
    wk = nc.dram_tensor("wk", [D, 512], F32, kind="ExternalInput").ap()
    wv = nc.dram_tensor("wv", [D, 512], F32, kind="ExternalInput").ap()
    wo = nc.dram_tensor("wo", [D, D], F32, kind="ExternalInput").ap()
    bout_d = nc.dram_tensor("bout", [128, NC], F32, kind="ExternalInput").ap()
    mask_diag_o = nc.dram_tensor("mask_diag_o", [128, 128], BF16, kind="ExternalInput").ap()
    wqkv_l = [nc.dram_tensor(f"wqkv{l}", [D, 1536], F32, kind="ExternalInput").ap()
              for l in range(2)]
    wo_il = [nc.dram_tensor(f"wo_i{l}", [512, D], F32, kind="ExternalInput").ap()
             for l in range(2)]
    scol_l = [nc.dram_tensor(f"scol{l}", [128, 4], F32, kind="ExternalInput").ap()
              for l in range(2)]
    tab_cs = nc.dram_tensor("tab_cs", [128, T], F32, kind="ExternalInput").ap()
    tab_sn = nc.dram_tensor("tab_sn", [128, T], F32, kind="ExternalInput").ap()
    tab_ci = nc.dram_tensor("tab_ci", [128, T], F32, kind="ExternalInput").ap()
    tab_si = nc.dram_tensor("tab_si", [128, T], F32, kind="ExternalInput").ap()
    r2_d = nc.dram_tensor("r2", [128, 128], F32, kind="ExternalInput").ap()
    bdsel_d = nc.dram_tensor("bdsel", [128, 32], BF16, kind="ExternalInput").ap()
    selbc_d = nc.dram_tensor("selbc", [128, 512], F32, kind="ExternalInput").ap()
    mask_di = nc.dram_tensor("mask_di", [128, 128], BF16, kind="ExternalInput").ap()
    mask_ai = nc.dram_tensor("mask_ai", [128, 128], BF16, kind="ExternalInput").ap()
    xf2 = nc.dram_tensor("xf2", [D, W], F32, kind="Internal").ap()
    xb2 = nc.dram_tensor("xb2", [D, W], BF16, kind="Internal").ap()
    xf3 = nc.dram_tensor("xf3", [D, W], F32, kind="Internal").ap()
    xb3 = nc.dram_tensor("xb3", [D, W], BF16, kind="Internal").ap()
    w1_l = [nc.dram_tensor(f"w1_{l}", [D, 5460], F32, kind="ExternalInput").ap()
            for l in range(2)]
    w2_l = [nc.dram_tensor(f"w2_{l}", [2730, D], F32, kind="ExternalInput").ap()
            for l in range(2)]
    b1a_l = [nc.dram_tensor(f"b1a_{l}", [128, 22], F32, kind="ExternalInput").ap()
             for l in range(2)]
    b1g_l = [nc.dram_tensor(f"b1g_{l}", [128, 22], F32, kind="ExternalInput").ap()
             for l in range(2)]
    b2_l = [nc.dram_tensor(f"b2_{l}", [128, NC], F32, kind="ExternalInput").ap()
            for l in range(2)]
    lng_l = [nc.dram_tensor(f"lng_{l}", [128, NC], F32, kind="ExternalInput").ap()
             for l in range(2)]
    lnb_l = [nc.dram_tensor(f"lnb_{l}", [128, NC], F32, kind="ExternalInput").ap()
             for l in range(2)]

    with ExitStack() as ctx:
        en = ctx.enter_context
        ident_f32 = en(nc.sbuf_tensor("ident_f32", [128, 128], F32))
        ident_bf = en(nc.sbuf_tensor("ident_bf", [128, 128], BF16))
        ones_bf = en(nc.sbuf_tensor("ones_bf", [128, 128], BF16))
        ones64f = en(nc.sbuf_tensor("ones64f", [128, 64], F32))
        sel65_f = en(nc.sbuf_tensor("sel65_f", [128, 128], BF16))
        boot = Ctr(en, nc, "boot")
        boot2 = Ctr(en, nc, "boot2")
        with nc.Block() as blk:
            @blk.gpsimd
            def _(g):
                g.dma_start(ident_f32[:, :], ident).then_inc(boot.sem, 16)
                g.dma_start(ident_bf[:, :], ident).then_inc(boot.sem, 16)
                g.memset(ones_bf[:, :], 1.0)
                g.memset(ones64f[:, :], 1.0)
                g.wait_ge(boot.sem, 32)

            @blk.vector
            def _(v):
                v.memset(sel65_f[:, :], 0.0).then_inc(boot2.sem, 1)
                v.wait_ge(boot2.sem, 1)
                v.memset(sel65_f[0:1, 0:64], 1.0)
                v.memset(sel65_f[64:65, 64:128], 1.0)

        _p0_transpose_in(nc, en, resid, xf0, xb0, ident_f32,
                         [xf1, xf2, xf3], [xb1, xb2, xb3])
        if LEVEL >= 1:
            _p1_outer_attn(nc, en, xf0, xb0, xf1, xb1, qt_hbm, wq, wk, wv, wo, bout_d,
                           mask_diag_o, ident_f32, ident_bf, ones_bf, ones64f,
                           sel65_f, dbg_zt)
            if LEVEL >= 2:
                _p2_inner_attn(nc, en, 0, xf1, xb1, xf2, xb2,
                               wqkv_l[0], wo_il[0], scol_l[0],
                               tab_cs, tab_sn, tab_ci, tab_si, r2_d, bdsel_d, selbc_d,
                               mask_di, mask_ai, sel65_f, ident_bf)
                if LEVEL >= 3:
                    _p3_inner_ffn(nc, en, 0, xf2, xb2, xf3, xb3,
                                  w1_l[0], w2_l[0], b1a_l[0], b1g_l[0], b2_l[0],
                                  lng_l[0], lnb_l[0], ones_bf)
                    if LEVEL >= 4:
                        _p2_inner_attn(nc, en, 1, xf3, xb3, xf2, xb2,
                                       wqkv_l[1], wo_il[1], scol_l[1],
                                       tab_cs, tab_sn, tab_ci, tab_si, r2_d, bdsel_d,
                                       selbc_d, mask_di, mask_ai, sel65_f, ident_bf)
                        _p3_inner_ffn(nc, en, 1, xf2, xb2, xf3, xb3,
                                      w1_l[1], w2_l[1], b1a_l[1], b1g_l[1], b2_l[1],
                                      lng_l[1], lnb_l[1], ones_bf)
                        _p6_final(nc, en, xf1, xf3, out, ident_f32, ones_bf)
                    else:
                        _p6_final(nc, en, xf3, xf3, out, ident_f32, ones_bf)
                else:
                    _p6_final(nc, en, xf2, xf2, out, ident_f32, ones_bf)
            else:
                _p6_final(nc, en, xf1, xf1, out, ident_f32, ones_bf)
        else:
            _p6_final(nc, en, xf0, xf0, out, ident_f32, ones_bf)

    return nc


def _p0_transpose_in(nc, en_sem, resid, xf0, xb0, ident_f32, xf_more=(), xb_more=()):
    ctx = ExitStack()
    en = ctx.enter_context
    t_in = en(nc.sbuf_tensor("p0_in", [128, 4 * D], F32))
    t_st = en(nc.sbuf_tensor("p0_stage", [128, 4 * D], F32))
    t_z = en(nc.sbuf_tensor("p0_z", [128, NC], F32))
    t_zb = en(nc.sbuf_tensor("p0_zb", [128, NC], BF16))
    ps = [en(nc.psum_tensor(f"p0_ps{i}", [128, 512], F32)) for i in range(4)]

    ld4 = [Ctr.get(en_sem, nc, f"p0_ld{i}") for i in range(4)]
    tp = Ctr.get(en_sem, nc, "p0_tp")
    evv = Ctr.get(en_sem, nc, "p0_evv")
    stf4 = [Ctr.get(en_sem, nc, f"p0_stf{i}") for i in range(4)]
    stb4 = [Ctr.get(en_sem, nc, f"p0_stb{i}") for i in range(4)]
    stz = Ctr.get(en_sem, nc, "p0_stz")

    NT = T // 128

    mz = Ctr.get(en_sem, nc, "p0_mz")
    with nc.Block() as blk:
        @blk.gpsimd
        def _(g):
            g.memset(t_z[:, :], 0.0)
            g.memset(t_zb[:, :], 0.0).then_inc(mz.sem, 1)
            for tt in range(NT):
                if tt % 2 == 0:
                    if tt >= 4:
                        g.wait_ge(tp.sem, (tt - 3) * NC)
                    g.dma_start(t_in[:, (tt % 4) * D:(tt % 4) * D + D],
                                resid[tt * 128:(tt + 1) * 128, :]).then_inc(ld4[tt % 4].sem, 16)
                    ld4[tt % 4].val += 16
                if tt >= 1:
                    g.wait_ge(evv.sem, tt * NC)
                    g.dma_start(xb0[:, 1 + (tt - 1) * 128:1 + tt * 128]
                                .rearrange("(c p) j -> p c j", p=128),
                                t_st[:, ((tt - 1) % 4) * D:((tt - 1) % 4) * D + D]
                                .rearrange("p (c j) -> p c j", c=NC)
                                ).then_inc(stb4[(tt - 1) % 4].sem, 16)
                    stb4[(tt - 1) % 4].val += 16
            g.wait_ge(evv.sem, NT * NC)
            g.dma_start(xb0[:, 1 + (NT - 1) * 128:1 + NT * 128]
                        .rearrange("(c p) j -> p c j", p=128),
                        t_st[:, ((NT - 1) % 4) * D:((NT - 1) % 4) * D + D]
                        .rearrange("p (c j) -> p c j", c=NC)).then_inc(stb4[(NT - 1) % 4].sem, 16)
            stb4[(NT - 1) % 4].val += 16
            for i in range(4):
                g.wait_ge(stb4[i].sem, stb4[i].val)

        @blk.tensor
        def _(t):
            for tt in range(NT):
                t.wait_ge(ld4[tt % 4].sem, (tt // 4 + 1) * 16)
                for c in range(NC):
                    gi = tt * NC + c
                    if gi >= 4:
                        t.wait_ge(evv.sem, gi - 3)
                    t.transpose(ps[gi % 4][:, 0:128],
                                t_in[:, (tt % 4) * D + c * 128:(tt % 4) * D + c * 128 + 128],
                                ident_f32[:, :]).then_inc(tp.sem, 1)
            tp.val += NT * NC

        @blk.vector
        def _(v):
            for tt in range(NT):
                if tt >= 4:
                    v.wait_ge(stf4[tt % 4].sem, ((tt - 4) // 4 + 1) * 16)
                    v.wait_ge(stb4[tt % 4].sem, ((tt - 4) // 4 + 1) * 16)
                for c in range(NC):
                    gi = tt * NC + c
                    v.wait_ge(tp.sem, gi + 1)
                    v.tensor_copy(t_st[:, (tt % 4) * D + c * 128:(tt % 4) * D + c * 128 + 128],
                                  ps[gi % 4][:, 0:128]).then_inc(evv.sem, 1)
            evv.val += NT * NC

        @blk.sync
        def _(sy):
            def load_odd(tt):
                if tt >= 4:
                    sy.wait_ge(tp.sem, (tt - 3) * NC)
                sy.dma_start(t_in[:, (tt % 4) * D:(tt % 4) * D + D],
                             resid[tt * 128:(tt + 1) * 128, :]).then_inc(ld4[tt % 4].sem, 16)
                ld4[tt % 4].val += 16

            load_odd(1)
            load_odd(3)
            sy.wait_ge(mz.sem, 1)
            for xf_t in [xf0] + list(xf_more):
                for c in range(NC):
                    sy.dma_start(xf_t[c * 128:(c + 1) * 128, 0:1], t_z[:, c:c + 1],
                                 allow_slow_non_contiguous=True).then_inc(stz.sem, 16)
                    stz.val += 16
            for xb_t in [xb0] + list(xb_more):
                for c in range(NC):
                    sy.dma_start(xb_t[c * 128:(c + 1) * 128, 0:1], t_zb[:, c:c + 1],
                                 allow_slow_non_contiguous=True).then_inc(stz.sem, 16)
                    stz.val += 16
            for tt in range(NT):
                nxt = tt + 4
                if nxt % 2 == 1 and nxt < NT:
                    load_odd(nxt)
                sy.wait_ge(evv.sem, (tt + 1) * NC)
                sy.dma_start(
                    xf0[:, 1 + tt * 128:1 + (tt + 1) * 128].rearrange("(c p) j -> p c j", p=128),
                    t_st[:, (tt % 4) * D:(tt % 4) * D + D].rearrange("p (c j) -> p c j", c=NC)
                    ).then_inc(stf4[tt % 4].sem, 16)
                stf4[tt % 4].val += 16
            sy.wait_ge(stz.sem, stz.val)
            for i in range(4):
                sy.wait_ge(stf4[i].sem, stf4[i].val)
    ctx.close()


def _p6_final(nc, en_sem, xfa, xfb, out, ident_f32, ones_bf):
    ctx = ExitStack()
    en = ctx.enter_context
    s_a = en(nc.sbuf_tensor("p6_a", [128, 2 * NC * QB], F32))
    s_b = en(nc.sbuf_tensor("p6_b", [128, 2 * NC * QB], F32))
    s_r = en(nc.sbuf_tensor("p6_r", [128, 2 * NC * QB], F32))
    s_rb = en(nc.sbuf_tensor("p6_rb", [128, 2 * NC * QB], BF16))
    s_sq = en(nc.sbuf_tensor("p6_sq", [128, 2 * NC * QB], BF16))
    s_mu = en(nc.sbuf_tensor("p6_mu", [128, 2 * QB], F32))
    s_var = en(nc.sbuf_tensor("p6_var", [128, 2 * QB], F32))
    s_rs = en(nc.sbuf_tensor("p6_rs", [128, 2 * QB], F32))
    s_ot = en(nc.sbuf_tensor("p6_ot", [128, 4 * D], F32))
    ps_s = [en(nc.psum_tensor(f"p6_pss{i}", [128, 512], F32)) for i in range(2)]
    ps_q = [en(nc.psum_tensor(f"p6_psq{i}", [128, 512], F32)) for i in range(2)]
    ps_t = [en(nc.psum_tensor(f"p6_pst{i}", [128, 512], F32)) for i in range(4)]

    ld2 = [Ctr.get(en_sem, nc, f"p6_ld{i}") for i in range(2)]
    vrr = Ctr.get(en_sem, nc, "g1")
    ssq = Ctr.get(en_sem, nc, "g2")
    tst = Ctr.get(en_sem, nc, "g3")
    vvs = Ctr.get(en_sem, nc, "g4")
    srs = Ctr.get(en_sem, nc, "g5")
    vxn = Ctr.get(en_sem, nc, "g6")
    ttp = Ctr.get(en_sem, nc, "g7")
    vev = Ctr.get(en_sem, nc, "g8")
    st = Ctr.get(en_sem, nc, "p6_st")
    vch = Ctr.get(en_sem, nc, "g10")
    sch = Ctr.get(en_sem, nc, "g11")
    b_ld2 = [c.val for c in ld2]
    b_vrr, b_ssq, b_tst = vrr.val, ssq.val, tst.val
    b_vvs, b_srs, b_vxn, b_ttp, b_vev, b_st = (vvs.val, srs.val, vxn.val, ttp.val,
                                               vev.val, st.val)

    with nc.Block() as blk:
        @blk.sync
        def _(sy):
            for qb in range(NQB + 1):
                if qb < NQB:
                    if qb >= 2:
                        sy.wait_ge(vrr.sem, b_vrr + (qb - 1) * NC)
                    xs = (qb % 2) * NC * QB
                    sy.dma_start(s_a[:, xs:xs + NC * QB].rearrange("p (c j) -> p c j", c=NC),
                                 xfa[:, 1 + qb * QB:1 + (qb + 1) * QB].rearrange(
                                     "(c p) j -> p c j", p=128)).then_inc(ld2[qb % 2].sem, 16)
                    sy.dma_start(s_b[:, xs:xs + NC * QB].rearrange("p (c j) -> p c j", c=NC),
                                 xfb[:, 1 + qb * QB:1 + (qb + 1) * QB].rearrange(
                                     "(c p) j -> p c j", p=128)).then_inc(ld2[qb % 2].sem, 16)
                    ld2[qb % 2].val += 32
                if qb >= 1:
                    q0 = qb - 1
                    for half in range(2):
                        sy.wait_ge(vev.sem, b_vev + q0 * 4 * NC + (half + 1) * 2 * NC)
                        sy.dma_start(out[q0 * QB + half * 256:q0 * QB + half * 256 + 256, :]
                                     .rearrange("(i p) j -> p i j", p=128),
                                     s_ot[:, half * 2 * D:(half * 2 + 2) * D]
                                     .rearrange("p (i j) -> p i j", i=2)
                                     ).then_inc(st.sem, 16)
                        st.val += 16
            sy.wait_ge(st.sem, st.val)

        @blk.vector
        def _(v):
            for qb in range(NQB):
                xs = (qb % 2) * NC * QB
                ms = (qb % 2) * QB
                v.wait_ge(ld2[qb % 2].sem, b_ld2[qb % 2] + 32 * (qb // 2 + 1))
                if qb >= 2:
                    v.wait_ge(ttp.sem, b_ttp + (qb - 1) * 4 * NC)
                    v.wait_ge(tst.sem, b_tst + 2 * (qb - 1))
                    v.wait_ge(srs.sem, b_srs + qb - 1)
                for c in range(NC):
                    sl = slice(xs + c * QB, xs + (c + 1) * QB)
                    v.tensor_tensor(s_r[:, sl], s_a[:, sl], s_b[:, sl],
                                    ALU.add).then_inc(vch.sem, 1)
                    vch.val += 1
                    v.wait_ge(vch.sem, vch.val)
                    v.tensor_copy(s_rb[:, sl], s_r[:, sl]).then_inc(vrr.sem, 1)
                    vrr.val += 1
                v.wait_ge(tst.sem, b_tst + 2 * (qb + 1))
                v.tensor_single_scalar(s_mu[:, ms:ms + QB], ps_s[qb % 2][:, :], 1.0 / D,
                                       ALU.mult).then_inc(vch.sem, 1)
                vch.val += 1
                v.wait_ge(vch.sem, vch.val)
                v.tensor_tensor(s_var[:, ms:ms + QB], s_mu[:, ms:ms + QB],
                                s_mu[:, ms:ms + QB], ALU.mult)
                v.tensor_scalar(s_rs[:, ms:ms + QB], ps_q[qb % 2][:, :], 1.0 / D, EPS,
                                ALU.mult, ALU.add).then_inc(vch.sem, 1)
                vch.val += 1
                v.wait_ge(vch.sem, vch.val)
                v.tensor_tensor(s_var[:, ms:ms + QB], s_rs[:, ms:ms + QB],
                                s_var[:, ms:ms + QB], ALU.subtract).then_inc(vvs.sem, 1)
                vvs.val += 1
                v.wait_ge(srs.sem, b_srs + qb + 1)
                for c in range(NC):
                    sl = slice(xs + c * QB, xs + (c + 1) * QB)
                    v.tensor_tensor(s_r[:, sl], s_r[:, sl], s_mu[:, ms:ms + QB],
                                    ALU.subtract).then_inc(vch.sem, 1)
                    vch.val += 1
                    v.wait_ge(vch.sem, vch.val)
                    v.tensor_tensor(s_r[:, sl], s_r[:, sl], s_rs[:, ms:ms + QB],
                                    ALU.mult).then_inc(vxn.sem, 1)
                    vxn.val += 1
                for ttile in range(4):
                    if qb >= 1 and ttile == 0:
                        v.wait_ge(st.sem, b_st + qb * 32)
                    for c in range(NC):
                        g = qb * 4 * NC + ttile * NC + c
                        v.wait_ge(ttp.sem, b_ttp + g + 1)
                        v.tensor_copy(s_ot[:, ttile * D + c * 128:ttile * D + c * 128 + 128],
                                      ps_t[g % 4][:, 0:128]).then_inc(vev.sem, 1)
                        vev.val += 1

        @blk.scalar
        def _(s):
            for qb in range(NQB):
                xs = (qb % 2) * NC * QB
                ms = (qb % 2) * QB
                if qb >= 2:
                    s.wait_ge(tst.sem, b_tst + 2 * (qb - 1))
                for c in range(NC):
                    sl = slice(xs + c * QB, xs + (c + 1) * QB)
                    s.wait_ge(vrr.sem, b_vrr + qb * NC + c + 1)
                    s.activation(s_sq[:, sl], s_rb[:, sl], AF.Square).then_inc(ssq.sem, 1)
                    ssq.val += 1
                s.wait_ge(vvs.sem, b_vvs + qb + 1)
                s.activation(s_rs[:, ms:ms + QB], s_var[:, ms:ms + QB],
                             AF.Ln).then_inc(sch.sem, 1)
                sch.val += 1
                s.wait_ge(sch.sem, sch.val)
                s.activation(s_rs[:, ms:ms + QB], s_rs[:, ms:ms + QB],
                             AF.Exp, scale=-0.5).then_inc(srs.sem, 1)
                srs.val += 1

        @blk.tensor
        def _(t):
            for qb in range(NQB):
                xs = (qb % 2) * NC * QB
                t.wait_ge(vrr.sem, b_vrr + (qb + 1) * NC)
                if qb >= 2:
                    t.wait_ge(vvs.sem, b_vvs + qb - 1)
                for c in range(NC):
                    t.matmul(ps_s[qb % 2][:, :], ones_bf[:, :],
                             s_rb[:, xs + c * QB:xs + (c + 1) * QB],
                             start=(c == 0), stop=(c == NC - 1))
                t.wait_ge(ssq.sem, b_ssq + (qb + 1) * NC)
                for c in range(NC):
                    inst = t.matmul(ps_q[qb % 2][:, :], ones_bf[:, :],
                                    s_sq[:, xs + c * QB:xs + (c + 1) * QB],
                                    start=(c == 0), stop=(c == NC - 1))
                inst.then_inc(tst.sem, 2)
                tst.val += 2
                for ttile in range(4):
                    for c in range(NC):
                        g = qb * 4 * NC + ttile * NC + c
                        if g >= 4:
                            t.wait_ge(vev.sem, b_vev + g - 3)
                        if ttile == 0 and c == 0:
                            t.wait_ge(vxn.sem, b_vxn + (qb + 1) * NC)
                        t.transpose(ps_t[g % 4][:, 0:128],
                                    s_r[:, xs + c * QB + ttile * 128:
                                        xs + c * QB + ttile * 128 + 128],
                                    ident_f32[:, :]).then_inc(ttp.sem, 1)
                        ttp.val += 1
    ctx.close()


_cache = {}


def kernel(**inputs):
    resid_full = np.ascontiguousarray(np.asarray(inputs["L0_attn_resid"], np.float32))
    B = resid_full.shape[0]
    ident = np.eye(128, dtype=np.float32)

    hperm = np.zeros(D, np.int64)
    for h in range(16):
        qch, qhf = (h // 4) * 2 + h % 2, (h // 2) % 2
        hperm[qch * 128 + qhf * 64: qch * 128 + qhf * 64 + 64] = np.arange(h * 64, h * 64 + 64)
    wq = np.ascontiguousarray(np.asarray(inputs["L0_attn_W_Q"], np.float32)
                              .reshape(D, D)[:, hperm])
    wk = np.ascontiguousarray(np.asarray(inputs["L0_attn_W_K"], np.float32).reshape(D, 512))
    wv = np.ascontiguousarray(np.asarray(inputs["L0_attn_W_V"], np.float32).reshape(D, 512))
    wo = np.ascontiguousarray(
        np.asarray(inputs["L0_attn_W_out"], np.float32).transpose(1, 0, 2).reshape(D, D)[hperm, :])
    bout = np.ascontiguousarray(
        np.asarray(inputs["L0_attn_b_out"], np.float32).reshape(NC, 128).T)
    ii = np.arange(128)
    mask_do = np.where(ii[:, None] <= ii[None, :], 0.0, -1e6).astype(ml_dtypes.bfloat16)
    mask_di = np.where(ii[:, None] <= ii[None, :], 0.0, -1e36).astype(ml_dtypes.bfloat16)
    mask_ai = np.where(ii[:, None] >= ii[None, :], 0.0, -1e36).astype(ml_dtypes.bfloat16)

    half = np.arange(0, 64, 2, dtype=np.float64)
    inv_freq = 1.0 / 10000.0 ** (half / 64.0)
    tt_ = np.arange(T, dtype=np.float64)
    freqs = np.concatenate([tt_[:, None] * inv_freq, tt_[:, None] * inv_freq], axis=1)
    base = np.concatenate([(half + 25.6) / 89.6] * 2)
    power = (tt_ - T // 2) / 256.0
    scale = base[None, :] ** power[:, None]
    cosf, sinf = np.cos(freqs), np.sin(freqs)
    t2 = lambda a: np.ascontiguousarray(np.tile(a.T, (2, 1)).astype(np.float32))
    tab_cs = t2(cosf * scale)
    tab_sn = t2(sinf * scale)
    tab_ci = t2(cosf / scale)
    tab_si = t2(sinf / scale)

    def scol_for(l):
        qs = np.asarray(inputs["L0_ffn_q_scale"], np.float32)[l]
        ks = np.asarray(inputs["L0_ffn_k_scale"], np.float32)[l]
        rot = lambda a: np.concatenate([a[32:], a[:32]])
        return np.ascontiguousarray(np.stack(
            [np.tile(qs, 2), np.tile(rot(qs), 2), np.tile(ks, 2), np.tile(rot(ks), 2)],
            axis=1))

    r2 = np.zeros((128, 128), np.float32)
    for b0 in (0, 64):
        for i in range(32):
            r2[b0 + 32 + i, b0 + i] = -1.0
            r2[b0 + i, b0 + 32 + i] = 1.0
    bdsel = np.zeros((128, 32), np.float32)
    for c in range(4):
        for p in range(128):
            bdsel[p, c * 8 + 2 * c + p // 64] = 1.0
    bdsel = bdsel.astype(ml_dtypes.bfloat16)
    selbc = np.zeros((128, 512), np.float32)
    for c in range(4):
        for p in range(128):
            selbc[2 * c + p // 64, c * 128 + p] = 1.0

    def colpad(v):
        out = np.zeros((128, 22), np.float32)
        for j in range(22):
            w = min(128, 2730 - j * 128)
            out[:w, j] = v[j * 128:j * 128 + w]
        return out

    nc = build_program()
    in_maps = []
    for b in range(B):
        in_maps.append(dict(
            resid=np.ascontiguousarray(resid_full[b]),
            ident=ident,
            wq=wq, wk=wk, wv=wv, wo=wo, bout=bout, mask_diag_o=mask_do,
            wqkv0=np.ascontiguousarray(np.asarray(inputs["L0_ffn_wqkv"], np.float32)[0]),
            wqkv1=np.ascontiguousarray(np.asarray(inputs["L0_ffn_wqkv"], np.float32)[1]),
            wo_i0=np.ascontiguousarray(np.asarray(inputs["L0_ffn_wo"], np.float32)[0]),
            wo_i1=np.ascontiguousarray(np.asarray(inputs["L0_ffn_wo"], np.float32)[1]),
            scol0=scol_for(0), scol1=scol_for(1),
            tab_cs=tab_cs, tab_sn=tab_sn, tab_ci=tab_ci, tab_si=tab_si,
            r2=r2, bdsel=bdsel, selbc=selbc, mask_di=mask_di, mask_ai=mask_ai,
            **{f"w1_{l}": np.ascontiguousarray(np.asarray(inputs["L0_ffn_w1"], np.float32)[l])
               for l in range(2)},
            **{f"w2_{l}": np.ascontiguousarray(np.asarray(inputs["L0_ffn_w2"], np.float32)[l])
               for l in range(2)},
            **{f"b1a_{l}": colpad(np.asarray(inputs["L0_ffn_b1"], np.float32)[l][:2730])
               for l in range(2)},
            **{f"b1g_{l}": colpad(np.asarray(inputs["L0_ffn_b1"], np.float32)[l][2730:])
               for l in range(2)},
            **{f"b2_{l}": np.ascontiguousarray(
                np.asarray(inputs["L0_ffn_b2"], np.float32)[l].reshape(NC, 128).T)
               for l in range(2)},
            **{f"lng_{l}": np.ascontiguousarray(
                np.asarray(inputs["L0_ffn_ln_g"], np.float32)[l].reshape(NC, 128).T)
               for l in range(2)},
            **{f"lnb_{l}": np.ascontiguousarray(
                np.asarray(inputs["L0_ffn_ln_b"], np.float32)[l].reshape(NC, 128).T)
               for l in range(2)},
        ))
    res = run_bass_kernel_spmd(nc, in_maps, core_ids=list(range(B)))
    if DEBUG_ZT:
        kernel._dbg_zt = np.stack([np.asarray(r["dbg_zt"], np.float32) for r in res.results])
    return np.stack([np.asarray(r["out"], np.float32) for r in res.results], axis=0)
